# version 14
# baseline (speedup 1.0000x reference)
import math
import numpy as np
import concourse.bass as bass
import concourse.mybir as mybir
from concourse.bass_utils import run_bass_kernel_spmd

F32 = mybir.dt.float32
BF16 = mybir.dt.bfloat16
I32 = mybir.dt.int32
AF = mybir.ActivationFunctionType
ALU = mybir.AluOpType
IOX = bass.IndirectOffsetOnAxis

EPS = 1e-5
PI = math.pi

CFG = dict(T=4096, D=1024, NH=8, DH=64, NE=16, CAP=256, DFF=2048, RH=128,
           QB=512, BIS=48)


class Prog:
    ENGINES = ("sync", "gpsimd", "vector", "scalar", "tensor")

    def __init__(self, nc):
        self.nc = nc
        self.ops = {e: [] for e in self.ENGINES}
        self.sems = {}
        self.counts = {}

    def _sem(self, name):
        if name not in self.sems:
            self.sems[name] = self.nc.alloc_semaphore(name)
            self.counts[name] = 0

    def emit(self, eng, fn, inc=None, inck=1, waits=()):
        w = []
        for s, v in waits:
            if v > 0:
                self._sem(s)
                w.append((s, v))
        v = None
        if inc is not None:
            self._sem(inc)
            self.counts[inc] += inck
            v = self.counts[inc]
        self.ops[eng].append((fn, inc, inck, w))
        return v

    def val(self, name):
        return self.counts.get(name, 0)

    def realize(self):
        with self.nc.Block() as block:
            for eng_name in self.ENGINES:
                ops = self.ops[eng_name]
                if not ops:
                    continue

                def body(eng, ops=ops):
                    for fn, inc, inck, waits in ops:
                        for s, v in waits:
                            eng.wait_ge(self.sems[s], v)
                        instr = fn(eng) if fn is not None else None
                        if inc is not None:
                            instr.then_inc(self.sems[inc], inck)
                getattr(block, eng_name)(body)


DISABLE_B1_INJECT = False


def build(cfg=CFG, debug=False):
    T, D, NH, DH = cfg["T"], cfg["D"], cfg["NH"], cfg["DH"]
    NE, CAP, DFF, RH = cfg["NE"], cfg["CAP"], cfg["DFF"], cfg["RH"]
    QB, BIS = cfg["QB"], cfg["BIS"]
    NTT = T // 128
    NQB = T // QB
    NKC = T // 128
    NDC = D // 128
    DM = NH * DH
    NMC = DM // 128
    NFC = DFF // 128
    NQ4 = DFF // 512
    EM = NE // 2
    CAPH = CAP // 128
    TH = T // 2
    NFT = TH // 128
    SUB = 128 // NE
    CPS = T // SUB
    NTC = T // 128
    VX = DH + 2
    NQC = QB // 512
    NT2 = T // 512

    nc = bass.Bass("TRN2", target_bir_lowering=False)

    x_in = nc.dram_tensor("x", [T, D], F32, kind="ExternalInput").ap()
    wi_in = nc.dram_tensor("wi", [D, DM], F32, kind="ExternalInput").ap()
    wo_in = nc.dram_tensor("wo", [D, DM], F32, kind="ExternalInput").ap()
    wv_in = nc.dram_tensor("wv", [D, DM], F32, kind="ExternalInput").ap()
    wout_in = nc.dram_tensor("wout", [DM, D], F32, kind="ExternalInput").ap()
    wa_in = nc.dram_tensor("wa", [D, NH], F32, kind="ExternalInput").ap()
    we_in = nc.dram_tensor("we", [D, NH], F32, kind="ExternalInput").ap()
    ba_in = nc.dram_tensor("ba", [1, NH], F32, kind="ExternalInput").ap()
    be_in = nc.dram_tensor("be", [1, NH], F32, kind="ExternalInput").ap()
    chp_in = nc.dram_tensor("chp", [1, 4 * NH], F32, kind="ExternalInput").ap()
    rw1_in = nc.dram_tensor("rw1", [D, RH], F32, kind="ExternalInput").ap()
    rw2_in = nc.dram_tensor("rw2", [RH, NE], F32, kind="ExternalInput").ap()
    w1_in = nc.dram_tensor("w1", [EM, D, DFF], F32, kind="ExternalInput").ap()
    w2_in = nc.dram_tensor("w2", [EM, D, DFF], F32, kind="ExternalInput").ap()
    w3_in = nc.dram_tensor("w3", [EM, DFF, D], F32, kind="ExternalInput").ap()
    tkb_in = nc.dram_tensor("tkb", [128, 1], F32, kind="ExternalInput").ap()
    myexp_in = nc.dram_tensor("myexp", [EM, 1], I32, kind="ExternalInput").ap()
    oh8_in = nc.dram_tensor("oh8", [SUB, NH * 64], F32, kind="ExternalInput").ap()
    l8_in = nc.dram_tensor("l8", [128, 128], F32, kind="ExternalInput").ap()
    bc16_in = nc.dram_tensor("bc16", [NE, 128], F32, kind="ExternalInput").ap()
    sel16_in = nc.dram_tensor("sel16", [128, NE], F32, kind="ExternalInput").ap()
    g128_in = nc.dram_tensor("g128", [128, 128], F32, kind="ExternalInput").ap()
    idn_in = nc.dram_tensor("idn", [128, 128], F32, kind="ExternalInput").ap()
    rep8_in = nc.dram_tensor("rep8", [8, 64], F32, kind="ExternalInput").ap()
    sub8_in = nc.dram_tensor("sub8", [64, 1], F32, kind="ExternalInput").ap()
    out_d = nc.dram_tensor("out", [TH, D], F32, kind="ExternalOutput").ap()

    xln_d = nc.dram_tensor("xln_d", [T, D], BF16, kind="Internal").ap()
    agin1 = nc.dram_tensor("agin1", [T, D], BF16, kind="Internal")
    agout1 = nc.dram_tensor("agout1", [2 * T, D], BF16, kind="Internal")
    x1_d = nc.dram_tensor("x1_d", [T, D], F32, kind="Internal").ap()
    h_d = nc.dram_tensor("h_d", [T, D], BF16, kind="Internal").ap()
    a_d = nc.dram_tensor("a_d", [NH, T], F32, kind="Internal").ap()
    lgt_d = nc.dram_tensor("lgt_d", [NE, T], F32, kind="Internal").ap()
    gmx_d = nc.dram_tensor("gmx_d", [128, 1], F32, kind="Internal").ap()
    selcw_d = nc.dram_tensor("selcw_d", [3, NE, T], F32, kind="Internal").ap()
    agin2 = nc.dram_tensor("agin2", [EM * CAP, 1032], F32, kind="Internal")
    agout2 = nc.dram_tensor("agout2", [NE * CAP, 1032], F32, kind="Internal")
    acc_d = nc.dram_tensor("acc_d", [TH, D], F32, kind="Internal").ap()

    dbg = {}
    if debug:
        for nm, shp in (("dbg_conv", [128, 2048]), ("dbg_rby", [NH, 512]),
                        ("dbg_idxw", [128, 2 * EM * CAPH]),
                        ("dbg_xe", [128, 2048]), ("dbg_gv", [128, CAP]),
                        ("dbg_xln", [T, D]), ("dbg_p1", [2 * T, D]),
                        ("dbg_x1", [T, D]), ("dbg_probs", [128, CPS]),
                        ("dbg_sel", [128, CPS]), ("dbg_agout2", [NE * CAP, 1032]),
                        ("dbg_acc", [TH, D])):
            dt = BF16 if nm in ("dbg_xln", "dbg_p1", "dbg_xe", "dbg_gv") else F32
            if nm in ("dbg_idxw", "dbg_rby"):
                dt = F32
            if nm == "dbg_conv":
                dt = BF16
            dbg[nm] = nc.dram_tensor(nm, shp, dt, kind="ExternalOutput").ap()

    NB = 75264
    NF = 10520
    AB = nc.alloc_sbuf_tensor("AB", [128, NB], BF16)
    AFR = nc.alloc_sbuf_tensor("AFR", [128, NF], F32)

    def bv(off, n, p0=0, p1=128):
        return AB[p0:p1, off:off + n]

    def fv(off, n, p0=0, p1=128):
        return AFR[p0:p1, off:off + n]

    O_INNERT = 0
    O_OUTERT = O_INNERT + NMC * T
    O_VX = O_OUTERT + NMC * T
    O_W = O_VX + NTT * VX * NH
    O_XLT = O_W + 3 * NDC * 512
    O_WOUT = O_XLT + NDC * 512
    O_STG = O_WOUT + NMC * 1024
    assert O_STG + 1024 <= NB, O_STG
    NPH_ = NH // 2
    O_COMBT = O_W
    O_UT = O_COMBT + 2 * NPH_ * QB
    assert O_UT + 2 * (2 * QB) <= O_XLT
    O_CONV = O_XLT
    B_HT = 0
    B_R1 = B_HT + NDC * 512
    B_XE = B_R1 + 512
    B_XET = B_XE + CAPH * 1024
    B_W1 = B_XET + 2 * NDC * CAP
    B_W2 = B_W1 + 2 * NDC * 512
    B_W3 = B_W2 + 2 * NDC * 512
    B_RW1 = B_W3 + 2 * 4 * 1024
    B_RW2 = B_RW1 + NDC * RH
    B_PAB = B_RW2 + NE
    B_HS = B_PAB + 4 * 1024
    B_GV = B_HS + 1024
    assert B_GV + 256 <= NB, B_GV

    F_XT = 0
    F_SA = 2048
    F_SB = 3072
    F_T1 = 4096
    F_T2 = 5120
    F_T3 = 6144
    F_T4 = 7168
    F_T5 = 8192
    F_SLOT = 9216
    F_COLS = 10300
    assert F_COLS + 220 <= NF

    iotas = nc.alloc_sbuf_tensor("iotas", [128, CAP], F32)
    tokc = nc.alloc_sbuf_tensor("tokc", [128, NTC], F32)
    l8 = nc.alloc_sbuf_tensor("l8s", [128, 128], F32)
    idn = nc.alloc_sbuf_tensor("idns", [128, 128], F32)
    g128 = nc.alloc_sbuf_tensor("g128s", [128, 128], F32)
    rep8 = nc.alloc_sbuf_tensor("rep8s", [8, 64], F32)
    sub8 = nc.alloc_sbuf_tensor("sub8s", [64, 1], F32)
    idx64 = nc.alloc_sbuf_tensor("idx64s", [64, 1], F32)
    idx64i = nc.alloc_sbuf_tensor("idx64is", [64, 3], I32)
    myexpf = nc.alloc_sbuf_tensor("myexpfs", [8, 1], F32)
    pihalf = nc.alloc_sbuf_tensor("pihalfs", [1, 1], F32)
    bc16 = nc.alloc_sbuf_tensor("bc16s", [NE, 128], F32)
    sel16 = nc.alloc_sbuf_tensor("sel16s", [128, NE], F32)
    oh8 = nc.alloc_sbuf_tensor("oh8s", [SUB, NH * 64], F32)
    ones = nc.alloc_sbuf_tensor("oness", [1, 512], F32)
    tkbase = nc.alloc_sbuf_tensor("tkbs", [128, 1], F32)
    iotac = nc.alloc_sbuf_tensor("iotacs", [128, 1], F32)
    chpf = nc.alloc_sbuf_tensor("chpfs", [1, 4 * NH], F32)
    casa = nc.alloc_sbuf_tensor("casas", [1, 2 * NH], F32)
    cb64 = nc.alloc_sbuf_tensor("cb64s", [64, 2 * NH], F32)
    baf = nc.alloc_sbuf_tensor("bafs", [1, NH], F32)
    bef = nc.alloc_sbuf_tensor("befs", [1, NH], F32)
    wa_sb = nc.alloc_sbuf_tensor("wa_sbs", [128, NDC * NH], BF16)
    we_sb = nc.alloc_sbuf_tensor("we_sbs", [128, NDC * NH], BF16)
    myexp_sb = nc.alloc_sbuf_tensor("myexps", [EM, 1], I32)
    iotI = nc.alloc_sbuf_tensor("iotI", [128, CAP], I32)
    iotI2 = nc.alloc_sbuf_tensor("iotI2", [128, NTC], I32)
    idxi = nc.alloc_sbuf_tensor("idxi", [128, EM * CAPH + 4], I32)
    idxw = nc.alloc_sbuf_tensor("idxw", [128, EM * CAPH * 2], F32)
    sgm = nc.alloc_sbuf_tensor("sgms", [1, 4 * NH], F32)
    tctmp = nc.alloc_sbuf_tensor("tctmp", [1, 2 * NH], F32)
    estg = nc.alloc_sbuf_tensor("estgs", [128, NH], F32)
    a_stg = nc.alloc_sbuf_tensor("a_stgs", [NH, 512], F32)
    selTm = nc.alloc_sbuf_tensor("selTm", [128, 512], F32)
    ceTm = nc.alloc_sbuf_tensor("ceTm", [128, 512], F32)
    wTm = nc.alloc_sbuf_tensor("wTm", [128, 512], F32)
    pselb = nc.alloc_sbuf_tensor("pselb", [128, CAP], F32)
    rhsa = nc.alloc_sbuf_tensor("rhsa", [128, 2 * NTC], F32)
    g16 = nc.alloc_sbuf_tensor("g16", [NE, 24], F32)

    psA = nc.alloc_psum_tensor("psA", [128, 1024], F32)
    psB = nc.alloc_psum_tensor("psB", [128, 1024], F32)
    psC = nc.alloc_psum_tensor("psC", [128, 1024], F32)
    psD = nc.alloc_psum_tensor("psD", [128, 1024], F32)

    P = Prog(nc)
    E = P.emit
    V = P.val

    def vop(fn, waits=()):
        return E("vector", fn, inc="vec", waits=list(waits) + [("vec", V("vec"))])

    def sop(fn, waits=()):
        return E("scalar", fn, inc="scc", waits=list(waits) + [("scc", V("scc"))])

    mcol = fv(F_COLS + 0, 1)
    vcol = fv(F_COLS + 1, 1)
    m2col = fv(F_COLS + 2, 1)
    rstd = fv(F_COLS + 3, 1)
    sxc = fv(F_COLS + 4, 1)
    sxxc = fv(F_COLS + 5, 1)
    mxc = fv(F_COLS + 6, 1)
    gcol = fv(F_COLS + 7, 1)
    scol = fv(F_COLS + 8, 1)
    pfxc = fv(F_COLS + 9, 1)
    rcol = fv(F_COLS + 10, 1)
    vcol2 = fv(F_COLS + 11, 1)
    dcol = fv(F_COLS + 12, 1)
    idf = fv(F_COLS + 13, 1)
    idm = fv(F_COLS + 14, 1)
    a_sxc = fv(F_COLS + 40, 1)
    a_sxxc = fv(F_COLS + 41, 1)
    a_m = fv(F_COLS + 42, 1)
    a_m2 = fv(F_COLS + 43, 1)
    a_nm2 = fv(F_COLS + 44, 1)
    a_v = fv(F_COLS + 45, 1)
    a_r = fv(F_COLS + 46, 1)
    a_eps = fv(F_COLS + 47, 1)
    vlnc = [[fv(F_COLS + 50 + 8 * b + i, 1) for i in range(6)] for b in (0, 1)]

    for dst, src in ((baf[:], ba_in), (bef[:], be_in), (chpf[:], chp_in),
                     (tkbase[:], tkb_in), (oh8[:], oh8_in), (l8[:], l8_in),
                     (bc16[:], bc16_in), (sel16[:], sel16_in), (idn[:], idn_in),
                     (g128[:], g128_in),
                     (rep8[:], rep8_in), (sub8[:], sub8_in)):
        E("sync", lambda e, dst=dst, src=src: e.dma_start(out=dst, in_=src),
          inc="dsm", inck=16)
    E("gpsimd", lambda e: e.dma_start(
        out=wa_sb[:].rearrange("p (c n) -> p c n", c=NDC), in_=wa_in.rearrange("(c p) n -> p c n", p=128)),
      inc="dwA", inck=16)
    E("gpsimd", lambda e: e.dma_start(
        out=we_sb[:].rearrange("p (c n) -> p c n", c=NDC), in_=we_in.rearrange("(c p) n -> p c n", p=128)),
      inc="dwA", inck=16)
    E("sync", lambda e: e.dma_start(out=myexp_sb[:], in_=myexp_in),
      inc="dsm", inck=16)
    v_dsm = V("dsm")

    E("gpsimd", lambda e: e.iota(iotI[:], pattern=[[1, CAP]], base=0,
                                 channel_multiplier=0), inc="gpc")
    E("gpsimd", lambda e: e.tensor_copy(iotas[:], iotI[:]), inc="gpc",
      waits=[("gpc", V("gpc"))])
    E("gpsimd", lambda e: e.iota(iotI2[:], pattern=[[128, NTC]], base=0,
                                 channel_multiplier=1), inc="gpc",
      waits=[("gpc", V("gpc"))])
    E("gpsimd", lambda e: e.tensor_copy(tokc[:], iotI2[:]), inc="gpc",
      waits=[("gpc", V("gpc"))])
    E("gpsimd", lambda e: e.iota(iotI2[:, 0:1], pattern=[[1, 1]], base=0,
                                 channel_multiplier=1), inc="gpc",
      waits=[("gpc", V("gpc"))])
    E("gpsimd", lambda e: e.tensor_copy(iotac[:], iotI2[:, 0:1]), inc="gpc",
      waits=[("gpc", V("gpc"))])
    E("gpsimd", lambda e: e.memset(ones[:], 1.0), inc="gpc",
      waits=[("gpc", V("gpc"))])
    E("gpsimd", lambda e: e.memset(pihalf[:], float(PI / 2)), inc="gpc",
      waits=[("gpc", V("gpc"))])
    E("gpsimd", lambda e: e.memset(a_eps, float(1e-5)), inc="gpc",
      waits=[("gpc", V("gpc"))])
    v_gpc0 = V("gpc")

    sop(lambda e: e.activation(sgm[:, 0:3 * NH], chpf[:, 0:3 * NH], AF.Sigmoid),
        waits=[("dsm", v_dsm)])
    sop(lambda e: e.activation(sgm[:, 3 * NH:4 * NH], chpf[:, 3 * NH:4 * NH], AF.Tanh))
    sop(lambda e: e.activation(tctmp[:, NH:2 * NH], sgm[:, 0:NH], AF.Sin,
                               scale=float(PI)))
    sop(lambda e: e.activation(tctmp[:, 0:NH], sgm[:, 0:NH], AF.Sin,
                               scale=float(-PI), bias=pihalf[:]),
        waits=[("gpc", V("gpc"))])
    v_sc_ch = V("scc")
    vop(lambda e: e.tensor_mul(sgm[:, NH:2 * NH], sgm[:, NH:2 * NH],
                               sgm[:, 2 * NH:3 * NH]), waits=[("scc", v_sc_ch)])
    vop(lambda e: e.tensor_mul(sgm[:, NH:2 * NH], sgm[:, NH:2 * NH],
                               sgm[:, 3 * NH:4 * NH]))
    vop(lambda e: e.tensor_mul(casa[:, 0:NH], tctmp[:, 0:NH], sgm[:, NH:2 * NH]))
    v_casa = vop(lambda e: e.tensor_mul(casa[:, NH:2 * NH], tctmp[:, NH:2 * NH],
                                        sgm[:, NH:2 * NH]))
    E("tensor", lambda e: e.matmul(psB[0:64, 0:2 * NH], ones[0:1, 0:64], casa[:],
                                   start=True, stop=True),
      inc="tdc", waits=[("vec", v_casa), ("gpc", v_gpc0)])
    v_cb64 = vop(lambda e: e.tensor_copy(cb64[:], psB[0:64, 0:2 * NH]),
                 waits=[("tdc", V("tdc"))])

    wi_sb = bv(O_W, NDC * 512)
    wo_sb = bv(O_W + NDC * 512, NDC * 512)
    wv_sb = bv(O_W + 2 * NDC * 512, NDC * 512)
    wout_sb = bv(O_WOUT, NMC * 1024)
    v_dw = {}
    for nm, dst, src in (("wi", wi_sb, wi_in), ("wo", wo_sb, wo_in),
                         ("wv", wv_sb, wv_in)):
        E("gpsimd", lambda e, dst=dst, src=src: e.dma_start(
            out=dst.rearrange("p (c n) -> p c n", c=NDC), in_=src.rearrange("(c p) n -> p c n", p=128)),
          inc="d" + nm, inck=16)
        v_dw[nm] = ("d" + nm, V("d" + nm))
    v_dwA = V("dwA")

    xt = [fv(F_XT, 1024), fv(F_XT + 1024, 1024)]
    sA = fv(F_SA, 1024)
    sB = fv(F_SB, 1024)
    xstg = [bv(O_WOUT, 1024), bv(O_WOUT + 1024, 1024)]

    ln_war = [0]

    def ln_emit(src, dst, extra_w=(), dst_w=()):
        s0 = sop(lambda e: e.activation(sB, src, AF.Identity, accum_out=a_sxc),
                 waits=list(extra_w) + [("vec", ln_war[0]), ("gpc", v_gpc0)])
        sop(lambda e: e.activation(sB, src, AF.Square, accum_out=a_sxxc))
        sop(lambda e: e.activation(a_m, a_sxc, AF.Identity, scale=1.0 / 1024.0))
        sop(lambda e: e.activation(a_m2, a_m, AF.Square))
        sop(lambda e: e.activation(a_nm2, a_m2, AF.Identity, scale=-1.0))
        sop(lambda e: e.activation(a_v, a_sxxc, AF.Identity,
                                   scale=1.0 / 1024.0, bias=a_nm2))
        sop(lambda e: e.activation(a_r, a_v, AF.Ln, bias=a_eps))
        s3 = sop(lambda e: e.activation(a_r, a_r, AF.Exp, scale=-0.5))
        vn = vop(lambda e: e.tensor_scalar(dst, src, a_m, a_r,
                                           ALU.subtract, ALU.mult),
                 waits=list(dst_w) + [("scc", s3)])
        ln_war[0] = vn
        return vn

    def emit_a1_tile(t):
        b = t % 2
        E("sync", lambda e, t=t, b=b: e.dma_start(
            out=xt[b], in_=x_in[t * 128:(t + 1) * 128, :]),
          inc=f"dx{b}", inck=16,
          waits=[(f"xf{b}", P.counts.get(f"xf{b}", 0))])
        vl = V(f"dx{b}")
        vn = ln_emit(xt[b], xstg[b],
                     extra_w=[(f"dx{b}", vl)],
                     dst_w=[(f"xs{b}", P.counts.get(f"xs{b}", 0))])
        E("vector", lambda e: e.nop(), inc=f"xf{b}", waits=[("vec", vn)])
        E("sync", lambda e, t=t, b=b: e.dma_start(
            out=xln_d[t * 128:(t + 1) * 128, :], in_=xstg[b]),
          inc=f"xs{b}", inck=16, waits=[("vec", vn)])

    xlt = [bv(O_XLT, NDC * 512), bv(O_WOUT + 2048, NDC * 512)]
    vx = bv(O_VX, NTT * VX * NH)
    innerT = bv(O_INNERT, NMC * T)
    outerT = bv(O_OUTERT, NMC * T)
    E("gpsimd", lambda e: e.memset(vx, 1.0), inc="gpc",
      waits=[("gpc", V("gpc"))])
    v_vxones = V("gpc")

    for t in range(8):
        emit_a1_tile(t)
    xlt_war = [0, 0]
    a2war = [0, 0, 0, 0]
    a2warv = [0, 0]
    sg_war = [0]
    ps_rot = [psC[:, 0:512], psC[:, 512:1024], psD[:, 0:512], psD[:, 512:1024]]

    def emit_xlt_trans(tc):
        tp = tc % 2
        for dc in range(NDC):
            E("sync", lambda e, tc=tc, dc=dc, tp=tp: e.dma_start_transpose(
                xlt[tp][:, dc * 512:(dc + 1) * 512],
                xln_d[tc * 512:(tc + 1) * 512, dc * 128:(dc + 1) * 128]),
              inc=f"tr{tp}", inck=16,
              waits=([("xs0", 16 * (2 * tc + 2)), ("xs1", 16 * (2 * tc + 2)),
                      ("tdc", xlt_war[tp])] if dc == 0 else []))

    emit_xlt_trans(0)
    for tc in range(NT2):
        tp = tc % 2
        vtr = V(f"tr{tp}")
        base_w = [(f"tr{tp}", vtr), ("dsm", v_dsm)]

        gi = 0
        for wnm, wsb, dstT in (("wi", wi_sb, innerT), ("wo", wo_sb, outerT)):
            for j in range(NMC):
                ps_o = ps_rot[gi % 4]
                for dc in range(NDC):
                    E("tensor", lambda e, wsb=wsb, j=j, dc=dc, tp=tp, ps_o=ps_o: e.matmul(
                        ps_o,
                        wsb[:, dc * 512 + j * 128:dc * 512 + (j + 1) * 128],
                        xlt[tp][:, dc * 512:(dc + 1) * 512],
                        start=(dc == 0), stop=(dc == NDC - 1)),
                      inc="tdc" if dc == NDC - 1 else None,
                      waits=(((base_w + [v_dw[wnm]]) if gi % 4 == 0 else [])
                             + [("vec", a2war[gi % 4])])
                      if dc == 0 else [])
                vmm = V("tdc")
                vop(lambda e, dstT=dstT, j=j, tc=tc, ps_o=ps_o: e.tensor_copy(
                    dstT[:, j * T + tc * 512:j * T + (tc + 1) * 512], ps_o),
                    waits=[("tdc", vmm)])
                a2war[gi % 4] = V("vec")
                gi += 1
        for ti in range(4):
            tt = tc * 4 + ti
            for dc in range(NDC):
                E("tensor", lambda e, ti=ti, dc=dc, tp=tp: e.matmul(
                    psB[:, 0:NH],
                    xlt[tp][:, dc * 512 + ti * 128:dc * 512 + (ti + 1) * 128],
                    we_sb[:, dc * NH:(dc + 1) * NH],
                    start=(dc == 0), stop=False),
                  waits=[("scc", sg_war[0]), ("dwA", v_dwA)] if dc == 0 else [])
            E("tensor", lambda e: e.matmul(psB[:, 0:NH], ones[0:1, 0:128],
                                           bef[:], start=False, stop=True),
              inc="tdc")
            vmm = V("tdc")
            vsg = sop(lambda e: e.activation(estg[:], psB[:, 0:NH], AF.Sigmoid),
                      waits=[("tdc", vmm), ("vec", V("vec"))])
            sg_war[0] = vsg
            vp = ti % 2
            for dc in range(NDC):
                E("tensor", lambda e, ti=ti, dc=dc, tp=tp, vp=vp: e.matmul(
                    psA[:, vp * 512:(vp + 1) * 512],
                    xlt[tp][:, dc * 512 + ti * 128:dc * 512 + (ti + 1) * 128],
                    wv_sb[:, dc * 512:(dc + 1) * 512],
                    start=(dc == 0), stop=(dc == NDC - 1)),
                  inc="tdc" if dc == NDC - 1 else None,
                  waits=[("vec", a2warv[vp]), ("gpc", v_vxones),
                         v_dw["wv"]] if dc == 0 else [])
            vmm2 = V("tdc")
            for h in range(NH):
                vop(lambda e, tt=tt, h=h, vp=vp: e.tensor_scalar(
                    vx[:, tt * VX * NH + h * VX:tt * VX * NH + h * VX + DH],
                    psA[:, vp * 512 + h * DH:vp * 512 + (h + 1) * DH],
                    estg[:, h:h + 1], None, ALU.mult),
                    waits=[("tdc", vmm2), ("scc", vsg)])
            vop(lambda e, tt=tt: e.tensor_copy(
                vx[:, tt * VX * NH:(tt + 1) * VX * NH].rearrange(
                    "p (h v) -> p h v", h=NH)[:, :, DH:DH + 1],
                estg[:].rearrange("p (h o) -> p h o", h=NH)),
                waits=[("scc", vsg)])
            a2warv[vp] = V("vec")
        for dc in range(NDC):
            E("tensor", lambda e, dc=dc, tp=tp: e.matmul(
                psB[0:NH, 512:1024],
                wa_sb[:, dc * NH:(dc + 1) * NH],
                xlt[tp][:, dc * 512:(dc + 1) * 512],
                start=(dc == 0), stop=False),
              waits=[("scc", sg_war[0]), ("dwA", v_dwA),
                     ("das", P.counts.get("das", 0))] if dc == 0 else [])
        E("tensor", lambda e: e.matmul(psB[0:NH, 512:1024], baf[:], ones[:],
                                       start=False, stop=True), inc="tdc")
        xlt_war[tp] = V("tdc")
        vmm3 = V("tdc")
        vsa = sop(lambda e: e.activation(a_stg[:], psB[0:NH, 512:1024], AF.Sigmoid),
                  waits=[("tdc", vmm3)])
        sg_war[0] = vsa
        E("sync", lambda e, tc=tc: e.dma_start(
            out=a_d[:, tc * 512:(tc + 1) * 512], in_=a_stg[:]),
          inc="das", inck=16, waits=[("scc", vsa)])
        if tc + 1 < NT2:
            emit_xlt_trans(tc + 1)
        if tc + 2 < NT2:
            for k in range(4):
                emit_a1_tile(4 * (tc + 2) + k)

    a2_tdc = V("tdc")
    E("gpsimd", lambda e: e.dma_start(
        out=wout_sb.rearrange("p (c n) -> p c n", c=NMC), in_=wout_in.rearrange("(c p) n -> p c n", p=128)),
      inc="dwA", inck=16, waits=[("tdc", a2_tdc)])
    v_dwout = V("dwA")
    a2_scc = V("scc")
    a2_vec = V("vec")
    a2_das = V("das")

    NPH = NH // 2
    stp = [psA, psB]
    Ut = [bv(O_UT, 2 * QB), bv(O_UT + 2 * QB, 2 * QB)]
    conv = [bv(O_CONV, NPH * QB), bv(O_CONV + NPH * QB, NPH * QB)]
    combT = [bv(O_COMBT, NPH * QB), bv(O_COMBT + NPH * QB, NPH * QB)]
    exstA = fv(F_T1, QB, 0, 2)
    exstB = fv(F_T1 + QB, QB, 0, 2)
    nzn = fv(F_T2, QB, 0, NH)
    nzz = fv(F_T3, QB, 0, NH)
    rby = fv(F_T4, QB, 0, NH)
    aq = fv(F_T5, QB, 0, NH)
    pstgb = [bv(O_STG, 512), bv(O_STG + 512, 512)]
    AGC = 1024
    NAG = T // AGC
    O_B1 = O_COMBT + 2 * NPH * QB + 2 * (2 * QB)
    pa_s = [bv(O_B1, 1024), bv(O_B1 + 1024, 1024)]
    pb_s = [bv(O_B1 + 2048, 1024), bv(O_B1 + 3072, 1024)]
    hstg = bv(O_B1 + 4096, 1024)
    assert O_B1 + 5120 <= O_XLT
    O_HT = O_STG + 1024
    assert O_HT + NDC * 512 <= NB
    hT = bv(O_HT, NDC * 512)
    r1T = nc.alloc_sbuf_tensor("r1Ts", [128, 256], BF16)[:, :]

    rw1_sb = bv(O_B1 + 5120, NDC * RH)
    rw2_sb = nc.alloc_sbuf_tensor("rw2s", [128, NE], BF16)[:, :]

    def ln_v_stats(src_ap, cols, extra_w=()):
        sx, sxx, m, vv, m2, rs = cols
        vop(lambda e: e.tensor_scalar(sB, src_ap, 1.0, 0.0, ALU.mult, ALU.add,
                                      accum_out=sx),
            waits=list(extra_w) + [("dlg", P.counts.get("dlg", 0))])
        vop(lambda e: e.scalar_tensor_tensor(sB, src_ap, 1.0, src_ap,
                                             ALU.mult, ALU.mult,
                                             accum_out=sxx))
        vop(lambda e: e.tensor_scalar(m, sx, 1.0 / 1024.0, None, ALU.mult))
        vop(lambda e: e.tensor_scalar(vv, sxx, 1.0 / 1024.0, None, ALU.mult))
        vop(lambda e: e.tensor_mul(m2, m, m))
        vop(lambda e: e.tensor_tensor(vv, vv, m2, ALU.subtract))
        return vop(lambda e: e.tensor_scalar(vv, vv, float(EPS), None, ALU.add))

    def ln_v_finish(src_ap, dst, cols, v0, extra_w=()):
        sx, sxx, m, vv, m2, rs = cols
        s2 = sop(lambda e: e.activation(rs, vv, AF.Ln), waits=[("vec", v0)])
        s3 = sop(lambda e: e.activation(rs, rs, AF.Exp, scale=-0.5))
        return vop(lambda e: e.tensor_scalar(dst, src_ap, m, rs,
                                             ALU.subtract, ALU.mult),
                   waits=list(extra_w) + [("scc", s3)])

    sAb = [sA, fv(F_SLOT, 1024)]
    _b1v = {}

    def emit_b1_part1(t):
        b = t % 2
        cch = (t * 128) // AGC
        row = cch * 2 * AGC + (t * 128) % AGC
        E("sync", lambda e, row=row, b=b: e.dma_start(
            out=pa_s[b], in_=agout1.ap()[row:row + 128, :]),
          inc=f"pa{b}", inck=16,
          waits=[("dcc", cch + 1), (f"pf{b}", P.counts.get(f"pf{b}", 0))])
        E("sync", lambda e, row=row, b=b: e.dma_start(
            out=pb_s[b], in_=agout1.ap()[row + AGC:row + AGC + 128, :]),
          inc=f"pa{b}", inck=16)
        E("sync", lambda e, t=t, b=b: e.dma_start(
            out=xt[b], in_=x_in[t * 128:(t + 1) * 128, :]),
          inc=f"dx{b}", inck=16, waits=[(f"xf{b}", P.counts.get(f"xf{b}", 0))])
        vpa = V(f"pa{b}")
        vs = vop(lambda e, b=b: e.tensor_tensor(sAb[b], pa_s[b], pb_s[b],
                                                ALU.add),
                 waits=[(f"pa{b}", vpa), ("dx1", P.counts.get("dx1", 0))])
        E("vector", lambda e: e.nop(), inc=f"pf{b}", waits=[("vec", vs)])
        _b1v[t] = (ln_v_stats(sAb[b], vlnc[b]), V(f"dx{b}"))

    def emit_b1_part2(t):
        b = t % 2
        v0, vdx = _b1v[t]
        ln_v_finish(sAb[b], sB, vlnc[b], v0)
        vx1 = vop(lambda e, b=b: e.tensor_tensor(sAb[b], xt[b], sB, ALU.add),
                  waits=[(f"dx{b}", vdx)])
        E("vector", lambda e: e.nop(), inc=f"xf{b}", waits=[("vec", vx1)])
        E("sync", lambda e, t=t, b=b: e.dma_start(
            out=x1_d[t * 128:(t + 1) * 128, :], in_=sAb[b]),
          inc="dx1", inck=16, waits=[("vec", vx1)])
        _b1v[t] = ln_v_stats(sAb[b], vlnc[b])

    def emit_b1_part3(t):
        b = t % 2
        v0 = _b1v.pop(t)
        vh = ln_v_finish(sAb[b], hstg, vlnc[b], v0,
                         extra_w=[("dht", P.counts.get("dht", 0))])
        E("sync", lambda e, t=t: e.dma_start(
            out=h_d[t * 128:(t + 1) * 128, :], in_=hstg),
          inc="dht", inck=16, waits=[("vec", vh)])

    def emit_b1_tile(t):
        emit_b1_part1(t)
        emit_b1_part2(t)
        emit_b1_part3(t)

    def emit_router_trans(tc, half):
        need_dht = 4 * (tc + 1)
        for dc in range(4 * half, 4 * half + 4):
            E("sync", lambda e, tc=tc, dc=dc: e.dma_start_transpose(
                hT[:, dc * 512:(dc + 1) * 512],
                h_d[tc * 512:(tc + 1) * 512, dc * 128:(dc + 1) * 128]),
              inc="thx", inck=16,
              waits=([("dht", 16 * need_dht), ("tdc", V("tdc"))]
                     if dc == 4 * half else []))

    def emit_router_mm(tc):
        vth = V("thx")
        for dc in range(NDC):
            E("tensor", lambda e, dc=dc: e.matmul(
                psD[:, 512:1024], rw1_sb[:, dc * RH:(dc + 1) * RH],
                hT[:, dc * 512:(dc + 1) * 512],
                start=(dc == 0), stop=(dc == NDC - 1)),
              inc="tdc" if dc == NDC - 1 else None,
              waits=[("thx", vth), ("dwB", P.counts.get("dwB", 0)),
                     ("scc", V("scc")), ("rbu", P.counts.get("rbu", 0)),
                     ("woe", P.counts.get("woe", 0))] if dc == 0 else [])
        for hh in range(2):
            vsi = sop(lambda e, hh=hh: e.activation(
                r1T, psD[:, 512 + 256 * hh:768 + 256 * hh], AF.Sigmoid),
                      waits=[("tdc", V("tdc"))])
            vmu = vop(lambda e, hh=hh: e.tensor_tensor(
                r1T, r1T, psD[:, 512 + 256 * hh:768 + 256 * hh], ALU.mult),
                waits=[("scc", vsi)] + ([("vec", V("vec"))] if hh == 0 else []))
            E("tensor", lambda e, hh=hh: e.matmul(
                psD[0:NE, 512 + 256 * hh:768 + 256 * hh], rw2_sb, r1T,
                start=True, stop=True),
              inc="tdc", waits=[("vec", vmu)])
        vlg = V("tdc")
        vle = vop(lambda e: e.tensor_copy(fv(F_SB, 512, 0, NE), psD[0:NE, 512:1024]),
                  waits=[("tdc", vlg), ("dlg", P.counts.get("dlg", 0))])
        E("vector", lambda e: e.nop(), inc="rtl", waits=[("vec", vle)])
        E("sync", lambda e, tc=tc: e.dma_start(
            out=lgt_d[:, tc * 512:(tc + 1) * 512], in_=fv(F_SB, 512, 0, NE)),
          inc="dlg", inck=16, waits=[("vec", vle)])

    def emit_router_tc(tc):
        emit_router_trans(tc, 0)
        emit_router_trans(tc, 1)
        emit_router_mm(tc)

    E("gpsimd", lambda e: e.dma_start(
        out=rw1_sb.rearrange("p (c n) -> p c n", c=NDC),
        in_=rw1_in.rearrange("(c p) n -> p c n", p=128)),
      inc="dwB", inck=16, waits=[("tdc", a2_tdc)])
    E("gpsimd", lambda e: e.dma_start(out=rw2_sb, in_=rw2_in), inc="dwB", inck=16)

    inject = {}

    def add_inject(qb, p, kc, fn):
        inject.setdefault((qb, p, kc), []).append(fn)

    last_bcast_tdc = [0]

    def emit_chamber_step(qbp, h):
        j = h // 2
        pb_ = 64 * (h % 2)
        cbq = combT[qbp % 2]
        cvq = conv[qbp % 2]
        vb = E("tensor", lambda e, h=h: e.matmul(
            psD[0:64, 512:1024],
            oh8[:, h * 64:(h + 1) * 64],
            rby[:, 0:QB],
            start=True, stop=True),
          inc="tdc",
          waits=[("rch", P.counts.get("rch", 0)),
                 ("rbu", P.counts.get("rbu", 0)),
                 ("woe", P.counts.get("woe", 0)),
                 ("rtl", P.counts.get("rtl", 0))])
        last_bcast_tdc[0] = V("tdc")
        cm = fv(F_SB, 512, 0, 64)
        t1 = fv(F_SB + 512, 512, 0, 32)
        vcm = vop(lambda e, j=j, pb_=pb_, cvq=cvq: e.tensor_tensor(
            cm, cvq[pb_:pb_ + 64, j * QB:(j + 1) * QB],
            psD[0:64, 512:1024][:, 0:QB], ALU.mult),
            waits=[("tdc", vb), ("cve", NPH * (qbp + 1)),
                   ("dlg", P.counts.get("dlg", 0))])
        E("vector", lambda e: e.nop(), inc="rbu", waits=[("vec", vcm)])
        vop(lambda e, h=h: e.tensor_scalar(
            t1, cm[32:64, :], cb64[0:32, NH + h:NH + h + 1], None, ALU.mult),
            waits=[("vec", v_cb64)])
        vop(lambda e, h=h, j=j, pb_=pb_, cbq=cbq: e.scalar_tensor_tensor(
            cbq[pb_:pb_ + 32, j * QB:(j + 1) * QB],
            cm[0:32, :], cb64[0:32, h:h + 1], t1, ALU.mult, ALU.subtract))
        vop(lambda e, h=h: e.tensor_scalar(
            t1, cm[32:64, :], cb64[0:32, h:h + 1], None, ALU.mult))
        v4 = vop(lambda e, h=h, j=j, pb_=pb_, cbq=cbq: e.scalar_tensor_tensor(
            cbq[pb_ + 32:pb_ + 64, j * QB:(j + 1) * QB],
            cm[0:32, :], cb64[0:32, NH + h:NH + h + 1], t1, ALU.mult, ALU.add))
        E("vector", lambda e: e.nop(), inc="chd", waits=[("vec", v4)])

    def emit_wout_group(qbp, g):
        tt, ncol = g // 2, g % 2
        cbq = combT[qbp % 2]
        for dc in range(NMC):
            E("tensor", lambda e, tt=tt, ncol=ncol, dc=dc, cbq=cbq: e.matmul(
                psD[:, 512:1024],
                cbq[:, dc * QB + tt * 128:dc * QB + (tt + 1) * 128],
                wout_sb[:, dc * 1024 + ncol * 512:dc * 1024 + (ncol + 1) * 512],
                start=(dc == 0), stop=(dc == NMC - 1)),
              inc="tdc" if dc == NMC - 1 else None,
              waits=[("chd", P.counts.get("chd", 0)),
                     ("woe", P.counts.get("woe", 0)),
                     ("dwA", v_dwout),
                     ("rtl", P.counts.get("rtl", 0))] if dc == 0 else [])
        vwo = V("tdc")
        sbi = g % 2
        vev = E("vector", lambda e, sbi=sbi: e.tensor_copy(pstgb[sbi], psD[:, 512:1024]),
                inc="woe",
                waits=[("tdc", vwo), ("vec", V("vec")),
                       (f"ps{sbi}", P.counts.get(f"ps{sbi}", 0))])
        E("vector", lambda e: e.nop(), inc="vec", waits=[("woe", vev)])
        E("sync", lambda e, qbp=qbp, tt=tt, ncol=ncol, sbi=sbi: e.dma_start(
            out=agin1.ap()[qbp * QB + tt * 128:qbp * QB + (tt + 1) * 128,
                           ncol * 512:(ncol + 1) * 512],
            in_=pstgb[sbi]),
          inc=f"ps{sbi}", inck=16, waits=[("woe", vev)])

    def emit_ag1(c):
        vps = (V("ps0"), V("ps1"))
        E("gpsimd", lambda e, c=c: e.collective_compute(
            "AllGather", ALU.bypass,
            replica_groups=[[0, 1], [2, 3], [4, 5], [6, 7]],
            ins=[agin1.ap()[c * AGC:(c + 1) * AGC, :].opt()],
            outs=[agout1.ap()[c * 2 * AGC:(c + 1) * 2 * AGC, :].opt()]),
          inc="dcc", waits=[("ps0", vps[0]), ("ps1", vps[1])])

    for qb in range(1, NQB):
        for h in range(NH):
            add_inject(qb, 1, 1 + (h * (NKC - 3)) // NH,
                       lambda qbp=qb - 1, h=h: emit_chamber_step(qbp, h))
        for g in range(2 * (QB // 128)):
            add_inject(qb, 2, 1 + (g * (NKC - 3)) // (2 * (QB // 128)),
                       lambda qbp=qb - 1, g=g: emit_wout_group(qbp, g))
    NSL = NKC - 4
    NS = 4 * NSL

    def slot_of(s):
        return (s // NSL, 2 + s % NSL)

    for c in range(NAG):
        qb_at = 2 * c + 2
        if qb_at < NQB:
            add_inject(qb_at, 3, 0, lambda c=c: emit_ag1(c))
        qb_b1 = 2 * c + 3
        if qb_b1 < NQB and not DISABLE_B1_INJECT:
            for i in range(8):
                t = c * 8 + i
                for k, fn in ((0, emit_b1_part1), (2, emit_b1_part2),
                              (4, emit_b1_part3)):
                    p_, kc_ = slot_of(6 * i + k)
                    add_inject(qb_b1, p_, kc_, lambda t=t, fn=fn: fn(t))
            for s, fn in ((min(26, NS - 1), lambda tc=2 * c: emit_router_trans(tc, 0)),
                          (min(28, NS - 1), lambda tc=2 * c: emit_router_trans(tc, 1)),
                          (min(34, NS - 1), lambda tc=2 * c: emit_router_mm(tc)),
                          (min(50, NS - 1), lambda tc=2 * c + 1: emit_router_trans(tc, 0)),
                          (min(54, NS - 1), lambda tc=2 * c + 1: emit_router_trans(tc, 1)),
                          (min(62, NS - 1), lambda tc=2 * c + 1: emit_router_mm(tc))):
                p_, kc_ = slot_of(s)
                add_inject(qb_b1, p_, kc_, fn)


    for qb in range(NQB):
        for p in range(NPH):
            def emit_S(kc):
                buf = kc % 2
                w0 = [(f"exp{buf}", P.counts.get(f"exp{buf}", 0))]
                if qb == 0 and p == 0 and kc < 2:
                    w0 += [("tdc", a2_tdc), ("vec", a2_vec), ("scc", a2_scc)]
                E("tensor", lambda e, p=p, kc=kc, qb=qb, buf=buf: e.matmul(
                    stp[buf][:, 0:512],
                    outerT[0:64, p * T + kc * 128:p * T + (kc + 1) * 128],
                    innerT[0:64, p * T + qb * QB:p * T + (qb + 1) * QB],
                    start=True, stop=True),
                  waits=w0)
                E("tensor", lambda e, p=p, kc=kc, qb=qb, buf=buf: e.matmul(
                    stp[buf][:, 512:1024],
                    outerT[64:128, p * T + kc * 128:p * T + (kc + 1) * 128],
                    innerT[64:128, p * T + qb * QB:p * T + (qb + 1) * QB],
                    start=True, stop=True),
                  inc="tdc")
                vst = V("tdc")
                E("scalar", lambda e, buf=buf: e.activation(
                    Ut[buf], stp[buf][:], AF.Exp, scale=0.125),
                  inc=f"exp{buf}",
                  waits=[("tdc", vst),
                         (f"pvd{buf}", P.counts.get(f"pvd{buf}", 0))])

            def emit_PVX(kc):
                buf = kc % 2
                vexp = V(f"exp{buf}")
                w1 = [(f"exp{buf}", vexp)]
                if kc == 0:
                    w1.append(("cve", P.counts.get("cve", 0)))
                E("tensor", lambda e, p=p, kc=kc, buf=buf: e.matmul(
                    psC[0:64, 0:512],
                    vx[:, kc * VX * NH + (2 * p) * VX:kc * VX * NH + (2 * p) * VX + DH],
                    Ut[buf][:, 0:512],
                    start=(kc == 0), stop=(kc == NKC - 1), tile_position=(0, 0)),
                  waits=w1)
                E("tensor", lambda e, p=p, kc=kc, buf=buf: e.matmul(
                    psC[64:128, 0:512],
                    vx[:, kc * VX * NH + (2 * p + 1) * VX:kc * VX * NH + (2 * p + 1) * VX + DH],
                    Ut[buf][:, 512:1024],
                    start=(kc == 0), stop=(kc == NKC - 1), tile_position=(0, 64),
                    skip_group_check=True))
                exm = 2 if kc == NKC - 1 else 64
                E("tensor", lambda e, p=p, kc=kc, buf=buf, exm=exm: e.matmul(
                    psC[0:exm, 512:1024],
                    vx[:, kc * VX * NH + (2 * p) * VX + DH:kc * VX * NH + (2 * p) * VX + DH + exm],
                    Ut[buf][:, 0:512],
                    start=(kc == 0), stop=(kc == NKC - 1), tile_position=(0, 0),
                    skip_group_check=True),
                  waits=[("exe", P.counts.get("exe", 0))] if kc == 0 else [])
                E("tensor", lambda e, p=p, kc=kc, buf=buf, exm=exm: e.matmul(
                    psD[64:64 + exm, 0:512],
                    vx[:, kc * VX * NH + (2 * p + 1) * VX + DH:kc * VX * NH + (2 * p + 1) * VX + DH + exm],
                    Ut[buf][:, 512:1024],
                    start=(kc == 0), stop=(kc == NKC - 1), tile_position=(0, 64),
                    skip_group_check=True),
                  inc=f"pvd{buf}")

            def do_inject(kc):
                for fn in inject.pop((qb, p, kc), []):
                    fn()

            do_inject(-1)
            if qb == 0 and p == 0:
                emit_S(0)
                emit_S(1)
            for kc in range(NKC):
                emit_PVX(kc)
                if kc + 2 < NKC:
                    emit_S(kc + 2)
                do_inject(kc)
            def prologue_kk(kk, nqb, np_):
                buf = kk % 2
                E("tensor", lambda e, p2=np_, kc=kk, qb2=nqb, buf=buf: e.matmul(
                    stp[buf][:, 0:512],
                    outerT[0:64, p2 * T + kc * 128:p2 * T + (kc + 1) * 128],
                    innerT[0:64, p2 * T + qb2 * QB:p2 * T + (qb2 + 1) * QB],
                    start=True, stop=True),
                  waits=[(f"exp{buf}", P.counts.get(f"exp{buf}", 0))])
                E("tensor", lambda e, p2=np_, kc=kk, qb2=nqb, buf=buf: e.matmul(
                    stp[buf][:, 512:1024],
                    outerT[64:128, p2 * T + kc * 128:p2 * T + (kc + 1) * 128],
                    innerT[64:128, p2 * T + qb2 * QB:p2 * T + (qb2 + 1) * QB],
                    start=True, stop=True),
                  inc="tdc")
                vst = V("tdc")
                E("scalar", lambda e, buf=buf: e.activation(
                    Ut[buf], stp[buf][:], AF.Exp, scale=0.125),
                  inc=f"exp{buf}",
                  waits=[("tdc", vst),
                         (f"pvd{buf}", P.counts.get(f"pvd{buf}", 0))])

            nqb, np_ = (qb, p + 1) if p + 1 < NPH else (qb + 1, 0)
            if nqb < NQB:
                prologue_kk(0, nqb, np_)
            vpv = V(f"pvd{(NKC - 1) % 2}")
            cw = [(f"pvd{(NKC - 1) % 2}", vpv)]
            if qb >= 2:
                cw.append(("rbu", NH * (qb - 1)))
            E("scalar", lambda e, p=p, qb=qb: e.activation(
                conv[qb % 2][:, p * QB:(p + 1) * QB], psC[0:128, 0:QB],
                AF.Identity),
              inc="cve", waits=cw)
            if nqb < NQB:
                prologue_kk(1, nqb, np_)
            E("scalar", lambda e: e.activation(exstA, psC[0:2, 512:512 + QB],
                                               AF.Identity),
              inc="exe", waits=[("dnzA", P.counts.get("dnzA", 0))])
            vexA = V("exe")
            E("scalar", lambda e: e.activation(exstB, psD[64:66, 0:QB],
                                               AF.Identity),
              inc="exe", waits=[("dnzB", P.counts.get("dnzB", 0))])
            vexB = V("exe")
            E("sync", lambda e, p=p: e.dma_start(out=_nzrow(AFR, F_T2, 2 * p, QB), in_=exstA[0:1, :]),
              inc="dnzA", inck=16,
              waits=[("exe", vexA), ("rch", P.counts.get("rch", 0))])
            E("sync", lambda e, p=p: e.dma_start(out=_nzrow(AFR, F_T3, 2 * p, QB), in_=exstA[1:2, :]),
              inc="dnzA", inck=16)
            E("sync", lambda e, p=p: e.dma_start(out=_nzrow(AFR, F_T2, 2 * p + 1, QB), in_=exstB[0:1, :]),
              inc="dnzB", inck=16,
              waits=[("exe", vexB), ("rch", P.counts.get("rch", 0))])
            E("sync", lambda e, p=p: e.dma_start(out=_nzrow(AFR, F_T3, 2 * p + 1, QB), in_=exstB[1:2, :]),
              inc="dnzB", inck=16)
        vnz_allA = V("dnzA")
        vnz_allB = V("dnzB")
        E("sync", lambda e, qb=qb: e.dma_start(
            out=aq, in_=a_d[:, qb * QB:(qb + 1) * QB]),
          inc="daq", inck=16, waits=[("das", a2_das), ("vec", V("vec"))])
        vdaq = V("daq")
        vop(lambda e: e.tensor_mul(rby, nzn, aq),
            waits=[("dnzA", vnz_allA), ("dnzB", vnz_allB), ("daq", vdaq),
                   ("tdc", last_bcast_tdc[0])])
        vop(lambda e: e.scalar_tensor_tensor(rby, nzz, 1e-8, rby, ALU.mult, ALU.add))
        vop(lambda e: e.reciprocal(rby, rby))
        vop(lambda e: e.tensor_mul(rby, rby, aq))
        E("vector", lambda e: e.nop(), inc="rch", waits=[("vec", V("vec"))])

    assert not inject, f"unfired injections: {list(inject.keys())}"
    for h in range(NH):
        emit_chamber_step(NQB - 1, h)
    for g in range(2 * (QB // 128)):
        emit_wout_group(NQB - 1, g)
    emit_ag1(NAG - 1)
    if DISABLE_B1_INJECT:
        for c in range(NAG - 1):
            for i in range(8):
                emit_b1_tile(c * 8 + i)
            emit_router_tc(2 * c)
            emit_router_tc(2 * c + 1)
    for i in range(8):
        emit_b1_tile((NAG - 1) * 8 + i)
    emit_router_tc(2 * (NAG - 1))
    emit_router_tc(2 * (NAG - 1) + 1)
    b1_dht = V("dht")
    b1_dx1 = V("dx1")
    b_dlg = V("dlg")

    pbl = fv(F_T1, CPS)
    pbx = fv(F_T2, CPS)
    c1 = fv(F_T3, CPS)
    c2 = fv(F_T4, CPS)
    c3 = fv(F_T5, CPS)
    c4 = fv(F_T1 + 512, CPS)
    c5 = fv(F_T2 + 512, CPS)
    c6 = fv(F_T3 + 512, CPS)
    NCS = int(math.log2(CPS))
    locol = fv(F_COLS + 20, 1)
    hicol = fv(F_COLS + 21, 1)
    micol = fv(F_COLS + 22, 1)
    lo16 = g16[:, 0:1]
    hi16 = g16[:, 1:2]
    mi16 = g16[:, 2:3]
    ct16 = g16[:, 3:4]
    gg16 = g16[:, 4:5]
    t16a = g16[:, 5:6]
    t16b = g16[:, 6:7]
    v16 = g16[:, 7:8]
    s16 = g16[:, 8:9]
    r16 = g16[:, 9:10]
    d16 = g16[:, 10:11]

    E("sync", lambda e: e.dma_start(
        out=pbl, in_=lgt_d.rearrange("e (s c) -> (e s) c", s=SUB)),
      inc="dpb", inck=16, waits=[("dlg", b_dlg)])
    v_dpb = V("dpb")

    def gsum(src_col, dst16, extra=()):
        vt = E("tensor", lambda e: e.matmul(psB[0:NE, 512:513], sel16[:], src_col,
                                            start=True, stop=True),
               inc="tdc", waits=list(extra) + [("vec", V("vec")), ("g16u", P.counts.get("g16u", 0))])
        return vop(lambda e: e.tensor_copy(dst16, psB[0:NE, 512:513]),
                   waits=[("tdc", vt)])

    def bcast16(src16, dstcol, extra=()):
        vt = E("tensor", lambda e: e.matmul(psB[:, 513:514], bc16[:], src16,
                                            start=True, stop=True),
               inc="tdc", waits=list(extra) + [("vec", V("vec"))])
        return vop(lambda e: e.tensor_copy(dstcol, psB[:, 513:514]),
                   waits=[("tdc", vt)])

    vop(lambda e: e.tensor_reduce(mxc, pbl, mybir.AxisListType.X, ALU.max),
        waits=[("dpb", v_dpb)])
    vq = V("vec")
    E("sync", lambda e: e.dma_start(out=gmx_d, in_=mxc), inc="dgm", inck=16,
      waits=[("vec", vq)])
    E("sync", lambda e: e.dma_start(
        out=g16[:, 11:11 + SUB], in_=gmx_d.rearrange("(e s) o -> e (s o)", s=SUB)),
      inc="dgm", inck=16, waits=[("dgm", V("dgm"))])
    v_dgm = V("dgm")
    vop(lambda e: e.tensor_reduce(t16a, g16[:, 11:11 + SUB], mybir.AxisListType.X,
                                  ALU.max), waits=[("dgm", v_dgm)])
    bcast16(t16a, mxc)
    vop(lambda e: e.tensor_scalar(mxc, mxc, -1.0, None, ALU.mult))
    vexp0 = V("vec")
    sxp = sop(lambda e: e.activation(pbx, pbl, AF.Exp, bias=mxc, accum_out=scol),
              waits=[("vec", vexp0)])
    E("vector", lambda e: e.nop(), inc="vec", waits=[("scc", sxp)])
    gsum(scol, t16b)
    bcast16(t16b, dcol)
    vop(lambda e: e.reciprocal(dcol, dcol))
    vpr = vop(lambda e: e.tensor_scalar(pbx, pbx, dcol, None, ALU.mult))

    loc = fv(F_COLS + 30, 1)
    hic = fv(F_COLS + 31, 1)
    mic = fv(F_COLS + 32, 1)
    ggc = fv(F_COLS + 33, 1)
    tac = fv(F_COLS + 34, 1)
    E("gpsimd", lambda e: e.memset(loc, 0.0), inc="gpc", waits=[("vec", vpr)])
    E("gpsimd", lambda e: e.memset(hic, 1.0), inc="gpc", waits=[("gpc", V("gpc"))])
    vg0 = V("gpc")
    vop(lambda e: e.nop(), waits=[("gpc", vg0)])
    for it in range(BIS):
        vop(lambda e: e.tensor_tensor(mic, loc, hic, ALU.add))
        vop(lambda e: e.tensor_scalar(mic, mic, 0.5, None, ALU.mult))
        vcm_ = vop(lambda e: e.tensor_scalar(c1, pbx, mic, 0.0, ALU.is_gt, ALU.add,
                                             accum_out=scol))
        vmm_ = E("tensor", lambda e: e.matmul(psB[:, 512:513], g128[:], scol,
                                              start=True, stop=True),
                 inc="tdc", waits=[("vec", vcm_), ("g16u", P.counts.get("g16u", 0))])
        vop(lambda e: e.tensor_scalar(ggc, psB[:, 512:513], float(CAP) - 0.5, None,
                                      ALU.is_gt),
            waits=[("tdc", vmm_)])
        E("vector", lambda e: e.nop(), inc="g16u", waits=[("vec", V("vec"))])
        vop(lambda e: e.tensor_tensor(tac, mic, loc, ALU.subtract))
        vop(lambda e: e.tensor_mul(tac, tac, ggc))
        vop(lambda e: e.tensor_tensor(loc, loc, tac, ALU.add))
        vop(lambda e: e.tensor_tensor(tac, hic, mic, ALU.subtract))
        vop(lambda e: e.tensor_mul(tac, tac, ggc))
        vlast = vop(lambda e: e.tensor_tensor(hic, mic, tac, ALU.add))
    hicol = hic
    vop(lambda e: e.tensor_scalar(c1, pbx, hicol, None, ALU.is_gt))
    vop(lambda e: e.scalar_tensor_tensor(c1, c1, -9.0, pbx, ALU.mult, ALU.add))
    vop(lambda e: e.tensor_reduce(mxc, c1, mybir.AxisListType.X, ALU.max))
    vq2 = V("vec")
    E("sync", lambda e: e.dma_start(out=gmx_d, in_=mxc), inc="dgm", inck=16,
      waits=[("vec", vq2)])
    E("sync", lambda e: e.dma_start(
        out=g16[:, 11:11 + SUB], in_=gmx_d.rearrange("(e s) o -> e (s o)", s=SUB)),
      inc="dgm", inck=16, waits=[("dgm", V("dgm"))])
    v_dgm2 = V("dgm")
    vop(lambda e: e.tensor_reduce(v16, g16[:, 11:11 + SUB], mybir.AxisListType.X,
                                  ALU.max), waits=[("dgm", v_dgm2)])
    bcast16(v16, vcol2)
    vop(lambda e: e.tensor_scalar(c1, pbx, vcol2, 0.0, ALU.is_gt, ALU.add,
                                  accum_out=scol))
    gsum(scol, s16)
    vop(lambda e: e.tensor_scalar(c2, pbx, vcol2, None, ALU.is_equal))
    vop(lambda e: e.tensor_scalar(r16, s16, -1.0, float(CAP), ALU.mult, ALU.add))
    bcast16(r16, rcol)

    def cumsum_emit(src, tmp, prefix_col):
        cur, oth = src, tmp
        for k in range(NCS):
            s = 1 << k
            vop(lambda e, cur=cur, oth=oth, s=s: e.tensor_copy(oth[:, 0:s], cur[:, 0:s]))
            vop(lambda e, cur=cur, oth=oth, s=s: e.tensor_tensor(
                oth[:, s:CPS], cur[:, s:CPS], cur[:, 0:CPS - s], ALU.add))
            cur, oth = oth, cur
        vt = E("tensor", lambda e, cur=cur: e.matmul(
            psB[:, 514:515], l8[:], cur[:, CPS - 1:CPS], start=True, stop=True),
          inc="tdc", waits=[("vec", V("vec"))])
        vop(lambda e: e.tensor_copy(prefix_col, psB[:, 514:515]),
            waits=[("tdc", vt)])
        return cur

    vop(lambda e: e.tensor_copy(c3, c2))
    cq = cumsum_emit(c3, c4, pfxc)
    vop(lambda e, cq=cq: e.tensor_tensor(c5, cq, c2, ALU.subtract))
    vop(lambda e: e.tensor_scalar(c5, c5, pfxc, None, ALU.add))
    vop(lambda e: e.tensor_scalar(c5, c5, rcol, None, ALU.is_lt))
    vop(lambda e: e.tensor_mul(c5, c5, c2))
    vop(lambda e: e.tensor_tensor(c1, c1, c5, ALU.add))
    vop(lambda e: e.scalar_tensor_tensor(c2, c1, 1.0, pbx, ALU.mult, ALU.mult,
                                         accum_out=scol))
    gsum(scol, d16)
    vop(lambda e: e.tensor_scalar(d16, d16, 1e-9, None, ALU.add))
    bcast16(d16, dcol)
    vop(lambda e: e.reciprocal(dcol, dcol))
    vop(lambda e: e.tensor_scalar(c2, c2, dcol, None, ALU.mult))
    vop(lambda e: e.tensor_copy(c4, c1))
    cq2 = cumsum_emit(c4, c6, pfxc)
    vop(lambda e, cq2=cq2: e.tensor_tensor(c3, cq2, c1, ALU.subtract))
    vsel = vop(lambda e: e.tensor_scalar(c3, c3, pfxc, None, ALU.add))
    for k, src in ((0, c1), (1, c3), (2, c2)):
        E("sync", lambda e, k=k, src=src: e.dma_start(
            out=selcw_d[k].rearrange("e (s c) -> (e s) c", s=SUB), in_=src),
          inc="dsw", inck=16, waits=[("vec", vsel)])
    v_dsw = V("dsw")
    vmf = vop(lambda e: e.tensor_copy(myexpf[:], myexp_sb[:]),
              waits=[("dsm", v_dsm)])
    vmx = E("tensor", lambda e: e.matmul(psC[0:64, 64:65], rep8[:], myexpf[:],
                                         start=True, stop=True),
            inc="tdc", waits=[("vec", vmf)])
    vop(lambda e: e.tensor_copy(idx64[:], psC[0:64, 64:65]), waits=[("tdc", vmx)])
    vop(lambda e: e.tensor_scalar(idx64[:], idx64[:], float(SUB), None, ALU.mult))
    vop(lambda e: e.tensor_tensor(idx64[:], idx64[:], sub8[:], ALU.add))
    vi64 = None
    for ki in range(3):
        vop(lambda e, ki=ki: e.tensor_copy(idx64i[:, ki:ki + 1], idx64[:]))
        vi64 = vop(lambda e, ki=ki: e.tensor_scalar(
            idx64i[:, ki:ki + 1], idx64i[:, ki:ki + 1], ki * NE * SUB, None, ALU.add))
    c6z = fv(F_T3 + 512, 512)
    E("gpsimd", lambda e: e.memset(c6z, 0.0), inc="gpc",
      waits=[("dsw", v_dsw), ("gpc", V("gpc"))])
    vz = V("gpc")
    for ft in range(NFT):
        E("sync", lambda e, ft=ft: e.dma_start(
            out=acc_d[ft * 128:(ft + 1) * 128, 0:512], in_=c6z),
          inc="dza", inck=16, waits=[("gpc", vz)])
        E("sync", lambda e, ft=ft: e.dma_start(
            out=acc_d[ft * 128:(ft + 1) * 128, 512:1024], in_=c6z),
          inc="dza", inck=16)
    v_dza = V("dza")
    minepb = fv(F_T1, 3 * CPS, 0, 64)
    NCB = CPS // 128
    for ki in range(3):
        E("gpsimd", lambda e, ki=ki: e.indirect_dma_start(
            out=minepb[:, ki * CPS:(ki + 1) * CPS], out_offset=None,
            in_=selcw_d.rearrange("k e (s c) -> (k e s) c", s=SUB),
            in_offset=IOX(idx64i[:, ki:ki + 1], 0)),
          inc="dtm", inck=16,
          waits=[("vec", vi64), ("dsw", v_dsw), ("dza", v_dza)])
    v_dtm = V("dtm")
    for ki, dst in ((0, selTm), (1, ceTm), (2, wTm)):
        for cb in range(NCB):
            vt = E("tensor", lambda e, ki=ki, cb=cb: e.transpose(
                psC[:, 0:64],
                minepb[:, ki * CPS + cb * 128:ki * CPS + (cb + 1) * 128],
                idn[0:64, 0:64]),
              inc="tdc",
              waits=[("dtm", v_dtm), ("vec", V("vec")), ("dsm", v_dsm)])
            vop(lambda e, dst=dst, cb=cb: e.tensor_copy(
                dst[:, cb * 64:(cb + 1) * 64], psC[:, 0:64]),
                waits=[("tdc", vt)])
    v_tok = V("vec")

    if debug:
        E("sync", lambda e: e.dma_start(out=dbg["dbg_probs"], in_=pbx),
          inc="dbgs", inck=16, waits=[("vec", vsel)])
        E("sync", lambda e: e.dma_start(out=dbg["dbg_sel"], in_=c1),
          inc="dbgs", inck=16)

    xe = bv(B_XE, CAPH * 1024)
    xeT = [bv(B_XET, NDC * CAP), bv(B_XET + NDC * CAP, NDC * CAP)]
    w1q = [bv(B_W1, NDC * 512), bv(B_W1 + NDC * 512, NDC * 512)]
    w2q = [bv(B_W2, NDC * 512), bv(B_W2 + NDC * 512, NDC * 512)]
    w3q = [bv(B_W3, 4 * 1024), bv(B_W3 + 4 * 1024, 4 * 1024)]
    gv = bv(B_GV, CAP)
    slotb = fv(F_SLOT, 1032)
    gs_f = fv(F_SB, CAP)

    E("gpsimd", lambda e: e.memset(slotb[:, 1024:1032], 0.0), inc="gpc",
      waits=[("gpc", V("gpc"))])
    v_padz = V("gpc")

    _vxt_of = {}
    _pselb2 = [pselb[:], fv(F_SA, CAP)]
    _ps_war = [0, 0]
    exsb = fv(F_SB + 512, CAP, 0, 2)

    def emit_prep(j):
        exw = [("vec", v_tok)]
        vop(lambda e: e.tensor_copy(
            rhsa[:].rearrange("p (t two) -> p t two", two=2)[:, :, 0:1],
            tokc[:].rearrange("p (t one) -> p t one", one=1)),
            waits=exw + [("exr", P.counts.get("exr", 0))])
        for cb in range(NCB):
            vop(lambda e, cb=cb, j=j: e.tensor_copy(
                rhsa[:].rearrange("p (s ncb two) -> p s ncb two",
                                  ncb=NCB, two=2)[:, :, cb, 1:2],
                wTm[:, cb * 64 + SUB * j:cb * 64 + SUB * j + SUB].rearrange(
                    "p (s one) -> p s one", one=1)))
        vrh = V("vec")
        for tc in range(NTC):
            col = (tc % NCB) * 64 + SUB * j + tc // NCB
            pb = _pselb2[tc % 2]
            vps = vop(lambda e, col=col, pb=pb: e.tensor_scalar(
                pb, iotas[:], ceTm[:, col:col + 1],
                selTm[:, col:col + 1], ALU.is_equal, ALU.mult),
                waits=[("tdc", _ps_war[tc % 2]),
                       ("dx1", P.counts.get("dx1", 0))])
            E("tensor", lambda e, tc=tc, pb=pb: e.matmul(
                psB[0:2, 768:768 + CAP],
                rhsa[:, 2 * tc:2 * tc + 2],
                pb,
                start=(tc == 0), stop=(tc == NTC - 1)),
              inc="tdc",
              waits=[("vec", vps)] + ([("vec", vrh),
                     ("exu", P.counts.get("exu", 0))] if tc == 0 else []))
            _ps_war[tc % 2] = V("tdc")
        vexd = V("tdc")
        vio0 = vop(lambda e: e.tensor_copy(exsb, psB[0:2, 768:768 + CAP]),
                   waits=[("tdc", vexd)])
        for hc in range(CAPH):
            E("tensor", lambda e, hc=hc: e.matmul(
                psB[:, 770 + 2 * hc:772 + 2 * hc],
                exsb[:, hc * 128:(hc + 1) * 128],
                idn[0:2, 0:2],
                start=True, stop=True, skip_group_check=True),
              inc="tdc", waits=[("vec", vio0)] if hc == 0 else [])
        vtx = V("tdc")
        vio = vop(lambda e, j=j: e.tensor_copy(
            idxw[:, j * 2 * CAPH:j * 2 * CAPH + 2], psB[:, 770:772]),
            waits=[("tdc", vtx)])
        if CAPH > 1:
            vio = vop(lambda e, j=j: e.tensor_copy(
                idxw[:, j * 2 * CAPH + 2:j * 2 * CAPH + 4], psB[:, 772:774]))
        E("vector", lambda e: e.nop(), inc="exu", waits=[("vec", vio)])
        E("vector", lambda e: e.nop(), inc="exr", waits=[("vec", vio)])
        vii = vop(lambda e, j=j: e.tensor_copy(
            idxi[:, j * CAPH:(j + 1) * CAPH],
            idxw[:, j * 2 * CAPH:(j + 1) * 2 * CAPH].rearrange(
                "p (h w) -> p h w", h=CAPH)[:, :, 0:1]))
        E("vector", lambda e: e.nop(), inc=f"exj{j}", waits=[("vec", vii)])
        for hc in range(CAPH):
            E("gpsimd", lambda e, j=j, hc=hc: e.indirect_dma_start(
                out=xe[:, hc * 1024:(hc + 1) * 1024], out_offset=None,
                in_=h_d, in_offset=IOX(idxi[:, j * CAPH + hc:j * CAPH + hc + 1], 0)),
              inc="dga", inck=16,
              waits=[(f"exj{j}", 1), ("dht", b1_dht),
                     ("dxt0", P.counts.get("dxt0", 0)),
                     ("dxt1", P.counts.get("dxt1", 0)),
                     ("dbgs", P.counts.get("dbgs", 0))] if hc == 0 else [])
        vga = V("dga")
        xb = j % 2
        for hc in range(CAPH):
            for dc in range(NDC):
                E("sync", lambda e, hc=hc, dc=dc, xb=xb: e.dma_start_transpose(
                    xeT[xb][:, dc * CAP + hc * 128:dc * CAP + (hc + 1) * 128],
                    xe[:, hc * 1024 + dc * 128:hc * 1024 + (dc + 1) * 128]),
                  inc=f"dxt{xb}", inck=16,
                  waits=[("dga", vga), (f"xtu{xb}", P.counts.get(f"xtu{xb}", 0))]
                  if (hc == 0 and dc == 0) else [])
        _vxt_of[j] = V(f"dxt{xb}")
        if debug and j == 0:
            E("sync", lambda e: e.dma_start(out=dbg["dbg_xe"][:, 0:CAPH * 1024],
                                            in_=xe[:, 0:CAPH * 1024]),
              inc="dbgs", inck=16, waits=[("dga", vga)])

    def emit_ffn(j):
        xb = j % 2
        vxt = _vxt_of[j]
        for q4 in range(NQ4):
            wb = q4 % 2
            E("gpsimd", lambda e, j=j, q4=q4, wb=wb: e.dma_start(
            out=w1q[wb].rearrange("p (c n) -> p c n", c=NDC), in_=w1_in[j][:, q4 * 512:(q4 + 1) * 512]
                .rearrange("(c p) n -> p c n", p=128)),
              inc=f"dw1{wb}", inck=16,
              waits=[(f"wu{wb}", P.counts.get(f"wu{wb}", 0))])
            E("gpsimd", lambda e, j=j, q4=q4, wb=wb: e.dma_start(
                out=w2q[wb].rearrange("p (c n) -> p c n", c=NDC), in_=w2_in[j][:, q4 * 512:(q4 + 1) * 512]
                .rearrange("(c p) n -> p c n", p=128)),
              inc=f"dw1{wb}", inck=16)
            E("gpsimd", lambda e, j=j, q4=q4, wb=wb: e.dma_start(
                out=w3q[wb].rearrange("p (c n) -> p c n", c=4), in_=w3_in[j][q4 * 512:(q4 + 1) * 512, :]
                .rearrange("(c p) n -> p c n", p=128)),
              inc=f"dw1{wb}", inck=16)
            for _ in range(2):
                if SCAT_PEND:
                    SCAT_PEND.pop(0)()
            vw = V(f"dw1{wb}")
            for f4 in range(4):
                fc = q4 * 4 + f4
                for dc in range(NDC):
                    E("tensor", lambda e, wb=wb, f4=f4, dc=dc, xb=xb: e.matmul(
                        psC[:, 0:CAP],
                        w1q[wb][:, dc * 512 + f4 * 128:dc * 512 + (f4 + 1) * 128],
                        xeT[xb][:, dc * CAP:(dc + 1) * CAP],
                        start=(dc == 0), stop=(dc == NDC - 1)),
                      inc="tdc" if dc == NDC - 1 else None,
                      waits=[(f"dw1{wb}", vw), (f"dxt{xb}", vxt),
                             ("scc", V("scc"))] if dc == 0 else [])
                for dc in range(NDC):
                    E("tensor", lambda e, wb=wb, f4=f4, dc=dc, xb=xb: e.matmul(
                        psC[:, 512:512 + CAP],
                        w2q[wb][:, dc * 512 + f4 * 128:dc * 512 + (f4 + 1) * 128],
                        xeT[xb][:, dc * CAP:(dc + 1) * CAP],
                        start=(dc == 0), stop=(dc == NDC - 1)),
                      inc="tdc" if dc == NDC - 1 else None,
                      waits=[("vec", V("vec"))] if dc == 0 else [])
                vgv = V("tdc")
                ssl = sop(lambda e: e.activation(gs_f, psC[:, 0:CAP], AF.Silu),
                          waits=[("tdc", vgv), ("vec", V("vec"))])
                vgm = vop(lambda e: e.tensor_tensor(gv, gs_f, psC[:, 512:512 + CAP],
                                                    ALU.mult), waits=[("scc", ssl)])
                if debug and j == 0 and fc == 0:
                    E("sync", lambda e: e.dma_start(out=dbg["dbg_gv"], in_=gv),
                      inc="dbgs", inck=16, waits=[("vec", vgm)])
                    vgm = vop(lambda e: e.nop(),
                              waits=[("dbgs", P.counts.get("dbgs", 0))])
                for sc2 in range(CAPH):
                    ps_o = psD if sc2 == 0 else psA
                    for nc2 in range(2):
                        E("tensor", lambda e, sc2=sc2, nc2=nc2, f4=f4, wb=wb, ps_o=ps_o, fc=fc: e.matmul(
                            ps_o[:, nc2 * 512:(nc2 + 1) * 512],
                            gv[:, sc2 * 128:(sc2 + 1) * 128],
                            w3q[wb][:, f4 * 1024 + nc2 * 512:f4 * 1024 + (nc2 + 1) * 512],
                            start=(fc == 0), stop=(fc == NFC - 1)),
                          inc="tdc" if (sc2 == CAPH - 1 and nc2 == 1) else None,
                          waits=[("vec", vgm), ("oeu", P.counts.get("oeu", 0))]
                          if (sc2 == 0 and nc2 == 0 and fc == 0) else
                          ([("vec", vgm)] if (sc2 == 0 and nc2 == 0) else []))
            E("tensor", lambda e: e.nop(), inc=f"wu{wb}",
              waits=[("tdc", V("tdc"))])
        E("tensor", lambda e: e.nop(), inc=f"xtu{xb}",
          waits=[("tdc", V("tdc"))])
        vffn = V("tdc")
        for hc in range(CAPH):
            ps_o = psD if hc == 0 else psA
            vsl = vop(lambda e, j=j, hc=hc, ps_o=ps_o: e.tensor_scalar(
                slotb[:, 0:1024], ps_o[:, 0:1024],
                idxw[:, j * 2 * CAPH + hc * 2 + 1:j * 2 * CAPH + hc * 2 + 2],
                None, ALU.mult),
                waits=[("tdc", vffn), ("gpc", v_padz),
                       ("dx1", P.counts.get("dx1", 0)),
                       ("dsl", P.counts.get("dsl", 0))])
            vsl2 = vop(lambda e, j=j, hc=hc: e.tensor_copy(
                slotb[:, 1024:1025], idxw[:, j * 2 * CAPH + hc * 2:j * 2 * CAPH + hc * 2 + 1]))
            E("vector", lambda e: e.nop(), inc="oeu", waits=[("vec", vsl2)])
            E("sync", lambda e, j=j, hc=hc: e.dma_start(
                out=agin2.ap()[j * CAP + hc * 128:j * CAP + (hc + 1) * 128, :],
                in_=slotb),
              inc="dsl", inck=16, waits=[("vec", vsl2)])
    NC2 = EM // 2

    CPC = 2 * CAP

    def emit_ag2(c):
        vsl = V("dsl")
        E("gpsimd", lambda e, c=c: e.collective_compute(
            "AllGather", ALU.bypass,
            replica_groups=[[0, 1], [2, 3], [4, 5], [6, 7]],
            ins=[agin2.ap()[c * CPC:(c + 1) * CPC, :].opt()],
            outs=[agout2.ap()[c * 2 * CPC:(c + 1) * 2 * CPC, :].opt()]),
          inc="dcc", waits=[("dsl", vsl)])

    scb = [fv(F_T1, 1032), fv(F_T1 + 1032, 1032)]
    idsc = idxi[:, EM * CAPH:EM * CAPH + 1]
    _scat_war = [0, 0]

    def emit_scatter_group(c, gg):
        ngg = 2 * CPC // 128
        g = c * ngg + gg
        b = g % 2
        E("sync", lambda e, g=g, b=b: e.dma_start(
            out=scb[b], in_=agout2.ap()[g * 128:(g + 1) * 128, :]),
          inc=f"dsc{b}", inck=16,
          waits=[("dcc", NAG + c + 1), ("dscat", _scat_war[b]),
                 ("dbgs", P.counts.get("dbgs", 0))])
        vld = V(f"dsc{b}")
        vop(lambda e, b=b: e.tensor_tensor(idf, scb[b][:, 1024:1025], tkbase[:],
                                           ALU.subtract),
            waits=[(f"dsc{b}", vld), ("dscat", P.counts.get("dscat", 0))])
        vop(lambda e: e.tensor_scalar(idm, idf, 0.0, None, ALU.is_lt))
        vop(lambda e: e.scalar_tensor_tensor(idf, idm, 1e6, idf, ALU.mult, ALU.add))
        vic = vop(lambda e: e.tensor_copy(idsc, idf))
        E("gpsimd", lambda e, b=b: e.indirect_dma_start(
            out=acc_d, out_offset=IOX(idsc, 0),
            in_=scb[b][:, 0:1024], in_offset=None,
            compute_op=ALU.add, bounds_check=TH - 1, oob_is_err=False),
          inc="dscat", inck=16,
          waits=[("vec", vic), ("dza", v_dza),
                 ("dscat", P.counts.get("dscat", 0))])
        _scat_war[b] = V("dscat")

    SCAT_PEND = []

    def emit_scatter_chunk(c):
        for gg in range(2 * CPC // 128):
            emit_scatter_group(c, gg)

    SCAT_INTERLEAVE = False
    for j in range(EM):
        emit_prep(j)
        if SCAT_INTERLEAVE and j >= 4 and j % 2 == 0:
            c = j // 2 - 2
            SCAT_PEND.extend(
                [lambda c=c, gg=gg: emit_scatter_group(c, gg)
                 for gg in range(2 * CPC // 128)])
        if j >= 1:
            emit_ffn(j - 1)
        if j >= 2 and j % 2 == 0:
            emit_ag2(j // 2 - 1)
        if not SCAT_INTERLEAVE and j >= 4 and j % 2 == 0:
            emit_scatter_chunk(j // 2 - 2)
    emit_ffn(EM - 1)
    emit_ag2(NC2 - 1)
    while SCAT_PEND:
        SCAT_PEND.pop(0)()
    if debug:
        E("sync", lambda e: e.dma_start(out=dbg["dbg_agout2"], in_=agout2.ap()),
          inc="dbgs", inck=16, waits=[("dcc", V("dcc"))])
    emit_scatter_chunk(NC2 - 2)
    emit_scatter_chunk(NC2 - 1)
    v_dsl = V("dsl")
    if debug:
        E("sync", lambda e: e.dma_start(out=dbg["dbg_idxw"], in_=idxw[:]),
          inc="dbgs", inck=16, waits=[("dsl", v_dsl)])

    v_scat = V("dscat")

    if debug:
        E("sync", lambda e: e.dma_start(out=dbg["dbg_acc"], in_=acc_d),
          inc="dbgs", inck=16, waits=[("dscat", v_scat)])
        E("sync", lambda e: e.dma_start(out=dbg["dbg_x1"], in_=x1_d),
          inc="dbgs", inck=16, waits=[("dx1", b1_dx1)])
        E("sync", lambda e: e.dma_start(out=dbg["dbg_xln"], in_=xln_d),
          inc="dbgs", inck=16, waits=[("xs0", 16 * NTT // 2), ("xs1", 16 * NTT // 2)])
        E("sync", lambda e: e.dma_start(out=dbg["dbg_p1"], in_=agout1.ap()),
          inc="dbgs", inck=16, waits=[("dcc", T // 1024)])

    idxf = idxi[:, EM * CAPH + 1:EM * CAPH + 2]
    for ft in range(NFT):
        b = ft % 2
        vop(lambda e, ft=ft: e.tensor_scalar(idf, tkbase[:], float(ft * 128),
                                             None, ALU.add),
            waits=[("dxg", P.counts.get("dxg", 0))])
        vop(lambda e: e.tensor_tensor(idf, idf, iotac[:], ALU.add))
        vif = vop(lambda e: e.tensor_copy(idxf, idf))
        E("gpsimd", lambda e, b=b: e.indirect_dma_start(
            out=xt[b], out_offset=None,
            in_=x1_d, in_offset=IOX(idxf, 0)),
          inc="dxg", inck=16,
          waits=[("vec", vif), ("dx1", b1_dx1),
                 (f"xf{b}", P.counts.get(f"xf{b}", 0))])
        vxg = V("dxg")
        E("sync", lambda e, ft=ft, b=b: e.dma_start(
            out=sA if False else fv(F_SA, 1024), in_=acc_d[ft * 128:(ft + 1) * 128, :]),
          inc="dac", inck=16,
          waits=[("dscat", v_scat), ("acu", P.counts.get("acu", 0)),
                 ("dout", P.counts.get("dout", 0))]
          + ([("tdc", V("tdc"))] if ft == 0 else []))
        vac = V("dac")
        vno = ln_emit(sA, sB, extra_w=[("dac", vac)]
                      + ([("tdc", V("tdc"))] if ft == 0 else []))
        vfin = vop(lambda e, b=b: e.tensor_tensor(sA, xt[b], sB, ALU.add),
                   waits=[("dxg", vxg)])
        ln_war[0] = vfin
        E("vector", lambda e: e.nop(), inc="acu", waits=[("vec", vfin)])
        E("vector", lambda e: e.nop(), inc=f"xf{b}", waits=[("vec", vfin)])
        E("sync", lambda e, ft=ft: e.dma_start(
            out=out_d[ft * 128:(ft + 1) * 128, :], in_=sA),
          inc="dout", inck=16, waits=[("vec", vfin)])
    E("sync", lambda e: e.nop(), waits=[("dout", V("dout"))])

    P.realize()
    return nc


def _nzrow(AFR, base, h, QB):
    return AFR[h:h + 1, base:base + QB]


_NC_CACHE = {}


def _host_inputs(inputs, core):
    T, D, NH, NE = 4096, 1024, 8, 16
    b = core // 2
    half = core % 2
    hs = slice(half * NH * 64, (half + 1) * NH * 64)
    hh = slice(half * NH, (half + 1) * NH)
    es = slice(half * NE // 2, (half + 1) * NE // 2)
    SUB = 128 // NE
    l8 = np.zeros((128, 128), np.float32)
    for p in range(128):
        for m in range(128):
            if p // SUB == m // SUB and p % SUB < m % SUB:
                l8[p, m] = 1.0
    bc16 = np.zeros((NE, 128), np.float32)
    for e in range(NE):
        bc16[e, e * SUB:(e + 1) * SUB] = 1.0
    sel16 = bc16.T.copy()
    oh8 = np.zeros((SUB, NH * 64), np.float32)
    for h in range(NH):
        oh8[h, h * 64:(h + 1) * 64] = 1.0
    chp = np.concatenate([inputs["L0_attn_beta_p"][hh], inputs["L0_attn_iv_p"][hh],
                          inputs["L0_attn_ov_p"][hh], inputs["L0_attn_chi"][hh]])
    return {
        "x": np.ascontiguousarray(inputs["L0_attn_x"][b]),
        "wi": np.ascontiguousarray(inputs["L0_attn_Wi"][:, hs]),
        "wo": np.ascontiguousarray(inputs["L0_attn_Wo"][:, hs]),
        "wv": np.ascontiguousarray(inputs["L0_attn_Wv"][:, hs]),
        "wout": np.ascontiguousarray(inputs["L0_attn_Wout"][hs, :]),
        "wa": np.ascontiguousarray(inputs["L0_attn_Wa"][:, hh]),
        "we": np.ascontiguousarray(inputs["L0_attn_We"][:, hh]),
        "ba": np.ascontiguousarray(inputs["L0_attn_ba"][hh]).reshape(1, NH),
        "be": np.ascontiguousarray(inputs["L0_attn_be"][hh]).reshape(1, NH),
        "chp": np.ascontiguousarray(chp).reshape(1, 4 * NH),
        "rw1": np.ascontiguousarray(inputs["L0_ffn_router_w1"]),
        "rw2": np.ascontiguousarray(inputs["L0_ffn_router_w2"]),
        "w1": np.ascontiguousarray(inputs["L0_ffn_w1"][es]),
        "w2": np.ascontiguousarray(inputs["L0_ffn_w2"][es]),
        "w3": np.ascontiguousarray(inputs["L0_ffn_w3"][es]),
        "tkb": np.full((128, 1), half * (T // 2), np.float32),
        "myexp": np.arange(half * NE // 2, (half + 1) * NE // 2,
                           dtype=np.int32).reshape(-1, 1),
        "oh8": oh8, "l8": l8, "bc16": bc16, "sel16": sel16,
        "g128": np.kron(np.eye(NE, dtype=np.float32),
                        np.ones((128 // NE, 128 // NE), np.float32)),
        "idn": np.eye(128, dtype=np.float32),
        "rep8": np.repeat(np.eye(8, dtype=np.float32), 8, axis=1),
        "sub8": np.tile(np.arange(8, dtype=np.float32), 8).reshape(64, 1),
    }


def kernel(**inputs):
    key = "full"
    if key not in _NC_CACHE:
        _NC_CACHE[key] = build()
    nc = _NC_CACHE[key]
    in_maps = [_host_inputs(inputs, c) for c in range(8)]
    res = run_bass_kernel_spmd(nc, in_maps, core_ids=list(range(8)))
    out = np.empty((4, 4096, 1024), np.float32)
    for c in range(8):
        b, half = c // 2, c % 2
        out[b, half * 2048:(half + 1) * 2048, :] = res.results[c]["out"]
    return out


# version 15
# speedup vs baseline: 1.0363x; 1.0363x over previous
import math
import numpy as np
import concourse.bass as bass
import concourse.mybir as mybir
from concourse.bass_utils import run_bass_kernel_spmd

F32 = mybir.dt.float32
BF16 = mybir.dt.bfloat16
I32 = mybir.dt.int32
AF = mybir.ActivationFunctionType
ALU = mybir.AluOpType
IOX = bass.IndirectOffsetOnAxis

EPS = 1e-5
PI = math.pi

CFG = dict(T=4096, D=1024, NH=8, DH=64, NE=16, CAP=256, DFF=2048, RH=128,
           QB=512, BIS=48)


class Prog:
    ENGINES = ("sync", "gpsimd", "vector", "scalar", "tensor")

    def __init__(self, nc):
        self.nc = nc
        self.ops = {e: [] for e in self.ENGINES}
        self.sems = {}
        self.counts = {}

    def _sem(self, name):
        if name not in self.sems:
            self.sems[name] = self.nc.alloc_semaphore(name)
            self.counts[name] = 0

    def emit(self, eng, fn, inc=None, inck=1, waits=()):
        w = []
        for s, v in waits:
            if v > 0:
                self._sem(s)
                w.append((s, v))
        v = None
        if inc is not None:
            self._sem(inc)
            self.counts[inc] += inck
            v = self.counts[inc]
        self.ops[eng].append((fn, inc, inck, w))
        return v

    def val(self, name):
        return self.counts.get(name, 0)

    def realize(self):
        with self.nc.Block() as block:
            for eng_name in self.ENGINES:
                ops = self.ops[eng_name]
                if not ops:
                    continue

                def body(eng, ops=ops):
                    for fn, inc, inck, waits in ops:
                        for s, v in waits:
                            eng.wait_ge(self.sems[s], v)
                        instr = fn(eng) if fn is not None else None
                        if inc is not None:
                            instr.then_inc(self.sems[inc], inck)
                getattr(block, eng_name)(body)


DISABLE_B1_INJECT = False


def build(cfg=CFG, debug=False):
    T, D, NH, DH = cfg["T"], cfg["D"], cfg["NH"], cfg["DH"]
    NE, CAP, DFF, RH = cfg["NE"], cfg["CAP"], cfg["DFF"], cfg["RH"]
    QB, BIS = cfg["QB"], cfg["BIS"]
    NTT = T // 128
    NQB = T // QB
    NKC = T // 128
    NDC = D // 128
    DM = NH * DH
    NMC = DM // 128
    NFC = DFF // 128
    NQ4 = DFF // 512
    EM = NE // 2
    CAPH = CAP // 128
    TH = T // 2
    NFT = TH // 128
    SUB = 128 // NE
    CPS = T // SUB
    NTC = T // 128
    VX = DH + 2
    NQC = QB // 512
    NT2 = T // 512

    nc = bass.Bass("TRN2", target_bir_lowering=False)

    x_in = nc.dram_tensor("x", [T, D], F32, kind="ExternalInput").ap()
    wi_in = nc.dram_tensor("wi", [D, DM], F32, kind="ExternalInput").ap()
    wo_in = nc.dram_tensor("wo", [D, DM], F32, kind="ExternalInput").ap()
    wv_in = nc.dram_tensor("wv", [D, DM], F32, kind="ExternalInput").ap()
    wout_in = nc.dram_tensor("wout", [DM, D], F32, kind="ExternalInput").ap()
    wa_in = nc.dram_tensor("wa", [D, NH], F32, kind="ExternalInput").ap()
    we_in = nc.dram_tensor("we", [D, NH], F32, kind="ExternalInput").ap()
    ba_in = nc.dram_tensor("ba", [1, NH], F32, kind="ExternalInput").ap()
    be_in = nc.dram_tensor("be", [1, NH], F32, kind="ExternalInput").ap()
    chp_in = nc.dram_tensor("chp", [1, 4 * NH], F32, kind="ExternalInput").ap()
    rw1_in = nc.dram_tensor("rw1", [D, RH], F32, kind="ExternalInput").ap()
    rw2_in = nc.dram_tensor("rw2", [RH, NE], F32, kind="ExternalInput").ap()
    w1_in = nc.dram_tensor("w1", [EM, D, DFF], F32, kind="ExternalInput").ap()
    w2_in = nc.dram_tensor("w2", [EM, D, DFF], F32, kind="ExternalInput").ap()
    w3_in = nc.dram_tensor("w3", [EM, DFF, D], F32, kind="ExternalInput").ap()
    tkb_in = nc.dram_tensor("tkb", [128, 1], F32, kind="ExternalInput").ap()
    myexp_in = nc.dram_tensor("myexp", [EM, 1], I32, kind="ExternalInput").ap()
    oh8_in = nc.dram_tensor("oh8", [SUB, NH * 64], F32, kind="ExternalInput").ap()
    l8_in = nc.dram_tensor("l8", [128, 128], F32, kind="ExternalInput").ap()
    bc16_in = nc.dram_tensor("bc16", [NE, 128], F32, kind="ExternalInput").ap()
    sel16_in = nc.dram_tensor("sel16", [128, NE], F32, kind="ExternalInput").ap()
    g128_in = nc.dram_tensor("g128", [128, 128], F32, kind="ExternalInput").ap()
    idn_in = nc.dram_tensor("idn", [128, 128], F32, kind="ExternalInput").ap()
    rep8_in = nc.dram_tensor("rep8", [8, 64], F32, kind="ExternalInput").ap()
    sub8_in = nc.dram_tensor("sub8", [64, 1], F32, kind="ExternalInput").ap()
    out_d = nc.dram_tensor("out", [TH, D], F32, kind="ExternalOutput").ap()

    xln_d = nc.dram_tensor("xln_d", [T, D], BF16, kind="Internal").ap()
    agin1 = nc.dram_tensor("agin1", [T, D], BF16, kind="Internal")
    agout1 = nc.dram_tensor("agout1", [2 * T, D], BF16, kind="Internal")
    x1_d = nc.dram_tensor("x1_d", [T, D], F32, kind="Internal").ap()
    h_d = nc.dram_tensor("h_d", [T, D], BF16, kind="Internal").ap()
    a_d = nc.dram_tensor("a_d", [NH, T], F32, kind="Internal").ap()
    lgt_d = nc.dram_tensor("lgt_d", [NE, T], F32, kind="Internal").ap()
    gmx_d = nc.dram_tensor("gmx_d", [128, 1], F32, kind="Internal").ap()
    selcw_d = nc.dram_tensor("selcw_d", [3, NE, T], F32, kind="Internal").ap()
    agin2 = nc.dram_tensor("agin2", [EM * CAP, 1032], F32, kind="Internal")
    agout2 = nc.dram_tensor("agout2", [NE * CAP, 1032], F32, kind="Internal")
    acc_d = nc.dram_tensor("acc_d", [TH, D], F32, kind="Internal").ap()

    dbg = {}
    if debug:
        for nm, shp in (("dbg_conv", [128, 2048]), ("dbg_rby", [NH, 512]),
                        ("dbg_idxw", [128, 2 * EM * CAPH]),
                        ("dbg_xe", [128, 2048]), ("dbg_gv", [128, CAP]),
                        ("dbg_xln", [T, D]), ("dbg_p1", [2 * T, D]),
                        ("dbg_x1", [T, D]), ("dbg_probs", [128, CPS]),
                        ("dbg_sel", [128, CPS]), ("dbg_agout2", [NE * CAP, 1032]),
                        ("dbg_acc", [TH, D])):
            dt = BF16 if nm in ("dbg_xln", "dbg_p1", "dbg_xe", "dbg_gv") else F32
            if nm in ("dbg_idxw", "dbg_rby"):
                dt = F32
            if nm == "dbg_conv":
                dt = BF16
            dbg[nm] = nc.dram_tensor(nm, shp, dt, kind="ExternalOutput").ap()

    NB = 75264
    NF = 10520
    AB = nc.alloc_sbuf_tensor("AB", [128, NB], BF16)
    AFR = nc.alloc_sbuf_tensor("AFR", [128, NF], F32)

    def bv(off, n, p0=0, p1=128):
        return AB[p0:p1, off:off + n]

    def fv(off, n, p0=0, p1=128):
        return AFR[p0:p1, off:off + n]

    O_INNERT = 0
    O_OUTERT = O_INNERT + NMC * T
    O_VX = O_OUTERT + NMC * T
    O_W = O_VX + NTT * VX * NH
    O_XLT = O_W + 3 * NDC * 512
    O_WOUT = O_XLT + NDC * 512
    O_STG = O_WOUT + NMC * 1024
    assert O_STG + 1024 <= NB, O_STG
    NPH_ = NH // 2
    O_COMBT = O_W
    O_UT = O_COMBT + 2 * NPH_ * QB
    assert O_UT + 2 * (2 * QB) <= O_XLT
    O_CONV = O_XLT
    B_HT = 0
    B_R1 = B_HT + NDC * 512
    B_XE = B_R1 + 512
    B_XET = B_XE + CAPH * 1024
    B_W1 = B_XET + 2 * NDC * CAP
    B_W2 = B_W1 + 2 * NDC * 512
    B_W3 = B_W2 + 2 * NDC * 512
    B_RW1 = B_W3 + 2 * 4 * 1024
    B_RW2 = B_RW1 + NDC * RH
    B_PAB = B_RW2 + NE
    B_HS = B_PAB + 4 * 1024
    B_GV = B_HS + 1024
    assert B_GV + 256 <= NB, B_GV

    F_XT = 0
    F_SA = 2048
    F_SB = 3072
    F_T1 = 4096
    F_T2 = 5120
    F_T3 = 6144
    F_T4 = 7168
    F_T5 = 8192
    F_SLOT = 9216
    F_COLS = 10300
    assert F_COLS + 220 <= NF

    iotas = nc.alloc_sbuf_tensor("iotas", [128, CAP], F32)
    tokc = nc.alloc_sbuf_tensor("tokc", [128, NTC], F32)
    l8 = nc.alloc_sbuf_tensor("l8s", [128, 128], F32)
    idn = nc.alloc_sbuf_tensor("idns", [128, 128], F32)
    g128 = nc.alloc_sbuf_tensor("g128s", [128, 128], F32)
    rep8 = nc.alloc_sbuf_tensor("rep8s", [8, 64], F32)
    sub8 = nc.alloc_sbuf_tensor("sub8s", [64, 1], F32)
    idx64 = nc.alloc_sbuf_tensor("idx64s", [64, 1], F32)
    idx64i = nc.alloc_sbuf_tensor("idx64is", [64, 3], I32)
    myexpf = nc.alloc_sbuf_tensor("myexpfs", [8, 1], F32)
    pihalf = nc.alloc_sbuf_tensor("pihalfs", [1, 1], F32)
    bc16 = nc.alloc_sbuf_tensor("bc16s", [NE, 128], F32)
    sel16 = nc.alloc_sbuf_tensor("sel16s", [128, NE], F32)
    oh8 = nc.alloc_sbuf_tensor("oh8s", [SUB, NH * 64], F32)
    ones = nc.alloc_sbuf_tensor("oness", [1, 512], F32)
    tkbase = nc.alloc_sbuf_tensor("tkbs", [128, 1], F32)
    iotac = nc.alloc_sbuf_tensor("iotacs", [128, 1], F32)
    chpf = nc.alloc_sbuf_tensor("chpfs", [1, 4 * NH], F32)
    casa = nc.alloc_sbuf_tensor("casas", [1, 2 * NH], F32)
    cb64 = nc.alloc_sbuf_tensor("cb64s", [64, 2 * NH], F32)
    baf = nc.alloc_sbuf_tensor("bafs", [1, NH], F32)
    bef = nc.alloc_sbuf_tensor("befs", [1, NH], F32)
    wa_sb = nc.alloc_sbuf_tensor("wa_sbs", [128, NDC * NH], BF16)
    we_sb = nc.alloc_sbuf_tensor("we_sbs", [128, NDC * NH], BF16)
    myexp_sb = nc.alloc_sbuf_tensor("myexps", [EM, 1], I32)
    iotI = nc.alloc_sbuf_tensor("iotI", [128, CAP], I32)
    iotI2 = nc.alloc_sbuf_tensor("iotI2", [128, NTC], I32)
    idxi = nc.alloc_sbuf_tensor("idxi", [128, EM * CAPH + 4], I32)
    idxw = nc.alloc_sbuf_tensor("idxw", [128, EM * CAPH * 2], F32)
    sgm = nc.alloc_sbuf_tensor("sgms", [1, 4 * NH], F32)
    tctmp = nc.alloc_sbuf_tensor("tctmp", [1, 2 * NH], F32)
    estg = nc.alloc_sbuf_tensor("estgs", [128, NH], F32)
    a_stg = nc.alloc_sbuf_tensor("a_stgs", [NH, 512], F32)
    selTm = nc.alloc_sbuf_tensor("selTm", [128, 512], F32)
    ceTm = nc.alloc_sbuf_tensor("ceTm", [128, 512], F32)
    wTm = nc.alloc_sbuf_tensor("wTm", [128, 512], F32)
    pselb = nc.alloc_sbuf_tensor("pselb", [128, CAP], F32)
    rhsa = nc.alloc_sbuf_tensor("rhsa", [128, 2 * NTC], F32)
    g16 = nc.alloc_sbuf_tensor("g16", [NE, 24], F32)

    psA = nc.alloc_psum_tensor("psA", [128, 1024], F32)
    psB = nc.alloc_psum_tensor("psB", [128, 1024], F32)
    psC = nc.alloc_psum_tensor("psC", [128, 1024], F32)
    psD = nc.alloc_psum_tensor("psD", [128, 1024], F32)

    P = Prog(nc)
    E = P.emit
    V = P.val

    def vop(fn, waits=()):
        return E("vector", fn, inc="vec", waits=list(waits) + [("vec", V("vec"))])

    def sop(fn, waits=()):
        return E("scalar", fn, inc="scc", waits=list(waits) + [("scc", V("scc"))])

    mcol = fv(F_COLS + 0, 1)
    vcol = fv(F_COLS + 1, 1)
    m2col = fv(F_COLS + 2, 1)
    rstd = fv(F_COLS + 3, 1)
    sxc = fv(F_COLS + 4, 1)
    sxxc = fv(F_COLS + 5, 1)
    mxc = fv(F_COLS + 6, 1)
    gcol = fv(F_COLS + 7, 1)
    scol = fv(F_COLS + 8, 1)
    pfxc = fv(F_COLS + 9, 1)
    rcol = fv(F_COLS + 10, 1)
    vcol2 = fv(F_COLS + 11, 1)
    dcol = fv(F_COLS + 12, 1)
    idf = fv(F_COLS + 13, 1)
    idm = fv(F_COLS + 14, 1)
    a_sxc = fv(F_COLS + 40, 1)
    a_sxxc = fv(F_COLS + 41, 1)
    a_m = fv(F_COLS + 42, 1)
    a_m2 = fv(F_COLS + 43, 1)
    a_nm2 = fv(F_COLS + 44, 1)
    a_v = fv(F_COLS + 45, 1)
    a_r = fv(F_COLS + 46, 1)
    a_eps = fv(F_COLS + 47, 1)
    vlnc = [[fv(F_COLS + 50 + 8 * b + i, 1) for i in range(6)] for b in (0, 1)]

    for dst, src in ((baf[:], ba_in), (bef[:], be_in), (chpf[:], chp_in),
                     (tkbase[:], tkb_in), (oh8[:], oh8_in), (l8[:], l8_in),
                     (bc16[:], bc16_in), (sel16[:], sel16_in), (idn[:], idn_in),
                     (g128[:], g128_in),
                     (rep8[:], rep8_in), (sub8[:], sub8_in)):
        E("sync", lambda e, dst=dst, src=src: e.dma_start(out=dst, in_=src),
          inc="dsm", inck=16)
    E("gpsimd", lambda e: e.dma_start(
        out=wa_sb[:].rearrange("p (c n) -> p c n", c=NDC), in_=wa_in.rearrange("(c p) n -> p c n", p=128)),
      inc="dwA", inck=16)
    E("gpsimd", lambda e: e.dma_start(
        out=we_sb[:].rearrange("p (c n) -> p c n", c=NDC), in_=we_in.rearrange("(c p) n -> p c n", p=128)),
      inc="dwA", inck=16)
    E("sync", lambda e: e.dma_start(out=myexp_sb[:], in_=myexp_in),
      inc="dsm", inck=16)
    v_dsm = V("dsm")

    E("gpsimd", lambda e: e.iota(iotI[:], pattern=[[1, CAP]], base=0,
                                 channel_multiplier=0), inc="gpc")
    E("gpsimd", lambda e: e.tensor_copy(iotas[:], iotI[:]), inc="gpc",
      waits=[("gpc", V("gpc"))])
    E("gpsimd", lambda e: e.iota(iotI2[:], pattern=[[128, NTC]], base=0,
                                 channel_multiplier=1), inc="gpc",
      waits=[("gpc", V("gpc"))])
    E("gpsimd", lambda e: e.tensor_copy(tokc[:], iotI2[:]), inc="gpc",
      waits=[("gpc", V("gpc"))])
    E("gpsimd", lambda e: e.iota(iotI2[:, 0:1], pattern=[[1, 1]], base=0,
                                 channel_multiplier=1), inc="gpc",
      waits=[("gpc", V("gpc"))])
    E("gpsimd", lambda e: e.tensor_copy(iotac[:], iotI2[:, 0:1]), inc="gpc",
      waits=[("gpc", V("gpc"))])
    E("gpsimd", lambda e: e.memset(ones[:], 1.0), inc="gpc",
      waits=[("gpc", V("gpc"))])
    E("gpsimd", lambda e: e.memset(pihalf[:], float(PI / 2)), inc="gpc",
      waits=[("gpc", V("gpc"))])
    E("gpsimd", lambda e: e.memset(a_eps, float(1e-5)), inc="gpc",
      waits=[("gpc", V("gpc"))])
    v_gpc0 = V("gpc")

    sop(lambda e: e.activation(sgm[:, 0:3 * NH], chpf[:, 0:3 * NH], AF.Sigmoid),
        waits=[("dsm", v_dsm)])
    sop(lambda e: e.activation(sgm[:, 3 * NH:4 * NH], chpf[:, 3 * NH:4 * NH], AF.Tanh))
    sop(lambda e: e.activation(tctmp[:, NH:2 * NH], sgm[:, 0:NH], AF.Sin,
                               scale=float(PI)))
    sop(lambda e: e.activation(tctmp[:, 0:NH], sgm[:, 0:NH], AF.Sin,
                               scale=float(-PI), bias=pihalf[:]),
        waits=[("gpc", V("gpc"))])
    v_sc_ch = V("scc")
    vop(lambda e: e.tensor_mul(sgm[:, NH:2 * NH], sgm[:, NH:2 * NH],
                               sgm[:, 2 * NH:3 * NH]), waits=[("scc", v_sc_ch)])
    vop(lambda e: e.tensor_mul(sgm[:, NH:2 * NH], sgm[:, NH:2 * NH],
                               sgm[:, 3 * NH:4 * NH]))
    vop(lambda e: e.tensor_mul(casa[:, 0:NH], tctmp[:, 0:NH], sgm[:, NH:2 * NH]))
    v_casa = vop(lambda e: e.tensor_mul(casa[:, NH:2 * NH], tctmp[:, NH:2 * NH],
                                        sgm[:, NH:2 * NH]))
    E("tensor", lambda e: e.matmul(psB[0:64, 0:2 * NH], ones[0:1, 0:64], casa[:],
                                   start=True, stop=True),
      inc="tdc", waits=[("vec", v_casa), ("gpc", v_gpc0)])
    v_cb64 = vop(lambda e: e.tensor_copy(cb64[:], psB[0:64, 0:2 * NH]),
                 waits=[("tdc", V("tdc"))])

    wi_sb = bv(O_W, NDC * 512)
    wo_sb = bv(O_W + NDC * 512, NDC * 512)
    wv_sb = bv(O_W + 2 * NDC * 512, NDC * 512)
    wout_sb = bv(O_WOUT, NMC * 1024)
    v_dw = {}
    for nm, dst, src in (("wi", wi_sb, wi_in), ("wo", wo_sb, wo_in),
                         ("wv", wv_sb, wv_in)):
        E("gpsimd", lambda e, dst=dst, src=src: e.dma_start(
            out=dst.rearrange("p (c n) -> p c n", c=NDC), in_=src.rearrange("(c p) n -> p c n", p=128)),
          inc="d" + nm, inck=16)
        v_dw[nm] = ("d" + nm, V("d" + nm))
    v_dwA = V("dwA")

    xt = [fv(F_XT, 1024), fv(F_XT + 1024, 1024)]
    sA = fv(F_SA, 1024)
    sB = fv(F_SB, 1024)
    xstg = [bv(O_WOUT, 1024), bv(O_WOUT + 1024, 1024)]

    ln_war = [0]

    def ln_emit(src, dst, extra_w=(), dst_w=()):
        s0 = sop(lambda e: e.activation(sB, src, AF.Identity, accum_out=a_sxc),
                 waits=list(extra_w) + [("vec", ln_war[0]), ("gpc", v_gpc0)])
        sop(lambda e: e.activation(sB, src, AF.Square, accum_out=a_sxxc))
        sop(lambda e: e.activation(a_m, a_sxc, AF.Identity, scale=1.0 / 1024.0))
        sop(lambda e: e.activation(a_m2, a_m, AF.Square))
        sop(lambda e: e.activation(a_nm2, a_m2, AF.Identity, scale=-1.0))
        sop(lambda e: e.activation(a_v, a_sxxc, AF.Identity,
                                   scale=1.0 / 1024.0, bias=a_nm2))
        sop(lambda e: e.activation(a_r, a_v, AF.Ln, bias=a_eps))
        s3 = sop(lambda e: e.activation(a_r, a_r, AF.Exp, scale=-0.5))
        vn = vop(lambda e: e.tensor_scalar(dst, src, a_m, a_r,
                                           ALU.subtract, ALU.mult),
                 waits=list(dst_w) + [("scc", s3)])
        ln_war[0] = vn
        return vn

    def emit_a1_tile(t):
        b = t % 2
        E("sync", lambda e, t=t, b=b: e.dma_start(
            out=xt[b], in_=x_in[t * 128:(t + 1) * 128, :]),
          inc=f"dx{b}", inck=16,
          waits=[(f"xf{b}", P.counts.get(f"xf{b}", 0))])
        vl = V(f"dx{b}")
        vn = ln_emit(xt[b], xstg[b],
                     extra_w=[(f"dx{b}", vl)],
                     dst_w=[(f"xs{b}", P.counts.get(f"xs{b}", 0))])
        E("vector", lambda e: e.nop(), inc=f"xf{b}", waits=[("vec", vn)])
        E("gpsimd", lambda e, t=t, b=b: e.dma_start(
            out=xln_d[t * 128:(t + 1) * 128, :], in_=xstg[b]),
          inc=f"xs{b}", inck=16, waits=[("vec", vn)])

    xlt = [bv(O_XLT, NDC * 512), bv(O_WOUT + 2048, NDC * 512)]
    vx = bv(O_VX, NTT * VX * NH)
    innerT = bv(O_INNERT, NMC * T)
    outerT = bv(O_OUTERT, NMC * T)
    E("gpsimd", lambda e: e.memset(vx, 1.0), inc="gpc",
      waits=[("gpc", V("gpc"))])
    v_vxones = V("gpc")

    for t in range(8):
        emit_a1_tile(t)
    xlt_war = [0, 0]
    a2war = [0, 0, 0, 0]
    a2warv = [0, 0]
    sg_war = [0]
    ps_rot = [psC[:, 0:512], psC[:, 512:1024], psD[:, 0:512], psD[:, 512:1024]]

    def emit_xlt_trans(tc):
        tp = tc % 2
        for dc in range(NDC):
            E("sync", lambda e, tc=tc, dc=dc, tp=tp: e.dma_start_transpose(
                xlt[tp][:, dc * 512:(dc + 1) * 512],
                xln_d[tc * 512:(tc + 1) * 512, dc * 128:(dc + 1) * 128]),
              inc=f"tr{tp}", inck=16,
              waits=([("xs0", 16 * (2 * tc + 2)), ("xs1", 16 * (2 * tc + 2)),
                      ("tdc", xlt_war[tp])] if dc == 0 else []))

    emit_xlt_trans(0)
    for tc in range(NT2):
        tp = tc % 2
        vtr = V(f"tr{tp}")
        base_w = [(f"tr{tp}", vtr), ("dsm", v_dsm)]

        gi = 0
        for wnm, wsb, dstT in (("wi", wi_sb, innerT), ("wo", wo_sb, outerT)):
            for j in range(NMC):
                ps_o = ps_rot[gi % 4]
                for dc in range(NDC):
                    E("tensor", lambda e, wsb=wsb, j=j, dc=dc, tp=tp, ps_o=ps_o: e.matmul(
                        ps_o,
                        wsb[:, dc * 512 + j * 128:dc * 512 + (j + 1) * 128],
                        xlt[tp][:, dc * 512:(dc + 1) * 512],
                        start=(dc == 0), stop=(dc == NDC - 1)),
                      inc="tdc" if dc == NDC - 1 else None,
                      waits=(((base_w + [v_dw[wnm]]) if gi % 4 == 0 else [])
                             + [("vec", a2war[gi % 4])])
                      if dc == 0 else [])
                vmm = V("tdc")
                vop(lambda e, dstT=dstT, j=j, tc=tc, ps_o=ps_o: e.tensor_copy(
                    dstT[:, j * T + tc * 512:j * T + (tc + 1) * 512], ps_o),
                    waits=[("tdc", vmm)])
                a2war[gi % 4] = V("vec")
                if tc + 2 < NT2 and gi % 2 == 0:
                    emit_a1_tile(4 * (tc + 2) + gi // 2)
                gi += 1
        for ti in range(4):
            tt = tc * 4 + ti
            for dc in range(NDC):
                E("tensor", lambda e, ti=ti, dc=dc, tp=tp: e.matmul(
                    psB[:, 0:NH],
                    xlt[tp][:, dc * 512 + ti * 128:dc * 512 + (ti + 1) * 128],
                    we_sb[:, dc * NH:(dc + 1) * NH],
                    start=(dc == 0), stop=False),
                  waits=[("scc", sg_war[0]), ("dwA", v_dwA)] if dc == 0 else [])
            E("tensor", lambda e: e.matmul(psB[:, 0:NH], ones[0:1, 0:128],
                                           bef[:], start=False, stop=True),
              inc="tdc")
            vmm = V("tdc")
            vsg = sop(lambda e: e.activation(estg[:], psB[:, 0:NH], AF.Sigmoid),
                      waits=[("tdc", vmm), ("vec", V("vec"))])
            sg_war[0] = vsg
            vp = ti % 2
            for dc in range(NDC):
                E("tensor", lambda e, ti=ti, dc=dc, tp=tp, vp=vp: e.matmul(
                    psA[:, vp * 512:(vp + 1) * 512],
                    xlt[tp][:, dc * 512 + ti * 128:dc * 512 + (ti + 1) * 128],
                    wv_sb[:, dc * 512:(dc + 1) * 512],
                    start=(dc == 0), stop=(dc == NDC - 1)),
                  inc="tdc" if dc == NDC - 1 else None,
                  waits=[("vec", a2warv[vp]), ("gpc", v_vxones),
                         v_dw["wv"]] if dc == 0 else [])
            vmm2 = V("tdc")
            for h in range(NH):
                vop(lambda e, tt=tt, h=h, vp=vp: e.tensor_scalar(
                    vx[:, tt * VX * NH + h * VX:tt * VX * NH + h * VX + DH],
                    psA[:, vp * 512 + h * DH:vp * 512 + (h + 1) * DH],
                    estg[:, h:h + 1], None, ALU.mult),
                    waits=[("tdc", vmm2), ("scc", vsg)])
            vop(lambda e, tt=tt: e.tensor_copy(
                vx[:, tt * VX * NH:(tt + 1) * VX * NH].rearrange(
                    "p (h v) -> p h v", h=NH)[:, :, DH:DH + 1],
                estg[:].rearrange("p (h o) -> p h o", h=NH)),
                waits=[("scc", vsg)])
            a2warv[vp] = V("vec")
        for dc in range(NDC):
            E("tensor", lambda e, dc=dc, tp=tp: e.matmul(
                psB[0:NH, 512:1024],
                wa_sb[:, dc * NH:(dc + 1) * NH],
                xlt[tp][:, dc * 512:(dc + 1) * 512],
                start=(dc == 0), stop=False),
              waits=[("scc", sg_war[0]), ("dwA", v_dwA),
                     ("das", P.counts.get("das", 0))] if dc == 0 else [])
        E("tensor", lambda e: e.matmul(psB[0:NH, 512:1024], baf[:], ones[:],
                                       start=False, stop=True), inc="tdc")
        xlt_war[tp] = V("tdc")
        vmm3 = V("tdc")
        vsa = sop(lambda e: e.activation(a_stg[:], psB[0:NH, 512:1024], AF.Sigmoid),
                  waits=[("tdc", vmm3)])
        sg_war[0] = vsa
        E("sync", lambda e, tc=tc: e.dma_start(
            out=a_d[:, tc * 512:(tc + 1) * 512], in_=a_stg[:]),
          inc="das", inck=16, waits=[("scc", vsa)])
        if tc + 1 < NT2:
            emit_xlt_trans(tc + 1)

    a2_tdc = V("tdc")
    E("gpsimd", lambda e: e.dma_start(
        out=wout_sb.rearrange("p (c n) -> p c n", c=NMC), in_=wout_in.rearrange("(c p) n -> p c n", p=128)),
      inc="dwA", inck=16, waits=[("tdc", a2_tdc)])
    v_dwout = V("dwA")
    a2_scc = V("scc")
    a2_vec = V("vec")
    a2_das = V("das")

    NPH = NH // 2
    stp = [psA, psB]
    Ut = [bv(O_UT, 2 * QB), bv(O_UT + 2 * QB, 2 * QB)]
    conv = [bv(O_CONV, NPH * QB), bv(O_CONV + NPH * QB, NPH * QB)]
    combT = [bv(O_COMBT, NPH * QB), bv(O_COMBT + NPH * QB, NPH * QB)]
    exstA = fv(F_T1, QB, 0, 2)
    exstB = fv(F_T1 + QB, QB, 0, 2)
    nzn = fv(F_T2, QB, 0, NH)
    nzz = fv(F_T3, QB, 0, NH)
    rby = fv(F_T4, QB, 0, NH)
    aq = fv(F_T5, QB, 0, NH)
    pstgb = [bv(O_STG, 512), bv(O_STG + 512, 512)]
    AGC = 1024
    NAG = T // AGC
    O_B1 = O_COMBT + 2 * NPH * QB + 2 * (2 * QB)
    pa_s = [bv(O_B1, 1024), bv(O_B1 + 1024, 1024)]
    pb_s = [bv(O_B1 + 2048, 1024), bv(O_B1 + 3072, 1024)]
    hstg = bv(O_B1 + 4096, 1024)
    assert O_B1 + 5120 <= O_XLT
    O_HT = O_STG + 1024
    assert O_HT + NDC * 512 <= NB
    hT = bv(O_HT, NDC * 512)
    r1T = nc.alloc_sbuf_tensor("r1Ts", [128, 256], BF16)[:, :]

    rw1_sb = bv(O_B1 + 5120, NDC * RH)
    rw2_sb = nc.alloc_sbuf_tensor("rw2s", [128, NE], BF16)[:, :]

    def ln_v_stats(src_ap, cols, extra_w=()):
        sx, sxx, m, vv, m2, rs = cols
        vop(lambda e: e.tensor_scalar(sB, src_ap, 1.0, 0.0, ALU.mult, ALU.add,
                                      accum_out=sx),
            waits=list(extra_w) + [("dlg", P.counts.get("dlg", 0))])
        vop(lambda e: e.scalar_tensor_tensor(sB, src_ap, 1.0, src_ap,
                                             ALU.mult, ALU.mult,
                                             accum_out=sxx))
        vop(lambda e: e.tensor_scalar(m, sx, 1.0 / 1024.0, None, ALU.mult))
        vop(lambda e: e.tensor_scalar(vv, sxx, 1.0 / 1024.0, None, ALU.mult))
        vop(lambda e: e.tensor_mul(m2, m, m))
        vop(lambda e: e.tensor_tensor(vv, vv, m2, ALU.subtract))
        return vop(lambda e: e.tensor_scalar(vv, vv, float(EPS), None, ALU.add))

    def ln_v_finish(src_ap, dst, cols, v0, extra_w=()):
        sx, sxx, m, vv, m2, rs = cols
        s2 = sop(lambda e: e.activation(rs, vv, AF.Ln), waits=[("vec", v0)])
        s3 = sop(lambda e: e.activation(rs, rs, AF.Exp, scale=-0.5))
        return vop(lambda e: e.tensor_scalar(dst, src_ap, m, rs,
                                             ALU.subtract, ALU.mult),
                   waits=list(extra_w) + [("scc", s3)])

    sAb = [sA, fv(F_SLOT, 1024)]
    _b1v = {}

    def emit_b1_part1(t):
        b = t % 2
        cch = (t * 128) // AGC
        row = cch * 2 * AGC + (t * 128) % AGC
        E("sync", lambda e, row=row, b=b: e.dma_start(
            out=pa_s[b], in_=agout1.ap()[row:row + 128, :]),
          inc=f"pa{b}", inck=16,
          waits=[("dcc", cch + 1), (f"pf{b}", P.counts.get(f"pf{b}", 0))])
        E("sync", lambda e, row=row, b=b: e.dma_start(
            out=pb_s[b], in_=agout1.ap()[row + AGC:row + AGC + 128, :]),
          inc=f"pa{b}", inck=16)
        E("sync", lambda e, t=t, b=b: e.dma_start(
            out=xt[b], in_=x_in[t * 128:(t + 1) * 128, :]),
          inc=f"dx{b}", inck=16, waits=[(f"xf{b}", P.counts.get(f"xf{b}", 0))])
        vpa = V(f"pa{b}")
        vs = vop(lambda e, b=b: e.tensor_tensor(sAb[b], pa_s[b], pb_s[b],
                                                ALU.add),
                 waits=[(f"pa{b}", vpa), ("dx1", P.counts.get("dx1", 0))])
        E("vector", lambda e: e.nop(), inc=f"pf{b}", waits=[("vec", vs)])
        _b1v[t] = (ln_v_stats(sAb[b], vlnc[b]), V(f"dx{b}"))

    def emit_b1_part2(t):
        b = t % 2
        v0, vdx = _b1v[t]
        ln_v_finish(sAb[b], sB, vlnc[b], v0)
        vx1 = vop(lambda e, b=b: e.tensor_tensor(sAb[b], xt[b], sB, ALU.add),
                  waits=[(f"dx{b}", vdx)])
        E("vector", lambda e: e.nop(), inc=f"xf{b}", waits=[("vec", vx1)])
        E("sync", lambda e, t=t, b=b: e.dma_start(
            out=x1_d[t * 128:(t + 1) * 128, :], in_=sAb[b]),
          inc="dx1", inck=16, waits=[("vec", vx1)])
        _b1v[t] = ln_v_stats(sAb[b], vlnc[b])

    def emit_b1_part3(t):
        b = t % 2
        v0 = _b1v.pop(t)
        vh = ln_v_finish(sAb[b], hstg, vlnc[b], v0,
                         extra_w=[("dht", P.counts.get("dht", 0))])
        E("sync", lambda e, t=t: e.dma_start(
            out=h_d[t * 128:(t + 1) * 128, :], in_=hstg),
          inc="dht", inck=16, waits=[("vec", vh)])

    def emit_b1_tile(t):
        emit_b1_part1(t)
        emit_b1_part2(t)
        emit_b1_part3(t)

    def emit_router_trans(tc, half):
        need_dht = 4 * (tc + 1)
        for dc in range(4 * half, 4 * half + 4):
            E("sync", lambda e, tc=tc, dc=dc: e.dma_start_transpose(
                hT[:, dc * 512:(dc + 1) * 512],
                h_d[tc * 512:(tc + 1) * 512, dc * 128:(dc + 1) * 128]),
              inc="thx", inck=16,
              waits=([("dht", 16 * need_dht), ("tdc", V("tdc"))]
                     if dc == 4 * half else []))

    def emit_router_mm(tc):
        vth = V("thx")
        for dc in range(NDC):
            E("tensor", lambda e, dc=dc: e.matmul(
                psD[:, 512:1024], rw1_sb[:, dc * RH:(dc + 1) * RH],
                hT[:, dc * 512:(dc + 1) * 512],
                start=(dc == 0), stop=(dc == NDC - 1)),
              inc="tdc" if dc == NDC - 1 else None,
              waits=[("thx", vth), ("dwB", P.counts.get("dwB", 0)),
                     ("scc", V("scc")), ("rbu", P.counts.get("rbu", 0)),
                     ("woe", P.counts.get("woe", 0))] if dc == 0 else [])
        for hh in range(2):
            vsi = sop(lambda e, hh=hh: e.activation(
                r1T, psD[:, 512 + 256 * hh:768 + 256 * hh], AF.Sigmoid),
                      waits=[("tdc", V("tdc"))])
            vmu = vop(lambda e, hh=hh: e.tensor_tensor(
                r1T, r1T, psD[:, 512 + 256 * hh:768 + 256 * hh], ALU.mult),
                waits=[("scc", vsi)] + ([("vec", V("vec"))] if hh == 0 else []))
            E("tensor", lambda e, hh=hh: e.matmul(
                psD[0:NE, 512 + 256 * hh:768 + 256 * hh], rw2_sb, r1T,
                start=True, stop=True),
              inc="tdc", waits=[("vec", vmu)])
        vlg = V("tdc")
        vle = vop(lambda e: e.tensor_copy(fv(F_SB, 512, 0, NE), psD[0:NE, 512:1024]),
                  waits=[("tdc", vlg), ("dlg", P.counts.get("dlg", 0))])
        E("vector", lambda e: e.nop(), inc="rtl", waits=[("vec", vle)])
        E("sync", lambda e, tc=tc: e.dma_start(
            out=lgt_d[:, tc * 512:(tc + 1) * 512], in_=fv(F_SB, 512, 0, NE)),
          inc="dlg", inck=16, waits=[("vec", vle)])

    def emit_router_tc(tc):
        emit_router_trans(tc, 0)
        emit_router_trans(tc, 1)
        emit_router_mm(tc)

    E("gpsimd", lambda e: e.dma_start(
        out=rw1_sb.rearrange("p (c n) -> p c n", c=NDC),
        in_=rw1_in.rearrange("(c p) n -> p c n", p=128)),
      inc="dwB", inck=16, waits=[("tdc", a2_tdc)])
    E("gpsimd", lambda e: e.dma_start(out=rw2_sb, in_=rw2_in), inc="dwB", inck=16)

    inject = {}

    def add_inject(qb, p, kc, fn):
        inject.setdefault((qb, p, kc), []).append(fn)

    last_bcast_tdc = [0]

    def emit_chamber_step(qbp, h):
        j = h // 2
        pb_ = 64 * (h % 2)
        cbq = combT[qbp % 2]
        cvq = conv[qbp % 2]
        vb = E("tensor", lambda e, h=h: e.matmul(
            psD[0:64, 512:1024],
            oh8[:, h * 64:(h + 1) * 64],
            rby[:, 0:QB],
            start=True, stop=True),
          inc="tdc",
          waits=[("rch", P.counts.get("rch", 0)),
                 ("rbu", P.counts.get("rbu", 0)),
                 ("woe", P.counts.get("woe", 0)),
                 ("rtl", P.counts.get("rtl", 0))])
        last_bcast_tdc[0] = V("tdc")
        cm = fv(F_SB, 512, 0, 64)
        t1 = fv(F_SB + 512, 512, 0, 32)
        vcm = vop(lambda e, j=j, pb_=pb_, cvq=cvq: e.tensor_tensor(
            cm, cvq[pb_:pb_ + 64, j * QB:(j + 1) * QB],
            psD[0:64, 512:1024][:, 0:QB], ALU.mult),
            waits=[("tdc", vb), ("cve", NPH * (qbp + 1)),
                   ("dlg", P.counts.get("dlg", 0))])
        E("vector", lambda e: e.nop(), inc="rbu", waits=[("vec", vcm)])
        vop(lambda e, h=h: e.tensor_scalar(
            t1, cm[32:64, :], cb64[0:32, NH + h:NH + h + 1], None, ALU.mult),
            waits=[("vec", v_cb64)])
        vop(lambda e, h=h, j=j, pb_=pb_, cbq=cbq: e.scalar_tensor_tensor(
            cbq[pb_:pb_ + 32, j * QB:(j + 1) * QB],
            cm[0:32, :], cb64[0:32, h:h + 1], t1, ALU.mult, ALU.subtract))
        vop(lambda e, h=h: e.tensor_scalar(
            t1, cm[32:64, :], cb64[0:32, h:h + 1], None, ALU.mult))
        v4 = vop(lambda e, h=h, j=j, pb_=pb_, cbq=cbq: e.scalar_tensor_tensor(
            cbq[pb_ + 32:pb_ + 64, j * QB:(j + 1) * QB],
            cm[0:32, :], cb64[0:32, NH + h:NH + h + 1], t1, ALU.mult, ALU.add))
        E("vector", lambda e: e.nop(), inc="chd", waits=[("vec", v4)])

    def emit_wout_group(qbp, g):
        tt, ncol = g // 2, g % 2
        cbq = combT[qbp % 2]
        for dc in range(NMC):
            E("tensor", lambda e, tt=tt, ncol=ncol, dc=dc, cbq=cbq: e.matmul(
                psD[:, 512:1024],
                cbq[:, dc * QB + tt * 128:dc * QB + (tt + 1) * 128],
                wout_sb[:, dc * 1024 + ncol * 512:dc * 1024 + (ncol + 1) * 512],
                start=(dc == 0), stop=(dc == NMC - 1)),
              inc="tdc" if dc == NMC - 1 else None,
              waits=[("chd", P.counts.get("chd", 0)),
                     ("woe", P.counts.get("woe", 0)),
                     ("dwA", v_dwout),
                     ("rtl", P.counts.get("rtl", 0))] if dc == 0 else [])
        vwo = V("tdc")
        sbi = g % 2
        vev = E("vector", lambda e, sbi=sbi: e.tensor_copy(pstgb[sbi], psD[:, 512:1024]),
                inc="woe",
                waits=[("tdc", vwo), ("vec", V("vec")),
                       (f"ps{sbi}", P.counts.get(f"ps{sbi}", 0))])
        E("vector", lambda e: e.nop(), inc="vec", waits=[("woe", vev)])
        E("sync", lambda e, qbp=qbp, tt=tt, ncol=ncol, sbi=sbi: e.dma_start(
            out=agin1.ap()[qbp * QB + tt * 128:qbp * QB + (tt + 1) * 128,
                           ncol * 512:(ncol + 1) * 512],
            in_=pstgb[sbi]),
          inc=f"ps{sbi}", inck=16, waits=[("woe", vev)])

    def emit_ag1(c):
        vps = (V("ps0"), V("ps1"))
        E("gpsimd", lambda e, c=c: e.collective_compute(
            "AllGather", ALU.bypass,
            replica_groups=[[0, 1], [2, 3], [4, 5], [6, 7]],
            ins=[agin1.ap()[c * AGC:(c + 1) * AGC, :].opt()],
            outs=[agout1.ap()[c * 2 * AGC:(c + 1) * 2 * AGC, :].opt()]),
          inc="dcc", waits=[("ps0", vps[0]), ("ps1", vps[1])])

    for qb in range(1, NQB):
        for h in range(NH):
            add_inject(qb, 1, 1 + (h * (NKC - 3)) // NH,
                       lambda qbp=qb - 1, h=h: emit_chamber_step(qbp, h))
        for g in range(2 * (QB // 128)):
            add_inject(qb, 2, 1 + (g * (NKC - 3)) // (2 * (QB // 128)),
                       lambda qbp=qb - 1, g=g: emit_wout_group(qbp, g))
    NSL = NKC - 4
    NS = 4 * NSL

    def slot_of(s):
        return (s // NSL, 2 + s % NSL)

    for c in range(NAG):
        qb_at = 2 * c + 2
        if qb_at < NQB:
            add_inject(qb_at, 3, 0, lambda c=c: emit_ag1(c))
        qb_b1 = 2 * c + 3
        if qb_b1 < NQB and not DISABLE_B1_INJECT:
            for i in range(8):
                t = c * 8 + i
                for k, fn in ((0, emit_b1_part1), (2, emit_b1_part2),
                              (4, emit_b1_part3)):
                    p_, kc_ = slot_of(6 * i + k)
                    add_inject(qb_b1, p_, kc_, lambda t=t, fn=fn: fn(t))
            for s, fn in ((min(26, NS - 1), lambda tc=2 * c: emit_router_trans(tc, 0)),
                          (min(28, NS - 1), lambda tc=2 * c: emit_router_trans(tc, 1)),
                          (min(34, NS - 1), lambda tc=2 * c: emit_router_mm(tc)),
                          (min(50, NS - 1), lambda tc=2 * c + 1: emit_router_trans(tc, 0)),
                          (min(54, NS - 1), lambda tc=2 * c + 1: emit_router_trans(tc, 1)),
                          (min(62, NS - 1), lambda tc=2 * c + 1: emit_router_mm(tc))):
                p_, kc_ = slot_of(s)
                add_inject(qb_b1, p_, kc_, fn)


    for qb in range(NQB):
        for p in range(NPH):
            def emit_S(kc):
                buf = kc % 2
                w0 = [(f"exp{buf}", P.counts.get(f"exp{buf}", 0))]
                if qb == 0 and p == 0 and kc < 2:
                    w0 += [("tdc", a2_tdc), ("vec", a2_vec), ("scc", a2_scc)]
                E("tensor", lambda e, p=p, kc=kc, qb=qb, buf=buf: e.matmul(
                    stp[buf][:, 0:512],
                    outerT[0:64, p * T + kc * 128:p * T + (kc + 1) * 128],
                    innerT[0:64, p * T + qb * QB:p * T + (qb + 1) * QB],
                    start=True, stop=True),
                  waits=w0)
                E("tensor", lambda e, p=p, kc=kc, qb=qb, buf=buf: e.matmul(
                    stp[buf][:, 512:1024],
                    outerT[64:128, p * T + kc * 128:p * T + (kc + 1) * 128],
                    innerT[64:128, p * T + qb * QB:p * T + (qb + 1) * QB],
                    start=True, stop=True),
                  inc="tdc")
                vst = V("tdc")
                E("scalar", lambda e, buf=buf: e.activation(
                    Ut[buf], stp[buf][:], AF.Exp, scale=0.125),
                  inc=f"exp{buf}",
                  waits=[("tdc", vst),
                         (f"pvd{buf}", P.counts.get(f"pvd{buf}", 0))])

            def emit_PVX(kc):
                buf = kc % 2
                vexp = V(f"exp{buf}")
                w1 = [(f"exp{buf}", vexp)]
                if kc == 0:
                    w1.append(("cve", P.counts.get("cve", 0)))
                E("tensor", lambda e, p=p, kc=kc, buf=buf: e.matmul(
                    psC[0:64, 0:512],
                    vx[:, kc * VX * NH + (2 * p) * VX:kc * VX * NH + (2 * p) * VX + DH],
                    Ut[buf][:, 0:512],
                    start=(kc == 0), stop=(kc == NKC - 1), tile_position=(0, 0)),
                  waits=w1)
                E("tensor", lambda e, p=p, kc=kc, buf=buf: e.matmul(
                    psC[64:128, 0:512],
                    vx[:, kc * VX * NH + (2 * p + 1) * VX:kc * VX * NH + (2 * p + 1) * VX + DH],
                    Ut[buf][:, 512:1024],
                    start=(kc == 0), stop=(kc == NKC - 1), tile_position=(0, 64),
                    skip_group_check=True))
                exm = 2 if kc == NKC - 1 else 64
                E("tensor", lambda e, p=p, kc=kc, buf=buf, exm=exm: e.matmul(
                    psC[0:exm, 512:1024],
                    vx[:, kc * VX * NH + (2 * p) * VX + DH:kc * VX * NH + (2 * p) * VX + DH + exm],
                    Ut[buf][:, 0:512],
                    start=(kc == 0), stop=(kc == NKC - 1), tile_position=(0, 0),
                    skip_group_check=True),
                  waits=[("exe", P.counts.get("exe", 0))] if kc == 0 else [])
                E("tensor", lambda e, p=p, kc=kc, buf=buf, exm=exm: e.matmul(
                    psD[64:64 + exm, 0:512],
                    vx[:, kc * VX * NH + (2 * p + 1) * VX + DH:kc * VX * NH + (2 * p + 1) * VX + DH + exm],
                    Ut[buf][:, 512:1024],
                    start=(kc == 0), stop=(kc == NKC - 1), tile_position=(0, 64),
                    skip_group_check=True),
                  inc=f"pvd{buf}")

            def do_inject(kc):
                for fn in inject.pop((qb, p, kc), []):
                    fn()

            do_inject(-1)
            if qb == 0 and p == 0:
                emit_S(0)
                emit_S(1)
            for kc in range(NKC):
                emit_PVX(kc)
                if kc + 2 < NKC:
                    emit_S(kc + 2)
                do_inject(kc)
            def prologue_kk(kk, nqb, np_):
                buf = kk % 2
                E("tensor", lambda e, p2=np_, kc=kk, qb2=nqb, buf=buf: e.matmul(
                    stp[buf][:, 0:512],
                    outerT[0:64, p2 * T + kc * 128:p2 * T + (kc + 1) * 128],
                    innerT[0:64, p2 * T + qb2 * QB:p2 * T + (qb2 + 1) * QB],
                    start=True, stop=True),
                  waits=[(f"exp{buf}", P.counts.get(f"exp{buf}", 0))])
                E("tensor", lambda e, p2=np_, kc=kk, qb2=nqb, buf=buf: e.matmul(
                    stp[buf][:, 512:1024],
                    outerT[64:128, p2 * T + kc * 128:p2 * T + (kc + 1) * 128],
                    innerT[64:128, p2 * T + qb2 * QB:p2 * T + (qb2 + 1) * QB],
                    start=True, stop=True),
                  inc="tdc")
                vst = V("tdc")
                E("scalar", lambda e, buf=buf: e.activation(
                    Ut[buf], stp[buf][:], AF.Exp, scale=0.125),
                  inc=f"exp{buf}",
                  waits=[("tdc", vst),
                         (f"pvd{buf}", P.counts.get(f"pvd{buf}", 0))])

            nqb, np_ = (qb, p + 1) if p + 1 < NPH else (qb + 1, 0)
            if nqb < NQB:
                prologue_kk(0, nqb, np_)
            vpv = V(f"pvd{(NKC - 1) % 2}")
            cw = [(f"pvd{(NKC - 1) % 2}", vpv)]
            if qb >= 2:
                cw.append(("rbu", NH * (qb - 1)))
            E("scalar", lambda e, p=p, qb=qb: e.activation(
                conv[qb % 2][:, p * QB:(p + 1) * QB], psC[0:128, 0:QB],
                AF.Identity),
              inc="cve", waits=cw)
            if nqb < NQB:
                prologue_kk(1, nqb, np_)
            E("scalar", lambda e: e.activation(exstA, psC[0:2, 512:512 + QB],
                                               AF.Identity),
              inc="exe", waits=[("dnzA", P.counts.get("dnzA", 0))])
            vexA = V("exe")
            E("scalar", lambda e: e.activation(exstB, psD[64:66, 0:QB],
                                               AF.Identity),
              inc="exe", waits=[("dnzB", P.counts.get("dnzB", 0))])
            vexB = V("exe")
            E("sync", lambda e, p=p: e.dma_start(out=_nzrow(AFR, F_T2, 2 * p, QB), in_=exstA[0:1, :]),
              inc="dnzA", inck=16,
              waits=[("exe", vexA), ("rch", P.counts.get("rch", 0))])
            E("sync", lambda e, p=p: e.dma_start(out=_nzrow(AFR, F_T3, 2 * p, QB), in_=exstA[1:2, :]),
              inc="dnzA", inck=16)
            E("sync", lambda e, p=p: e.dma_start(out=_nzrow(AFR, F_T2, 2 * p + 1, QB), in_=exstB[0:1, :]),
              inc="dnzB", inck=16,
              waits=[("exe", vexB), ("rch", P.counts.get("rch", 0))])
            E("sync", lambda e, p=p: e.dma_start(out=_nzrow(AFR, F_T3, 2 * p + 1, QB), in_=exstB[1:2, :]),
              inc="dnzB", inck=16)
        vnz_allA = V("dnzA")
        vnz_allB = V("dnzB")
        E("sync", lambda e, qb=qb: e.dma_start(
            out=aq, in_=a_d[:, qb * QB:(qb + 1) * QB]),
          inc="daq", inck=16, waits=[("das", a2_das), ("vec", V("vec"))])
        vdaq = V("daq")
        vop(lambda e: e.tensor_mul(rby, nzn, aq),
            waits=[("dnzA", vnz_allA), ("dnzB", vnz_allB), ("daq", vdaq),
                   ("tdc", last_bcast_tdc[0])])
        vop(lambda e: e.scalar_tensor_tensor(rby, nzz, 1e-8, rby, ALU.mult, ALU.add))
        vop(lambda e: e.reciprocal(rby, rby))
        vop(lambda e: e.tensor_mul(rby, rby, aq))
        E("vector", lambda e: e.nop(), inc="rch", waits=[("vec", V("vec"))])

    assert not inject, f"unfired injections: {list(inject.keys())}"
    for h in range(NH):
        emit_chamber_step(NQB - 1, h)
    for g in range(2 * (QB // 128)):
        emit_wout_group(NQB - 1, g)
    emit_ag1(NAG - 1)
    if DISABLE_B1_INJECT:
        for c in range(NAG - 1):
            for i in range(8):
                emit_b1_tile(c * 8 + i)
            emit_router_tc(2 * c)
            emit_router_tc(2 * c + 1)
    for i in range(8):
        emit_b1_tile((NAG - 1) * 8 + i)
    emit_router_tc(2 * (NAG - 1))
    emit_router_tc(2 * (NAG - 1) + 1)
    b1_dht = V("dht")
    b1_dx1 = V("dx1")
    b_dlg = V("dlg")

    pbl = fv(F_T1, CPS)
    pbx = fv(F_T2, CPS)
    c1 = fv(F_T3, CPS)
    c2 = fv(F_T4, CPS)
    c3 = fv(F_T5, CPS)
    c4 = fv(F_T1 + 512, CPS)
    c5 = fv(F_T2 + 512, CPS)
    c6 = fv(F_T3 + 512, CPS)
    NCS = int(math.log2(CPS))
    locol = fv(F_COLS + 20, 1)
    hicol = fv(F_COLS + 21, 1)
    micol = fv(F_COLS + 22, 1)
    lo16 = g16[:, 0:1]
    hi16 = g16[:, 1:2]
    mi16 = g16[:, 2:3]
    ct16 = g16[:, 3:4]
    gg16 = g16[:, 4:5]
    t16a = g16[:, 5:6]
    t16b = g16[:, 6:7]
    v16 = g16[:, 7:8]
    s16 = g16[:, 8:9]
    r16 = g16[:, 9:10]
    d16 = g16[:, 10:11]

    E("sync", lambda e: e.dma_start(
        out=pbl, in_=lgt_d.rearrange("e (s c) -> (e s) c", s=SUB)),
      inc="dpb", inck=16, waits=[("dlg", b_dlg)])
    v_dpb = V("dpb")

    def gsum(src_col, dst16, extra=()):
        vt = E("tensor", lambda e: e.matmul(psB[0:NE, 512:513], sel16[:], src_col,
                                            start=True, stop=True),
               inc="tdc", waits=list(extra) + [("vec", V("vec")), ("g16u", P.counts.get("g16u", 0))])
        return vop(lambda e: e.tensor_copy(dst16, psB[0:NE, 512:513]),
                   waits=[("tdc", vt)])

    def bcast16(src16, dstcol, extra=()):
        vt = E("tensor", lambda e: e.matmul(psB[:, 513:514], bc16[:], src16,
                                            start=True, stop=True),
               inc="tdc", waits=list(extra) + [("vec", V("vec"))])
        return vop(lambda e: e.tensor_copy(dstcol, psB[:, 513:514]),
                   waits=[("tdc", vt)])

    vop(lambda e: e.tensor_reduce(mxc, pbl, mybir.AxisListType.X, ALU.max),
        waits=[("dpb", v_dpb)])
    vq = V("vec")
    E("sync", lambda e: e.dma_start(out=gmx_d, in_=mxc), inc="dgm", inck=16,
      waits=[("vec", vq)])
    E("sync", lambda e: e.dma_start(
        out=g16[:, 11:11 + SUB], in_=gmx_d.rearrange("(e s) o -> e (s o)", s=SUB)),
      inc="dgm", inck=16, waits=[("dgm", V("dgm"))])
    v_dgm = V("dgm")
    vop(lambda e: e.tensor_reduce(t16a, g16[:, 11:11 + SUB], mybir.AxisListType.X,
                                  ALU.max), waits=[("dgm", v_dgm)])
    bcast16(t16a, mxc)
    vop(lambda e: e.tensor_scalar(mxc, mxc, -1.0, None, ALU.mult))
    vexp0 = V("vec")
    sxp = sop(lambda e: e.activation(pbx, pbl, AF.Exp, bias=mxc, accum_out=scol),
              waits=[("vec", vexp0)])
    E("vector", lambda e: e.nop(), inc="vec", waits=[("scc", sxp)])
    gsum(scol, t16b)
    bcast16(t16b, dcol)
    vop(lambda e: e.reciprocal(dcol, dcol))
    vpr = vop(lambda e: e.tensor_scalar(pbx, pbx, dcol, None, ALU.mult))

    loc = fv(F_COLS + 30, 1)
    hic = fv(F_COLS + 31, 1)
    mic = fv(F_COLS + 32, 1)
    ggc = fv(F_COLS + 33, 1)
    tac = fv(F_COLS + 34, 1)
    E("gpsimd", lambda e: e.memset(loc, 0.0), inc="gpc", waits=[("vec", vpr)])
    E("gpsimd", lambda e: e.memset(hic, 1.0), inc="gpc", waits=[("gpc", V("gpc"))])
    vg0 = V("gpc")
    vop(lambda e: e.nop(), waits=[("gpc", vg0)])
    for it in range(BIS):
        vop(lambda e: e.tensor_tensor(mic, loc, hic, ALU.add))
        vop(lambda e: e.tensor_scalar(mic, mic, 0.5, None, ALU.mult))
        vcm_ = vop(lambda e: e.tensor_scalar(c1, pbx, mic, 0.0, ALU.is_gt, ALU.add,
                                             accum_out=scol))
        vmm_ = E("tensor", lambda e: e.matmul(psB[:, 512:513], g128[:], scol,
                                              start=True, stop=True),
                 inc="tdc", waits=[("vec", vcm_), ("g16u", P.counts.get("g16u", 0))])
        vop(lambda e: e.tensor_scalar(ggc, psB[:, 512:513], float(CAP) - 0.5, None,
                                      ALU.is_gt),
            waits=[("tdc", vmm_)])
        E("vector", lambda e: e.nop(), inc="g16u", waits=[("vec", V("vec"))])
        vop(lambda e: e.tensor_tensor(tac, mic, loc, ALU.subtract))
        vop(lambda e: e.tensor_mul(tac, tac, ggc))
        vop(lambda e: e.tensor_tensor(loc, loc, tac, ALU.add))
        vop(lambda e: e.tensor_tensor(tac, hic, mic, ALU.subtract))
        vop(lambda e: e.tensor_mul(tac, tac, ggc))
        vlast = vop(lambda e: e.tensor_tensor(hic, mic, tac, ALU.add))
    hicol = hic
    vop(lambda e: e.tensor_scalar(c1, pbx, hicol, None, ALU.is_gt))
    vop(lambda e: e.scalar_tensor_tensor(c1, c1, -9.0, pbx, ALU.mult, ALU.add))
    vop(lambda e: e.tensor_reduce(mxc, c1, mybir.AxisListType.X, ALU.max))
    vq2 = V("vec")
    E("sync", lambda e: e.dma_start(out=gmx_d, in_=mxc), inc="dgm", inck=16,
      waits=[("vec", vq2)])
    E("sync", lambda e: e.dma_start(
        out=g16[:, 11:11 + SUB], in_=gmx_d.rearrange("(e s) o -> e (s o)", s=SUB)),
      inc="dgm", inck=16, waits=[("dgm", V("dgm"))])
    v_dgm2 = V("dgm")
    vop(lambda e: e.tensor_reduce(v16, g16[:, 11:11 + SUB], mybir.AxisListType.X,
                                  ALU.max), waits=[("dgm", v_dgm2)])
    bcast16(v16, vcol2)
    vop(lambda e: e.tensor_scalar(c1, pbx, vcol2, 0.0, ALU.is_gt, ALU.add,
                                  accum_out=scol))
    gsum(scol, s16)
    vop(lambda e: e.tensor_scalar(c2, pbx, vcol2, None, ALU.is_equal))
    vop(lambda e: e.tensor_scalar(r16, s16, -1.0, float(CAP), ALU.mult, ALU.add))
    bcast16(r16, rcol)

    def cumsum_emit(src, tmp, prefix_col):
        cur, oth = src, tmp
        for k in range(NCS):
            s = 1 << k
            vop(lambda e, cur=cur, oth=oth, s=s: e.tensor_copy(oth[:, 0:s], cur[:, 0:s]))
            vop(lambda e, cur=cur, oth=oth, s=s: e.tensor_tensor(
                oth[:, s:CPS], cur[:, s:CPS], cur[:, 0:CPS - s], ALU.add))
            cur, oth = oth, cur
        vt = E("tensor", lambda e, cur=cur: e.matmul(
            psB[:, 514:515], l8[:], cur[:, CPS - 1:CPS], start=True, stop=True),
          inc="tdc", waits=[("vec", V("vec"))])
        vop(lambda e: e.tensor_copy(prefix_col, psB[:, 514:515]),
            waits=[("tdc", vt)])
        return cur

    vop(lambda e: e.tensor_copy(c3, c2))
    cq = cumsum_emit(c3, c4, pfxc)
    vop(lambda e, cq=cq: e.tensor_tensor(c5, cq, c2, ALU.subtract))
    vop(lambda e: e.tensor_scalar(c5, c5, pfxc, None, ALU.add))
    vop(lambda e: e.tensor_scalar(c5, c5, rcol, None, ALU.is_lt))
    vop(lambda e: e.tensor_mul(c5, c5, c2))
    vop(lambda e: e.tensor_tensor(c1, c1, c5, ALU.add))
    vop(lambda e: e.scalar_tensor_tensor(c2, c1, 1.0, pbx, ALU.mult, ALU.mult,
                                         accum_out=scol))
    gsum(scol, d16)
    vop(lambda e: e.tensor_scalar(d16, d16, 1e-9, None, ALU.add))
    bcast16(d16, dcol)
    vop(lambda e: e.reciprocal(dcol, dcol))
    vop(lambda e: e.tensor_scalar(c2, c2, dcol, None, ALU.mult))
    vop(lambda e: e.tensor_copy(c4, c1))
    cq2 = cumsum_emit(c4, c6, pfxc)
    vop(lambda e, cq2=cq2: e.tensor_tensor(c3, cq2, c1, ALU.subtract))
    vsel = vop(lambda e: e.tensor_scalar(c3, c3, pfxc, None, ALU.add))
    for k, src in ((0, c1), (1, c3), (2, c2)):
        E("sync", lambda e, k=k, src=src: e.dma_start(
            out=selcw_d[k].rearrange("e (s c) -> (e s) c", s=SUB), in_=src),
          inc="dsw", inck=16, waits=[("vec", vsel)])
    v_dsw = V("dsw")
    vmf = vop(lambda e: e.tensor_copy(myexpf[:], myexp_sb[:]),
              waits=[("dsm", v_dsm)])
    vmx = E("tensor", lambda e: e.matmul(psC[0:64, 64:65], rep8[:], myexpf[:],
                                         start=True, stop=True),
            inc="tdc", waits=[("vec", vmf)])
    vop(lambda e: e.tensor_copy(idx64[:], psC[0:64, 64:65]), waits=[("tdc", vmx)])
    vop(lambda e: e.tensor_scalar(idx64[:], idx64[:], float(SUB), None, ALU.mult))
    vop(lambda e: e.tensor_tensor(idx64[:], idx64[:], sub8[:], ALU.add))
    vi64 = None
    for ki in range(3):
        vop(lambda e, ki=ki: e.tensor_copy(idx64i[:, ki:ki + 1], idx64[:]))
        vi64 = vop(lambda e, ki=ki: e.tensor_scalar(
            idx64i[:, ki:ki + 1], idx64i[:, ki:ki + 1], ki * NE * SUB, None, ALU.add))
    c6z = fv(F_T3 + 512, 512)
    E("gpsimd", lambda e: e.memset(c6z, 0.0), inc="gpc",
      waits=[("dsw", v_dsw), ("gpc", V("gpc"))])
    vz = V("gpc")
    for ft in range(NFT):
        E("sync", lambda e, ft=ft: e.dma_start(
            out=acc_d[ft * 128:(ft + 1) * 128, 0:512], in_=c6z),
          inc="dza", inck=16, waits=[("gpc", vz)])
        E("sync", lambda e, ft=ft: e.dma_start(
            out=acc_d[ft * 128:(ft + 1) * 128, 512:1024], in_=c6z),
          inc="dza", inck=16)
    v_dza = V("dza")
    minepb = fv(F_T1, 3 * CPS, 0, 64)
    NCB = CPS // 128
    for ki in range(3):
        E("gpsimd", lambda e, ki=ki: e.indirect_dma_start(
            out=minepb[:, ki * CPS:(ki + 1) * CPS], out_offset=None,
            in_=selcw_d.rearrange("k e (s c) -> (k e s) c", s=SUB),
            in_offset=IOX(idx64i[:, ki:ki + 1], 0)),
          inc="dtm", inck=16,
          waits=[("vec", vi64), ("dsw", v_dsw), ("dza", v_dza)])
    v_dtm = V("dtm")
    for ki, dst in ((0, selTm), (1, ceTm), (2, wTm)):
        for cb in range(NCB):
            vt = E("tensor", lambda e, ki=ki, cb=cb: e.transpose(
                psC[:, 0:64],
                minepb[:, ki * CPS + cb * 128:ki * CPS + (cb + 1) * 128],
                idn[0:64, 0:64]),
              inc="tdc",
              waits=[("dtm", v_dtm), ("vec", V("vec")), ("dsm", v_dsm)])
            vop(lambda e, dst=dst, cb=cb: e.tensor_copy(
                dst[:, cb * 64:(cb + 1) * 64], psC[:, 0:64]),
                waits=[("tdc", vt)])
    v_tok = V("vec")

    if debug:
        E("sync", lambda e: e.dma_start(out=dbg["dbg_probs"], in_=pbx),
          inc="dbgs", inck=16, waits=[("vec", vsel)])
        E("sync", lambda e: e.dma_start(out=dbg["dbg_sel"], in_=c1),
          inc="dbgs", inck=16)

    xe = bv(B_XE, CAPH * 1024)
    xeT = [bv(B_XET, NDC * CAP), bv(B_XET + NDC * CAP, NDC * CAP)]
    w1q = [bv(B_W1, NDC * 512), bv(B_W1 + NDC * 512, NDC * 512)]
    w2q = [bv(B_W2, NDC * 512), bv(B_W2 + NDC * 512, NDC * 512)]
    w3q = [bv(B_W3, 4 * 1024), bv(B_W3 + 4 * 1024, 4 * 1024)]
    gv = bv(B_GV, CAP)
    slotb = fv(F_SLOT, 1032)
    gs_f = fv(F_SB, CAP)

    E("gpsimd", lambda e: e.memset(slotb[:, 1024:1032], 0.0), inc="gpc",
      waits=[("gpc", V("gpc"))])
    v_padz = V("gpc")

    _vxt_of = {}
    _pselb2 = [pselb[:], fv(F_SA, CAP)]
    _ps_war = [0, 0]
    exsb = fv(F_SB + 512, CAP, 0, 2)

    def emit_prep(j):
        exw = [("vec", v_tok)]
        vop(lambda e: e.tensor_copy(
            rhsa[:].rearrange("p (t two) -> p t two", two=2)[:, :, 0:1],
            tokc[:].rearrange("p (t one) -> p t one", one=1)),
            waits=exw + [("exr", P.counts.get("exr", 0))])
        for cb in range(NCB):
            vop(lambda e, cb=cb, j=j: e.tensor_copy(
                rhsa[:].rearrange("p (s ncb two) -> p s ncb two",
                                  ncb=NCB, two=2)[:, :, cb, 1:2],
                wTm[:, cb * 64 + SUB * j:cb * 64 + SUB * j + SUB].rearrange(
                    "p (s one) -> p s one", one=1)))
        vrh = V("vec")
        for tc in range(NTC):
            col = (tc % NCB) * 64 + SUB * j + tc // NCB
            pb = _pselb2[tc % 2]
            vps = vop(lambda e, col=col, pb=pb: e.tensor_scalar(
                pb, iotas[:], ceTm[:, col:col + 1],
                selTm[:, col:col + 1], ALU.is_equal, ALU.mult),
                waits=[("tdc", _ps_war[tc % 2]),
                       ("dx1", P.counts.get("dx1", 0))])
            E("tensor", lambda e, tc=tc, pb=pb: e.matmul(
                psB[0:2, 768:768 + CAP],
                rhsa[:, 2 * tc:2 * tc + 2],
                pb,
                start=(tc == 0), stop=(tc == NTC - 1)),
              inc="tdc",
              waits=[("vec", vps)] + ([("vec", vrh),
                     ("exu", P.counts.get("exu", 0))] if tc == 0 else []))
            _ps_war[tc % 2] = V("tdc")
        vexd = V("tdc")
        vio0 = vop(lambda e: e.tensor_copy(exsb, psB[0:2, 768:768 + CAP]),
                   waits=[("tdc", vexd)])
        for hc in range(CAPH):
            E("tensor", lambda e, hc=hc: e.matmul(
                psB[:, 770 + 2 * hc:772 + 2 * hc],
                exsb[:, hc * 128:(hc + 1) * 128],
                idn[0:2, 0:2],
                start=True, stop=True, skip_group_check=True),
              inc="tdc", waits=[("vec", vio0)] if hc == 0 else [])
        vtx = V("tdc")
        vio = vop(lambda e, j=j: e.tensor_copy(
            idxw[:, j * 2 * CAPH:j * 2 * CAPH + 2], psB[:, 770:772]),
            waits=[("tdc", vtx)])
        if CAPH > 1:
            vio = vop(lambda e, j=j: e.tensor_copy(
                idxw[:, j * 2 * CAPH + 2:j * 2 * CAPH + 4], psB[:, 772:774]))
        E("vector", lambda e: e.nop(), inc="exu", waits=[("vec", vio)])
        E("vector", lambda e: e.nop(), inc="exr", waits=[("vec", vio)])
        vii = vop(lambda e, j=j: e.tensor_copy(
            idxi[:, j * CAPH:(j + 1) * CAPH],
            idxw[:, j * 2 * CAPH:(j + 1) * 2 * CAPH].rearrange(
                "p (h w) -> p h w", h=CAPH)[:, :, 0:1]))
        E("vector", lambda e: e.nop(), inc=f"exj{j}", waits=[("vec", vii)])
        for hc in range(CAPH):
            E("gpsimd", lambda e, j=j, hc=hc: e.indirect_dma_start(
                out=xe[:, hc * 1024:(hc + 1) * 1024], out_offset=None,
                in_=h_d, in_offset=IOX(idxi[:, j * CAPH + hc:j * CAPH + hc + 1], 0)),
              inc="dga", inck=16,
              waits=[(f"exj{j}", 1), ("dht", b1_dht),
                     ("dxt0", P.counts.get("dxt0", 0)),
                     ("dxt1", P.counts.get("dxt1", 0)),
                     ("dbgs", P.counts.get("dbgs", 0))] if hc == 0 else [])
        vga = V("dga")
        xb = j % 2
        for hc in range(CAPH):
            for dc in range(NDC):
                E("sync", lambda e, hc=hc, dc=dc, xb=xb: e.dma_start_transpose(
                    xeT[xb][:, dc * CAP + hc * 128:dc * CAP + (hc + 1) * 128],
                    xe[:, hc * 1024 + dc * 128:hc * 1024 + (dc + 1) * 128]),
                  inc=f"dxt{xb}", inck=16,
                  waits=[("dga", vga), (f"xtu{xb}", P.counts.get(f"xtu{xb}", 0))]
                  if (hc == 0 and dc == 0) else [])
        _vxt_of[j] = V(f"dxt{xb}")
        if debug and j == 0:
            E("sync", lambda e: e.dma_start(out=dbg["dbg_xe"][:, 0:CAPH * 1024],
                                            in_=xe[:, 0:CAPH * 1024]),
              inc="dbgs", inck=16, waits=[("dga", vga)])

    def emit_ffn(j):
        xb = j % 2
        vxt = _vxt_of[j]
        for q4 in range(NQ4):
            wb = q4 % 2
            E("gpsimd", lambda e, j=j, q4=q4, wb=wb: e.dma_start(
            out=w1q[wb].rearrange("p (c n) -> p c n", c=NDC), in_=w1_in[j][:, q4 * 512:(q4 + 1) * 512]
                .rearrange("(c p) n -> p c n", p=128)),
              inc=f"dw1{wb}", inck=16,
              waits=[(f"wu{wb}", P.counts.get(f"wu{wb}", 0))])
            E("gpsimd", lambda e, j=j, q4=q4, wb=wb: e.dma_start(
                out=w2q[wb].rearrange("p (c n) -> p c n", c=NDC), in_=w2_in[j][:, q4 * 512:(q4 + 1) * 512]
                .rearrange("(c p) n -> p c n", p=128)),
              inc=f"dw1{wb}", inck=16)
            E("gpsimd", lambda e, j=j, q4=q4, wb=wb: e.dma_start(
                out=w3q[wb].rearrange("p (c n) -> p c n", c=4), in_=w3_in[j][q4 * 512:(q4 + 1) * 512, :]
                .rearrange("(c p) n -> p c n", p=128)),
              inc=f"dw1{wb}", inck=16)
            for _ in range(2):
                if SCAT_PEND:
                    SCAT_PEND.pop(0)()
            vw = V(f"dw1{wb}")
            for f4 in range(4):
                fc = q4 * 4 + f4
                for dc in range(NDC):
                    E("tensor", lambda e, wb=wb, f4=f4, dc=dc, xb=xb: e.matmul(
                        psC[:, 0:CAP],
                        w1q[wb][:, dc * 512 + f4 * 128:dc * 512 + (f4 + 1) * 128],
                        xeT[xb][:, dc * CAP:(dc + 1) * CAP],
                        start=(dc == 0), stop=(dc == NDC - 1)),
                      inc="tdc" if dc == NDC - 1 else None,
                      waits=[(f"dw1{wb}", vw), (f"dxt{xb}", vxt),
                             ("scc", V("scc"))] if dc == 0 else [])
                for dc in range(NDC):
                    E("tensor", lambda e, wb=wb, f4=f4, dc=dc, xb=xb: e.matmul(
                        psC[:, 512:512 + CAP],
                        w2q[wb][:, dc * 512 + f4 * 128:dc * 512 + (f4 + 1) * 128],
                        xeT[xb][:, dc * CAP:(dc + 1) * CAP],
                        start=(dc == 0), stop=(dc == NDC - 1)),
                      inc="tdc" if dc == NDC - 1 else None,
                      waits=[("vec", V("vec"))] if dc == 0 else [])
                vgv = V("tdc")
                ssl = sop(lambda e: e.activation(gs_f, psC[:, 0:CAP], AF.Silu),
                          waits=[("tdc", vgv), ("vec", V("vec"))])
                vgm = vop(lambda e: e.tensor_tensor(gv, gs_f, psC[:, 512:512 + CAP],
                                                    ALU.mult), waits=[("scc", ssl)])
                if debug and j == 0 and fc == 0:
                    E("sync", lambda e: e.dma_start(out=dbg["dbg_gv"], in_=gv),
                      inc="dbgs", inck=16, waits=[("vec", vgm)])
                    vgm = vop(lambda e: e.nop(),
                              waits=[("dbgs", P.counts.get("dbgs", 0))])
                for sc2 in range(CAPH):
                    ps_o = psD if sc2 == 0 else psA
                    for nc2 in range(2):
                        E("tensor", lambda e, sc2=sc2, nc2=nc2, f4=f4, wb=wb, ps_o=ps_o, fc=fc: e.matmul(
                            ps_o[:, nc2 * 512:(nc2 + 1) * 512],
                            gv[:, sc2 * 128:(sc2 + 1) * 128],
                            w3q[wb][:, f4 * 1024 + nc2 * 512:f4 * 1024 + (nc2 + 1) * 512],
                            start=(fc == 0), stop=(fc == NFC - 1)),
                          inc="tdc" if (sc2 == CAPH - 1 and nc2 == 1) else None,
                          waits=[("vec", vgm), ("oeu", P.counts.get("oeu", 0))]
                          if (sc2 == 0 and nc2 == 0 and fc == 0) else
                          ([("vec", vgm)] if (sc2 == 0 and nc2 == 0) else []))
            E("tensor", lambda e: e.nop(), inc=f"wu{wb}",
              waits=[("tdc", V("tdc"))])
        E("tensor", lambda e: e.nop(), inc=f"xtu{xb}",
          waits=[("tdc", V("tdc"))])
        vffn = V("tdc")
        for hc in range(CAPH):
            ps_o = psD if hc == 0 else psA
            vsl = vop(lambda e, j=j, hc=hc, ps_o=ps_o: e.tensor_scalar(
                slotb[:, 0:1024], ps_o[:, 0:1024],
                idxw[:, j * 2 * CAPH + hc * 2 + 1:j * 2 * CAPH + hc * 2 + 2],
                None, ALU.mult),
                waits=[("tdc", vffn), ("gpc", v_padz),
                       ("dx1", P.counts.get("dx1", 0)),
                       ("dsl", P.counts.get("dsl", 0))])
            vsl2 = vop(lambda e, j=j, hc=hc: e.tensor_copy(
                slotb[:, 1024:1025], idxw[:, j * 2 * CAPH + hc * 2:j * 2 * CAPH + hc * 2 + 1]))
            E("vector", lambda e: e.nop(), inc="oeu", waits=[("vec", vsl2)])
            E("sync", lambda e, j=j, hc=hc: e.dma_start(
                out=agin2.ap()[j * CAP + hc * 128:j * CAP + (hc + 1) * 128, :],
                in_=slotb),
              inc="dsl", inck=16, waits=[("vec", vsl2)])
    NC2 = EM // 2

    CPC = 2 * CAP

    def emit_ag2(c):
        vsl = V("dsl")
        E("gpsimd", lambda e, c=c: e.collective_compute(
            "AllGather", ALU.bypass,
            replica_groups=[[0, 1], [2, 3], [4, 5], [6, 7]],
            ins=[agin2.ap()[c * CPC:(c + 1) * CPC, :].opt()],
            outs=[agout2.ap()[c * 2 * CPC:(c + 1) * 2 * CPC, :].opt()]),
          inc="dcc", waits=[("dsl", vsl)])

    scb = [fv(F_T1, 1032), fv(F_T1 + 1032, 1032)]
    idsc = idxi[:, EM * CAPH:EM * CAPH + 1]
    _scat_war = [0, 0]

    def emit_scatter_group(c, gg):
        ngg = 2 * CPC // 128
        g = c * ngg + gg
        b = g % 2
        E("sync", lambda e, g=g, b=b: e.dma_start(
            out=scb[b], in_=agout2.ap()[g * 128:(g + 1) * 128, :]),
          inc=f"dsc{b}", inck=16,
          waits=[("dcc", NAG + c + 1), ("dscat", _scat_war[b]),
                 ("dbgs", P.counts.get("dbgs", 0))])
        vld = V(f"dsc{b}")
        vop(lambda e, b=b: e.tensor_tensor(idf, scb[b][:, 1024:1025], tkbase[:],
                                           ALU.subtract),
            waits=[(f"dsc{b}", vld), ("dscat", P.counts.get("dscat", 0))])
        vop(lambda e: e.tensor_scalar(idm, idf, 0.0, None, ALU.is_lt))
        vop(lambda e: e.scalar_tensor_tensor(idf, idm, 1e6, idf, ALU.mult, ALU.add))
        vic = vop(lambda e: e.tensor_copy(idsc, idf))
        E("gpsimd", lambda e, b=b: e.indirect_dma_start(
            out=acc_d, out_offset=IOX(idsc, 0),
            in_=scb[b][:, 0:1024], in_offset=None,
            compute_op=ALU.add, bounds_check=TH - 1, oob_is_err=False),
          inc="dscat", inck=16,
          waits=[("vec", vic), ("dza", v_dza),
                 ("dscat", P.counts.get("dscat", 0))])
        _scat_war[b] = V("dscat")

    SCAT_PEND = []

    def emit_scatter_chunk(c):
        for gg in range(2 * CPC // 128):
            emit_scatter_group(c, gg)

    SCAT_INTERLEAVE = False
    for j in range(EM):
        emit_prep(j)
        if SCAT_INTERLEAVE and j >= 4 and j % 2 == 0:
            c = j // 2 - 2
            SCAT_PEND.extend(
                [lambda c=c, gg=gg: emit_scatter_group(c, gg)
                 for gg in range(2 * CPC // 128)])
        if j >= 1:
            emit_ffn(j - 1)
        if j >= 2 and j % 2 == 0:
            emit_ag2(j // 2 - 1)
        if not SCAT_INTERLEAVE and j >= 4 and j % 2 == 0:
            emit_scatter_chunk(j // 2 - 2)
    emit_ffn(EM - 1)
    emit_ag2(NC2 - 1)
    while SCAT_PEND:
        SCAT_PEND.pop(0)()
    if debug:
        E("sync", lambda e: e.dma_start(out=dbg["dbg_agout2"], in_=agout2.ap()),
          inc="dbgs", inck=16, waits=[("dcc", V("dcc"))])
    emit_scatter_chunk(NC2 - 2)
    emit_scatter_chunk(NC2 - 1)
    v_dsl = V("dsl")
    if debug:
        E("sync", lambda e: e.dma_start(out=dbg["dbg_idxw"], in_=idxw[:]),
          inc="dbgs", inck=16, waits=[("dsl", v_dsl)])

    v_scat = V("dscat")

    if debug:
        E("sync", lambda e: e.dma_start(out=dbg["dbg_acc"], in_=acc_d),
          inc="dbgs", inck=16, waits=[("dscat", v_scat)])
        E("sync", lambda e: e.dma_start(out=dbg["dbg_x1"], in_=x1_d),
          inc="dbgs", inck=16, waits=[("dx1", b1_dx1)])
        E("sync", lambda e: e.dma_start(out=dbg["dbg_xln"], in_=xln_d),
          inc="dbgs", inck=16, waits=[("xs0", 16 * NTT // 2), ("xs1", 16 * NTT // 2)])
        E("sync", lambda e: e.dma_start(out=dbg["dbg_p1"], in_=agout1.ap()),
          inc="dbgs", inck=16, waits=[("dcc", T // 1024)])

    idxf = idxi[:, EM * CAPH + 1:EM * CAPH + 2]
    for ft in range(NFT):
        b = ft % 2
        vop(lambda e, ft=ft: e.tensor_scalar(idf, tkbase[:], float(ft * 128),
                                             None, ALU.add),
            waits=[("dxg", P.counts.get("dxg", 0))])
        vop(lambda e: e.tensor_tensor(idf, idf, iotac[:], ALU.add))
        vif = vop(lambda e: e.tensor_copy(idxf, idf))
        E("gpsimd", lambda e, b=b: e.indirect_dma_start(
            out=xt[b], out_offset=None,
            in_=x1_d, in_offset=IOX(idxf, 0)),
          inc="dxg", inck=16,
          waits=[("vec", vif), ("dx1", b1_dx1),
                 (f"xf{b}", P.counts.get(f"xf{b}", 0))])
        vxg = V("dxg")
        E("sync", lambda e, ft=ft, b=b: e.dma_start(
            out=sA if False else fv(F_SA, 1024), in_=acc_d[ft * 128:(ft + 1) * 128, :]),
          inc="dac", inck=16,
          waits=[("dscat", v_scat), ("acu", P.counts.get("acu", 0)),
                 ("dout", P.counts.get("dout", 0))]
          + ([("tdc", V("tdc"))] if ft == 0 else []))
        vac = V("dac")
        vno = ln_emit(sA, sB, extra_w=[("dac", vac)]
                      + ([("tdc", V("tdc"))] if ft == 0 else []))
        vfin = vop(lambda e, b=b: e.tensor_tensor(sA, xt[b], sB, ALU.add),
                   waits=[("dxg", vxg)])
        ln_war[0] = vfin
        E("vector", lambda e: e.nop(), inc="acu", waits=[("vec", vfin)])
        E("vector", lambda e: e.nop(), inc=f"xf{b}", waits=[("vec", vfin)])
        E("sync", lambda e, ft=ft: e.dma_start(
            out=out_d[ft * 128:(ft + 1) * 128, :], in_=sA),
          inc="dout", inck=16, waits=[("vec", vfin)])
    E("sync", lambda e: e.nop(), waits=[("dout", V("dout"))])

    P.realize()
    return nc


def _nzrow(AFR, base, h, QB):
    return AFR[h:h + 1, base:base + QB]


_NC_CACHE = {}


def _host_inputs(inputs, core):
    T, D, NH, NE = 4096, 1024, 8, 16
    b = core // 2
    half = core % 2
    hs = slice(half * NH * 64, (half + 1) * NH * 64)
    hh = slice(half * NH, (half + 1) * NH)
    es = slice(half * NE // 2, (half + 1) * NE // 2)
    SUB = 128 // NE
    l8 = np.zeros((128, 128), np.float32)
    for p in range(128):
        for m in range(128):
            if p // SUB == m // SUB and p % SUB < m % SUB:
                l8[p, m] = 1.0
    bc16 = np.zeros((NE, 128), np.float32)
    for e in range(NE):
        bc16[e, e * SUB:(e + 1) * SUB] = 1.0
    sel16 = bc16.T.copy()
    oh8 = np.zeros((SUB, NH * 64), np.float32)
    for h in range(NH):
        oh8[h, h * 64:(h + 1) * 64] = 1.0
    chp = np.concatenate([inputs["L0_attn_beta_p"][hh], inputs["L0_attn_iv_p"][hh],
                          inputs["L0_attn_ov_p"][hh], inputs["L0_attn_chi"][hh]])
    return {
        "x": np.ascontiguousarray(inputs["L0_attn_x"][b]),
        "wi": np.ascontiguousarray(inputs["L0_attn_Wi"][:, hs]),
        "wo": np.ascontiguousarray(inputs["L0_attn_Wo"][:, hs]),
        "wv": np.ascontiguousarray(inputs["L0_attn_Wv"][:, hs]),
        "wout": np.ascontiguousarray(inputs["L0_attn_Wout"][hs, :]),
        "wa": np.ascontiguousarray(inputs["L0_attn_Wa"][:, hh]),
        "we": np.ascontiguousarray(inputs["L0_attn_We"][:, hh]),
        "ba": np.ascontiguousarray(inputs["L0_attn_ba"][hh]).reshape(1, NH),
        "be": np.ascontiguousarray(inputs["L0_attn_be"][hh]).reshape(1, NH),
        "chp": np.ascontiguousarray(chp).reshape(1, 4 * NH),
        "rw1": np.ascontiguousarray(inputs["L0_ffn_router_w1"]),
        "rw2": np.ascontiguousarray(inputs["L0_ffn_router_w2"]),
        "w1": np.ascontiguousarray(inputs["L0_ffn_w1"][es]),
        "w2": np.ascontiguousarray(inputs["L0_ffn_w2"][es]),
        "w3": np.ascontiguousarray(inputs["L0_ffn_w3"][es]),
        "tkb": np.full((128, 1), half * (T // 2), np.float32),
        "myexp": np.arange(half * NE // 2, (half + 1) * NE // 2,
                           dtype=np.int32).reshape(-1, 1),
        "oh8": oh8, "l8": l8, "bc16": bc16, "sel16": sel16,
        "g128": np.kron(np.eye(NE, dtype=np.float32),
                        np.ones((128 // NE, 128 // NE), np.float32)),
        "idn": np.eye(128, dtype=np.float32),
        "rep8": np.repeat(np.eye(8, dtype=np.float32), 8, axis=1),
        "sub8": np.tile(np.arange(8, dtype=np.float32), 8).reshape(64, 1),
    }


def kernel(**inputs):
    key = "full"
    if key not in _NC_CACHE:
        _NC_CACHE[key] = build()
    nc = _NC_CACHE[key]
    in_maps = [_host_inputs(inputs, c) for c in range(8)]
    res = run_bass_kernel_spmd(nc, in_maps, core_ids=list(range(8)))
    out = np.empty((4, 4096, 1024), np.float32)
    for c in range(8):
        b, half = c // 2, c % 2
        out[b, half * 2048:(half + 1) * 2048, :] = res.results[c]["out"]
    return out


# version 16
# speedup vs baseline: 1.0409x; 1.0045x over previous
import math
import numpy as np
import concourse.bass as bass
import concourse.mybir as mybir
from concourse.bass_utils import run_bass_kernel_spmd

F32 = mybir.dt.float32
BF16 = mybir.dt.bfloat16
I32 = mybir.dt.int32
AF = mybir.ActivationFunctionType
ALU = mybir.AluOpType
IOX = bass.IndirectOffsetOnAxis

EPS = 1e-5
PI = math.pi

CFG = dict(T=4096, D=1024, NH=8, DH=64, NE=16, CAP=256, DFF=2048, RH=128,
           QB=512, BIS=48)


class Prog:
    ENGINES = ("sync", "gpsimd", "vector", "scalar", "tensor")

    def __init__(self, nc):
        self.nc = nc
        self.ops = {e: [] for e in self.ENGINES}
        self.sems = {}
        self.counts = {}

    def _sem(self, name):
        if name not in self.sems:
            self.sems[name] = self.nc.alloc_semaphore(name)
            self.counts[name] = 0

    def emit(self, eng, fn, inc=None, inck=1, waits=()):
        w = []
        for s, v in waits:
            if v > 0:
                self._sem(s)
                w.append((s, v))
        v = None
        if inc is not None:
            self._sem(inc)
            self.counts[inc] += inck
            v = self.counts[inc]
        self.ops[eng].append((fn, inc, inck, w))
        return v

    def val(self, name):
        return self.counts.get(name, 0)

    def realize(self):
        with self.nc.Block() as block:
            for eng_name in self.ENGINES:
                ops = self.ops[eng_name]
                if not ops:
                    continue

                def body(eng, ops=ops):
                    for fn, inc, inck, waits in ops:
                        for s, v in waits:
                            eng.wait_ge(self.sems[s], v)
                        instr = fn(eng) if fn is not None else None
                        if inc is not None:
                            instr.then_inc(self.sems[inc], inck)
                getattr(block, eng_name)(body)


DISABLE_B1_INJECT = False


def build(cfg=CFG, debug=False):
    T, D, NH, DH = cfg["T"], cfg["D"], cfg["NH"], cfg["DH"]
    NE, CAP, DFF, RH = cfg["NE"], cfg["CAP"], cfg["DFF"], cfg["RH"]
    QB, BIS = cfg["QB"], cfg["BIS"]
    NTT = T // 128
    NQB = T // QB
    NKC = T // 128
    NDC = D // 128
    DM = NH * DH
    NMC = DM // 128
    NFC = DFF // 128
    NQ4 = DFF // 512
    EM = NE // 2
    CAPH = CAP // 128
    TH = T // 2
    NFT = TH // 128
    SUB = 128 // NE
    CPS = T // SUB
    NTC = T // 128
    VX = DH + 2
    NQC = QB // 512
    NT2 = T // 512

    nc = bass.Bass("TRN2", target_bir_lowering=False)

    x_in = nc.dram_tensor("x", [T, D], F32, kind="ExternalInput").ap()
    wi_in = nc.dram_tensor("wi", [D, DM], F32, kind="ExternalInput").ap()
    wo_in = nc.dram_tensor("wo", [D, DM], F32, kind="ExternalInput").ap()
    wv_in = nc.dram_tensor("wv", [D, DM], F32, kind="ExternalInput").ap()
    wout_in = nc.dram_tensor("wout", [DM, D], F32, kind="ExternalInput").ap()
    wa_in = nc.dram_tensor("wa", [D, NH], F32, kind="ExternalInput").ap()
    we_in = nc.dram_tensor("we", [D, NH], F32, kind="ExternalInput").ap()
    ba_in = nc.dram_tensor("ba", [1, NH], F32, kind="ExternalInput").ap()
    be_in = nc.dram_tensor("be", [1, NH], F32, kind="ExternalInput").ap()
    chp_in = nc.dram_tensor("chp", [1, 4 * NH], F32, kind="ExternalInput").ap()
    rw1_in = nc.dram_tensor("rw1", [D, RH], F32, kind="ExternalInput").ap()
    rw2_in = nc.dram_tensor("rw2", [RH, NE], F32, kind="ExternalInput").ap()
    w1_in = nc.dram_tensor("w1", [EM, D, DFF], F32, kind="ExternalInput").ap()
    w2_in = nc.dram_tensor("w2", [EM, D, DFF], F32, kind="ExternalInput").ap()
    w3_in = nc.dram_tensor("w3", [EM, DFF, D], F32, kind="ExternalInput").ap()
    tkb_in = nc.dram_tensor("tkb", [128, 1], F32, kind="ExternalInput").ap()
    myexp_in = nc.dram_tensor("myexp", [EM, 1], I32, kind="ExternalInput").ap()
    oh8_in = nc.dram_tensor("oh8", [SUB, NH * 64], F32, kind="ExternalInput").ap()
    l8_in = nc.dram_tensor("l8", [128, 128], F32, kind="ExternalInput").ap()
    bc16_in = nc.dram_tensor("bc16", [NE, 128], F32, kind="ExternalInput").ap()
    sel16_in = nc.dram_tensor("sel16", [128, NE], F32, kind="ExternalInput").ap()
    g128_in = nc.dram_tensor("g128", [128, 128], F32, kind="ExternalInput").ap()
    idn_in = nc.dram_tensor("idn", [128, 128], F32, kind="ExternalInput").ap()
    rep8_in = nc.dram_tensor("rep8", [8, 64], F32, kind="ExternalInput").ap()
    sub8_in = nc.dram_tensor("sub8", [64, 1], F32, kind="ExternalInput").ap()
    out_d = nc.dram_tensor("out", [TH, D], F32, kind="ExternalOutput").ap()

    xln_d = nc.dram_tensor("xln_d", [T, D], BF16, kind="Internal").ap()
    agin1 = nc.dram_tensor("agin1", [T, D], BF16, kind="Internal")
    agout1 = nc.dram_tensor("agout1", [2 * T, D], BF16, kind="Internal")
    x1_d = nc.dram_tensor("x1_d", [T, D], F32, kind="Internal").ap()
    h_d = nc.dram_tensor("h_d", [T, D], BF16, kind="Internal").ap()
    a_d = nc.dram_tensor("a_d", [NH, T], F32, kind="Internal").ap()
    lgt_d = nc.dram_tensor("lgt_d", [NE, T], F32, kind="Internal").ap()
    gmx_d = nc.dram_tensor("gmx_d", [128, 1], F32, kind="Internal").ap()
    selcw_d = nc.dram_tensor("selcw_d", [3, NE, T], F32, kind="Internal").ap()
    agin2 = nc.dram_tensor("agin2", [EM * CAP, 1032], F32, kind="Internal")
    agout2 = nc.dram_tensor("agout2", [NE * CAP, 1032], F32, kind="Internal")
    acc_d = nc.dram_tensor("acc_d", [TH, D], F32, kind="Internal").ap()

    dbg = {}
    if debug:
        for nm, shp in (("dbg_conv", [128, 2048]), ("dbg_rby", [NH, 512]),
                        ("dbg_idxw", [128, 2 * EM * CAPH]),
                        ("dbg_xe", [128, 2048]), ("dbg_gv", [128, CAP]),
                        ("dbg_xln", [T, D]), ("dbg_p1", [2 * T, D]),
                        ("dbg_x1", [T, D]), ("dbg_probs", [128, CPS]),
                        ("dbg_sel", [128, CPS]), ("dbg_agout2", [NE * CAP, 1032]),
                        ("dbg_acc", [TH, D])):
            dt = BF16 if nm in ("dbg_xln", "dbg_p1", "dbg_xe", "dbg_gv") else F32
            if nm in ("dbg_idxw", "dbg_rby"):
                dt = F32
            if nm == "dbg_conv":
                dt = BF16
            dbg[nm] = nc.dram_tensor(nm, shp, dt, kind="ExternalOutput").ap()

    NB = 75264
    NF = 10520
    AB = nc.alloc_sbuf_tensor("AB", [128, NB], BF16)
    AFR = nc.alloc_sbuf_tensor("AFR", [128, NF], F32)

    def bv(off, n, p0=0, p1=128):
        return AB[p0:p1, off:off + n]

    def fv(off, n, p0=0, p1=128):
        return AFR[p0:p1, off:off + n]

    O_INNERT = 0
    O_OUTERT = O_INNERT + NMC * T
    O_VX = O_OUTERT + NMC * T
    O_W = O_VX + NTT * VX * NH
    O_XLT = O_W + 3 * NDC * 512
    O_WOUT = O_XLT + NDC * 512
    O_STG = O_WOUT + NMC * 1024
    assert O_STG + 1024 <= NB, O_STG
    NPH_ = NH // 2
    O_COMBT = O_W
    O_UT = O_COMBT + 2 * NPH_ * QB
    assert O_UT + 2 * (2 * QB) <= O_XLT
    O_CONV = O_XLT
    B_HT = 0
    B_R1 = B_HT + NDC * 512
    B_XE = B_R1 + 512
    B_XET = B_XE + CAPH * 1024
    B_W1 = B_XET + 2 * NDC * CAP
    B_W2 = B_W1 + 2 * NDC * 512
    B_W3 = B_W2 + 2 * NDC * 512
    B_RW1 = B_W3 + 2 * 4 * 1024
    B_RW2 = B_RW1 + NDC * RH
    B_PAB = B_RW2 + NE
    B_HS = B_PAB + 4 * 1024
    B_GV = B_HS + 1024
    assert B_GV + 256 <= NB, B_GV

    F_XT = 0
    F_SA = 2048
    F_SB = 3072
    F_T1 = 4096
    F_T2 = 5120
    F_T3 = 6144
    F_T4 = 7168
    F_T5 = 8192
    F_SLOT = 9216
    F_COLS = 10300
    assert F_COLS + 220 <= NF

    iotas = nc.alloc_sbuf_tensor("iotas", [128, CAP], F32)
    tokc = nc.alloc_sbuf_tensor("tokc", [128, NTC], F32)
    l8 = nc.alloc_sbuf_tensor("l8s", [128, 128], F32)
    idn = nc.alloc_sbuf_tensor("idns", [128, 128], F32)
    g128 = nc.alloc_sbuf_tensor("g128s", [128, 128], F32)
    rep8 = nc.alloc_sbuf_tensor("rep8s", [8, 64], F32)
    sub8 = nc.alloc_sbuf_tensor("sub8s", [64, 1], F32)
    idx64 = nc.alloc_sbuf_tensor("idx64s", [64, 1], F32)
    idx64i = nc.alloc_sbuf_tensor("idx64is", [64, 3], I32)
    myexpf = nc.alloc_sbuf_tensor("myexpfs", [8, 1], F32)
    pihalf = nc.alloc_sbuf_tensor("pihalfs", [1, 1], F32)
    bc16 = nc.alloc_sbuf_tensor("bc16s", [NE, 128], F32)
    sel16 = nc.alloc_sbuf_tensor("sel16s", [128, NE], F32)
    oh8 = nc.alloc_sbuf_tensor("oh8s", [SUB, NH * 64], F32)
    ones = nc.alloc_sbuf_tensor("oness", [1, 512], F32)
    tkbase = nc.alloc_sbuf_tensor("tkbs", [128, 1], F32)
    iotac = nc.alloc_sbuf_tensor("iotacs", [128, 1], F32)
    chpf = nc.alloc_sbuf_tensor("chpfs", [1, 4 * NH], F32)
    casa = nc.alloc_sbuf_tensor("casas", [1, 2 * NH], F32)
    cb64 = nc.alloc_sbuf_tensor("cb64s", [64, 2 * NH], F32)
    baf = nc.alloc_sbuf_tensor("bafs", [1, NH], F32)
    bef = nc.alloc_sbuf_tensor("befs", [1, NH], F32)
    wa_sb = nc.alloc_sbuf_tensor("wa_sbs", [128, NDC * NH], BF16)
    we_sb = nc.alloc_sbuf_tensor("we_sbs", [128, NDC * NH], BF16)
    myexp_sb = nc.alloc_sbuf_tensor("myexps", [EM, 1], I32)
    iotI = nc.alloc_sbuf_tensor("iotI", [128, CAP], I32)
    iotI2 = nc.alloc_sbuf_tensor("iotI2", [128, NTC], I32)
    idxi = nc.alloc_sbuf_tensor("idxi", [128, EM * CAPH + 4], I32)
    idxw = nc.alloc_sbuf_tensor("idxw", [128, EM * CAPH * 2], F32)
    sgm = nc.alloc_sbuf_tensor("sgms", [1, 4 * NH], F32)
    tctmp = nc.alloc_sbuf_tensor("tctmp", [1, 2 * NH], F32)
    estg = nc.alloc_sbuf_tensor("estgs", [128, NH], F32)
    a_stg = nc.alloc_sbuf_tensor("a_stgs", [NH, 512], F32)
    selTm = nc.alloc_sbuf_tensor("selTm", [128, 512], F32)
    ceTm = nc.alloc_sbuf_tensor("ceTm", [128, 512], F32)
    wTm = nc.alloc_sbuf_tensor("wTm", [128, 512], F32)
    pselb = nc.alloc_sbuf_tensor("pselb", [128, CAP], F32)
    rhsa = nc.alloc_sbuf_tensor("rhsa", [128, 2 * NTC], F32)
    g16 = nc.alloc_sbuf_tensor("g16", [NE, 24], F32)

    psA = nc.alloc_psum_tensor("psA", [128, 1024], F32)
    psB = nc.alloc_psum_tensor("psB", [128, 1024], F32)
    psC = nc.alloc_psum_tensor("psC", [128, 1024], F32)
    psD = nc.alloc_psum_tensor("psD", [128, 1024], F32)

    P = Prog(nc)
    E = P.emit
    V = P.val

    def vop(fn, waits=()):
        return E("vector", fn, inc="vec", waits=list(waits) + [("vec", V("vec"))])

    def sop(fn, waits=()):
        return E("scalar", fn, inc="scc", waits=list(waits) + [("scc", V("scc"))])

    mcol = fv(F_COLS + 0, 1)
    vcol = fv(F_COLS + 1, 1)
    m2col = fv(F_COLS + 2, 1)
    rstd = fv(F_COLS + 3, 1)
    sxc = fv(F_COLS + 4, 1)
    sxxc = fv(F_COLS + 5, 1)
    mxc = fv(F_COLS + 6, 1)
    gcol = fv(F_COLS + 7, 1)
    scol = fv(F_COLS + 8, 1)
    pfxc = fv(F_COLS + 9, 1)
    rcol = fv(F_COLS + 10, 1)
    vcol2 = fv(F_COLS + 11, 1)
    dcol = fv(F_COLS + 12, 1)
    idf = fv(F_COLS + 13, 1)
    idm = fv(F_COLS + 14, 1)
    a_sxc = fv(F_COLS + 40, 1)
    a_sxxc = fv(F_COLS + 41, 1)
    a_m = fv(F_COLS + 42, 1)
    a_m2 = fv(F_COLS + 43, 1)
    a_nm2 = fv(F_COLS + 44, 1)
    a_v = fv(F_COLS + 45, 1)
    a_r = fv(F_COLS + 46, 1)
    a_eps = fv(F_COLS + 47, 1)
    vlnc = [[fv(F_COLS + 50 + 8 * b + i, 1) for i in range(6)] for b in (0, 1)]

    for dst, src in ((baf[:], ba_in), (bef[:], be_in), (chpf[:], chp_in),
                     (tkbase[:], tkb_in), (oh8[:], oh8_in), (l8[:], l8_in),
                     (bc16[:], bc16_in), (sel16[:], sel16_in), (idn[:], idn_in),
                     (g128[:], g128_in),
                     (rep8[:], rep8_in), (sub8[:], sub8_in)):
        E("sync", lambda e, dst=dst, src=src: e.dma_start(out=dst, in_=src),
          inc="dsm", inck=16)
    E("gpsimd", lambda e: e.dma_start(
        out=wa_sb[:].rearrange("p (c n) -> p c n", c=NDC), in_=wa_in.rearrange("(c p) n -> p c n", p=128)),
      inc="dwA", inck=16)
    E("gpsimd", lambda e: e.dma_start(
        out=we_sb[:].rearrange("p (c n) -> p c n", c=NDC), in_=we_in.rearrange("(c p) n -> p c n", p=128)),
      inc="dwA", inck=16)
    E("sync", lambda e: e.dma_start(out=myexp_sb[:], in_=myexp_in),
      inc="dsm", inck=16)
    v_dsm = V("dsm")

    E("gpsimd", lambda e: e.iota(iotI[:], pattern=[[1, CAP]], base=0,
                                 channel_multiplier=0), inc="gpc")
    E("gpsimd", lambda e: e.tensor_copy(iotas[:], iotI[:]), inc="gpc",
      waits=[("gpc", V("gpc"))])
    E("gpsimd", lambda e: e.iota(iotI2[:], pattern=[[128, NTC]], base=0,
                                 channel_multiplier=1), inc="gpc",
      waits=[("gpc", V("gpc"))])
    E("gpsimd", lambda e: e.tensor_copy(tokc[:], iotI2[:]), inc="gpc",
      waits=[("gpc", V("gpc"))])
    E("gpsimd", lambda e: e.iota(iotI2[:, 0:1], pattern=[[1, 1]], base=0,
                                 channel_multiplier=1), inc="gpc",
      waits=[("gpc", V("gpc"))])
    E("gpsimd", lambda e: e.tensor_copy(iotac[:], iotI2[:, 0:1]), inc="gpc",
      waits=[("gpc", V("gpc"))])
    E("gpsimd", lambda e: e.memset(ones[:], 1.0), inc="gpc",
      waits=[("gpc", V("gpc"))])
    E("gpsimd", lambda e: e.memset(pihalf[:], float(PI / 2)), inc="gpc",
      waits=[("gpc", V("gpc"))])
    E("gpsimd", lambda e: e.memset(a_eps, float(1e-5)), inc="gpc",
      waits=[("gpc", V("gpc"))])
    v_gpc0 = V("gpc")

    sop(lambda e: e.activation(sgm[:, 0:3 * NH], chpf[:, 0:3 * NH], AF.Sigmoid),
        waits=[("dsm", v_dsm)])
    sop(lambda e: e.activation(sgm[:, 3 * NH:4 * NH], chpf[:, 3 * NH:4 * NH], AF.Tanh))
    sop(lambda e: e.activation(tctmp[:, NH:2 * NH], sgm[:, 0:NH], AF.Sin,
                               scale=float(PI)))
    sop(lambda e: e.activation(tctmp[:, 0:NH], sgm[:, 0:NH], AF.Sin,
                               scale=float(-PI), bias=pihalf[:]),
        waits=[("gpc", V("gpc"))])
    v_sc_ch = V("scc")
    vop(lambda e: e.tensor_mul(sgm[:, NH:2 * NH], sgm[:, NH:2 * NH],
                               sgm[:, 2 * NH:3 * NH]), waits=[("scc", v_sc_ch)])
    vop(lambda e: e.tensor_mul(sgm[:, NH:2 * NH], sgm[:, NH:2 * NH],
                               sgm[:, 3 * NH:4 * NH]))
    vop(lambda e: e.tensor_mul(casa[:, 0:NH], tctmp[:, 0:NH], sgm[:, NH:2 * NH]))
    v_casa = vop(lambda e: e.tensor_mul(casa[:, NH:2 * NH], tctmp[:, NH:2 * NH],
                                        sgm[:, NH:2 * NH]))
    E("tensor", lambda e: e.matmul(psB[0:64, 0:2 * NH], ones[0:1, 0:64], casa[:],
                                   start=True, stop=True),
      inc="tdc", waits=[("vec", v_casa), ("gpc", v_gpc0)])
    v_cb64 = vop(lambda e: e.tensor_copy(cb64[:], psB[0:64, 0:2 * NH]),
                 waits=[("tdc", V("tdc"))])

    wi_sb = bv(O_W, NDC * 512)
    wo_sb = bv(O_W + NDC * 512, NDC * 512)
    wv_sb = bv(O_W + 2 * NDC * 512, NDC * 512)
    wout_sb = bv(O_WOUT, NMC * 1024)
    v_dw = {}
    for nm, dst, src in (("wi", wi_sb, wi_in), ("wo", wo_sb, wo_in),
                         ("wv", wv_sb, wv_in)):
        E("gpsimd", lambda e, dst=dst, src=src: e.dma_start(
            out=dst.rearrange("p (c n) -> p c n", c=NDC), in_=src.rearrange("(c p) n -> p c n", p=128)),
          inc="d" + nm, inck=16)
        v_dw[nm] = ("d" + nm, V("d" + nm))
    v_dwA = V("dwA")

    xt = [fv(F_XT, 1024), fv(F_XT + 1024, 1024)]
    sA = fv(F_SA, 1024)
    sB = fv(F_SB, 1024)
    xstg = [bv(O_WOUT, 1024), bv(O_WOUT + 1024, 1024)]

    ln_war = [0]

    def ln_emit(src, dst, extra_w=(), dst_w=()):
        s0 = sop(lambda e: e.activation(sB, src, AF.Identity, accum_out=a_sxc),
                 waits=list(extra_w) + [("vec", ln_war[0]), ("gpc", v_gpc0)])
        sop(lambda e: e.activation(sB, src, AF.Square, accum_out=a_sxxc))
        sop(lambda e: e.activation(a_m, a_sxc, AF.Identity, scale=1.0 / 1024.0))
        sop(lambda e: e.activation(a_m2, a_m, AF.Square))
        sop(lambda e: e.activation(a_nm2, a_m2, AF.Identity, scale=-1.0))
        sop(lambda e: e.activation(a_v, a_sxxc, AF.Identity,
                                   scale=1.0 / 1024.0, bias=a_nm2))
        sop(lambda e: e.activation(a_r, a_v, AF.Ln, bias=a_eps))
        s3 = sop(lambda e: e.activation(a_r, a_r, AF.Exp, scale=-0.5))
        vn = vop(lambda e: e.tensor_scalar(dst, src, a_m, a_r,
                                           ALU.subtract, ALU.mult),
                 waits=list(dst_w) + [("scc", s3)])
        ln_war[0] = vn
        return vn

    def emit_a1_tile(t):
        b = t % 2
        E("sync", lambda e, t=t, b=b: e.dma_start(
            out=xt[b], in_=x_in[t * 128:(t + 1) * 128, :]),
          inc=f"dx{b}", inck=16,
          waits=[(f"xf{b}", P.counts.get(f"xf{b}", 0))])
        vl = V(f"dx{b}")
        vn = ln_emit(xt[b], xstg[b],
                     extra_w=[(f"dx{b}", vl)],
                     dst_w=[(f"xs{b}", P.counts.get(f"xs{b}", 0))])
        E("vector", lambda e: e.nop(), inc=f"xf{b}", waits=[("vec", vn)])
        E("gpsimd", lambda e, t=t, b=b: e.dma_start(
            out=xln_d[t * 128:(t + 1) * 128, :], in_=xstg[b]),
          inc=f"xs{b}", inck=16, waits=[("vec", vn)])

    xlt = [bv(O_XLT, NDC * 512), bv(O_WOUT + 2048, NDC * 512)]
    vx = bv(O_VX, NTT * VX * NH)
    innerT = bv(O_INNERT, NMC * T)
    outerT = bv(O_OUTERT, NMC * T)
    E("gpsimd", lambda e: e.memset(vx, 1.0), inc="gpc",
      waits=[("gpc", V("gpc"))])
    v_vxones = V("gpc")

    for t in range(8):
        emit_a1_tile(t)
    xlt_war = [0, 0]
    a2war = [0, 0, 0, 0]
    a2warv = [0, 0]
    sg_war = [0]
    ps_rot = [psC[:, 0:512], psC[:, 512:1024], psD[:, 0:512], psD[:, 512:1024]]

    def emit_xlt_trans(tc):
        tp = tc % 2
        for dc in range(NDC):
            E("sync", lambda e, tc=tc, dc=dc, tp=tp: e.dma_start_transpose(
                xlt[tp][:, dc * 512:(dc + 1) * 512],
                xln_d[tc * 512:(tc + 1) * 512, dc * 128:(dc + 1) * 128]),
              inc=f"tr{tp}", inck=16,
              waits=([("xs0", 16 * (2 * tc + 2)), ("xs1", 16 * (2 * tc + 2)),
                      ("tdc", xlt_war[tp])] if dc == 0 else []))

    emit_xlt_trans(0)
    for tc in range(NT2):
        tp = tc % 2
        vtr = V(f"tr{tp}")
        base_w = [(f"tr{tp}", vtr), ("dsm", v_dsm)]

        gi = 0
        for wnm, wsb, dstT in (("wi", wi_sb, innerT), ("wo", wo_sb, outerT)):
            for j in range(NMC):
                ps_o = ps_rot[gi % 4]
                for dc in range(NDC):
                    E("tensor", lambda e, wsb=wsb, j=j, dc=dc, tp=tp, ps_o=ps_o: e.matmul(
                        ps_o,
                        wsb[:, dc * 512 + j * 128:dc * 512 + (j + 1) * 128],
                        xlt[tp][:, dc * 512:(dc + 1) * 512],
                        start=(dc == 0), stop=(dc == NDC - 1)),
                      inc="tdc" if dc == NDC - 1 else None,
                      waits=(((base_w + [v_dw[wnm]]) if gi % 4 == 0 else [])
                             + [("vec", a2war[gi % 4])])
                      if dc == 0 else [])
                vmm = V("tdc")
                vop(lambda e, dstT=dstT, j=j, tc=tc, ps_o=ps_o: e.tensor_copy(
                    dstT[:, j * T + tc * 512:j * T + (tc + 1) * 512], ps_o),
                    waits=[("tdc", vmm)])
                a2war[gi % 4] = V("vec")
                if tc + 2 < NT2 and gi % 2 == 0:
                    emit_a1_tile(4 * (tc + 2) + gi // 2)
                gi += 1
        for ti in range(4):
            tt = tc * 4 + ti
            for dc in range(NDC):
                E("tensor", lambda e, ti=ti, dc=dc, tp=tp: e.matmul(
                    psB[:, 0:NH],
                    xlt[tp][:, dc * 512 + ti * 128:dc * 512 + (ti + 1) * 128],
                    we_sb[:, dc * NH:(dc + 1) * NH],
                    start=(dc == 0), stop=False),
                  waits=[("scc", sg_war[0]), ("dwA", v_dwA)] if dc == 0 else [])
            E("tensor", lambda e: e.matmul(psB[:, 0:NH], ones[0:1, 0:128],
                                           bef[:], start=False, stop=True),
              inc="tdc")
            vmm = V("tdc")
            vsg = sop(lambda e: e.activation(estg[:], psB[:, 0:NH], AF.Sigmoid),
                      waits=[("tdc", vmm), ("vec", V("vec"))])
            sg_war[0] = vsg
            vp = ti % 2
            for dc in range(NDC):
                E("tensor", lambda e, ti=ti, dc=dc, tp=tp, vp=vp: e.matmul(
                    psA[:, vp * 512:(vp + 1) * 512],
                    xlt[tp][:, dc * 512 + ti * 128:dc * 512 + (ti + 1) * 128],
                    wv_sb[:, dc * 512:(dc + 1) * 512],
                    start=(dc == 0), stop=(dc == NDC - 1)),
                  inc="tdc" if dc == NDC - 1 else None,
                  waits=[("vec", a2warv[vp]), ("gpc", v_vxones),
                         v_dw["wv"]] if dc == 0 else [])
            vmm2 = V("tdc")
            for h in range(NH):
                vop(lambda e, tt=tt, h=h, vp=vp: e.tensor_scalar(
                    vx[:, tt * VX * NH + h * VX:tt * VX * NH + h * VX + DH],
                    psA[:, vp * 512 + h * DH:vp * 512 + (h + 1) * DH],
                    estg[:, h:h + 1], None, ALU.mult),
                    waits=[("tdc", vmm2), ("scc", vsg)])
            vop(lambda e, tt=tt: e.tensor_copy(
                vx[:, tt * VX * NH:(tt + 1) * VX * NH].rearrange(
                    "p (h v) -> p h v", h=NH)[:, :, DH:DH + 1],
                estg[:].rearrange("p (h o) -> p h o", h=NH)),
                waits=[("scc", vsg)])
            a2warv[vp] = V("vec")
        for dc in range(NDC):
            E("tensor", lambda e, dc=dc, tp=tp: e.matmul(
                psB[0:NH, 512:1024],
                wa_sb[:, dc * NH:(dc + 1) * NH],
                xlt[tp][:, dc * 512:(dc + 1) * 512],
                start=(dc == 0), stop=False),
              waits=[("scc", sg_war[0]), ("dwA", v_dwA),
                     ("das", P.counts.get("das", 0))] if dc == 0 else [])
        E("tensor", lambda e: e.matmul(psB[0:NH, 512:1024], baf[:], ones[:],
                                       start=False, stop=True), inc="tdc")
        xlt_war[tp] = V("tdc")
        vmm3 = V("tdc")
        vsa = sop(lambda e: e.activation(a_stg[:], psB[0:NH, 512:1024], AF.Sigmoid),
                  waits=[("tdc", vmm3)])
        sg_war[0] = vsa
        E("sync", lambda e, tc=tc: e.dma_start(
            out=a_d[:, tc * 512:(tc + 1) * 512], in_=a_stg[:]),
          inc="das", inck=16, waits=[("scc", vsa)])
        if tc + 1 < NT2:
            emit_xlt_trans(tc + 1)

    a2_tdc = V("tdc")
    E("gpsimd", lambda e: e.dma_start(
        out=wout_sb.rearrange("p (c n) -> p c n", c=NMC), in_=wout_in.rearrange("(c p) n -> p c n", p=128)),
      inc="dwA", inck=16, waits=[("tdc", a2_tdc)])
    v_dwout = V("dwA")
    a2_scc = V("scc")
    a2_vec = V("vec")
    a2_das = V("das")

    NPH = NH // 2
    stp = [psA, psB]
    Ut = [bv(O_UT, 2 * QB), bv(O_UT + 2 * QB, 2 * QB)]
    conv = [bv(O_CONV, NPH * QB), bv(O_CONV + NPH * QB, NPH * QB)]
    combT = [bv(O_COMBT, NPH * QB), bv(O_COMBT + NPH * QB, NPH * QB)]
    exstA = fv(F_T1, QB, 0, 2)
    exstB = fv(F_T1 + QB, QB, 0, 2)
    nzn = fv(F_T2, QB, 0, NH)
    nzz = fv(F_T3, QB, 0, NH)
    rby = fv(F_T4, QB, 0, NH)
    aq = fv(F_T5, QB, 0, NH)
    pstgb = [bv(O_STG, 512), bv(O_STG + 512, 512)]
    AGC = 1024
    NAG = T // AGC
    O_B1 = O_COMBT + 2 * NPH * QB + 2 * (2 * QB)
    pa_s = [bv(O_B1, 1024), bv(O_B1 + 1024, 1024)]
    pb_s = [bv(O_B1 + 2048, 1024), bv(O_B1 + 3072, 1024)]
    hstg = bv(O_B1 + 4096, 1024)
    assert O_B1 + 5120 <= O_XLT
    O_HT = O_STG + 1024
    assert O_HT + NDC * 512 <= NB
    hT = bv(O_HT, NDC * 512)
    r1T = nc.alloc_sbuf_tensor("r1Ts", [128, 256], BF16)[:, :]

    rw1_sb = bv(O_B1 + 5120, NDC * RH)
    rw2_sb = nc.alloc_sbuf_tensor("rw2s", [128, NE], BF16)[:, :]

    def ln_v_stats(src_ap, cols, extra_w=()):
        sx, sxx, m, vv, m2, rs = cols
        vop(lambda e: e.tensor_scalar(sB, src_ap, 1.0, 0.0, ALU.mult, ALU.add,
                                      accum_out=sx),
            waits=list(extra_w) + [("dlg", P.counts.get("dlg", 0))])
        vop(lambda e: e.scalar_tensor_tensor(sB, src_ap, 1.0, src_ap,
                                             ALU.mult, ALU.mult,
                                             accum_out=sxx))
        vop(lambda e: e.tensor_scalar(m, sx, 1.0 / 1024.0, None, ALU.mult))
        vop(lambda e: e.tensor_scalar(vv, sxx, 1.0 / 1024.0, None, ALU.mult))
        vop(lambda e: e.tensor_mul(m2, m, m))
        vop(lambda e: e.tensor_tensor(vv, vv, m2, ALU.subtract))
        return vop(lambda e: e.tensor_scalar(vv, vv, float(EPS), None, ALU.add))

    def ln_v_finish(src_ap, dst, cols, v0, extra_w=()):
        sx, sxx, m, vv, m2, rs = cols
        s2 = sop(lambda e: e.activation(rs, vv, AF.Ln), waits=[("vec", v0)])
        s3 = sop(lambda e: e.activation(rs, rs, AF.Exp, scale=-0.5))
        return vop(lambda e: e.tensor_scalar(dst, src_ap, m, rs,
                                             ALU.subtract, ALU.mult),
                   waits=list(extra_w) + [("scc", s3)])

    sAb = [sA, fv(F_SLOT, 1024)]
    _b1v = {}

    def emit_b1_part1(t):
        b = t % 2
        rowa, rowb, need = b1_src(t)
        E("sync", lambda e, rowa=rowa, b=b: e.dma_start(
            out=pa_s[b], in_=agout1.ap()[rowa:rowa + 128, :]),
          inc=f"pa{b}", inck=16,
          waits=[("dcc", need), (f"pf{b}", P.counts.get(f"pf{b}", 0))])
        E("sync", lambda e, rowb=rowb, b=b: e.dma_start(
            out=pb_s[b], in_=agout1.ap()[rowb:rowb + 128, :]),
          inc=f"pa{b}", inck=16)
        E("sync", lambda e, t=t, b=b: e.dma_start(
            out=xt[b], in_=x_in[t * 128:(t + 1) * 128, :]),
          inc=f"dx{b}", inck=16, waits=[(f"xf{b}", P.counts.get(f"xf{b}", 0))])
        vpa = V(f"pa{b}")
        vs = vop(lambda e, b=b: e.tensor_tensor(sAb[b], pa_s[b], pb_s[b],
                                                ALU.add),
                 waits=[(f"pa{b}", vpa), ("dx1", P.counts.get("dx1", 0))])
        E("vector", lambda e: e.nop(), inc=f"pf{b}", waits=[("vec", vs)])
        _b1v[t] = (ln_v_stats(sAb[b], vlnc[b]), V(f"dx{b}"))

    def emit_b1_part2(t):
        b = t % 2
        v0, vdx = _b1v[t]
        ln_v_finish(sAb[b], sB, vlnc[b], v0)
        vx1 = vop(lambda e, b=b: e.tensor_tensor(sAb[b], xt[b], sB, ALU.add),
                  waits=[(f"dx{b}", vdx)])
        E("vector", lambda e: e.nop(), inc=f"xf{b}", waits=[("vec", vx1)])
        E("sync", lambda e, t=t, b=b: e.dma_start(
            out=x1_d[t * 128:(t + 1) * 128, :], in_=sAb[b]),
          inc="dx1", inck=16,
          waits=[("vec", vx1), ("dx1", P.counts.get("dx1", 0))])
        _b1v[t] = ln_v_stats(sAb[b], vlnc[b])

    def emit_b1_part3(t):
        b = t % 2
        v0 = _b1v.pop(t)
        vh = ln_v_finish(sAb[b], hstg, vlnc[b], v0,
                         extra_w=[("dht", P.counts.get("dht", 0))])
        E("sync", lambda e, t=t: e.dma_start(
            out=h_d[t * 128:(t + 1) * 128, :], in_=hstg),
          inc="dht", inck=16,
          waits=[("vec", vh), ("dht", P.counts.get("dht", 0))])

    def emit_b1_tile(t):
        emit_b1_part1(t)
        emit_b1_part2(t)
        emit_b1_part3(t)

    def emit_router_trans(tc, half):
        need_dht = 4 * (tc + 1)
        for dc in range(4 * half, 4 * half + 4):
            E("sync", lambda e, tc=tc, dc=dc: e.dma_start_transpose(
                hT[:, dc * 512:(dc + 1) * 512],
                h_d[tc * 512:(tc + 1) * 512, dc * 128:(dc + 1) * 128]),
              inc="thx", inck=16,
              waits=([("dht", 16 * need_dht), ("tdc", V("tdc"))]
                     if dc == 4 * half else []))

    def emit_router_mm(tc):
        vth = V("thx")
        for dc in range(NDC):
            E("tensor", lambda e, dc=dc: e.matmul(
                psD[:, 512:1024], rw1_sb[:, dc * RH:(dc + 1) * RH],
                hT[:, dc * 512:(dc + 1) * 512],
                start=(dc == 0), stop=(dc == NDC - 1)),
              inc="tdc" if dc == NDC - 1 else None,
              waits=[("thx", vth), ("dwB", P.counts.get("dwB", 0)),
                     ("scc", V("scc")), ("rbu", P.counts.get("rbu", 0)),
                     ("woe", P.counts.get("woe", 0))] if dc == 0 else [])
        for hh in range(2):
            vsi = sop(lambda e, hh=hh: e.activation(
                r1T, psD[:, 512 + 256 * hh:768 + 256 * hh], AF.Sigmoid),
                      waits=[("tdc", V("tdc"))])
            vmu = vop(lambda e, hh=hh: e.tensor_tensor(
                r1T, r1T, psD[:, 512 + 256 * hh:768 + 256 * hh], ALU.mult),
                waits=[("scc", vsi)] + ([("vec", V("vec"))] if hh == 0 else []))
            E("tensor", lambda e, hh=hh: e.matmul(
                psD[0:NE, 512 + 256 * hh:768 + 256 * hh], rw2_sb, r1T,
                start=True, stop=True),
              inc="tdc", waits=[("vec", vmu)])
        vlg = V("tdc")
        vle = vop(lambda e: e.tensor_copy(fv(F_SB, 512, 0, NE), psD[0:NE, 512:1024]),
                  waits=[("tdc", vlg), ("dlg", P.counts.get("dlg", 0))])
        E("vector", lambda e: e.nop(), inc="rtl", waits=[("vec", vle)])
        E("sync", lambda e, tc=tc: e.dma_start(
            out=lgt_d[:, tc * 512:(tc + 1) * 512], in_=fv(F_SB, 512, 0, NE)),
          inc="dlg", inck=16, waits=[("vec", vle)])

    def emit_router_tc(tc):
        emit_router_trans(tc, 0)
        emit_router_trans(tc, 1)
        emit_router_mm(tc)

    E("gpsimd", lambda e: e.dma_start(
        out=rw1_sb.rearrange("p (c n) -> p c n", c=NDC),
        in_=rw1_in.rearrange("(c p) n -> p c n", p=128)),
      inc="dwB", inck=16, waits=[("tdc", a2_tdc)])
    E("gpsimd", lambda e: e.dma_start(out=rw2_sb, in_=rw2_in), inc="dwB", inck=16)

    inject = {}

    def add_inject(qb, p, kc, fn):
        inject.setdefault((qb, p, kc), []).append(fn)

    last_bcast_tdc = [0]

    def emit_chamber_step(qbp, h):
        j = h // 2
        pb_ = 64 * (h % 2)
        cbq = combT[qbp % 2]
        cvq = conv[qbp % 2]
        vb = E("tensor", lambda e, h=h: e.matmul(
            psD[0:64, 512:1024],
            oh8[:, h * 64:(h + 1) * 64],
            rby[:, 0:QB],
            start=True, stop=True),
          inc="tdc",
          waits=[("rch", P.counts.get("rch", 0)),
                 ("rbu", P.counts.get("rbu", 0)),
                 ("woe", P.counts.get("woe", 0)),
                 ("rtl", P.counts.get("rtl", 0))])
        last_bcast_tdc[0] = V("tdc")
        cm = fv(F_SB, 512, 0, 64)
        t1 = fv(F_SB + 512, 512, 0, 32)
        vcm = vop(lambda e, j=j, pb_=pb_, cvq=cvq: e.tensor_tensor(
            cm, cvq[pb_:pb_ + 64, j * QB:(j + 1) * QB],
            psD[0:64, 512:1024][:, 0:QB], ALU.mult),
            waits=[("tdc", vb), ("cve", NPH * (qbp + 1)),
                   ("dlg", P.counts.get("dlg", 0))])
        E("vector", lambda e: e.nop(), inc="rbu", waits=[("vec", vcm)])
        vop(lambda e, h=h: e.tensor_scalar(
            t1, cm[32:64, :], cb64[0:32, NH + h:NH + h + 1], None, ALU.mult),
            waits=[("vec", v_cb64)])
        vop(lambda e, h=h, j=j, pb_=pb_, cbq=cbq: e.scalar_tensor_tensor(
            cbq[pb_:pb_ + 32, j * QB:(j + 1) * QB],
            cm[0:32, :], cb64[0:32, h:h + 1], t1, ALU.mult, ALU.subtract))
        vop(lambda e, h=h: e.tensor_scalar(
            t1, cm[32:64, :], cb64[0:32, h:h + 1], None, ALU.mult))
        v4 = vop(lambda e, h=h, j=j, pb_=pb_, cbq=cbq: e.scalar_tensor_tensor(
            cbq[pb_ + 32:pb_ + 64, j * QB:(j + 1) * QB],
            cm[0:32, :], cb64[0:32, NH + h:NH + h + 1], t1, ALU.mult, ALU.add))
        E("vector", lambda e: e.nop(), inc="chd", waits=[("vec", v4)])

    def emit_wout_group(qbp, g):
        tt, ncol = g // 2, g % 2
        cbq = combT[qbp % 2]
        for dc in range(NMC):
            E("tensor", lambda e, tt=tt, ncol=ncol, dc=dc, cbq=cbq: e.matmul(
                psD[:, 512:1024],
                cbq[:, dc * QB + tt * 128:dc * QB + (tt + 1) * 128],
                wout_sb[:, dc * 1024 + ncol * 512:dc * 1024 + (ncol + 1) * 512],
                start=(dc == 0), stop=(dc == NMC - 1)),
              inc="tdc" if dc == NMC - 1 else None,
              waits=[("chd", P.counts.get("chd", 0)),
                     ("woe", P.counts.get("woe", 0)),
                     ("dwA", v_dwout),
                     ("rtl", P.counts.get("rtl", 0))] if dc == 0 else [])
        vwo = V("tdc")
        sbi = g % 2
        vev = E("vector", lambda e, sbi=sbi: e.tensor_copy(pstgb[sbi], psD[:, 512:1024]),
                inc="woe",
                waits=[("tdc", vwo), ("vec", V("vec")),
                       (f"ps{sbi}", P.counts.get(f"ps{sbi}", 0))])
        E("vector", lambda e: e.nop(), inc="vec", waits=[("woe", vev)])
        E("sync", lambda e, qbp=qbp, tt=tt, ncol=ncol, sbi=sbi: e.dma_start(
            out=agin1.ap()[qbp * QB + tt * 128:qbp * QB + (tt + 1) * 128,
                           ncol * 512:(ncol + 1) * 512],
            in_=pstgb[sbi]),
          inc=f"ps{sbi}", inck=16, waits=[("woe", vev)])

    def emit_ag1_rows(r0, nr, o0):
        vps = (V("ps0"), V("ps1"))
        E("gpsimd", lambda e, r0=r0, nr=nr, o0=o0: e.collective_compute(
            "AllGather", ALU.bypass,
            replica_groups=[[0, 1], [2, 3], [4, 5], [6, 7]],
            ins=[agin1.ap()[r0:r0 + nr, :].opt()],
            outs=[agout1.ap()[o0:o0 + 2 * nr, :].opt()]),
          inc="dcc", waits=[("ps0", vps[0]), ("ps1", vps[1])])

    def emit_ag1(c):
        emit_ag1_rows(c * AGC, AGC, c * 2 * AGC)

    def b1_src(t):
        TPC = AGC // 128
        if t < (NAG - 1) * TPC:
            cch = (t * 128) // AGC
            row = cch * 2 * AGC + (t * 128) % AGC
            return row, row + AGC, cch + 1
        SPL = AGC // 2
        off = t * 128 - (NAG - 1) * AGC
        half = off // SPL
        base = (NAG - 1) * 2 * AGC + half * AGC
        return base + off % SPL, base + SPL + off % SPL, NAG + half

    for qb in range(1, NQB):
        for h in range(NH):
            add_inject(qb, 1, 1 + (h * (NKC - 3)) // NH,
                       lambda qbp=qb - 1, h=h: emit_chamber_step(qbp, h))
        for g in range(2 * (QB // 128)):
            add_inject(qb, 2, 1 + (g * (NKC - 3)) // (2 * (QB // 128)),
                       lambda qbp=qb - 1, g=g: emit_wout_group(qbp, g))
    NSL = NKC - 4
    NS = 4 * NSL

    def slot_of(s):
        return (s // NSL, 2 + s % NSL)

    for c in range(NAG):
        qb_at = 2 * c + 2
        if qb_at < NQB:
            add_inject(qb_at, 3, 0, lambda c=c: emit_ag1(c))
        qb_b1 = 2 * c + 3
        if qb_b1 < NQB and not DISABLE_B1_INJECT:
            for i in range(8):
                t = c * 8 + i
                for k, fn in ((0, emit_b1_part1), (2, emit_b1_part2),
                              (4, emit_b1_part3)):
                    p_, kc_ = slot_of(6 * i + k)
                    add_inject(qb_b1, p_, kc_, lambda t=t, fn=fn: fn(t))
            for s, fn in ((min(26, NS - 1), lambda tc=2 * c: emit_router_trans(tc, 0)),
                          (min(28, NS - 1), lambda tc=2 * c: emit_router_trans(tc, 1)),
                          (min(34, NS - 1), lambda tc=2 * c: emit_router_mm(tc)),
                          (min(50, NS - 1), lambda tc=2 * c + 1: emit_router_trans(tc, 0)),
                          (min(54, NS - 1), lambda tc=2 * c + 1: emit_router_trans(tc, 1)),
                          (min(62, NS - 1), lambda tc=2 * c + 1: emit_router_mm(tc))):
                p_, kc_ = slot_of(s)
                add_inject(qb_b1, p_, kc_, fn)


    for qb in range(NQB):
        for p in range(NPH):
            def emit_S(kc):
                buf = kc % 2
                w0 = [(f"exp{buf}", P.counts.get(f"exp{buf}", 0))]
                if qb == 0 and p == 0 and kc < 2:
                    w0 += [("tdc", a2_tdc), ("vec", a2_vec), ("scc", a2_scc)]
                E("tensor", lambda e, p=p, kc=kc, qb=qb, buf=buf: e.matmul(
                    stp[buf][:, 0:512],
                    outerT[0:64, p * T + kc * 128:p * T + (kc + 1) * 128],
                    innerT[0:64, p * T + qb * QB:p * T + (qb + 1) * QB],
                    start=True, stop=True),
                  waits=w0)
                E("tensor", lambda e, p=p, kc=kc, qb=qb, buf=buf: e.matmul(
                    stp[buf][:, 512:1024],
                    outerT[64:128, p * T + kc * 128:p * T + (kc + 1) * 128],
                    innerT[64:128, p * T + qb * QB:p * T + (qb + 1) * QB],
                    start=True, stop=True),
                  inc="tdc")
                vst = V("tdc")
                E("scalar", lambda e, buf=buf: e.activation(
                    Ut[buf], stp[buf][:], AF.Exp, scale=0.125),
                  inc=f"exp{buf}",
                  waits=[("tdc", vst),
                         (f"pvd{buf}", P.counts.get(f"pvd{buf}", 0))])

            def emit_PVX(kc):
                buf = kc % 2
                vexp = V(f"exp{buf}")
                w1 = [(f"exp{buf}", vexp)]
                if kc == 0:
                    w1.append(("cve", P.counts.get("cve", 0)))
                E("tensor", lambda e, p=p, kc=kc, buf=buf: e.matmul(
                    psC[0:64, 0:512],
                    vx[:, kc * VX * NH + (2 * p) * VX:kc * VX * NH + (2 * p) * VX + DH],
                    Ut[buf][:, 0:512],
                    start=(kc == 0), stop=(kc == NKC - 1), tile_position=(0, 0)),
                  waits=w1)
                E("tensor", lambda e, p=p, kc=kc, buf=buf: e.matmul(
                    psC[64:128, 0:512],
                    vx[:, kc * VX * NH + (2 * p + 1) * VX:kc * VX * NH + (2 * p + 1) * VX + DH],
                    Ut[buf][:, 512:1024],
                    start=(kc == 0), stop=(kc == NKC - 1), tile_position=(0, 64),
                    skip_group_check=True))
                exm = 2 if kc == NKC - 1 else 64
                E("tensor", lambda e, p=p, kc=kc, buf=buf, exm=exm: e.matmul(
                    psC[0:exm, 512:1024],
                    vx[:, kc * VX * NH + (2 * p) * VX + DH:kc * VX * NH + (2 * p) * VX + DH + exm],
                    Ut[buf][:, 0:512],
                    start=(kc == 0), stop=(kc == NKC - 1), tile_position=(0, 0),
                    skip_group_check=True),
                  waits=[("exe", P.counts.get("exe", 0))] if kc == 0 else [])
                E("tensor", lambda e, p=p, kc=kc, buf=buf, exm=exm: e.matmul(
                    psD[64:64 + exm, 0:512],
                    vx[:, kc * VX * NH + (2 * p + 1) * VX + DH:kc * VX * NH + (2 * p + 1) * VX + DH + exm],
                    Ut[buf][:, 512:1024],
                    start=(kc == 0), stop=(kc == NKC - 1), tile_position=(0, 64),
                    skip_group_check=True),
                  inc=f"pvd{buf}")

            def do_inject(kc):
                for fn in inject.pop((qb, p, kc), []):
                    fn()

            do_inject(-1)
            if qb == 0 and p == 0:
                emit_S(0)
                emit_S(1)
            for kc in range(NKC):
                emit_PVX(kc)
                if kc + 2 < NKC:
                    emit_S(kc + 2)
                do_inject(kc)
            def prologue_kk(kk, nqb, np_):
                buf = kk % 2
                E("tensor", lambda e, p2=np_, kc=kk, qb2=nqb, buf=buf: e.matmul(
                    stp[buf][:, 0:512],
                    outerT[0:64, p2 * T + kc * 128:p2 * T + (kc + 1) * 128],
                    innerT[0:64, p2 * T + qb2 * QB:p2 * T + (qb2 + 1) * QB],
                    start=True, stop=True),
                  waits=[(f"exp{buf}", P.counts.get(f"exp{buf}", 0))])
                E("tensor", lambda e, p2=np_, kc=kk, qb2=nqb, buf=buf: e.matmul(
                    stp[buf][:, 512:1024],
                    outerT[64:128, p2 * T + kc * 128:p2 * T + (kc + 1) * 128],
                    innerT[64:128, p2 * T + qb2 * QB:p2 * T + (qb2 + 1) * QB],
                    start=True, stop=True),
                  inc="tdc")
                vst = V("tdc")
                E("scalar", lambda e, buf=buf: e.activation(
                    Ut[buf], stp[buf][:], AF.Exp, scale=0.125),
                  inc=f"exp{buf}",
                  waits=[("tdc", vst),
                         (f"pvd{buf}", P.counts.get(f"pvd{buf}", 0))])

            nqb, np_ = (qb, p + 1) if p + 1 < NPH else (qb + 1, 0)
            if nqb < NQB:
                prologue_kk(0, nqb, np_)
            vpv = V(f"pvd{(NKC - 1) % 2}")
            cw = [(f"pvd{(NKC - 1) % 2}", vpv)]
            if qb >= 2:
                cw.append(("rbu", NH * (qb - 1)))
            E("scalar", lambda e, p=p, qb=qb: e.activation(
                conv[qb % 2][:, p * QB:(p + 1) * QB], psC[0:128, 0:QB],
                AF.Identity),
              inc="cve", waits=cw)
            if nqb < NQB:
                prologue_kk(1, nqb, np_)
            E("scalar", lambda e: e.activation(exstA, psC[0:2, 512:512 + QB],
                                               AF.Identity),
              inc="exe", waits=[("dnzA", P.counts.get("dnzA", 0))])
            vexA = V("exe")
            E("scalar", lambda e: e.activation(exstB, psD[64:66, 0:QB],
                                               AF.Identity),
              inc="exe", waits=[("dnzB", P.counts.get("dnzB", 0))])
            vexB = V("exe")
            E("sync", lambda e, p=p: e.dma_start(out=_nzrow(AFR, F_T2, 2 * p, QB), in_=exstA[0:1, :]),
              inc="dnzA", inck=16,
              waits=[("exe", vexA), ("rch", P.counts.get("rch", 0))])
            E("sync", lambda e, p=p: e.dma_start(out=_nzrow(AFR, F_T3, 2 * p, QB), in_=exstA[1:2, :]),
              inc="dnzA", inck=16)
            E("sync", lambda e, p=p: e.dma_start(out=_nzrow(AFR, F_T2, 2 * p + 1, QB), in_=exstB[0:1, :]),
              inc="dnzB", inck=16,
              waits=[("exe", vexB), ("rch", P.counts.get("rch", 0))])
            E("sync", lambda e, p=p: e.dma_start(out=_nzrow(AFR, F_T3, 2 * p + 1, QB), in_=exstB[1:2, :]),
              inc="dnzB", inck=16)
        vnz_allA = V("dnzA")
        vnz_allB = V("dnzB")
        E("sync", lambda e, qb=qb: e.dma_start(
            out=aq, in_=a_d[:, qb * QB:(qb + 1) * QB]),
          inc="daq", inck=16, waits=[("das", a2_das), ("vec", V("vec"))])
        vdaq = V("daq")
        vop(lambda e: e.tensor_mul(rby, nzn, aq),
            waits=[("dnzA", vnz_allA), ("dnzB", vnz_allB), ("daq", vdaq),
                   ("tdc", last_bcast_tdc[0])])
        vop(lambda e: e.scalar_tensor_tensor(rby, nzz, 1e-8, rby, ALU.mult, ALU.add))
        vop(lambda e: e.reciprocal(rby, rby))
        vop(lambda e: e.tensor_mul(rby, rby, aq))
        E("vector", lambda e: e.nop(), inc="rch", waits=[("vec", V("vec"))])

    assert not inject, f"unfired injections: {list(inject.keys())}"
    SPL = AGC // 2
    emit_ag1_rows((NAG - 1) * AGC, SPL, (NAG - 1) * 2 * AGC)
    for h in range(NH):
        emit_chamber_step(NQB - 1, h)
    for g in range(2 * (QB // 128)):
        emit_wout_group(NQB - 1, g)
    for i in range(4):
        emit_b1_tile((NAG - 1) * 8 + i)
    emit_ag1_rows((NAG - 1) * AGC + SPL, SPL, (NAG - 1) * 2 * AGC + AGC)
    emit_router_tc(2 * (NAG - 1))
    for i in range(4, 8):
        emit_b1_tile((NAG - 1) * 8 + i)
    emit_router_tc(2 * (NAG - 1) + 1)
    b1_dht = V("dht")
    b1_dx1 = V("dx1")
    b_dlg = V("dlg")

    pbl = fv(F_T1, CPS)
    pbx = fv(F_T2, CPS)
    c1 = fv(F_T3, CPS)
    c2 = fv(F_T4, CPS)
    c3 = fv(F_T5, CPS)
    c4 = fv(F_T1 + 512, CPS)
    c5 = fv(F_T2 + 512, CPS)
    c6 = fv(F_T3 + 512, CPS)
    NCS = int(math.log2(CPS))
    locol = fv(F_COLS + 20, 1)
    hicol = fv(F_COLS + 21, 1)
    micol = fv(F_COLS + 22, 1)
    lo16 = g16[:, 0:1]
    hi16 = g16[:, 1:2]
    mi16 = g16[:, 2:3]
    ct16 = g16[:, 3:4]
    gg16 = g16[:, 4:5]
    t16a = g16[:, 5:6]
    t16b = g16[:, 6:7]
    v16 = g16[:, 7:8]
    s16 = g16[:, 8:9]
    r16 = g16[:, 9:10]
    d16 = g16[:, 10:11]

    E("sync", lambda e: e.dma_start(
        out=pbl, in_=lgt_d.rearrange("e (s c) -> (e s) c", s=SUB)),
      inc="dpb", inck=16, waits=[("dlg", b_dlg)])
    v_dpb = V("dpb")

    def gsum(src_col, dst16, extra=()):
        vt = E("tensor", lambda e: e.matmul(psB[0:NE, 512:513], sel16[:], src_col,
                                            start=True, stop=True),
               inc="tdc", waits=list(extra) + [("vec", V("vec")), ("g16u", P.counts.get("g16u", 0))])
        return vop(lambda e: e.tensor_copy(dst16, psB[0:NE, 512:513]),
                   waits=[("tdc", vt)])

    def bcast16(src16, dstcol, extra=()):
        vt = E("tensor", lambda e: e.matmul(psB[:, 513:514], bc16[:], src16,
                                            start=True, stop=True),
               inc="tdc", waits=list(extra) + [("vec", V("vec"))])
        return vop(lambda e: e.tensor_copy(dstcol, psB[:, 513:514]),
                   waits=[("tdc", vt)])

    vop(lambda e: e.tensor_reduce(mxc, pbl, mybir.AxisListType.X, ALU.max),
        waits=[("dpb", v_dpb)])
    vq = V("vec")
    E("sync", lambda e: e.dma_start(out=gmx_d, in_=mxc), inc="dgm", inck=16,
      waits=[("vec", vq)])
    E("sync", lambda e: e.dma_start(
        out=g16[:, 11:11 + SUB], in_=gmx_d.rearrange("(e s) o -> e (s o)", s=SUB)),
      inc="dgm", inck=16, waits=[("dgm", V("dgm"))])
    v_dgm = V("dgm")
    vop(lambda e: e.tensor_reduce(t16a, g16[:, 11:11 + SUB], mybir.AxisListType.X,
                                  ALU.max), waits=[("dgm", v_dgm)])
    bcast16(t16a, mxc)
    vop(lambda e: e.tensor_scalar(mxc, mxc, -1.0, None, ALU.mult))
    vexp0 = V("vec")
    sxp = sop(lambda e: e.activation(pbx, pbl, AF.Exp, bias=mxc, accum_out=scol),
              waits=[("vec", vexp0)])
    E("vector", lambda e: e.nop(), inc="vec", waits=[("scc", sxp)])
    gsum(scol, t16b)
    bcast16(t16b, dcol)
    vop(lambda e: e.reciprocal(dcol, dcol))
    vpr = vop(lambda e: e.tensor_scalar(pbx, pbx, dcol, None, ALU.mult))

    loc = fv(F_COLS + 30, 1)
    hic = fv(F_COLS + 31, 1)
    mic = fv(F_COLS + 32, 1)
    ggc = fv(F_COLS + 33, 1)
    tac = fv(F_COLS + 34, 1)
    E("gpsimd", lambda e: e.memset(loc, 0.0), inc="gpc", waits=[("vec", vpr)])
    E("gpsimd", lambda e: e.memset(hic, 1.0), inc="gpc", waits=[("gpc", V("gpc"))])
    vg0 = V("gpc")
    vop(lambda e: e.nop(), waits=[("gpc", vg0)])
    for it in range(BIS):
        vop(lambda e: e.tensor_tensor(mic, loc, hic, ALU.add))
        vop(lambda e: e.tensor_scalar(mic, mic, 0.5, None, ALU.mult))
        vcm_ = vop(lambda e: e.tensor_scalar(c1, pbx, mic, 0.0, ALU.is_gt, ALU.add,
                                             accum_out=scol))
        vmm_ = E("tensor", lambda e: e.matmul(psB[:, 512:513], g128[:], scol,
                                              start=True, stop=True),
                 inc="tdc", waits=[("vec", vcm_), ("g16u", P.counts.get("g16u", 0))])
        vop(lambda e: e.tensor_scalar(ggc, psB[:, 512:513], float(CAP) - 0.5, None,
                                      ALU.is_gt),
            waits=[("tdc", vmm_)])
        E("vector", lambda e: e.nop(), inc="g16u", waits=[("vec", V("vec"))])
        vop(lambda e: e.tensor_tensor(tac, mic, loc, ALU.subtract))
        vop(lambda e: e.tensor_mul(tac, tac, ggc))
        vop(lambda e: e.tensor_tensor(loc, loc, tac, ALU.add))
        vop(lambda e: e.tensor_tensor(tac, hic, mic, ALU.subtract))
        vop(lambda e: e.tensor_mul(tac, tac, ggc))
        vlast = vop(lambda e: e.tensor_tensor(hic, mic, tac, ALU.add))
    hicol = hic
    vop(lambda e: e.tensor_scalar(c1, pbx, hicol, None, ALU.is_gt))
    vop(lambda e: e.scalar_tensor_tensor(c1, c1, -9.0, pbx, ALU.mult, ALU.add))
    vop(lambda e: e.tensor_reduce(mxc, c1, mybir.AxisListType.X, ALU.max))
    vq2 = V("vec")
    E("sync", lambda e: e.dma_start(out=gmx_d, in_=mxc), inc="dgm", inck=16,
      waits=[("vec", vq2)])
    E("sync", lambda e: e.dma_start(
        out=g16[:, 11:11 + SUB], in_=gmx_d.rearrange("(e s) o -> e (s o)", s=SUB)),
      inc="dgm", inck=16, waits=[("dgm", V("dgm"))])
    v_dgm2 = V("dgm")
    vop(lambda e: e.tensor_reduce(v16, g16[:, 11:11 + SUB], mybir.AxisListType.X,
                                  ALU.max), waits=[("dgm", v_dgm2)])
    bcast16(v16, vcol2)
    vop(lambda e: e.tensor_scalar(c1, pbx, vcol2, 0.0, ALU.is_gt, ALU.add,
                                  accum_out=scol))
    gsum(scol, s16)
    vop(lambda e: e.tensor_scalar(c2, pbx, vcol2, None, ALU.is_equal))
    vop(lambda e: e.tensor_scalar(r16, s16, -1.0, float(CAP), ALU.mult, ALU.add))
    bcast16(r16, rcol)

    def cumsum_emit(src, tmp, prefix_col):
        cur, oth = src, tmp
        for k in range(NCS):
            s = 1 << k
            vop(lambda e, cur=cur, oth=oth, s=s: e.tensor_copy(oth[:, 0:s], cur[:, 0:s]))
            vop(lambda e, cur=cur, oth=oth, s=s: e.tensor_tensor(
                oth[:, s:CPS], cur[:, s:CPS], cur[:, 0:CPS - s], ALU.add))
            cur, oth = oth, cur
        vt = E("tensor", lambda e, cur=cur: e.matmul(
            psB[:, 514:515], l8[:], cur[:, CPS - 1:CPS], start=True, stop=True),
          inc="tdc", waits=[("vec", V("vec"))])
        vop(lambda e: e.tensor_copy(prefix_col, psB[:, 514:515]),
            waits=[("tdc", vt)])
        return cur

    vop(lambda e: e.tensor_copy(c3, c2))
    cq = cumsum_emit(c3, c4, pfxc)
    vop(lambda e, cq=cq: e.tensor_tensor(c5, cq, c2, ALU.subtract))
    vop(lambda e: e.tensor_scalar(c5, c5, pfxc, None, ALU.add))
    vop(lambda e: e.tensor_scalar(c5, c5, rcol, None, ALU.is_lt))
    vop(lambda e: e.tensor_mul(c5, c5, c2))
    vop(lambda e: e.tensor_tensor(c1, c1, c5, ALU.add))
    vop(lambda e: e.scalar_tensor_tensor(c2, c1, 1.0, pbx, ALU.mult, ALU.mult,
                                         accum_out=scol))
    gsum(scol, d16)
    vop(lambda e: e.tensor_scalar(d16, d16, 1e-9, None, ALU.add))
    bcast16(d16, dcol)
    vop(lambda e: e.reciprocal(dcol, dcol))
    vop(lambda e: e.tensor_scalar(c2, c2, dcol, None, ALU.mult))
    vop(lambda e: e.tensor_copy(c4, c1))
    cq2 = cumsum_emit(c4, c6, pfxc)
    vop(lambda e, cq2=cq2: e.tensor_tensor(c3, cq2, c1, ALU.subtract))
    vsel = vop(lambda e: e.tensor_scalar(c3, c3, pfxc, None, ALU.add))
    for k, src in ((0, c1), (1, c3), (2, c2)):
        E("sync", lambda e, k=k, src=src: e.dma_start(
            out=selcw_d[k].rearrange("e (s c) -> (e s) c", s=SUB), in_=src),
          inc="dsw", inck=16, waits=[("vec", vsel)])
    v_dsw = V("dsw")
    vmf = vop(lambda e: e.tensor_copy(myexpf[:], myexp_sb[:]),
              waits=[("dsm", v_dsm)])
    vmx = E("tensor", lambda e: e.matmul(psC[0:64, 64:65], rep8[:], myexpf[:],
                                         start=True, stop=True),
            inc="tdc", waits=[("vec", vmf)])
    vop(lambda e: e.tensor_copy(idx64[:], psC[0:64, 64:65]), waits=[("tdc", vmx)])
    vop(lambda e: e.tensor_scalar(idx64[:], idx64[:], float(SUB), None, ALU.mult))
    vop(lambda e: e.tensor_tensor(idx64[:], idx64[:], sub8[:], ALU.add))
    vi64 = None
    for ki in range(3):
        vop(lambda e, ki=ki: e.tensor_copy(idx64i[:, ki:ki + 1], idx64[:]))
        vi64 = vop(lambda e, ki=ki: e.tensor_scalar(
            idx64i[:, ki:ki + 1], idx64i[:, ki:ki + 1], ki * NE * SUB, None, ALU.add))
    c6z = fv(F_T3 + 512, 512)
    E("gpsimd", lambda e: e.memset(c6z, 0.0), inc="gpc",
      waits=[("dsw", v_dsw), ("gpc", V("gpc"))])
    vz = V("gpc")
    for ft in range(NFT):
        E("sync", lambda e, ft=ft: e.dma_start(
            out=acc_d[ft * 128:(ft + 1) * 128, 0:512], in_=c6z),
          inc="dza", inck=16, waits=[("gpc", vz)])
        E("sync", lambda e, ft=ft: e.dma_start(
            out=acc_d[ft * 128:(ft + 1) * 128, 512:1024], in_=c6z),
          inc="dza", inck=16)
    v_dza = V("dza")
    minepb = fv(F_T1, 3 * CPS, 0, 64)
    NCB = CPS // 128
    for ki in range(3):
        E("gpsimd", lambda e, ki=ki: e.indirect_dma_start(
            out=minepb[:, ki * CPS:(ki + 1) * CPS], out_offset=None,
            in_=selcw_d.rearrange("k e (s c) -> (k e s) c", s=SUB),
            in_offset=IOX(idx64i[:, ki:ki + 1], 0)),
          inc="dtm", inck=16,
          waits=[("vec", vi64), ("dsw", v_dsw), ("dza", v_dza)])
    v_dtm = V("dtm")
    for ki, dst in ((0, selTm), (1, ceTm), (2, wTm)):
        for cb in range(NCB):
            vt = E("tensor", lambda e, ki=ki, cb=cb: e.transpose(
                psC[:, 0:64],
                minepb[:, ki * CPS + cb * 128:ki * CPS + (cb + 1) * 128],
                idn[0:64, 0:64]),
              inc="tdc",
              waits=[("dtm", v_dtm), ("vec", V("vec")), ("dsm", v_dsm)])
            vop(lambda e, dst=dst, cb=cb: e.tensor_copy(
                dst[:, cb * 64:(cb + 1) * 64], psC[:, 0:64]),
                waits=[("tdc", vt)])
    v_tok = V("vec")

    if debug:
        E("sync", lambda e: e.dma_start(out=dbg["dbg_probs"], in_=pbx),
          inc="dbgs", inck=16, waits=[("vec", vsel)])
        E("sync", lambda e: e.dma_start(out=dbg["dbg_sel"], in_=c1),
          inc="dbgs", inck=16)

    xe = bv(B_XE, CAPH * 1024)
    xeT = [bv(B_XET, NDC * CAP), bv(B_XET + NDC * CAP, NDC * CAP)]
    w1q = [bv(B_W1, NDC * 512), bv(B_W1 + NDC * 512, NDC * 512)]
    w2q = [bv(B_W2, NDC * 512), bv(B_W2 + NDC * 512, NDC * 512)]
    w3q = [bv(B_W3, 4 * 1024), bv(B_W3 + 4 * 1024, 4 * 1024)]
    gv = bv(B_GV, CAP)
    slotb = fv(F_SLOT, 1032)
    gs_f = fv(F_SB, CAP)

    E("gpsimd", lambda e: e.memset(slotb[:, 1024:1032], 0.0), inc="gpc",
      waits=[("gpc", V("gpc"))])
    v_padz = V("gpc")

    _vxt_of = {}
    _pselb2 = [pselb[:], fv(F_SA, CAP)]
    _ps_war = [0, 0]
    exsb = fv(F_SB + 512, CAP, 0, 2)

    def emit_prep(j):
        exw = [("vec", v_tok)]
        vop(lambda e: e.tensor_copy(
            rhsa[:].rearrange("p (t two) -> p t two", two=2)[:, :, 0:1],
            tokc[:].rearrange("p (t one) -> p t one", one=1)),
            waits=exw + [("exr", P.counts.get("exr", 0))])
        for cb in range(NCB):
            vop(lambda e, cb=cb, j=j: e.tensor_copy(
                rhsa[:].rearrange("p (s ncb two) -> p s ncb two",
                                  ncb=NCB, two=2)[:, :, cb, 1:2],
                wTm[:, cb * 64 + SUB * j:cb * 64 + SUB * j + SUB].rearrange(
                    "p (s one) -> p s one", one=1)))
        vrh = V("vec")
        for tc in range(NTC):
            col = (tc % NCB) * 64 + SUB * j + tc // NCB
            pb = _pselb2[tc % 2]
            vps = vop(lambda e, col=col, pb=pb: e.tensor_scalar(
                pb, iotas[:], ceTm[:, col:col + 1],
                selTm[:, col:col + 1], ALU.is_equal, ALU.mult),
                waits=[("tdc", _ps_war[tc % 2]),
                       ("dx1", P.counts.get("dx1", 0))])
            E("tensor", lambda e, tc=tc, pb=pb: e.matmul(
                psB[0:2, 768:768 + CAP],
                rhsa[:, 2 * tc:2 * tc + 2],
                pb,
                start=(tc == 0), stop=(tc == NTC - 1)),
              inc="tdc",
              waits=[("vec", vps)] + ([("vec", vrh),
                     ("exu", P.counts.get("exu", 0))] if tc == 0 else []))
            _ps_war[tc % 2] = V("tdc")
        vexd = V("tdc")
        vio0 = vop(lambda e: e.tensor_copy(exsb, psB[0:2, 768:768 + CAP]),
                   waits=[("tdc", vexd)])
        for hc in range(CAPH):
            E("tensor", lambda e, hc=hc: e.matmul(
                psB[:, 770 + 2 * hc:772 + 2 * hc],
                exsb[:, hc * 128:(hc + 1) * 128],
                idn[0:2, 0:2],
                start=True, stop=True, skip_group_check=True),
              inc="tdc", waits=[("vec", vio0)] if hc == 0 else [])
        vtx = V("tdc")
        vio = vop(lambda e, j=j: e.tensor_copy(
            idxw[:, j * 2 * CAPH:j * 2 * CAPH + 2], psB[:, 770:772]),
            waits=[("tdc", vtx)])
        if CAPH > 1:
            vio = vop(lambda e, j=j: e.tensor_copy(
                idxw[:, j * 2 * CAPH + 2:j * 2 * CAPH + 4], psB[:, 772:774]))
        E("vector", lambda e: e.nop(), inc="exu", waits=[("vec", vio)])
        E("vector", lambda e: e.nop(), inc="exr", waits=[("vec", vio)])
        vii = vop(lambda e, j=j: e.tensor_copy(
            idxi[:, j * CAPH:(j + 1) * CAPH],
            idxw[:, j * 2 * CAPH:(j + 1) * 2 * CAPH].rearrange(
                "p (h w) -> p h w", h=CAPH)[:, :, 0:1]))
        E("vector", lambda e: e.nop(), inc=f"exj{j}", waits=[("vec", vii)])
        for hc in range(CAPH):
            E("gpsimd", lambda e, j=j, hc=hc: e.indirect_dma_start(
                out=xe[:, hc * 1024:(hc + 1) * 1024], out_offset=None,
                in_=h_d, in_offset=IOX(idxi[:, j * CAPH + hc:j * CAPH + hc + 1], 0)),
              inc="dga", inck=16,
              waits=[(f"exj{j}", 1), ("dht", b1_dht),
                     ("dxt0", P.counts.get("dxt0", 0)),
                     ("dxt1", P.counts.get("dxt1", 0)),
                     ("dbgs", P.counts.get("dbgs", 0))] if hc == 0 else [])
        vga = V("dga")
        xb = j % 2
        for hc in range(CAPH):
            for dc in range(NDC):
                E("sync", lambda e, hc=hc, dc=dc, xb=xb: e.dma_start_transpose(
                    xeT[xb][:, dc * CAP + hc * 128:dc * CAP + (hc + 1) * 128],
                    xe[:, hc * 1024 + dc * 128:hc * 1024 + (dc + 1) * 128]),
                  inc=f"dxt{xb}", inck=16,
                  waits=[("dga", vga), (f"xtu{xb}", P.counts.get(f"xtu{xb}", 0))]
                  if (hc == 0 and dc == 0) else [])
        _vxt_of[j] = V(f"dxt{xb}")
        if debug and j == 0:
            E("sync", lambda e: e.dma_start(out=dbg["dbg_xe"][:, 0:CAPH * 1024],
                                            in_=xe[:, 0:CAPH * 1024]),
              inc="dbgs", inck=16, waits=[("dga", vga)])

    def emit_ffn(j):
        xb = j % 2
        vxt = _vxt_of[j]
        for q4 in range(NQ4):
            wb = q4 % 2
            E("gpsimd", lambda e, j=j, q4=q4, wb=wb: e.dma_start(
            out=w1q[wb].rearrange("p (c n) -> p c n", c=NDC), in_=w1_in[j][:, q4 * 512:(q4 + 1) * 512]
                .rearrange("(c p) n -> p c n", p=128)),
              inc=f"dw1{wb}", inck=16,
              waits=[(f"wu{wb}", P.counts.get(f"wu{wb}", 0))])
            E("gpsimd", lambda e, j=j, q4=q4, wb=wb: e.dma_start(
                out=w2q[wb].rearrange("p (c n) -> p c n", c=NDC), in_=w2_in[j][:, q4 * 512:(q4 + 1) * 512]
                .rearrange("(c p) n -> p c n", p=128)),
              inc=f"dw1{wb}", inck=16)
            E("gpsimd", lambda e, j=j, q4=q4, wb=wb: e.dma_start(
                out=w3q[wb].rearrange("p (c n) -> p c n", c=4), in_=w3_in[j][q4 * 512:(q4 + 1) * 512, :]
                .rearrange("(c p) n -> p c n", p=128)),
              inc=f"dw1{wb}", inck=16)
            for _ in range(2):
                if SCAT_PEND:
                    SCAT_PEND.pop(0)()
            vw = V(f"dw1{wb}")
            for f4 in range(4):
                fc = q4 * 4 + f4
                for dc in range(NDC):
                    E("tensor", lambda e, wb=wb, f4=f4, dc=dc, xb=xb: e.matmul(
                        psC[:, 0:CAP],
                        w1q[wb][:, dc * 512 + f4 * 128:dc * 512 + (f4 + 1) * 128],
                        xeT[xb][:, dc * CAP:(dc + 1) * CAP],
                        start=(dc == 0), stop=(dc == NDC - 1)),
                      inc="tdc" if dc == NDC - 1 else None,
                      waits=[(f"dw1{wb}", vw), (f"dxt{xb}", vxt),
                             ("scc", V("scc"))] if dc == 0 else [])
                for dc in range(NDC):
                    E("tensor", lambda e, wb=wb, f4=f4, dc=dc, xb=xb: e.matmul(
                        psC[:, 512:512 + CAP],
                        w2q[wb][:, dc * 512 + f4 * 128:dc * 512 + (f4 + 1) * 128],
                        xeT[xb][:, dc * CAP:(dc + 1) * CAP],
                        start=(dc == 0), stop=(dc == NDC - 1)),
                      inc="tdc" if dc == NDC - 1 else None,
                      waits=[("vec", V("vec"))] if dc == 0 else [])
                vgv = V("tdc")
                ssl = sop(lambda e: e.activation(gs_f, psC[:, 0:CAP], AF.Silu),
                          waits=[("tdc", vgv), ("vec", V("vec"))])
                vgm = vop(lambda e: e.tensor_tensor(gv, gs_f, psC[:, 512:512 + CAP],
                                                    ALU.mult), waits=[("scc", ssl)])
                if debug and j == 0 and fc == 0:
                    E("sync", lambda e: e.dma_start(out=dbg["dbg_gv"], in_=gv),
                      inc="dbgs", inck=16, waits=[("vec", vgm)])
                    vgm = vop(lambda e: e.nop(),
                              waits=[("dbgs", P.counts.get("dbgs", 0))])
                for sc2 in range(CAPH):
                    ps_o = psD if sc2 == 0 else psA
                    for nc2 in range(2):
                        E("tensor", lambda e, sc2=sc2, nc2=nc2, f4=f4, wb=wb, ps_o=ps_o, fc=fc: e.matmul(
                            ps_o[:, nc2 * 512:(nc2 + 1) * 512],
                            gv[:, sc2 * 128:(sc2 + 1) * 128],
                            w3q[wb][:, f4 * 1024 + nc2 * 512:f4 * 1024 + (nc2 + 1) * 512],
                            start=(fc == 0), stop=(fc == NFC - 1)),
                          inc="tdc" if (sc2 == CAPH - 1 and nc2 == 1) else None,
                          waits=[("vec", vgm), ("oeu", P.counts.get("oeu", 0))]
                          if (sc2 == 0 and nc2 == 0 and fc == 0) else
                          ([("vec", vgm)] if (sc2 == 0 and nc2 == 0) else []))
            E("tensor", lambda e: e.nop(), inc=f"wu{wb}",
              waits=[("tdc", V("tdc"))])
        E("tensor", lambda e: e.nop(), inc=f"xtu{xb}",
          waits=[("tdc", V("tdc"))])
        vffn = V("tdc")
        for hc in range(CAPH):
            ps_o = psD if hc == 0 else psA
            vsl = vop(lambda e, j=j, hc=hc, ps_o=ps_o: e.tensor_scalar(
                slotb[:, 0:1024], ps_o[:, 0:1024],
                idxw[:, j * 2 * CAPH + hc * 2 + 1:j * 2 * CAPH + hc * 2 + 2],
                None, ALU.mult),
                waits=[("tdc", vffn), ("gpc", v_padz),
                       ("dx1", P.counts.get("dx1", 0)),
                       ("dsl", P.counts.get("dsl", 0))])
            vsl2 = vop(lambda e, j=j, hc=hc: e.tensor_copy(
                slotb[:, 1024:1025], idxw[:, j * 2 * CAPH + hc * 2:j * 2 * CAPH + hc * 2 + 1]))
            E("vector", lambda e: e.nop(), inc="oeu", waits=[("vec", vsl2)])
            E("sync", lambda e, j=j, hc=hc: e.dma_start(
                out=agin2.ap()[j * CAP + hc * 128:j * CAP + (hc + 1) * 128, :],
                in_=slotb),
              inc="dsl", inck=16, waits=[("vec", vsl2)])
    NC2 = EM // 2

    CPC = 2 * CAP

    def emit_ag2(c):
        vsl = V("dsl")
        E("gpsimd", lambda e, c=c: e.collective_compute(
            "AllGather", ALU.bypass,
            replica_groups=[[0, 1], [2, 3], [4, 5], [6, 7]],
            ins=[agin2.ap()[c * CPC:(c + 1) * CPC, :].opt()],
            outs=[agout2.ap()[c * 2 * CPC:(c + 1) * 2 * CPC, :].opt()]),
          inc="dcc", waits=[("dsl", vsl)])

    scb = [fv(F_T1, 1032), fv(F_T1 + 1032, 1032)]
    idsc = idxi[:, EM * CAPH:EM * CAPH + 1]
    _scat_war = [0, 0]

    def emit_scatter_group(c, gg):
        ngg = 2 * CPC // 128
        g = c * ngg + gg
        b = g % 2
        E("sync", lambda e, g=g, b=b: e.dma_start(
            out=scb[b], in_=agout2.ap()[g * 128:(g + 1) * 128, :]),
          inc=f"dsc{b}", inck=16,
          waits=[("dcc", NAG + 2 + c), ("dscat", _scat_war[b]),
                 ("dbgs", P.counts.get("dbgs", 0))])
        vld = V(f"dsc{b}")
        vop(lambda e, b=b: e.tensor_tensor(idf, scb[b][:, 1024:1025], tkbase[:],
                                           ALU.subtract),
            waits=[(f"dsc{b}", vld), ("dscat", P.counts.get("dscat", 0))])
        vop(lambda e: e.tensor_scalar(idm, idf, 0.0, None, ALU.is_lt))
        vop(lambda e: e.scalar_tensor_tensor(idf, idm, 1e6, idf, ALU.mult, ALU.add))
        vic = vop(lambda e: e.tensor_copy(idsc, idf))
        E("gpsimd", lambda e, b=b: e.indirect_dma_start(
            out=acc_d, out_offset=IOX(idsc, 0),
            in_=scb[b][:, 0:1024], in_offset=None,
            compute_op=ALU.add, bounds_check=TH - 1, oob_is_err=False),
          inc="dscat", inck=16,
          waits=[("vec", vic), ("dza", v_dza),
                 ("dscat", P.counts.get("dscat", 0))])
        _scat_war[b] = V("dscat")

    SCAT_PEND = []

    def emit_scatter_chunk(c):
        for gg in range(2 * CPC // 128):
            emit_scatter_group(c, gg)

    SCAT_INTERLEAVE = False
    for j in range(EM):
        emit_prep(j)
        if SCAT_INTERLEAVE and j >= 4 and j % 2 == 0:
            c = j // 2 - 2
            SCAT_PEND.extend(
                [lambda c=c, gg=gg: emit_scatter_group(c, gg)
                 for gg in range(2 * CPC // 128)])
        if j >= 1:
            emit_ffn(j - 1)
        if j >= 2 and j % 2 == 0:
            emit_ag2(j // 2 - 1)
        if not SCAT_INTERLEAVE and j >= 4 and j % 2 == 0:
            emit_scatter_chunk(j // 2 - 2)
    emit_ffn(EM - 1)
    emit_ag2(NC2 - 1)
    while SCAT_PEND:
        SCAT_PEND.pop(0)()
    if debug:
        E("sync", lambda e: e.dma_start(out=dbg["dbg_agout2"], in_=agout2.ap()),
          inc="dbgs", inck=16, waits=[("dcc", V("dcc"))])
    emit_scatter_chunk(NC2 - 2)
    emit_scatter_chunk(NC2 - 1)
    v_dsl = V("dsl")
    if debug:
        E("sync", lambda e: e.dma_start(out=dbg["dbg_idxw"], in_=idxw[:]),
          inc="dbgs", inck=16, waits=[("dsl", v_dsl)])

    v_scat = V("dscat")

    if debug:
        E("sync", lambda e: e.dma_start(out=dbg["dbg_acc"], in_=acc_d),
          inc="dbgs", inck=16, waits=[("dscat", v_scat)])
        E("sync", lambda e: e.dma_start(out=dbg["dbg_x1"], in_=x1_d),
          inc="dbgs", inck=16, waits=[("dx1", b1_dx1)])
        E("sync", lambda e: e.dma_start(out=dbg["dbg_xln"], in_=xln_d),
          inc="dbgs", inck=16, waits=[("xs0", 16 * NTT // 2), ("xs1", 16 * NTT // 2)])
        E("sync", lambda e: e.dma_start(out=dbg["dbg_p1"], in_=agout1.ap()),
          inc="dbgs", inck=16, waits=[("dcc", T // 1024)])

    idxf = idxi[:, EM * CAPH + 1:EM * CAPH + 2]
    for ft in range(NFT):
        b = ft % 2
        vop(lambda e, ft=ft: e.tensor_scalar(idf, tkbase[:], float(ft * 128),
                                             None, ALU.add),
            waits=[("dxg", P.counts.get("dxg", 0))])
        vop(lambda e: e.tensor_tensor(idf, idf, iotac[:], ALU.add))
        vif = vop(lambda e: e.tensor_copy(idxf, idf))
        E("gpsimd", lambda e, b=b: e.indirect_dma_start(
            out=xt[b], out_offset=None,
            in_=x1_d, in_offset=IOX(idxf, 0)),
          inc="dxg", inck=16,
          waits=[("vec", vif), ("dx1", b1_dx1),
                 (f"xf{b}", P.counts.get(f"xf{b}", 0))])
        vxg = V("dxg")
        E("sync", lambda e, ft=ft, b=b: e.dma_start(
            out=sA if False else fv(F_SA, 1024), in_=acc_d[ft * 128:(ft + 1) * 128, :]),
          inc="dac", inck=16,
          waits=[("dscat", v_scat), ("acu", P.counts.get("acu", 0)),
                 ("dout", P.counts.get("dout", 0))]
          + ([("tdc", V("tdc"))] if ft == 0 else []))
        vac = V("dac")
        vno = ln_emit(sA, sB, extra_w=[("dac", vac)]
                      + ([("tdc", V("tdc"))] if ft == 0 else []))
        vfin = vop(lambda e, b=b: e.tensor_tensor(sA, xt[b], sB, ALU.add),
                   waits=[("dxg", vxg)])
        ln_war[0] = vfin
        E("vector", lambda e: e.nop(), inc="acu", waits=[("vec", vfin)])
        E("vector", lambda e: e.nop(), inc=f"xf{b}", waits=[("vec", vfin)])
        E("sync", lambda e, ft=ft: e.dma_start(
            out=out_d[ft * 128:(ft + 1) * 128, :], in_=sA),
          inc="dout", inck=16, waits=[("vec", vfin)])
    E("sync", lambda e: e.nop(), waits=[("dout", V("dout"))])

    P.realize()
    return nc


def _nzrow(AFR, base, h, QB):
    return AFR[h:h + 1, base:base + QB]


_NC_CACHE = {}


def _host_inputs(inputs, core):
    T, D, NH, NE = 4096, 1024, 8, 16
    b = core // 2
    half = core % 2
    hs = slice(half * NH * 64, (half + 1) * NH * 64)
    hh = slice(half * NH, (half + 1) * NH)
    es = slice(half * NE // 2, (half + 1) * NE // 2)
    SUB = 128 // NE
    l8 = np.zeros((128, 128), np.float32)
    for p in range(128):
        for m in range(128):
            if p // SUB == m // SUB and p % SUB < m % SUB:
                l8[p, m] = 1.0
    bc16 = np.zeros((NE, 128), np.float32)
    for e in range(NE):
        bc16[e, e * SUB:(e + 1) * SUB] = 1.0
    sel16 = bc16.T.copy()
    oh8 = np.zeros((SUB, NH * 64), np.float32)
    for h in range(NH):
        oh8[h, h * 64:(h + 1) * 64] = 1.0
    chp = np.concatenate([inputs["L0_attn_beta_p"][hh], inputs["L0_attn_iv_p"][hh],
                          inputs["L0_attn_ov_p"][hh], inputs["L0_attn_chi"][hh]])
    return {
        "x": np.ascontiguousarray(inputs["L0_attn_x"][b]),
        "wi": np.ascontiguousarray(inputs["L0_attn_Wi"][:, hs]),
        "wo": np.ascontiguousarray(inputs["L0_attn_Wo"][:, hs]),
        "wv": np.ascontiguousarray(inputs["L0_attn_Wv"][:, hs]),
        "wout": np.ascontiguousarray(inputs["L0_attn_Wout"][hs, :]),
        "wa": np.ascontiguousarray(inputs["L0_attn_Wa"][:, hh]),
        "we": np.ascontiguousarray(inputs["L0_attn_We"][:, hh]),
        "ba": np.ascontiguousarray(inputs["L0_attn_ba"][hh]).reshape(1, NH),
        "be": np.ascontiguousarray(inputs["L0_attn_be"][hh]).reshape(1, NH),
        "chp": np.ascontiguousarray(chp).reshape(1, 4 * NH),
        "rw1": np.ascontiguousarray(inputs["L0_ffn_router_w1"]),
        "rw2": np.ascontiguousarray(inputs["L0_ffn_router_w2"]),
        "w1": np.ascontiguousarray(inputs["L0_ffn_w1"][es]),
        "w2": np.ascontiguousarray(inputs["L0_ffn_w2"][es]),
        "w3": np.ascontiguousarray(inputs["L0_ffn_w3"][es]),
        "tkb": np.full((128, 1), half * (T // 2), np.float32),
        "myexp": np.arange(half * NE // 2, (half + 1) * NE // 2,
                           dtype=np.int32).reshape(-1, 1),
        "oh8": oh8, "l8": l8, "bc16": bc16, "sel16": sel16,
        "g128": np.kron(np.eye(NE, dtype=np.float32),
                        np.ones((128 // NE, 128 // NE), np.float32)),
        "idn": np.eye(128, dtype=np.float32),
        "rep8": np.repeat(np.eye(8, dtype=np.float32), 8, axis=1),
        "sub8": np.tile(np.arange(8, dtype=np.float32), 8).reshape(64, 1),
    }


def kernel(**inputs):
    key = "full"
    if key not in _NC_CACHE:
        _NC_CACHE[key] = build()
    nc = _NC_CACHE[key]
    in_maps = [_host_inputs(inputs, c) for c in range(8)]
    res = run_bass_kernel_spmd(nc, in_maps, core_ids=list(range(8)))
    out = np.empty((4, 4096, 1024), np.float32)
    for c in range(8):
        b, half = c // 2, c % 2
        out[b, half * 2048:(half + 1) * 2048, :] = res.results[c]["out"]
    return out


# version 17
# speedup vs baseline: 1.0592x; 1.0175x over previous
import math
import numpy as np
import concourse.bass as bass
import concourse.mybir as mybir
from concourse.bass_utils import run_bass_kernel_spmd

F32 = mybir.dt.float32
BF16 = mybir.dt.bfloat16
I32 = mybir.dt.int32
AF = mybir.ActivationFunctionType
ALU = mybir.AluOpType
IOX = bass.IndirectOffsetOnAxis

EPS = 1e-5
PI = math.pi

CFG = dict(T=4096, D=1024, NH=8, DH=64, NE=16, CAP=256, DFF=2048, RH=128,
           QB=512, BIS=48)


class Prog:
    ENGINES = ("sync", "gpsimd", "vector", "scalar", "tensor")

    def __init__(self, nc):
        self.nc = nc
        self.ops = {e: [] for e in self.ENGINES}
        self.sems = {}
        self.counts = {}

    def _sem(self, name):
        if name not in self.sems:
            self.sems[name] = self.nc.alloc_semaphore(name)
            self.counts[name] = 0

    def emit(self, eng, fn, inc=None, inck=1, waits=()):
        w = []
        for s, v in waits:
            if v > 0:
                self._sem(s)
                w.append((s, v))
        v = None
        if inc is not None:
            self._sem(inc)
            self.counts[inc] += inck
            v = self.counts[inc]
        self.ops[eng].append((fn, inc, inck, w))
        return v

    def val(self, name):
        return self.counts.get(name, 0)

    def realize(self):
        with self.nc.Block() as block:
            for eng_name in self.ENGINES:
                ops = self.ops[eng_name]
                if not ops:
                    continue

                def body(eng, ops=ops):
                    for fn, inc, inck, waits in ops:
                        for s, v in waits:
                            eng.wait_ge(self.sems[s], v)
                        instr = fn(eng) if fn is not None else None
                        if inc is not None:
                            instr.then_inc(self.sems[inc], inck)
                getattr(block, eng_name)(body)


DISABLE_B1_INJECT = False


def build(cfg=CFG, debug=False):
    T, D, NH, DH = cfg["T"], cfg["D"], cfg["NH"], cfg["DH"]
    NE, CAP, DFF, RH = cfg["NE"], cfg["CAP"], cfg["DFF"], cfg["RH"]
    QB, BIS = cfg["QB"], cfg["BIS"]
    NTT = T // 128
    NQB = T // QB
    NKC = T // 128
    NDC = D // 128
    DM = NH * DH
    NMC = DM // 128
    NFC = DFF // 128
    NQ4 = DFF // 512
    EM = NE // 2
    CAPH = CAP // 128
    TH = T // 2
    NFT = TH // 128
    SUB = 128 // NE
    CPS = T // SUB
    NTC = T // 128
    VX = DH + 2
    NQC = QB // 512
    NT2 = T // 512

    nc = bass.Bass("TRN2", target_bir_lowering=False)

    x_in = nc.dram_tensor("x", [T, D], F32, kind="ExternalInput").ap()
    wi_in = nc.dram_tensor("wi", [D, DM], F32, kind="ExternalInput").ap()
    wo_in = nc.dram_tensor("wo", [D, DM], F32, kind="ExternalInput").ap()
    wv_in = nc.dram_tensor("wv", [D, DM], F32, kind="ExternalInput").ap()
    wout_in = nc.dram_tensor("wout", [DM, D], F32, kind="ExternalInput").ap()
    wa_in = nc.dram_tensor("wa", [D, NH], F32, kind="ExternalInput").ap()
    we_in = nc.dram_tensor("we", [D, NH], F32, kind="ExternalInput").ap()
    ba_in = nc.dram_tensor("ba", [1, NH], F32, kind="ExternalInput").ap()
    be_in = nc.dram_tensor("be", [1, NH], F32, kind="ExternalInput").ap()
    chp_in = nc.dram_tensor("chp", [1, 4 * NH], F32, kind="ExternalInput").ap()
    rw1_in = nc.dram_tensor("rw1", [D, RH], F32, kind="ExternalInput").ap()
    rw2_in = nc.dram_tensor("rw2", [RH, NE], F32, kind="ExternalInput").ap()
    w1_in = nc.dram_tensor("w1", [EM, D, DFF], F32, kind="ExternalInput").ap()
    w2_in = nc.dram_tensor("w2", [EM, D, DFF], F32, kind="ExternalInput").ap()
    w3_in = nc.dram_tensor("w3", [EM, DFF, D], F32, kind="ExternalInput").ap()
    tkb_in = nc.dram_tensor("tkb", [128, 1], F32, kind="ExternalInput").ap()
    myexp_in = nc.dram_tensor("myexp", [EM, 1], I32, kind="ExternalInput").ap()
    oh8_in = nc.dram_tensor("oh8", [SUB, NH * 64], F32, kind="ExternalInput").ap()
    l8_in = nc.dram_tensor("l8", [128, 128], F32, kind="ExternalInput").ap()
    bc16_in = nc.dram_tensor("bc16", [NE, 128], F32, kind="ExternalInput").ap()
    sel16_in = nc.dram_tensor("sel16", [128, NE], F32, kind="ExternalInput").ap()
    g128_in = nc.dram_tensor("g128", [128, 128], F32, kind="ExternalInput").ap()
    idn_in = nc.dram_tensor("idn", [128, 128], F32, kind="ExternalInput").ap()
    rep8_in = nc.dram_tensor("rep8", [8, 64], F32, kind="ExternalInput").ap()
    sub8_in = nc.dram_tensor("sub8", [64, 1], F32, kind="ExternalInput").ap()
    out_d = nc.dram_tensor("out", [TH, D], F32, kind="ExternalOutput").ap()

    xln_d = nc.dram_tensor("xln_d", [T, D], BF16, kind="Internal").ap()
    agin1 = nc.dram_tensor("agin1", [T, D], BF16, kind="Internal")
    agout1 = nc.dram_tensor("agout1", [2 * T, D], BF16, kind="Internal")
    x1_d = nc.dram_tensor("x1_d", [T, D], F32, kind="Internal").ap()
    h_d = nc.dram_tensor("h_d", [T, D], BF16, kind="Internal").ap()
    a_d = nc.dram_tensor("a_d", [NH, T], F32, kind="Internal").ap()
    lgt_d = nc.dram_tensor("lgt_d", [NE, T], F32, kind="Internal").ap()
    gmx_d = nc.dram_tensor("gmx_d", [128, 1], F32, kind="Internal").ap()
    selcw_d = nc.dram_tensor("selcw_d", [3, NE, T], F32, kind="Internal").ap()
    agin2 = nc.dram_tensor("agin2", [EM * CAP, 1032], F32, kind="Internal")
    agout2 = nc.dram_tensor("agout2", [NE * CAP, 1032], F32, kind="Internal")
    acc_d = nc.dram_tensor("acc_d", [TH, D], F32, kind="Internal").ap()

    dbg = {}
    if debug:
        for nm, shp in (("dbg_conv", [128, 2048]), ("dbg_rby", [NH, 512]),
                        ("dbg_idxw", [128, 2 * EM * CAPH]),
                        ("dbg_xe", [128, 2048]), ("dbg_gv", [128, CAP]),
                        ("dbg_xln", [T, D]), ("dbg_p1", [2 * T, D]),
                        ("dbg_x1", [T, D]), ("dbg_probs", [128, CPS]),
                        ("dbg_sel", [128, CPS]), ("dbg_agout2", [NE * CAP, 1032]),
                        ("dbg_acc", [TH, D])):
            dt = BF16 if nm in ("dbg_xln", "dbg_p1", "dbg_xe", "dbg_gv") else F32
            if nm in ("dbg_idxw", "dbg_rby"):
                dt = F32
            if nm == "dbg_conv":
                dt = BF16
            dbg[nm] = nc.dram_tensor(nm, shp, dt, kind="ExternalOutput").ap()

    NB = 75264
    NF = 10520
    AB = nc.alloc_sbuf_tensor("AB", [128, NB], BF16)
    AFR = nc.alloc_sbuf_tensor("AFR", [128, NF], F32)

    def bv(off, n, p0=0, p1=128):
        return AB[p0:p1, off:off + n]

    def fv(off, n, p0=0, p1=128):
        return AFR[p0:p1, off:off + n]

    O_INNERT = 0
    O_OUTERT = O_INNERT + NMC * T
    O_VX = O_OUTERT + NMC * T
    O_W = O_VX + NTT * VX * NH
    O_XLT = O_W + 3 * NDC * 512
    O_WOUT = O_XLT + NDC * 512
    O_STG = O_WOUT + NMC * 1024
    assert O_STG + 1024 <= NB, O_STG
    NPH_ = NH // 2
    O_COMBT = O_W
    O_UT = O_COMBT + 2 * NPH_ * QB
    assert O_UT + 2 * (2 * QB) <= O_XLT
    O_CONV = O_XLT
    B_HT = 0
    B_R1 = B_HT + NDC * 512
    B_XE = B_R1 + 512
    B_XET = B_XE + CAPH * 1024
    B_W1 = B_XET + 2 * NDC * CAP
    B_W2 = B_W1 + 2 * NDC * 512
    B_W3 = B_W2 + 2 * NDC * 512
    B_RW1 = B_W3 + 2 * 4 * 1024
    B_RW2 = B_RW1 + NDC * RH
    B_PAB = B_RW2 + NE
    B_HS = B_PAB + 4 * 1024
    B_GV = B_HS + 1024
    assert B_GV + 256 <= NB, B_GV

    F_XT = 0
    F_SA = 2048
    F_SB = 3072
    F_T1 = 4096
    F_T2 = 5120
    F_T3 = 6144
    F_T4 = 7168
    F_T5 = 8192
    F_SLOT = 9216
    F_COLS = 10300
    assert F_COLS + 220 <= NF

    iotas = nc.alloc_sbuf_tensor("iotas", [128, CAP], F32)
    tokc = nc.alloc_sbuf_tensor("tokc", [128, NTC], F32)
    l8 = nc.alloc_sbuf_tensor("l8s", [128, 128], F32)
    idn = nc.alloc_sbuf_tensor("idns", [128, 128], F32)
    g128 = nc.alloc_sbuf_tensor("g128s", [128, 128], F32)
    rep8 = nc.alloc_sbuf_tensor("rep8s", [8, 64], F32)
    sub8 = nc.alloc_sbuf_tensor("sub8s", [64, 1], F32)
    idx64 = nc.alloc_sbuf_tensor("idx64s", [64, 1], F32)
    idx64i = nc.alloc_sbuf_tensor("idx64is", [64, 3], I32)
    myexpf = nc.alloc_sbuf_tensor("myexpfs", [8, 1], F32)
    pihalf = nc.alloc_sbuf_tensor("pihalfs", [1, 1], F32)
    bc16 = nc.alloc_sbuf_tensor("bc16s", [NE, 128], F32)
    sel16 = nc.alloc_sbuf_tensor("sel16s", [128, NE], F32)
    oh8 = nc.alloc_sbuf_tensor("oh8s", [SUB, NH * 64], F32)
    ones = nc.alloc_sbuf_tensor("oness", [1, 512], F32)
    tkbase = nc.alloc_sbuf_tensor("tkbs", [128, 1], F32)
    iotac = nc.alloc_sbuf_tensor("iotacs", [128, 1], F32)
    chpf = nc.alloc_sbuf_tensor("chpfs", [1, 4 * NH], F32)
    casa = nc.alloc_sbuf_tensor("casas", [1, 2 * NH], F32)
    cb64 = nc.alloc_sbuf_tensor("cb64s", [64, 2 * NH], F32)
    baf = nc.alloc_sbuf_tensor("bafs", [1, NH], F32)
    bef = nc.alloc_sbuf_tensor("befs", [1, NH], F32)
    wa_sb = nc.alloc_sbuf_tensor("wa_sbs", [128, NDC * NH], BF16)
    we_sb = nc.alloc_sbuf_tensor("we_sbs", [128, NDC * NH], BF16)
    myexp_sb = nc.alloc_sbuf_tensor("myexps", [EM, 1], I32)
    iotI = nc.alloc_sbuf_tensor("iotI", [128, CAP], I32)
    iotI2 = nc.alloc_sbuf_tensor("iotI2", [128, NTC], I32)
    idxi = nc.alloc_sbuf_tensor("idxi", [128, EM * CAPH + 4], I32)
    idxw = nc.alloc_sbuf_tensor("idxw", [128, EM * CAPH * 2], F32)
    sgm = nc.alloc_sbuf_tensor("sgms", [1, 4 * NH], F32)
    tctmp = nc.alloc_sbuf_tensor("tctmp", [1, 2 * NH], F32)
    estg = nc.alloc_sbuf_tensor("estgs", [128, NH], F32)
    a_stg = nc.alloc_sbuf_tensor("a_stgs", [NH, 512], F32)
    selTm = nc.alloc_sbuf_tensor("selTm", [128, 512], F32)
    ceTm = nc.alloc_sbuf_tensor("ceTm", [128, 512], F32)
    wTm = nc.alloc_sbuf_tensor("wTm", [128, 512], F32)
    pselb = nc.alloc_sbuf_tensor("pselb", [128, CAP], F32)
    rhsa = nc.alloc_sbuf_tensor("rhsa", [128, 2 * NTC], F32)
    g16 = nc.alloc_sbuf_tensor("g16", [NE, 24], F32)

    psA = nc.alloc_psum_tensor("psA", [128, 1024], F32)
    psB = nc.alloc_psum_tensor("psB", [128, 1024], F32)
    psC = nc.alloc_psum_tensor("psC", [128, 1024], F32)
    psD = nc.alloc_psum_tensor("psD", [128, 1024], F32)

    P = Prog(nc)
    E = P.emit
    V = P.val

    def vop(fn, waits=()):
        return E("vector", fn, inc="vec", waits=list(waits) + [("vec", V("vec"))])

    def sop(fn, waits=()):
        return E("scalar", fn, inc="scc", waits=list(waits) + [("scc", V("scc"))])

    mcol = fv(F_COLS + 0, 1)
    vcol = fv(F_COLS + 1, 1)
    m2col = fv(F_COLS + 2, 1)
    rstd = fv(F_COLS + 3, 1)
    sxc = fv(F_COLS + 4, 1)
    sxxc = fv(F_COLS + 5, 1)
    mxc = fv(F_COLS + 6, 1)
    gcol = fv(F_COLS + 7, 1)
    scol = fv(F_COLS + 8, 1)
    pfxc = fv(F_COLS + 9, 1)
    rcol = fv(F_COLS + 10, 1)
    vcol2 = fv(F_COLS + 11, 1)
    dcol = fv(F_COLS + 12, 1)
    idf = fv(F_COLS + 13, 1)
    idm = fv(F_COLS + 14, 1)
    a_sxc = fv(F_COLS + 40, 1)
    a_sxxc = fv(F_COLS + 41, 1)
    a_m = fv(F_COLS + 42, 1)
    a_m2 = fv(F_COLS + 43, 1)
    a_nm2 = fv(F_COLS + 44, 1)
    a_v = fv(F_COLS + 45, 1)
    a_r = fv(F_COLS + 46, 1)
    a_eps = fv(F_COLS + 47, 1)
    vlnc = [[fv(F_COLS + 50 + 8 * b + i, 1) for i in range(6)] for b in (0, 1)]

    for dst, src in ((baf[:], ba_in), (bef[:], be_in), (chpf[:], chp_in),
                     (tkbase[:], tkb_in), (oh8[:], oh8_in), (l8[:], l8_in),
                     (bc16[:], bc16_in), (sel16[:], sel16_in), (idn[:], idn_in),
                     (g128[:], g128_in),
                     (rep8[:], rep8_in), (sub8[:], sub8_in)):
        E("sync", lambda e, dst=dst, src=src: e.dma_start(out=dst, in_=src),
          inc="dsm", inck=16)
    E("gpsimd", lambda e: e.dma_start(
        out=wa_sb[:].rearrange("p (c n) -> p c n", c=NDC), in_=wa_in.rearrange("(c p) n -> p c n", p=128)),
      inc="dwA", inck=16)
    E("gpsimd", lambda e: e.dma_start(
        out=we_sb[:].rearrange("p (c n) -> p c n", c=NDC), in_=we_in.rearrange("(c p) n -> p c n", p=128)),
      inc="dwA", inck=16)
    E("sync", lambda e: e.dma_start(out=myexp_sb[:], in_=myexp_in),
      inc="dsm", inck=16)
    v_dsm = V("dsm")

    E("gpsimd", lambda e: e.iota(iotI[:], pattern=[[1, CAP]], base=0,
                                 channel_multiplier=0), inc="gpc")
    E("gpsimd", lambda e: e.tensor_copy(iotas[:], iotI[:]), inc="gpc",
      waits=[("gpc", V("gpc"))])
    E("gpsimd", lambda e: e.iota(iotI2[:], pattern=[[128, NTC]], base=0,
                                 channel_multiplier=1), inc="gpc",
      waits=[("gpc", V("gpc"))])
    E("gpsimd", lambda e: e.tensor_copy(tokc[:], iotI2[:]), inc="gpc",
      waits=[("gpc", V("gpc"))])
    E("gpsimd", lambda e: e.iota(iotI2[:, 0:1], pattern=[[1, 1]], base=0,
                                 channel_multiplier=1), inc="gpc",
      waits=[("gpc", V("gpc"))])
    E("gpsimd", lambda e: e.tensor_copy(iotac[:], iotI2[:, 0:1]), inc="gpc",
      waits=[("gpc", V("gpc"))])
    E("gpsimd", lambda e: e.memset(ones[:], 1.0), inc="gpc",
      waits=[("gpc", V("gpc"))])
    E("gpsimd", lambda e: e.memset(pihalf[:], float(PI / 2)), inc="gpc",
      waits=[("gpc", V("gpc"))])
    E("gpsimd", lambda e: e.memset(a_eps, float(1e-5)), inc="gpc",
      waits=[("gpc", V("gpc"))])
    v_gpc0 = V("gpc")

    sop(lambda e: e.activation(sgm[:, 0:3 * NH], chpf[:, 0:3 * NH], AF.Sigmoid),
        waits=[("dsm", v_dsm)])
    sop(lambda e: e.activation(sgm[:, 3 * NH:4 * NH], chpf[:, 3 * NH:4 * NH], AF.Tanh))
    sop(lambda e: e.activation(tctmp[:, NH:2 * NH], sgm[:, 0:NH], AF.Sin,
                               scale=float(PI)))
    sop(lambda e: e.activation(tctmp[:, 0:NH], sgm[:, 0:NH], AF.Sin,
                               scale=float(-PI), bias=pihalf[:]),
        waits=[("gpc", V("gpc"))])
    v_sc_ch = V("scc")
    vop(lambda e: e.tensor_mul(sgm[:, NH:2 * NH], sgm[:, NH:2 * NH],
                               sgm[:, 2 * NH:3 * NH]), waits=[("scc", v_sc_ch)])
    vop(lambda e: e.tensor_mul(sgm[:, NH:2 * NH], sgm[:, NH:2 * NH],
                               sgm[:, 3 * NH:4 * NH]))
    vop(lambda e: e.tensor_mul(casa[:, 0:NH], tctmp[:, 0:NH], sgm[:, NH:2 * NH]))
    v_casa = vop(lambda e: e.tensor_mul(casa[:, NH:2 * NH], tctmp[:, NH:2 * NH],
                                        sgm[:, NH:2 * NH]))
    E("tensor", lambda e: e.matmul(psB[0:64, 0:2 * NH], ones[0:1, 0:64], casa[:],
                                   start=True, stop=True),
      inc="tdc", waits=[("vec", v_casa), ("gpc", v_gpc0)])
    v_cb64 = vop(lambda e: e.tensor_copy(cb64[:], psB[0:64, 0:2 * NH]),
                 waits=[("tdc", V("tdc"))])

    wi_sb = bv(O_W, NDC * 512)
    wo_sb = bv(O_W + NDC * 512, NDC * 512)
    wv_sb = bv(O_W + 2 * NDC * 512, NDC * 512)
    wout_sb = bv(O_WOUT, NMC * 1024)
    v_dw = {}
    for nm, dst, src in (("wi", wi_sb, wi_in), ("wo", wo_sb, wo_in),
                         ("wv", wv_sb, wv_in)):
        E("gpsimd", lambda e, dst=dst, src=src: e.dma_start(
            out=dst.rearrange("p (c n) -> p c n", c=NDC), in_=src.rearrange("(c p) n -> p c n", p=128)),
          inc="d" + nm, inck=16)
        v_dw[nm] = ("d" + nm, V("d" + nm))
    v_dwA = V("dwA")

    xt = [fv(F_XT, 1024), fv(F_XT + 1024, 1024)]
    sA = fv(F_SA, 1024)
    sB = fv(F_SB, 1024)
    xstg = [bv(O_WOUT, 1024), bv(O_WOUT + 1024, 1024)]

    ln_war = [0]

    def ln_emit(src, dst, extra_w=(), dst_w=()):
        s0 = sop(lambda e: e.activation(sB, src, AF.Identity, accum_out=a_sxc),
                 waits=list(extra_w) + [("vec", ln_war[0]), ("gpc", v_gpc0)])
        sop(lambda e: e.activation(sB, src, AF.Square, accum_out=a_sxxc))
        sop(lambda e: e.activation(a_m, a_sxc, AF.Identity, scale=1.0 / 1024.0))
        sop(lambda e: e.activation(a_m2, a_m, AF.Square))
        sop(lambda e: e.activation(a_nm2, a_m2, AF.Identity, scale=-1.0))
        sop(lambda e: e.activation(a_v, a_sxxc, AF.Identity,
                                   scale=1.0 / 1024.0, bias=a_nm2))
        sop(lambda e: e.activation(a_r, a_v, AF.Ln, bias=a_eps))
        s3 = sop(lambda e: e.activation(a_r, a_r, AF.Exp, scale=-0.5))
        vn = vop(lambda e: e.tensor_scalar(dst, src, a_m, a_r,
                                           ALU.subtract, ALU.mult),
                 waits=list(dst_w) + [("scc", s3)])
        ln_war[0] = vn
        return vn

    def emit_a1_tile(t):
        b = t % 2
        E("sync", lambda e, t=t, b=b: e.dma_start(
            out=xt[b], in_=x_in[t * 128:(t + 1) * 128, :]),
          inc=f"dx{b}", inck=16,
          waits=[(f"xf{b}", P.counts.get(f"xf{b}", 0))])
        vl = V(f"dx{b}")
        vn = ln_emit(xt[b], xstg[b],
                     extra_w=[(f"dx{b}", vl)],
                     dst_w=[(f"xs{b}", P.counts.get(f"xs{b}", 0))])
        E("vector", lambda e: e.nop(), inc=f"xf{b}", waits=[("vec", vn)])
        E("gpsimd", lambda e, t=t, b=b: e.dma_start(
            out=xln_d[t * 128:(t + 1) * 128, :], in_=xstg[b]),
          inc=f"xs{b}", inck=16, waits=[("vec", vn)])

    xlt = [bv(O_XLT, NDC * 512), bv(O_WOUT + 2048, NDC * 512)]
    vx = bv(O_VX, NTT * VX * NH)
    innerT = bv(O_INNERT, NMC * T)
    outerT = bv(O_OUTERT, NMC * T)
    E("gpsimd", lambda e: e.memset(vx, 1.0), inc="gpc",
      waits=[("gpc", V("gpc"))])
    v_vxones = V("gpc")

    for t in range(8):
        emit_a1_tile(t)
    xlt_war = [0, 0]
    a2war = [0, 0, 0, 0]
    a2warv = [0, 0]
    sg_war = [0]
    ps_rot = [psC[:, 0:512], psC[:, 512:1024], psD[:, 0:512], psD[:, 512:1024]]

    def emit_xlt_trans(tc):
        tp = tc % 2
        for dc in range(NDC):
            E("sync", lambda e, tc=tc, dc=dc, tp=tp: e.dma_start_transpose(
                xlt[tp][:, dc * 512:(dc + 1) * 512],
                xln_d[tc * 512:(tc + 1) * 512, dc * 128:(dc + 1) * 128]),
              inc=f"tr{tp}", inck=16,
              waits=([("xs0", 16 * (2 * tc + 2)), ("xs1", 16 * (2 * tc + 2)),
                      ("tdc", xlt_war[tp])] if dc == 0 else []))

    emit_xlt_trans(0)
    for tc in range(NT2):
        tp = tc % 2
        vtr = V(f"tr{tp}")
        base_w = [(f"tr{tp}", vtr), ("dsm", v_dsm)]

        gi = 0
        for wnm, wsb, dstT in (("wi", wi_sb, innerT), ("wo", wo_sb, outerT)):
            for j in range(NMC):
                ps_o = ps_rot[gi % 4]
                for dc in range(NDC):
                    E("tensor", lambda e, wsb=wsb, j=j, dc=dc, tp=tp, ps_o=ps_o: e.matmul(
                        ps_o,
                        wsb[:, dc * 512 + j * 128:dc * 512 + (j + 1) * 128],
                        xlt[tp][:, dc * 512:(dc + 1) * 512],
                        start=(dc == 0), stop=(dc == NDC - 1)),
                      inc="tdc" if dc == NDC - 1 else None,
                      waits=(((base_w + [v_dw[wnm]]) if gi % 4 == 0 else [])
                             + [("vec", a2war[gi % 4])])
                      if dc == 0 else [])
                vmm = V("tdc")
                vop(lambda e, dstT=dstT, j=j, tc=tc, ps_o=ps_o: e.tensor_copy(
                    dstT[:, j * T + tc * 512:j * T + (tc + 1) * 512], ps_o),
                    waits=[("tdc", vmm)])
                a2war[gi % 4] = V("vec")
                if tc + 2 < NT2 and gi % 2 == 0:
                    emit_a1_tile(4 * (tc + 2) + gi // 2)
                gi += 1
        for ti in range(4):
            tt = tc * 4 + ti
            for dc in range(NDC):
                E("tensor", lambda e, ti=ti, dc=dc, tp=tp: e.matmul(
                    psB[:, 0:NH],
                    xlt[tp][:, dc * 512 + ti * 128:dc * 512 + (ti + 1) * 128],
                    we_sb[:, dc * NH:(dc + 1) * NH],
                    start=(dc == 0), stop=False),
                  waits=[("scc", sg_war[0]), ("dwA", v_dwA)] if dc == 0 else [])
            E("tensor", lambda e: e.matmul(psB[:, 0:NH], ones[0:1, 0:128],
                                           bef[:], start=False, stop=True),
              inc="tdc")
            vmm = V("tdc")
            vsg = sop(lambda e: e.activation(estg[:], psB[:, 0:NH], AF.Sigmoid),
                      waits=[("tdc", vmm), ("vec", V("vec"))])
            sg_war[0] = vsg
            vp = ti % 2
            for dc in range(NDC):
                E("tensor", lambda e, ti=ti, dc=dc, tp=tp, vp=vp: e.matmul(
                    psA[:, vp * 512:(vp + 1) * 512],
                    xlt[tp][:, dc * 512 + ti * 128:dc * 512 + (ti + 1) * 128],
                    wv_sb[:, dc * 512:(dc + 1) * 512],
                    start=(dc == 0), stop=(dc == NDC - 1)),
                  inc="tdc" if dc == NDC - 1 else None,
                  waits=[("vec", a2warv[vp]), ("gpc", v_vxones),
                         v_dw["wv"]] if dc == 0 else [])
            vmm2 = V("tdc")
            for h in range(NH):
                vop(lambda e, tt=tt, h=h, vp=vp: e.tensor_scalar(
                    vx[:, tt * VX * NH + h * VX:tt * VX * NH + h * VX + DH],
                    psA[:, vp * 512 + h * DH:vp * 512 + (h + 1) * DH],
                    estg[:, h:h + 1], None, ALU.mult),
                    waits=[("tdc", vmm2), ("scc", vsg)])
            vop(lambda e, tt=tt: e.tensor_copy(
                vx[:, tt * VX * NH:(tt + 1) * VX * NH].rearrange(
                    "p (h v) -> p h v", h=NH)[:, :, DH:DH + 1],
                estg[:].rearrange("p (h o) -> p h o", h=NH)),
                waits=[("scc", vsg)])
            a2warv[vp] = V("vec")
        for dc in range(NDC):
            E("tensor", lambda e, dc=dc, tp=tp: e.matmul(
                psB[0:NH, 512:1024],
                wa_sb[:, dc * NH:(dc + 1) * NH],
                xlt[tp][:, dc * 512:(dc + 1) * 512],
                start=(dc == 0), stop=False),
              waits=[("scc", sg_war[0]), ("dwA", v_dwA),
                     ("das", P.counts.get("das", 0))] if dc == 0 else [])
        E("tensor", lambda e: e.matmul(psB[0:NH, 512:1024], baf[:], ones[:],
                                       start=False, stop=True), inc="tdc")
        xlt_war[tp] = V("tdc")
        vmm3 = V("tdc")
        vsa = sop(lambda e: e.activation(a_stg[:], psB[0:NH, 512:1024], AF.Sigmoid),
                  waits=[("tdc", vmm3)])
        sg_war[0] = vsa
        E("sync", lambda e, tc=tc: e.dma_start(
            out=a_d[:, tc * 512:(tc + 1) * 512], in_=a_stg[:]),
          inc="das", inck=16, waits=[("scc", vsa)])
        if tc + 1 < NT2:
            emit_xlt_trans(tc + 1)

    a2_tdc = V("tdc")
    E("gpsimd", lambda e: e.dma_start(
        out=wout_sb.rearrange("p (c n) -> p c n", c=NMC), in_=wout_in.rearrange("(c p) n -> p c n", p=128)),
      inc="dwA", inck=16, waits=[("tdc", a2_tdc)])
    v_dwout = V("dwA")
    a2_scc = V("scc")
    a2_vec = V("vec")
    a2_das = V("das")

    NPH = NH // 2
    stp = [psA, psB]
    Ut = [bv(O_UT, 2 * QB), bv(O_UT + 2 * QB, 2 * QB)]
    conv = [bv(O_CONV, NPH * QB), bv(O_CONV + NPH * QB, NPH * QB)]
    combT = [bv(O_COMBT, NPH * QB), bv(O_COMBT + NPH * QB, NPH * QB)]
    exstA = fv(F_T1, QB, 0, 2)
    exstB = fv(F_T1 + QB, QB, 0, 2)
    nzn = fv(F_T2, QB, 0, NH)
    nzz = fv(F_T3, QB, 0, NH)
    rby = fv(F_T4, QB, 0, NH)
    aq = fv(F_T5, QB, 0, NH)
    pstgb = [bv(O_STG, 512), bv(O_STG + 512, 512)]
    AGC = 1024
    NAG = T // AGC
    O_B1 = O_COMBT + 2 * NPH * QB + 2 * (2 * QB)
    pa_s = [bv(O_B1, 1024), bv(O_B1 + 1024, 1024)]
    pb_s = [bv(O_B1 + 2048, 1024), bv(O_B1 + 3072, 1024)]
    hstg = bv(O_B1 + 4096, 1024)
    assert O_B1 + 5120 <= O_XLT
    O_HT = O_STG + 1024
    assert O_HT + NDC * 512 <= NB
    hT = bv(O_HT, NDC * 512)
    r1T = nc.alloc_sbuf_tensor("r1Ts", [128, 256], BF16)[:, :]

    rw1_sb = bv(O_B1 + 5120, NDC * RH)
    rw2_sb = nc.alloc_sbuf_tensor("rw2s", [128, NE], BF16)[:, :]

    def ln_v_stats(src_ap, cols, extra_w=()):
        sx, sxx, m, vv, m2, rs = cols
        vop(lambda e: e.tensor_scalar(sB, src_ap, 1.0, 0.0, ALU.mult, ALU.add,
                                      accum_out=sx),
            waits=list(extra_w) + [("dlg", P.counts.get("dlg", 0))])
        vop(lambda e: e.scalar_tensor_tensor(sB, src_ap, 1.0, src_ap,
                                             ALU.mult, ALU.mult,
                                             accum_out=sxx))
        vop(lambda e: e.tensor_scalar(m, sx, 1.0 / 1024.0, None, ALU.mult))
        vop(lambda e: e.tensor_scalar(vv, sxx, 1.0 / 1024.0, None, ALU.mult))
        vop(lambda e: e.tensor_mul(m2, m, m))
        vop(lambda e: e.tensor_tensor(vv, vv, m2, ALU.subtract))
        return vop(lambda e: e.tensor_scalar(vv, vv, float(EPS), None, ALU.add))

    def ln_v_finish(src_ap, dst, cols, v0, extra_w=()):
        sx, sxx, m, vv, m2, rs = cols
        s2 = sop(lambda e: e.activation(rs, vv, AF.Ln), waits=[("vec", v0)])
        s3 = sop(lambda e: e.activation(rs, rs, AF.Exp, scale=-0.5))
        return vop(lambda e: e.tensor_scalar(dst, src_ap, m, rs,
                                             ALU.subtract, ALU.mult),
                   waits=list(extra_w) + [("scc", s3)])

    sAb = [sA, fv(F_SLOT, 1024)]
    _b1v = {}

    def emit_b1_part1(t):
        b = t % 2
        rowa, rowb, need = b1_src(t)
        E("sync", lambda e, rowa=rowa, b=b: e.dma_start(
            out=pa_s[b], in_=agout1.ap()[rowa:rowa + 128, :]),
          inc=f"pa{b}", inck=16,
          waits=[("dcc", need), (f"pf{b}", P.counts.get(f"pf{b}", 0))])
        E("sync", lambda e, rowb=rowb, b=b: e.dma_start(
            out=pb_s[b], in_=agout1.ap()[rowb:rowb + 128, :]),
          inc=f"pa{b}", inck=16)
        E("sync", lambda e, t=t, b=b: e.dma_start(
            out=xt[b], in_=x_in[t * 128:(t + 1) * 128, :]),
          inc=f"dx{b}", inck=16, waits=[(f"xf{b}", P.counts.get(f"xf{b}", 0))])
        vpa = V(f"pa{b}")
        vs = vop(lambda e, b=b: e.tensor_tensor(sAb[b], pa_s[b], pb_s[b],
                                                ALU.add),
                 waits=[(f"pa{b}", vpa), ("dx1", P.counts.get("dx1", 0))])
        E("vector", lambda e: e.nop(), inc=f"pf{b}", waits=[("vec", vs)])
        _b1v[t] = (ln_v_stats(sAb[b], vlnc[b]), V(f"dx{b}"))

    def emit_b1_part2(t):
        b = t % 2
        v0, vdx = _b1v[t]
        ln_v_finish(sAb[b], sB, vlnc[b], v0)
        vx1 = vop(lambda e, b=b: e.tensor_tensor(sAb[b], xt[b], sB, ALU.add),
                  waits=[(f"dx{b}", vdx)])
        E("vector", lambda e: e.nop(), inc=f"xf{b}", waits=[("vec", vx1)])
        E("sync", lambda e, t=t, b=b: e.dma_start(
            out=x1_d[t * 128:(t + 1) * 128, :], in_=sAb[b]),
          inc="dx1", inck=16,
          waits=[("vec", vx1), ("dx1", P.counts.get("dx1", 0))])
        _b1v[t] = ln_v_stats(sAb[b], vlnc[b])

    def emit_b1_part3(t):
        b = t % 2
        v0 = _b1v.pop(t)
        vh = ln_v_finish(sAb[b], hstg, vlnc[b], v0,
                         extra_w=[("dht", P.counts.get("dht", 0))])
        E("sync", lambda e, t=t: e.dma_start(
            out=h_d[t * 128:(t + 1) * 128, :], in_=hstg),
          inc="dht", inck=16,
          waits=[("vec", vh), ("dht", P.counts.get("dht", 0))])

    def emit_b1_tile(t):
        emit_b1_part1(t)
        emit_b1_part2(t)
        emit_b1_part3(t)

    def emit_router_trans(tc, half):
        need_dht = 4 * (tc + 1)
        for dc in range(4 * half, 4 * half + 4):
            E("sync", lambda e, tc=tc, dc=dc: e.dma_start_transpose(
                hT[:, dc * 512:(dc + 1) * 512],
                h_d[tc * 512:(tc + 1) * 512, dc * 128:(dc + 1) * 128]),
              inc="thx", inck=16,
              waits=([("dht", 16 * need_dht), ("tdc", V("tdc"))]
                     if dc == 4 * half else []))

    def emit_router_mm(tc):
        vth = V("thx")
        for dc in range(NDC):
            E("tensor", lambda e, dc=dc: e.matmul(
                psD[:, 512:1024], rw1_sb[:, dc * RH:(dc + 1) * RH],
                hT[:, dc * 512:(dc + 1) * 512],
                start=(dc == 0), stop=(dc == NDC - 1)),
              inc="tdc" if dc == NDC - 1 else None,
              waits=[("thx", vth), ("dwB", P.counts.get("dwB", 0)),
                     ("scc", V("scc")), ("rbu", P.counts.get("rbu", 0)),
                     ("woe", P.counts.get("woe", 0))] if dc == 0 else [])
        for hh in range(2):
            vsi = sop(lambda e, hh=hh: e.activation(
                r1T, psD[:, 512 + 256 * hh:768 + 256 * hh], AF.Sigmoid),
                      waits=[("tdc", V("tdc"))])
            vmu = vop(lambda e, hh=hh: e.tensor_tensor(
                r1T, r1T, psD[:, 512 + 256 * hh:768 + 256 * hh], ALU.mult),
                waits=[("scc", vsi)] + ([("vec", V("vec"))] if hh == 0 else []))
            E("tensor", lambda e, hh=hh: e.matmul(
                psD[0:NE, 512 + 256 * hh:768 + 256 * hh], rw2_sb, r1T,
                start=True, stop=True),
              inc="tdc", waits=[("vec", vmu)])
        vlg = V("tdc")
        vle = vop(lambda e: e.tensor_copy(fv(F_SB, 512, 0, NE), psD[0:NE, 512:1024]),
                  waits=[("tdc", vlg), ("dlg", P.counts.get("dlg", 0))])
        E("vector", lambda e: e.nop(), inc="rtl", waits=[("vec", vle)])
        E("sync", lambda e, tc=tc: e.dma_start(
            out=lgt_d[:, tc * 512:(tc + 1) * 512], in_=fv(F_SB, 512, 0, NE)),
          inc="dlg", inck=16, waits=[("vec", vle)])

    def emit_router_tc(tc):
        emit_router_trans(tc, 0)
        emit_router_trans(tc, 1)
        emit_router_mm(tc)

    E("gpsimd", lambda e: e.dma_start(
        out=rw1_sb.rearrange("p (c n) -> p c n", c=NDC),
        in_=rw1_in.rearrange("(c p) n -> p c n", p=128)),
      inc="dwB", inck=16, waits=[("tdc", a2_tdc)])
    E("gpsimd", lambda e: e.dma_start(out=rw2_sb, in_=rw2_in), inc="dwB", inck=16)

    inject = {}

    def add_inject(qb, p, kc, fn):
        inject.setdefault((qb, p, kc), []).append(fn)

    last_bcast_tdc = [0]

    def emit_chamber_step(qbp, h):
        j = h // 2
        pb_ = 64 * (h % 2)
        cbq = combT[qbp % 2]
        cvq = conv[qbp % 2]
        vb = E("tensor", lambda e, h=h: e.matmul(
            psD[0:64, 512:1024],
            oh8[:, h * 64:(h + 1) * 64],
            rby[:, 0:QB],
            start=True, stop=True),
          inc="tdc",
          waits=[("rch", P.counts.get("rch", 0)),
                 ("rbu", P.counts.get("rbu", 0)),
                 ("woe", P.counts.get("woe", 0)),
                 ("rtl", P.counts.get("rtl", 0))])
        last_bcast_tdc[0] = V("tdc")
        cm = fv(F_SB, 512, 0, 64)
        t1 = fv(F_SB + 512, 512, 0, 32)
        vcm = vop(lambda e, j=j, pb_=pb_, cvq=cvq: e.tensor_tensor(
            cm, cvq[pb_:pb_ + 64, j * QB:(j + 1) * QB],
            psD[0:64, 512:1024][:, 0:QB], ALU.mult),
            waits=[("tdc", vb), ("cve", NPH * (qbp + 1)),
                   ("dlg", P.counts.get("dlg", 0))])
        E("vector", lambda e: e.nop(), inc="rbu", waits=[("vec", vcm)])
        vop(lambda e, h=h: e.tensor_scalar(
            t1, cm[32:64, :], cb64[0:32, NH + h:NH + h + 1], None, ALU.mult),
            waits=[("vec", v_cb64)])
        vop(lambda e, h=h, j=j, pb_=pb_, cbq=cbq: e.scalar_tensor_tensor(
            cbq[pb_:pb_ + 32, j * QB:(j + 1) * QB],
            cm[0:32, :], cb64[0:32, h:h + 1], t1, ALU.mult, ALU.subtract))
        vop(lambda e, h=h: e.tensor_scalar(
            t1, cm[32:64, :], cb64[0:32, h:h + 1], None, ALU.mult))
        v4 = vop(lambda e, h=h, j=j, pb_=pb_, cbq=cbq: e.scalar_tensor_tensor(
            cbq[pb_ + 32:pb_ + 64, j * QB:(j + 1) * QB],
            cm[0:32, :], cb64[0:32, NH + h:NH + h + 1], t1, ALU.mult, ALU.add))
        E("vector", lambda e: e.nop(), inc="chd", waits=[("vec", v4)])

    def emit_wout_group(qbp, g):
        tt, ncol = g // 2, g % 2
        cbq = combT[qbp % 2]
        for dc in range(NMC):
            E("tensor", lambda e, tt=tt, ncol=ncol, dc=dc, cbq=cbq: e.matmul(
                psD[:, 512:1024],
                cbq[:, dc * QB + tt * 128:dc * QB + (tt + 1) * 128],
                wout_sb[:, dc * 1024 + ncol * 512:dc * 1024 + (ncol + 1) * 512],
                start=(dc == 0), stop=(dc == NMC - 1)),
              inc="tdc" if dc == NMC - 1 else None,
              waits=[("chd", P.counts.get("chd", 0)),
                     ("woe", P.counts.get("woe", 0)),
                     ("dwA", v_dwout),
                     ("rtl", P.counts.get("rtl", 0))] if dc == 0 else [])
        vwo = V("tdc")
        sbi = g % 2
        vev = E("vector", lambda e, sbi=sbi: e.tensor_copy(pstgb[sbi], psD[:, 512:1024]),
                inc="woe",
                waits=[("tdc", vwo), ("vec", V("vec")),
                       (f"ps{sbi}", P.counts.get(f"ps{sbi}", 0))])
        E("vector", lambda e: e.nop(), inc="vec", waits=[("woe", vev)])
        E("sync", lambda e, qbp=qbp, tt=tt, ncol=ncol, sbi=sbi: e.dma_start(
            out=agin1.ap()[qbp * QB + tt * 128:qbp * QB + (tt + 1) * 128,
                           ncol * 512:(ncol + 1) * 512],
            in_=pstgb[sbi]),
          inc=f"ps{sbi}", inck=16, waits=[("woe", vev)])

    def emit_ag1_rows(r0, nr, o0):
        vps = (V("ps0"), V("ps1"))
        E("gpsimd", lambda e, r0=r0, nr=nr, o0=o0: e.collective_compute(
            "AllGather", ALU.bypass,
            replica_groups=[[0, 1], [2, 3], [4, 5], [6, 7]],
            ins=[agin1.ap()[r0:r0 + nr, :].opt()],
            outs=[agout1.ap()[o0:o0 + 2 * nr, :].opt()]),
          inc="dcc", waits=[("ps0", vps[0]), ("ps1", vps[1])])

    def emit_ag1(c):
        emit_ag1_rows(c * AGC, AGC, c * 2 * AGC)

    def b1_src(t):
        TPC = AGC // 128
        if t < (NAG - 1) * TPC:
            cch = (t * 128) // AGC
            row = cch * 2 * AGC + (t * 128) % AGC
            return row, row + AGC, cch + 1
        SPL = AGC // 2
        off = t * 128 - (NAG - 1) * AGC
        half = off // SPL
        base = (NAG - 1) * 2 * AGC + half * AGC
        return base + off % SPL, base + SPL + off % SPL, NAG + half

    for qb in range(1, NQB):
        for h in range(NH):
            add_inject(qb, 1, 1 + (h * (NKC - 3)) // NH,
                       lambda qbp=qb - 1, h=h: emit_chamber_step(qbp, h))
        for g in range(2 * (QB // 128)):
            add_inject(qb, 2, 1 + (g * (NKC - 3)) // (2 * (QB // 128)),
                       lambda qbp=qb - 1, g=g: emit_wout_group(qbp, g))
    NSL = NKC - 4
    NS = 4 * NSL

    def slot_of(s):
        return (s // NSL, 2 + s % NSL)

    for c in range(NAG):
        qb_at = 2 * c + 2
        if qb_at < NQB:
            add_inject(qb_at, 3, 0, lambda c=c: emit_ag1(c))
        qb_b1 = 2 * c + 3
        if qb_b1 < NQB and not DISABLE_B1_INJECT:
            for i in range(8):
                t = c * 8 + i
                for k, fn in ((0, emit_b1_part1), (2, emit_b1_part2),
                              (4, emit_b1_part3)):
                    p_, kc_ = slot_of(6 * i + k)
                    add_inject(qb_b1, p_, kc_, lambda t=t, fn=fn: fn(t))
            for s, fn in ((min(26, NS - 1), lambda tc=2 * c: emit_router_trans(tc, 0)),
                          (min(28, NS - 1), lambda tc=2 * c: emit_router_trans(tc, 1)),
                          (min(34, NS - 1), lambda tc=2 * c: emit_router_mm(tc)),
                          (min(50, NS - 1), lambda tc=2 * c + 1: emit_router_trans(tc, 0)),
                          (min(54, NS - 1), lambda tc=2 * c + 1: emit_router_trans(tc, 1)),
                          (min(62, NS - 1), lambda tc=2 * c + 1: emit_router_mm(tc))):
                p_, kc_ = slot_of(s)
                add_inject(qb_b1, p_, kc_, fn)


    for qb in range(NQB):
        for p in range(NPH):
            def emit_S(kc):
                buf = kc % 2
                w0 = [(f"exp{buf}", P.counts.get(f"exp{buf}", 0))]
                if qb == 0 and p == 0 and kc < 2:
                    w0 += [("tdc", a2_tdc), ("vec", a2_vec), ("scc", a2_scc)]
                E("tensor", lambda e, p=p, kc=kc, qb=qb, buf=buf: e.matmul(
                    stp[buf][:, 0:512],
                    outerT[0:64, p * T + kc * 128:p * T + (kc + 1) * 128],
                    innerT[0:64, p * T + qb * QB:p * T + (qb + 1) * QB],
                    start=True, stop=True),
                  waits=w0)
                E("tensor", lambda e, p=p, kc=kc, qb=qb, buf=buf: e.matmul(
                    stp[buf][:, 512:1024],
                    outerT[64:128, p * T + kc * 128:p * T + (kc + 1) * 128],
                    innerT[64:128, p * T + qb * QB:p * T + (qb + 1) * QB],
                    start=True, stop=True),
                  inc="tdc")
                vst = V("tdc")
                E("scalar", lambda e, buf=buf: e.activation(
                    Ut[buf], stp[buf][:], AF.Exp, scale=0.125),
                  inc=f"exp{buf}",
                  waits=[("tdc", vst),
                         (f"pvd{buf}", P.counts.get(f"pvd{buf}", 0))])

            def emit_PVX(kc):
                buf = kc % 2
                vexp = V(f"exp{buf}")
                w1 = [(f"exp{buf}", vexp)]
                if kc == 0:
                    w1.append(("cve", P.counts.get("cve", 0)))
                E("tensor", lambda e, p=p, kc=kc, buf=buf: e.matmul(
                    psC[0:64, 0:512],
                    vx[:, kc * VX * NH + (2 * p) * VX:kc * VX * NH + (2 * p) * VX + DH],
                    Ut[buf][:, 0:512],
                    start=(kc == 0), stop=(kc == NKC - 1), tile_position=(0, 0)),
                  waits=w1)
                E("tensor", lambda e, p=p, kc=kc, buf=buf: e.matmul(
                    psC[64:128, 0:512],
                    vx[:, kc * VX * NH + (2 * p + 1) * VX:kc * VX * NH + (2 * p + 1) * VX + DH],
                    Ut[buf][:, 512:1024],
                    start=(kc == 0), stop=(kc == NKC - 1), tile_position=(0, 64),
                    skip_group_check=True))
                exm = 2 if kc == NKC - 1 else 64
                E("tensor", lambda e, p=p, kc=kc, buf=buf, exm=exm: e.matmul(
                    psC[0:exm, 512:1024],
                    vx[:, kc * VX * NH + (2 * p) * VX + DH:kc * VX * NH + (2 * p) * VX + DH + exm],
                    Ut[buf][:, 0:512],
                    start=(kc == 0), stop=(kc == NKC - 1), tile_position=(0, 0),
                    skip_group_check=True),
                  waits=[("exe", P.counts.get("exe", 0))] if kc == 0 else [])
                E("tensor", lambda e, p=p, kc=kc, buf=buf, exm=exm: e.matmul(
                    psD[64:64 + exm, 0:512],
                    vx[:, kc * VX * NH + (2 * p + 1) * VX + DH:kc * VX * NH + (2 * p + 1) * VX + DH + exm],
                    Ut[buf][:, 512:1024],
                    start=(kc == 0), stop=(kc == NKC - 1), tile_position=(0, 64),
                    skip_group_check=True),
                  inc=f"pvd{buf}")

            def do_inject(kc):
                for fn in inject.pop((qb, p, kc), []):
                    fn()

            do_inject(-1)
            if qb == 0 and p == 0:
                emit_S(0)
                emit_S(1)
            for kc in range(NKC):
                emit_PVX(kc)
                if kc + 2 < NKC:
                    emit_S(kc + 2)
                do_inject(kc)
            def prologue_kk(kk, nqb, np_):
                buf = kk % 2
                E("tensor", lambda e, p2=np_, kc=kk, qb2=nqb, buf=buf: e.matmul(
                    stp[buf][:, 0:512],
                    outerT[0:64, p2 * T + kc * 128:p2 * T + (kc + 1) * 128],
                    innerT[0:64, p2 * T + qb2 * QB:p2 * T + (qb2 + 1) * QB],
                    start=True, stop=True),
                  waits=[(f"exp{buf}", P.counts.get(f"exp{buf}", 0))])
                E("tensor", lambda e, p2=np_, kc=kk, qb2=nqb, buf=buf: e.matmul(
                    stp[buf][:, 512:1024],
                    outerT[64:128, p2 * T + kc * 128:p2 * T + (kc + 1) * 128],
                    innerT[64:128, p2 * T + qb2 * QB:p2 * T + (qb2 + 1) * QB],
                    start=True, stop=True),
                  inc="tdc")
                vst = V("tdc")
                E("scalar", lambda e, buf=buf: e.activation(
                    Ut[buf], stp[buf][:], AF.Exp, scale=0.125),
                  inc=f"exp{buf}",
                  waits=[("tdc", vst),
                         (f"pvd{buf}", P.counts.get(f"pvd{buf}", 0))])

            nqb, np_ = (qb, p + 1) if p + 1 < NPH else (qb + 1, 0)
            if nqb < NQB:
                prologue_kk(0, nqb, np_)
            vpv = V(f"pvd{(NKC - 1) % 2}")
            cw = [(f"pvd{(NKC - 1) % 2}", vpv)]
            if qb >= 2:
                cw.append(("rbu", NH * (qb - 1)))
            E("scalar", lambda e, p=p, qb=qb: e.activation(
                conv[qb % 2][:, p * QB:(p + 1) * QB], psC[0:128, 0:QB],
                AF.Identity),
              inc="cve", waits=cw)
            if nqb < NQB:
                prologue_kk(1, nqb, np_)
            E("scalar", lambda e: e.activation(exstA, psC[0:2, 512:512 + QB],
                                               AF.Identity),
              inc="exe", waits=[("dnzA", P.counts.get("dnzA", 0))])
            vexA = V("exe")
            E("scalar", lambda e: e.activation(exstB, psD[64:66, 0:QB],
                                               AF.Identity),
              inc="exe", waits=[("dnzB", P.counts.get("dnzB", 0))])
            vexB = V("exe")
            E("sync", lambda e, p=p: e.dma_start(out=_nzrow(AFR, F_T2, 2 * p, QB), in_=exstA[0:1, :]),
              inc="dnzA", inck=16,
              waits=[("exe", vexA), ("rch", P.counts.get("rch", 0))])
            E("sync", lambda e, p=p: e.dma_start(out=_nzrow(AFR, F_T3, 2 * p, QB), in_=exstA[1:2, :]),
              inc="dnzA", inck=16)
            E("sync", lambda e, p=p: e.dma_start(out=_nzrow(AFR, F_T2, 2 * p + 1, QB), in_=exstB[0:1, :]),
              inc="dnzB", inck=16,
              waits=[("exe", vexB), ("rch", P.counts.get("rch", 0))])
            E("sync", lambda e, p=p: e.dma_start(out=_nzrow(AFR, F_T3, 2 * p + 1, QB), in_=exstB[1:2, :]),
              inc="dnzB", inck=16)
        vnz_allA = V("dnzA")
        vnz_allB = V("dnzB")
        E("sync", lambda e, qb=qb: e.dma_start(
            out=aq, in_=a_d[:, qb * QB:(qb + 1) * QB]),
          inc="daq", inck=16, waits=[("das", a2_das), ("vec", V("vec"))])
        vdaq = V("daq")
        vop(lambda e: e.tensor_mul(rby, nzn, aq),
            waits=[("dnzA", vnz_allA), ("dnzB", vnz_allB), ("daq", vdaq),
                   ("tdc", last_bcast_tdc[0])])
        vop(lambda e: e.scalar_tensor_tensor(rby, nzz, 1e-8, rby, ALU.mult, ALU.add))
        vop(lambda e: e.reciprocal(rby, rby))
        vop(lambda e: e.tensor_mul(rby, rby, aq))
        E("vector", lambda e: e.nop(), inc="rch", waits=[("vec", V("vec"))])

    assert not inject, f"unfired injections: {list(inject.keys())}"
    SPL = AGC // 2
    emit_ag1_rows((NAG - 1) * AGC, SPL, (NAG - 1) * 2 * AGC)
    for h in range(NH):
        emit_chamber_step(NQB - 1, h)
    for g in range(2 * (QB // 128)):
        emit_wout_group(NQB - 1, g)
    for i in range(4):
        emit_b1_tile((NAG - 1) * 8 + i)
    emit_ag1_rows((NAG - 1) * AGC + SPL, SPL, (NAG - 1) * 2 * AGC + AGC)
    emit_router_tc(2 * (NAG - 1))
    for i in range(4, 8):
        emit_b1_tile((NAG - 1) * 8 + i)
    emit_router_tc(2 * (NAG - 1) + 1)
    b1_dht = V("dht")
    b1_dx1 = V("dx1")
    b_dlg = V("dlg")

    pbl = fv(F_T1, CPS)
    pbx = fv(F_T2, CPS)
    c1 = fv(F_T3, CPS)
    c2 = fv(F_T4, CPS)
    c3 = fv(F_T5, CPS)
    c4 = fv(F_T1 + 512, CPS)
    c5 = fv(F_T2 + 512, CPS)
    c6 = fv(F_T3 + 512, CPS)
    NCS = int(math.log2(CPS))
    locol = fv(F_COLS + 20, 1)
    hicol = fv(F_COLS + 21, 1)
    micol = fv(F_COLS + 22, 1)
    lo16 = g16[:, 0:1]
    hi16 = g16[:, 1:2]
    mi16 = g16[:, 2:3]
    ct16 = g16[:, 3:4]
    gg16 = g16[:, 4:5]
    t16a = g16[:, 5:6]
    t16b = g16[:, 6:7]
    v16 = g16[:, 7:8]
    s16 = g16[:, 8:9]
    r16 = g16[:, 9:10]
    d16 = g16[:, 10:11]

    E("sync", lambda e: e.dma_start(
        out=pbl, in_=lgt_d.rearrange("e (s c) -> (e s) c", s=SUB)),
      inc="dpb", inck=16, waits=[("dlg", b_dlg)])
    v_dpb = V("dpb")

    def gsum(src_col, dst16, extra=()):
        vt = E("tensor", lambda e: e.matmul(psB[0:NE, 512:513], sel16[:], src_col,
                                            start=True, stop=True),
               inc="tdc", waits=list(extra) + [("vec", V("vec")), ("g16u", P.counts.get("g16u", 0))])
        return vop(lambda e: e.tensor_copy(dst16, psB[0:NE, 512:513]),
                   waits=[("tdc", vt)])

    def bcast16(src16, dstcol, extra=()):
        vt = E("tensor", lambda e: e.matmul(psB[:, 513:514], bc16[:], src16,
                                            start=True, stop=True),
               inc="tdc", waits=list(extra) + [("vec", V("vec"))])
        return vop(lambda e: e.tensor_copy(dstcol, psB[:, 513:514]),
                   waits=[("tdc", vt)])

    vop(lambda e: e.tensor_reduce(mxc, pbl, mybir.AxisListType.X, ALU.max),
        waits=[("dpb", v_dpb)])
    vq = V("vec")
    E("sync", lambda e: e.dma_start(out=gmx_d, in_=mxc), inc="dgm", inck=16,
      waits=[("vec", vq)])
    E("sync", lambda e: e.dma_start(
        out=g16[:, 11:11 + SUB], in_=gmx_d.rearrange("(e s) o -> e (s o)", s=SUB)),
      inc="dgm", inck=16, waits=[("dgm", V("dgm"))])
    v_dgm = V("dgm")
    vop(lambda e: e.tensor_reduce(t16a, g16[:, 11:11 + SUB], mybir.AxisListType.X,
                                  ALU.max), waits=[("dgm", v_dgm)])
    bcast16(t16a, mxc)
    vop(lambda e: e.tensor_scalar(mxc, mxc, -1.0, None, ALU.mult))
    vexp0 = V("vec")
    sxp = sop(lambda e: e.activation(pbx, pbl, AF.Exp, bias=mxc, accum_out=scol),
              waits=[("vec", vexp0)])
    E("vector", lambda e: e.nop(), inc="vec", waits=[("scc", sxp)])
    gsum(scol, t16b)
    bcast16(t16b, dcol)
    vop(lambda e: e.reciprocal(dcol, dcol))
    vpr = vop(lambda e: e.tensor_scalar(pbx, pbx, dcol, None, ALU.mult))

    loc = fv(F_COLS + 30, 1)
    hic = fv(F_COLS + 31, 1)
    mic = fv(F_COLS + 32, 1)
    ggc = fv(F_COLS + 33, 1)
    tac = fv(F_COLS + 34, 1)
    E("gpsimd", lambda e: e.memset(loc, 0.0), inc="gpc", waits=[("vec", vpr)])
    E("gpsimd", lambda e: e.memset(hic, 1.0), inc="gpc", waits=[("gpc", V("gpc"))])
    vg0 = V("gpc")
    vop(lambda e: e.nop(), waits=[("gpc", vg0)])
    for it in range(BIS):
        vop(lambda e: e.tensor_tensor(mic, loc, hic, ALU.add))
        vop(lambda e: e.tensor_scalar(mic, mic, 0.5, None, ALU.mult))
        vcm_ = vop(lambda e: e.tensor_scalar(c1, pbx, mic, 0.0, ALU.is_gt, ALU.add,
                                             accum_out=scol))
        vmm_ = E("tensor", lambda e: e.matmul(psB[:, 512:513], g128[:], scol,
                                              start=True, stop=True),
                 inc="tdc", waits=[("vec", vcm_), ("g16u", P.counts.get("g16u", 0))])
        vop(lambda e: e.tensor_scalar(ggc, psB[:, 512:513], float(CAP) - 0.5, None,
                                      ALU.is_gt),
            waits=[("tdc", vmm_)])
        E("vector", lambda e: e.nop(), inc="g16u", waits=[("vec", V("vec"))])
        vop(lambda e: e.tensor_tensor(tac, mic, loc, ALU.subtract))
        vop(lambda e: e.tensor_mul(tac, tac, ggc))
        vop(lambda e: e.tensor_tensor(loc, loc, tac, ALU.add))
        vop(lambda e: e.tensor_tensor(tac, hic, mic, ALU.subtract))
        vop(lambda e: e.tensor_mul(tac, tac, ggc))
        vlast = vop(lambda e: e.tensor_tensor(hic, mic, tac, ALU.add))
    hicol = hic
    vop(lambda e: e.tensor_scalar(c1, pbx, hicol, None, ALU.is_gt))
    vop(lambda e: e.scalar_tensor_tensor(c1, c1, -9.0, pbx, ALU.mult, ALU.add))
    vop(lambda e: e.tensor_reduce(mxc, c1, mybir.AxisListType.X, ALU.max))
    vq2 = V("vec")
    E("sync", lambda e: e.dma_start(out=gmx_d, in_=mxc), inc="dgm", inck=16,
      waits=[("vec", vq2)])
    E("sync", lambda e: e.dma_start(
        out=g16[:, 11:11 + SUB], in_=gmx_d.rearrange("(e s) o -> e (s o)", s=SUB)),
      inc="dgm", inck=16, waits=[("dgm", V("dgm"))])
    v_dgm2 = V("dgm")
    vop(lambda e: e.tensor_reduce(v16, g16[:, 11:11 + SUB], mybir.AxisListType.X,
                                  ALU.max), waits=[("dgm", v_dgm2)])
    bcast16(v16, vcol2)
    vop(lambda e: e.tensor_scalar(c1, pbx, vcol2, 0.0, ALU.is_gt, ALU.add,
                                  accum_out=scol))
    gsum(scol, s16)
    vop(lambda e: e.tensor_scalar(c2, pbx, vcol2, None, ALU.is_equal))
    vop(lambda e: e.tensor_scalar(r16, s16, -1.0, float(CAP), ALU.mult, ALU.add))
    bcast16(r16, rcol)

    def cumsum_emit(src, tmp, prefix_col):
        cur, oth = src, tmp
        for k in range(NCS):
            s = 1 << k
            vop(lambda e, cur=cur, oth=oth, s=s: e.tensor_copy(oth[:, 0:s], cur[:, 0:s]))
            vop(lambda e, cur=cur, oth=oth, s=s: e.tensor_tensor(
                oth[:, s:CPS], cur[:, s:CPS], cur[:, 0:CPS - s], ALU.add))
            cur, oth = oth, cur
        vt = E("tensor", lambda e, cur=cur: e.matmul(
            psB[:, 514:515], l8[:], cur[:, CPS - 1:CPS], start=True, stop=True),
          inc="tdc", waits=[("vec", V("vec"))])
        vop(lambda e: e.tensor_copy(prefix_col, psB[:, 514:515]),
            waits=[("tdc", vt)])
        return cur

    vop(lambda e: e.tensor_copy(c3, c2))
    cq = cumsum_emit(c3, c4, pfxc)
    vop(lambda e, cq=cq: e.tensor_tensor(c5, cq, c2, ALU.subtract))
    vop(lambda e: e.tensor_scalar(c5, c5, pfxc, None, ALU.add))
    vop(lambda e: e.tensor_scalar(c5, c5, rcol, None, ALU.is_lt))
    vop(lambda e: e.tensor_mul(c5, c5, c2))
    vop(lambda e: e.tensor_tensor(c1, c1, c5, ALU.add))
    vop(lambda e: e.scalar_tensor_tensor(c2, c1, 1.0, pbx, ALU.mult, ALU.mult,
                                         accum_out=scol))
    gsum(scol, d16)
    vop(lambda e: e.tensor_scalar(d16, d16, 1e-9, None, ALU.add))
    bcast16(d16, dcol)
    vop(lambda e: e.reciprocal(dcol, dcol))
    vop(lambda e: e.tensor_scalar(c2, c2, dcol, None, ALU.mult))
    vop(lambda e: e.tensor_copy(c4, c1))
    cq2 = cumsum_emit(c4, c6, pfxc)
    vop(lambda e, cq2=cq2: e.tensor_tensor(c3, cq2, c1, ALU.subtract))
    vsel = vop(lambda e: e.tensor_scalar(c3, c3, pfxc, None, ALU.add))
    for k, src in ((0, c1), (1, c3), (2, c2)):
        E("sync", lambda e, k=k, src=src: e.dma_start(
            out=selcw_d[k].rearrange("e (s c) -> (e s) c", s=SUB), in_=src),
          inc="dsw", inck=16, waits=[("vec", vsel)])
    v_dsw = V("dsw")
    vmf = vop(lambda e: e.tensor_copy(myexpf[:], myexp_sb[:]),
              waits=[("dsm", v_dsm)])
    vmx = E("tensor", lambda e: e.matmul(psC[0:64, 64:65], rep8[:], myexpf[:],
                                         start=True, stop=True),
            inc="tdc", waits=[("vec", vmf)])
    vop(lambda e: e.tensor_copy(idx64[:], psC[0:64, 64:65]), waits=[("tdc", vmx)])
    vop(lambda e: e.tensor_scalar(idx64[:], idx64[:], float(SUB), None, ALU.mult))
    vop(lambda e: e.tensor_tensor(idx64[:], idx64[:], sub8[:], ALU.add))
    vi64 = None
    for ki in range(3):
        vop(lambda e, ki=ki: e.tensor_copy(idx64i[:, ki:ki + 1], idx64[:]))
        vi64 = vop(lambda e, ki=ki: e.tensor_scalar(
            idx64i[:, ki:ki + 1], idx64i[:, ki:ki + 1], ki * NE * SUB, None, ALU.add))
    c6z = fv(F_T3 + 512, 512)
    E("gpsimd", lambda e: e.memset(c6z, 0.0), inc="gpc",
      waits=[("dsw", v_dsw), ("gpc", V("gpc"))])
    vz = V("gpc")
    for ft in range(NFT):
        E("sync", lambda e, ft=ft: e.dma_start(
            out=acc_d[ft * 128:(ft + 1) * 128, 0:512], in_=c6z),
          inc="dza", inck=16, waits=[("gpc", vz)])
        E("sync", lambda e, ft=ft: e.dma_start(
            out=acc_d[ft * 128:(ft + 1) * 128, 512:1024], in_=c6z),
          inc="dza", inck=16)
    v_dza = V("dza")
    minepb = fv(F_T1, 3 * CPS, 0, 64)
    NCB = CPS // 128
    for ki in range(3):
        E("gpsimd", lambda e, ki=ki: e.indirect_dma_start(
            out=minepb[:, ki * CPS:(ki + 1) * CPS], out_offset=None,
            in_=selcw_d.rearrange("k e (s c) -> (k e s) c", s=SUB),
            in_offset=IOX(idx64i[:, ki:ki + 1], 0)),
          inc="dtm", inck=16,
          waits=[("vec", vi64), ("dsw", v_dsw), ("dza", v_dza)])
    v_dtm = V("dtm")
    for ki, dst in ((0, selTm), (1, ceTm), (2, wTm)):
        for cb in range(NCB):
            vt = E("tensor", lambda e, ki=ki, cb=cb: e.transpose(
                psC[:, 0:64],
                minepb[:, ki * CPS + cb * 128:ki * CPS + (cb + 1) * 128],
                idn[0:64, 0:64]),
              inc="tdc",
              waits=[("dtm", v_dtm), ("vec", V("vec")), ("dsm", v_dsm)])
            vop(lambda e, dst=dst, cb=cb: e.tensor_copy(
                dst[:, cb * 64:(cb + 1) * 64], psC[:, 0:64]),
                waits=[("tdc", vt)])
    v_tok = V("vec")

    if debug:
        E("sync", lambda e: e.dma_start(out=dbg["dbg_probs"], in_=pbx),
          inc="dbgs", inck=16, waits=[("vec", vsel)])
        E("sync", lambda e: e.dma_start(out=dbg["dbg_sel"], in_=c1),
          inc="dbgs", inck=16)

    xe = bv(B_XE, CAPH * 1024)
    xeT = [bv(B_XET, NDC * CAP), bv(B_XET + NDC * CAP, NDC * CAP)]
    w1q = [bv(B_W1, NDC * 512), bv(B_W1 + NDC * 512, NDC * 512)]
    w2q = [bv(B_W2, NDC * 512), bv(B_W2 + NDC * 512, NDC * 512)]
    w3q = [bv(B_W3, 4 * 1024), bv(B_W3 + 4 * 1024, 4 * 1024)]
    gv = bv(B_GV, CAP)
    slotb = fv(F_SLOT, 1032)
    gs_f = fv(F_SB, CAP)

    E("gpsimd", lambda e: e.memset(slotb[:, 1024:1032], 0.0), inc="gpc",
      waits=[("gpc", V("gpc"))])
    v_padz = V("gpc")

    _vxt_of = {}
    _pselb2 = [pselb[:], fv(F_SA, CAP)]
    _ps_war = [0, 0]
    exsb = fv(F_SB + 512, CAP, 0, 2)

    def emit_prep(j):
        exw = [("vec", v_tok)]
        vop(lambda e: e.tensor_copy(
            rhsa[:].rearrange("p (t two) -> p t two", two=2)[:, :, 0:1],
            tokc[:].rearrange("p (t one) -> p t one", one=1)),
            waits=exw + [("exr", P.counts.get("exr", 0))])
        for cb in range(NCB):
            vop(lambda e, cb=cb, j=j: e.tensor_copy(
                rhsa[:].rearrange("p (s ncb two) -> p s ncb two",
                                  ncb=NCB, two=2)[:, :, cb, 1:2],
                wTm[:, cb * 64 + SUB * j:cb * 64 + SUB * j + SUB].rearrange(
                    "p (s one) -> p s one", one=1)))
        vrh = V("vec")
        for tc in range(NTC):
            col = (tc % NCB) * 64 + SUB * j + tc // NCB
            pb = _pselb2[tc % 2]
            vps = vop(lambda e, col=col, pb=pb: e.tensor_scalar(
                pb, iotas[:], ceTm[:, col:col + 1],
                selTm[:, col:col + 1], ALU.is_equal, ALU.mult),
                waits=[("tdc", _ps_war[tc % 2]),
                       ("dx1", P.counts.get("dx1", 0))])
            E("tensor", lambda e, tc=tc, pb=pb: e.matmul(
                psB[0:2, 768:768 + CAP],
                rhsa[:, 2 * tc:2 * tc + 2],
                pb,
                start=(tc == 0), stop=(tc == NTC - 1)),
              inc="tdc",
              waits=[("vec", vps)] + ([("vec", vrh),
                     ("exu", P.counts.get("exu", 0))] if tc == 0 else []))
            _ps_war[tc % 2] = V("tdc")
        vexd = V("tdc")
        vio0 = vop(lambda e: e.tensor_copy(exsb, psB[0:2, 768:768 + CAP]),
                   waits=[("tdc", vexd)])
        for hc in range(CAPH):
            E("tensor", lambda e, hc=hc: e.matmul(
                psB[:, 770 + 2 * hc:772 + 2 * hc],
                exsb[:, hc * 128:(hc + 1) * 128],
                idn[0:2, 0:2],
                start=True, stop=True, skip_group_check=True),
              inc="tdc", waits=[("vec", vio0)] if hc == 0 else [])
        vtx = V("tdc")
        vio = vop(lambda e, j=j: e.tensor_copy(
            idxw[:, j * 2 * CAPH:j * 2 * CAPH + 2], psB[:, 770:772]),
            waits=[("tdc", vtx)])
        if CAPH > 1:
            vio = vop(lambda e, j=j: e.tensor_copy(
                idxw[:, j * 2 * CAPH + 2:j * 2 * CAPH + 4], psB[:, 772:774]))
        E("vector", lambda e: e.nop(), inc="exu", waits=[("vec", vio)])
        E("vector", lambda e: e.nop(), inc="exr", waits=[("vec", vio)])
        vii = vop(lambda e, j=j: e.tensor_copy(
            idxi[:, j * CAPH:(j + 1) * CAPH],
            idxw[:, j * 2 * CAPH:(j + 1) * 2 * CAPH].rearrange(
                "p (h w) -> p h w", h=CAPH)[:, :, 0:1]))
        E("vector", lambda e: e.nop(), inc=f"exj{j}", waits=[("vec", vii)])
        for hc in range(CAPH):
            E("gpsimd", lambda e, j=j, hc=hc: e.indirect_dma_start(
                out=xe[:, hc * 1024:(hc + 1) * 1024], out_offset=None,
                in_=h_d, in_offset=IOX(idxi[:, j * CAPH + hc:j * CAPH + hc + 1], 0)),
              inc="dga", inck=16,
              waits=[(f"exj{j}", 1), ("dht", b1_dht),
                     ("dxt0", P.counts.get("dxt0", 0)),
                     ("dxt1", P.counts.get("dxt1", 0)),
                     ("dbgs", P.counts.get("dbgs", 0))] if hc == 0 else [])
        vga = V("dga")
        xb = j % 2
        for hc in range(CAPH):
            for dc in range(NDC):
                E("sync", lambda e, hc=hc, dc=dc, xb=xb: e.dma_start_transpose(
                    xeT[xb][:, dc * CAP + hc * 128:dc * CAP + (hc + 1) * 128],
                    xe[:, hc * 1024 + dc * 128:hc * 1024 + (dc + 1) * 128]),
                  inc=f"dxt{xb}", inck=16,
                  waits=[("dga", vga), (f"xtu{xb}", P.counts.get(f"xtu{xb}", 0))]
                  if (hc == 0 and dc == 0) else [])
        _vxt_of[j] = V(f"dxt{xb}")
        if debug and j == 0:
            E("sync", lambda e: e.dma_start(out=dbg["dbg_xe"][:, 0:CAPH * 1024],
                                            in_=xe[:, 0:CAPH * 1024]),
              inc="dbgs", inck=16, waits=[("dga", vga)])

    def emit_ffn(j):
        xb = j % 2
        vxt = _vxt_of[j]
        for q4 in range(NQ4):
            wb = q4 % 2
            E("gpsimd", lambda e, j=j, q4=q4, wb=wb: e.dma_start(
            out=w1q[wb].rearrange("p (c n) -> p c n", c=NDC), in_=w1_in[j][:, q4 * 512:(q4 + 1) * 512]
                .rearrange("(c p) n -> p c n", p=128)),
              inc=f"dw1{wb}", inck=16,
              waits=[(f"wu{wb}", P.counts.get(f"wu{wb}", 0))])
            E("gpsimd", lambda e, j=j, q4=q4, wb=wb: e.dma_start(
                out=w2q[wb].rearrange("p (c n) -> p c n", c=NDC), in_=w2_in[j][:, q4 * 512:(q4 + 1) * 512]
                .rearrange("(c p) n -> p c n", p=128)),
              inc=f"dw1{wb}", inck=16)
            E("gpsimd", lambda e, j=j, q4=q4, wb=wb: e.dma_start(
                out=w3q[wb].rearrange("p (c n) -> p c n", c=4), in_=w3_in[j][q4 * 512:(q4 + 1) * 512, :]
                .rearrange("(c p) n -> p c n", p=128)),
              inc=f"dw1{wb}", inck=16)
            for _ in range(2):
                if SCAT_PEND:
                    SCAT_PEND.pop(0)()
            vw = V(f"dw1{wb}")
            for f4 in range(4):
                fc = q4 * 4 + f4
                for dc in range(NDC):
                    E("tensor", lambda e, wb=wb, f4=f4, dc=dc, xb=xb: e.matmul(
                        psC[:, 0:CAP],
                        w1q[wb][:, dc * 512 + f4 * 128:dc * 512 + (f4 + 1) * 128],
                        xeT[xb][:, dc * CAP:(dc + 1) * CAP],
                        start=(dc == 0), stop=(dc == NDC - 1)),
                      inc="tdc" if dc == NDC - 1 else None,
                      waits=[(f"dw1{wb}", vw), (f"dxt{xb}", vxt),
                             ("scc", V("scc"))] if dc == 0 else [])
                for dc in range(NDC):
                    E("tensor", lambda e, wb=wb, f4=f4, dc=dc, xb=xb: e.matmul(
                        psC[:, 512:512 + CAP],
                        w2q[wb][:, dc * 512 + f4 * 128:dc * 512 + (f4 + 1) * 128],
                        xeT[xb][:, dc * CAP:(dc + 1) * CAP],
                        start=(dc == 0), stop=(dc == NDC - 1)),
                      inc="tdc" if dc == NDC - 1 else None,
                      waits=[("vec", V("vec"))] if dc == 0 else [])
                vgv = V("tdc")
                ssl = sop(lambda e: e.activation(gs_f, psC[:, 0:CAP], AF.Silu),
                          waits=[("tdc", vgv), ("vec", V("vec"))])
                vgm = vop(lambda e: e.tensor_tensor(gv, gs_f, psC[:, 512:512 + CAP],
                                                    ALU.mult), waits=[("scc", ssl)])
                if debug and j == 0 and fc == 0:
                    E("sync", lambda e: e.dma_start(out=dbg["dbg_gv"], in_=gv),
                      inc="dbgs", inck=16, waits=[("vec", vgm)])
                    vgm = vop(lambda e: e.nop(),
                              waits=[("dbgs", P.counts.get("dbgs", 0))])
                for sc2 in range(CAPH):
                    ps_o = psD if sc2 == 0 else psA
                    for nc2 in range(2):
                        E("tensor", lambda e, sc2=sc2, nc2=nc2, f4=f4, wb=wb, ps_o=ps_o, fc=fc: e.matmul(
                            ps_o[:, nc2 * 512:(nc2 + 1) * 512],
                            gv[:, sc2 * 128:(sc2 + 1) * 128],
                            w3q[wb][:, f4 * 1024 + nc2 * 512:f4 * 1024 + (nc2 + 1) * 512],
                            start=(fc == 0), stop=(fc == NFC - 1)),
                          inc="tdc" if (sc2 == CAPH - 1 and nc2 == 1) else None,
                          waits=[("vec", vgm), ("oeu", P.counts.get("oeu", 0))]
                          if (sc2 == 0 and nc2 == 0 and fc == 0) else
                          ([("vec", vgm)] if (sc2 == 0 and nc2 == 0) else []))
            E("tensor", lambda e: e.nop(), inc=f"wu{wb}",
              waits=[("tdc", V("tdc"))])
        E("tensor", lambda e: e.nop(), inc=f"xtu{xb}",
          waits=[("tdc", V("tdc"))])
        vffn = V("tdc")
        for hc in range(CAPH):
            ps_o = psD if hc == 0 else psA
            vsl = vop(lambda e, j=j, hc=hc, ps_o=ps_o: e.tensor_scalar(
                slotb[:, 0:1024], ps_o[:, 0:1024],
                idxw[:, j * 2 * CAPH + hc * 2 + 1:j * 2 * CAPH + hc * 2 + 2],
                None, ALU.mult),
                waits=[("tdc", vffn), ("gpc", v_padz),
                       ("dx1", P.counts.get("dx1", 0)),
                       ("dsl", P.counts.get("dsl", 0))])
            vsl2 = vop(lambda e, j=j, hc=hc: e.tensor_copy(
                slotb[:, 1024:1025], idxw[:, j * 2 * CAPH + hc * 2:j * 2 * CAPH + hc * 2 + 1]))
            E("vector", lambda e: e.nop(), inc="oeu", waits=[("vec", vsl2)])
            E("sync", lambda e, j=j, hc=hc: e.dma_start(
                out=agin2.ap()[j * CAP + hc * 128:j * CAP + (hc + 1) * 128, :],
                in_=slotb),
              inc="dsl", inck=16, waits=[("vec", vsl2)])
    NC2 = EM // 2

    CPC = 2 * CAP

    def emit_ag2(c):
        vsl = V("dsl")
        E("gpsimd", lambda e, c=c: e.collective_compute(
            "AllGather", ALU.bypass,
            replica_groups=[[0, 1], [2, 3], [4, 5], [6, 7]],
            ins=[agin2.ap()[c * CPC:(c + 1) * CPC, :].opt()],
            outs=[agout2.ap()[c * 2 * CPC:(c + 1) * 2 * CPC, :].opt()]),
          inc="dcc", waits=[("dsl", vsl)])

    scb = [fv(F_T1, 1032), fv(F_T1 + 1032, 1032)]
    idsc = idxi[:, EM * CAPH:EM * CAPH + 1]
    _scat_war = [0, 0]

    def emit_scatter_group(c, gg):
        ngg = 2 * CPC // 128
        g = c * ngg + gg
        b = g % 2
        E("sync", lambda e, g=g, b=b: e.dma_start(
            out=scb[b], in_=agout2.ap()[g * 128:(g + 1) * 128, :]),
          inc=f"dsc{b}", inck=16,
          waits=[("dcc", NAG + 2 + c), ("dscat", _scat_war[b]),
                 ("dbgs", P.counts.get("dbgs", 0))])
        vld = V(f"dsc{b}")
        vop(lambda e, b=b: e.tensor_tensor(idf, scb[b][:, 1024:1025], tkbase[:],
                                           ALU.subtract),
            waits=[(f"dsc{b}", vld), ("dscat", P.counts.get("dscat", 0))])
        vop(lambda e: e.tensor_scalar(idm, idf, 0.0, None, ALU.is_lt))
        vop(lambda e: e.scalar_tensor_tensor(idf, idm, 1e6, idf, ALU.mult, ALU.add))
        vic = vop(lambda e: e.tensor_copy(idsc, idf))
        E("gpsimd", lambda e, b=b: e.indirect_dma_start(
            out=acc_d, out_offset=IOX(idsc, 0),
            in_=scb[b][:, 0:1024], in_offset=None,
            compute_op=ALU.add, bounds_check=TH - 1, oob_is_err=False),
          inc="dscat", inck=16,
          waits=[("vec", vic), ("dza", v_dza),
                 ("dscat", P.counts.get("dscat", 0))])
        _scat_war[b] = V("dscat")

    SCAT_PEND = []

    def emit_scatter_chunk(c):
        for gg in range(2 * CPC // 128):
            emit_scatter_group(c, gg)

    SCAT_INTERLEAVE = False
    for j in range(EM):
        emit_prep(j)
        if SCAT_INTERLEAVE and j >= 4 and j % 2 == 0:
            c = j // 2 - 2
            SCAT_PEND.extend(
                [lambda c=c, gg=gg: emit_scatter_group(c, gg)
                 for gg in range(2 * CPC // 128)])
        if j >= 1:
            emit_ffn(j - 1)
        if j >= 2 and j % 2 == 0:
            emit_ag2(j // 2 - 1)
        if not SCAT_INTERLEAVE and j >= 4 and j % 2 == 0:
            emit_scatter_chunk(j // 2 - 2)
    emit_ffn(EM - 1)
    emit_ag2(NC2 - 1)
    while SCAT_PEND:
        SCAT_PEND.pop(0)()
    if debug:
        E("sync", lambda e: e.dma_start(out=dbg["dbg_agout2"], in_=agout2.ap()),
          inc="dbgs", inck=16, waits=[("dcc", V("dcc"))])
    emit_scatter_chunk(NC2 - 2)
    emit_scatter_chunk(NC2 - 1)
    v_dsl = V("dsl")
    if debug:
        E("sync", lambda e: e.dma_start(out=dbg["dbg_idxw"], in_=idxw[:]),
          inc="dbgs", inck=16, waits=[("dsl", v_dsl)])

    v_scat = V("dscat")

    if debug:
        E("sync", lambda e: e.dma_start(out=dbg["dbg_acc"], in_=acc_d),
          inc="dbgs", inck=16, waits=[("dscat", v_scat)])
        E("sync", lambda e: e.dma_start(out=dbg["dbg_x1"], in_=x1_d),
          inc="dbgs", inck=16, waits=[("dx1", b1_dx1)])
        E("sync", lambda e: e.dma_start(out=dbg["dbg_xln"], in_=xln_d),
          inc="dbgs", inck=16, waits=[("xs0", 16 * NTT // 2), ("xs1", 16 * NTT // 2)])
        E("sync", lambda e: e.dma_start(out=dbg["dbg_p1"], in_=agout1.ap()),
          inc="dbgs", inck=16, waits=[("dcc", T // 1024)])

    idxf = idxi[:, EM * CAPH + 1:EM * CAPH + 2]
    saf = [fv(F_SA, 1024), fv(F_SLOT, 1024)]
    sbf = [sB, fv(F_T1, 1024)]
    for ft in range(NFT):
        b = ft % 2
        vop(lambda e, ft=ft: e.tensor_scalar(idf, tkbase[:], float(ft * 128),
                                             None, ALU.add),
            waits=[("dxg", P.counts.get("dxg", 0))])
        vop(lambda e: e.tensor_tensor(idf, idf, iotac[:], ALU.add))
        vif = vop(lambda e: e.tensor_copy(idxf, idf))
        E("gpsimd", lambda e, b=b: e.indirect_dma_start(
            out=xt[b], out_offset=None,
            in_=x1_d, in_offset=IOX(idxf, 0)),
          inc="dxg", inck=16,
          waits=[("vec", vif), ("dx1", b1_dx1),
                 (f"xf{b}", P.counts.get(f"xf{b}", 0))])
        vxg = V("dxg")
        E("sync", lambda e, ft=ft, b=b: e.dma_start(
            out=saf[b], in_=acc_d[ft * 128:(ft + 1) * 128, :]),
          inc="dac", inck=16,
          waits=[("dscat", v_scat),
                 ("dout", 16 * (ft - 1) if ft >= 2 else 0)]
          + ([("tdc", V("tdc"))] if ft == 0 else []))
        vac = V("dac")
        vno = ln_emit(saf[b], sbf[b], extra_w=[("dac", vac)]
                      + ([("tdc", V("tdc"))] if ft == 0 else []),
                      dst_w=[("acu", P.counts.get("acu", 0)),
                             ("dscat", v_scat)])
        vfin = vop(lambda e, b=b: e.tensor_tensor(saf[b], xt[b], sbf[b],
                                                  ALU.add),
                   waits=[("dxg", vxg)])
        E("vector", lambda e: e.nop(), inc="acu", waits=[("vec", vfin)])
        E("vector", lambda e: e.nop(), inc=f"xf{b}", waits=[("vec", vfin)])
        E("sync", lambda e, ft=ft, b=b: e.dma_start(
            out=out_d[ft * 128:(ft + 1) * 128, :], in_=saf[b]),
          inc="dout", inck=16,
          waits=[("vec", vfin), ("dout", P.counts.get("dout", 0))])
    E("sync", lambda e: e.nop(), waits=[("dout", V("dout"))])

    P.realize()
    return nc


def _nzrow(AFR, base, h, QB):
    return AFR[h:h + 1, base:base + QB]


_NC_CACHE = {}


def _host_inputs(inputs, core):
    T, D, NH, NE = 4096, 1024, 8, 16
    b = core // 2
    half = core % 2
    hs = slice(half * NH * 64, (half + 1) * NH * 64)
    hh = slice(half * NH, (half + 1) * NH)
    es = slice(half * NE // 2, (half + 1) * NE // 2)
    SUB = 128 // NE
    l8 = np.zeros((128, 128), np.float32)
    for p in range(128):
        for m in range(128):
            if p // SUB == m // SUB and p % SUB < m % SUB:
                l8[p, m] = 1.0
    bc16 = np.zeros((NE, 128), np.float32)
    for e in range(NE):
        bc16[e, e * SUB:(e + 1) * SUB] = 1.0
    sel16 = bc16.T.copy()
    oh8 = np.zeros((SUB, NH * 64), np.float32)
    for h in range(NH):
        oh8[h, h * 64:(h + 1) * 64] = 1.0
    chp = np.concatenate([inputs["L0_attn_beta_p"][hh], inputs["L0_attn_iv_p"][hh],
                          inputs["L0_attn_ov_p"][hh], inputs["L0_attn_chi"][hh]])
    return {
        "x": np.ascontiguousarray(inputs["L0_attn_x"][b]),
        "wi": np.ascontiguousarray(inputs["L0_attn_Wi"][:, hs]),
        "wo": np.ascontiguousarray(inputs["L0_attn_Wo"][:, hs]),
        "wv": np.ascontiguousarray(inputs["L0_attn_Wv"][:, hs]),
        "wout": np.ascontiguousarray(inputs["L0_attn_Wout"][hs, :]),
        "wa": np.ascontiguousarray(inputs["L0_attn_Wa"][:, hh]),
        "we": np.ascontiguousarray(inputs["L0_attn_We"][:, hh]),
        "ba": np.ascontiguousarray(inputs["L0_attn_ba"][hh]).reshape(1, NH),
        "be": np.ascontiguousarray(inputs["L0_attn_be"][hh]).reshape(1, NH),
        "chp": np.ascontiguousarray(chp).reshape(1, 4 * NH),
        "rw1": np.ascontiguousarray(inputs["L0_ffn_router_w1"]),
        "rw2": np.ascontiguousarray(inputs["L0_ffn_router_w2"]),
        "w1": np.ascontiguousarray(inputs["L0_ffn_w1"][es]),
        "w2": np.ascontiguousarray(inputs["L0_ffn_w2"][es]),
        "w3": np.ascontiguousarray(inputs["L0_ffn_w3"][es]),
        "tkb": np.full((128, 1), half * (T // 2), np.float32),
        "myexp": np.arange(half * NE // 2, (half + 1) * NE // 2,
                           dtype=np.int32).reshape(-1, 1),
        "oh8": oh8, "l8": l8, "bc16": bc16, "sel16": sel16,
        "g128": np.kron(np.eye(NE, dtype=np.float32),
                        np.ones((128 // NE, 128 // NE), np.float32)),
        "idn": np.eye(128, dtype=np.float32),
        "rep8": np.repeat(np.eye(8, dtype=np.float32), 8, axis=1),
        "sub8": np.tile(np.arange(8, dtype=np.float32), 8).reshape(64, 1),
    }


def kernel(**inputs):
    key = "full"
    if key not in _NC_CACHE:
        _NC_CACHE[key] = build()
    nc = _NC_CACHE[key]
    in_maps = [_host_inputs(inputs, c) for c in range(8)]
    res = run_bass_kernel_spmd(nc, in_maps, core_ids=list(range(8)))
    out = np.empty((4, 4096, 1024), np.float32)
    for c in range(8):
        b, half = c // 2, c % 2
        out[b, half * 2048:(half + 1) * 2048, :] = res.results[c]["out"]
    return out


# version 18 (final)
# speedup vs baseline: 1.0640x; 1.0045x over previous
import math
import numpy as np
import concourse.bass as bass
import concourse.mybir as mybir
from concourse.bass_utils import run_bass_kernel_spmd

F32 = mybir.dt.float32
BF16 = mybir.dt.bfloat16
I32 = mybir.dt.int32
AF = mybir.ActivationFunctionType
ALU = mybir.AluOpType
IOX = bass.IndirectOffsetOnAxis

EPS = 1e-5
PI = math.pi

CFG = dict(T=4096, D=1024, NH=8, DH=64, NE=16, CAP=256, DFF=2048, RH=128,
           QB=512, BIS=48)


class Prog:
    ENGINES = ("sync", "gpsimd", "vector", "scalar", "tensor")

    def __init__(self, nc):
        self.nc = nc
        self.ops = {e: [] for e in self.ENGINES}
        self.sems = {}
        self.counts = {}

    def _sem(self, name):
        if name not in self.sems:
            self.sems[name] = self.nc.alloc_semaphore(name)
            self.counts[name] = 0

    def emit(self, eng, fn, inc=None, inck=1, waits=()):
        w = []
        for s, v in waits:
            if v > 0:
                self._sem(s)
                w.append((s, v))
        v = None
        if inc is not None:
            self._sem(inc)
            self.counts[inc] += inck
            v = self.counts[inc]
        self.ops[eng].append((fn, inc, inck, w))
        return v

    def val(self, name):
        return self.counts.get(name, 0)

    def realize(self):
        with self.nc.Block() as block:
            for eng_name in self.ENGINES:
                ops = self.ops[eng_name]
                if not ops:
                    continue

                def body(eng, ops=ops):
                    for fn, inc, inck, waits in ops:
                        for s, v in waits:
                            eng.wait_ge(self.sems[s], v)
                        instr = fn(eng) if fn is not None else None
                        if inc is not None:
                            instr.then_inc(self.sems[inc], inck)
                getattr(block, eng_name)(body)


DISABLE_B1_INJECT = False


def build(cfg=CFG, debug=False):
    T, D, NH, DH = cfg["T"], cfg["D"], cfg["NH"], cfg["DH"]
    NE, CAP, DFF, RH = cfg["NE"], cfg["CAP"], cfg["DFF"], cfg["RH"]
    QB, BIS = cfg["QB"], cfg["BIS"]
    NTT = T // 128
    NQB = T // QB
    NKC = T // 128
    NDC = D // 128
    DM = NH * DH
    NMC = DM // 128
    NFC = DFF // 128
    NQ4 = DFF // 512
    EM = NE // 2
    CAPH = CAP // 128
    TH = T // 2
    NFT = TH // 128
    SUB = 128 // NE
    CPS = T // SUB
    NTC = T // 128
    VX = DH + 2
    NQC = QB // 512
    NT2 = T // 512

    nc = bass.Bass("TRN2", target_bir_lowering=False)

    x_in = nc.dram_tensor("x", [T, D], F32, kind="ExternalInput").ap()
    wi_in = nc.dram_tensor("wi", [D, DM], F32, kind="ExternalInput").ap()
    wo_in = nc.dram_tensor("wo", [D, DM], F32, kind="ExternalInput").ap()
    wv_in = nc.dram_tensor("wv", [D, DM], F32, kind="ExternalInput").ap()
    wout_in = nc.dram_tensor("wout", [DM, D], F32, kind="ExternalInput").ap()
    wa_in = nc.dram_tensor("wa", [D, NH], F32, kind="ExternalInput").ap()
    we_in = nc.dram_tensor("we", [D, NH], F32, kind="ExternalInput").ap()
    ba_in = nc.dram_tensor("ba", [1, NH], F32, kind="ExternalInput").ap()
    be_in = nc.dram_tensor("be", [1, NH], F32, kind="ExternalInput").ap()
    chp_in = nc.dram_tensor("chp", [1, 4 * NH], F32, kind="ExternalInput").ap()
    rw1_in = nc.dram_tensor("rw1", [D, RH], F32, kind="ExternalInput").ap()
    rw2_in = nc.dram_tensor("rw2", [RH, NE], F32, kind="ExternalInput").ap()
    w1_in = nc.dram_tensor("w1", [EM, D, DFF], F32, kind="ExternalInput").ap()
    w2_in = nc.dram_tensor("w2", [EM, D, DFF], F32, kind="ExternalInput").ap()
    w3_in = nc.dram_tensor("w3", [EM, DFF, D], F32, kind="ExternalInput").ap()
    tkb_in = nc.dram_tensor("tkb", [128, 1], F32, kind="ExternalInput").ap()
    myexp_in = nc.dram_tensor("myexp", [EM, 1], I32, kind="ExternalInput").ap()
    oh8_in = nc.dram_tensor("oh8", [SUB, NH * 64], F32, kind="ExternalInput").ap()
    l8_in = nc.dram_tensor("l8", [128, 128], F32, kind="ExternalInput").ap()
    bc16_in = nc.dram_tensor("bc16", [NE, 128], F32, kind="ExternalInput").ap()
    sel16_in = nc.dram_tensor("sel16", [128, NE], F32, kind="ExternalInput").ap()
    g128_in = nc.dram_tensor("g128", [128, 128], F32, kind="ExternalInput").ap()
    idn_in = nc.dram_tensor("idn", [128, 128], F32, kind="ExternalInput").ap()
    rep8_in = nc.dram_tensor("rep8", [8, 64], F32, kind="ExternalInput").ap()
    sub8_in = nc.dram_tensor("sub8", [64, 1], F32, kind="ExternalInput").ap()
    out_d = nc.dram_tensor("out", [TH, D], F32, kind="ExternalOutput").ap()

    xln_d = nc.dram_tensor("xln_d", [T, D], BF16, kind="Internal").ap()
    agin1 = nc.dram_tensor("agin1", [T, D], BF16, kind="Internal")
    agout1 = nc.dram_tensor("agout1", [2 * T, D], BF16, kind="Internal")
    x1_d = nc.dram_tensor("x1_d", [T, D], F32, kind="Internal").ap()
    h_d = nc.dram_tensor("h_d", [T, D], BF16, kind="Internal").ap()
    a_d = nc.dram_tensor("a_d", [NH, T], F32, kind="Internal").ap()
    lgt_d = nc.dram_tensor("lgt_d", [NE, T], F32, kind="Internal").ap()
    gmx_d = nc.dram_tensor("gmx_d", [128, 1], F32, kind="Internal").ap()
    selcw_d = nc.dram_tensor("selcw_d", [3, NE, T], F32, kind="Internal").ap()
    agin2 = nc.dram_tensor("agin2", [EM * CAP, 1032], F32, kind="Internal")
    agout2 = nc.dram_tensor("agout2", [NE * CAP, 1032], F32, kind="Internal")
    acc_d = nc.dram_tensor("acc_d", [TH, D], F32, kind="Internal").ap()

    dbg = {}
    if debug:
        for nm, shp in (("dbg_conv", [128, 2048]), ("dbg_rby", [NH, 512]),
                        ("dbg_idxw", [128, 2 * EM * CAPH]),
                        ("dbg_xe", [128, 2048]), ("dbg_gv", [128, CAP]),
                        ("dbg_xln", [T, D]), ("dbg_p1", [2 * T, D]),
                        ("dbg_x1", [T, D]), ("dbg_probs", [128, CPS]),
                        ("dbg_sel", [128, CPS]), ("dbg_agout2", [NE * CAP, 1032]),
                        ("dbg_acc", [TH, D])):
            dt = BF16 if nm in ("dbg_xln", "dbg_p1", "dbg_xe", "dbg_gv") else F32
            if nm in ("dbg_idxw", "dbg_rby"):
                dt = F32
            if nm == "dbg_conv":
                dt = BF16
            dbg[nm] = nc.dram_tensor(nm, shp, dt, kind="ExternalOutput").ap()

    NB = 75264
    NF = 10520
    AB = nc.alloc_sbuf_tensor("AB", [128, NB], BF16)
    AFR = nc.alloc_sbuf_tensor("AFR", [128, NF], F32)

    def bv(off, n, p0=0, p1=128):
        return AB[p0:p1, off:off + n]

    def fv(off, n, p0=0, p1=128):
        return AFR[p0:p1, off:off + n]

    O_INNERT = 0
    O_OUTERT = O_INNERT + NMC * T
    O_VX = O_OUTERT + NMC * T
    O_W = O_VX + NTT * VX * NH
    O_XLT = O_W + 3 * NDC * 512
    O_WOUT = O_XLT + NDC * 512
    O_STG = O_WOUT + NMC * 1024
    assert O_STG + 1024 <= NB, O_STG
    NPH_ = NH // 2
    O_COMBT = O_W
    O_UT = O_COMBT + 2 * NPH_ * QB
    assert O_UT + 2 * (2 * QB) <= O_XLT
    O_CONV = O_XLT
    B_HT = 0
    B_R1 = B_HT + NDC * 512
    B_XE = B_R1 + 512
    B_XET = B_XE + CAPH * 1024
    B_W1 = B_XET + 2 * NDC * CAP
    B_W2 = B_W1 + 2 * NDC * 512
    B_W3 = B_W2 + 2 * NDC * 512
    B_RW1 = B_W3 + 2 * 4 * 1024
    B_RW2 = B_RW1 + NDC * RH
    B_PAB = B_RW2 + NE
    B_HS = B_PAB + 4 * 1024
    B_GV = B_HS + 1024
    assert B_GV + 256 <= NB, B_GV

    F_XT = 0
    F_SA = 2048
    F_SB = 3072
    F_T1 = 4096
    F_T2 = 5120
    F_T3 = 6144
    F_T4 = 7168
    F_T5 = 8192
    F_SLOT = 9216
    F_COLS = 10300
    assert F_COLS + 220 <= NF

    iotas = nc.alloc_sbuf_tensor("iotas", [128, CAP], F32)
    tokc = nc.alloc_sbuf_tensor("tokc", [128, NTC], F32)
    l8 = nc.alloc_sbuf_tensor("l8s", [128, 128], F32)
    idn = nc.alloc_sbuf_tensor("idns", [128, 128], F32)
    g128 = nc.alloc_sbuf_tensor("g128s", [128, 128], F32)
    rep8 = nc.alloc_sbuf_tensor("rep8s", [8, 64], F32)
    sub8 = nc.alloc_sbuf_tensor("sub8s", [64, 1], F32)
    idx64 = nc.alloc_sbuf_tensor("idx64s", [64, 1], F32)
    idx64i = nc.alloc_sbuf_tensor("idx64is", [64, 3], I32)
    myexpf = nc.alloc_sbuf_tensor("myexpfs", [8, 1], F32)
    pihalf = nc.alloc_sbuf_tensor("pihalfs", [1, 1], F32)
    bc16 = nc.alloc_sbuf_tensor("bc16s", [NE, 128], F32)
    sel16 = nc.alloc_sbuf_tensor("sel16s", [128, NE], F32)
    oh8 = nc.alloc_sbuf_tensor("oh8s", [SUB, NH * 64], F32)
    ones = nc.alloc_sbuf_tensor("oness", [1, 512], F32)
    tkbase = nc.alloc_sbuf_tensor("tkbs", [128, 1], F32)
    iotac = nc.alloc_sbuf_tensor("iotacs", [128, 1], F32)
    chpf = nc.alloc_sbuf_tensor("chpfs", [1, 4 * NH], F32)
    casa = nc.alloc_sbuf_tensor("casas", [1, 2 * NH], F32)
    cb64 = nc.alloc_sbuf_tensor("cb64s", [64, 2 * NH], F32)
    baf = nc.alloc_sbuf_tensor("bafs", [1, NH], F32)
    bef = nc.alloc_sbuf_tensor("befs", [1, NH], F32)
    wa_sb = nc.alloc_sbuf_tensor("wa_sbs", [128, NDC * NH], BF16)
    we_sb = nc.alloc_sbuf_tensor("we_sbs", [128, NDC * NH], BF16)
    myexp_sb = nc.alloc_sbuf_tensor("myexps", [EM, 1], I32)
    iotI = nc.alloc_sbuf_tensor("iotI", [128, CAP], I32)
    iotI2 = nc.alloc_sbuf_tensor("iotI2", [128, NTC], I32)
    idxi = nc.alloc_sbuf_tensor("idxi", [128, EM * CAPH + 4], I32)
    idxw = nc.alloc_sbuf_tensor("idxw", [128, EM * CAPH * 2], F32)
    sgm = nc.alloc_sbuf_tensor("sgms", [1, 4 * NH], F32)
    tctmp = nc.alloc_sbuf_tensor("tctmp", [1, 2 * NH], F32)
    estg = nc.alloc_sbuf_tensor("estgs", [128, 2 * NH], F32)
    a_stg = nc.alloc_sbuf_tensor("a_stgs", [NH, 512], F32)
    selTm = nc.alloc_sbuf_tensor("selTm", [128, 512], F32)
    ceTm = nc.alloc_sbuf_tensor("ceTm", [128, 512], F32)
    wTm = nc.alloc_sbuf_tensor("wTm", [128, 512], F32)
    pselb = nc.alloc_sbuf_tensor("pselb", [128, CAP], F32)
    rhsa = nc.alloc_sbuf_tensor("rhsa", [128, 2 * NTC], F32)
    g16 = nc.alloc_sbuf_tensor("g16", [NE, 24], F32)

    psA = nc.alloc_psum_tensor("psA", [128, 1024], F32)
    psB = nc.alloc_psum_tensor("psB", [128, 1024], F32)
    psC = nc.alloc_psum_tensor("psC", [128, 1024], F32)
    psD = nc.alloc_psum_tensor("psD", [128, 1024], F32)

    P = Prog(nc)
    E = P.emit
    V = P.val

    def vop(fn, waits=()):
        return E("vector", fn, inc="vec", waits=list(waits) + [("vec", V("vec"))])

    def sop(fn, waits=()):
        return E("scalar", fn, inc="scc", waits=list(waits) + [("scc", V("scc"))])

    mcol = fv(F_COLS + 0, 1)
    vcol = fv(F_COLS + 1, 1)
    m2col = fv(F_COLS + 2, 1)
    rstd = fv(F_COLS + 3, 1)
    sxc = fv(F_COLS + 4, 1)
    sxxc = fv(F_COLS + 5, 1)
    mxc = fv(F_COLS + 6, 1)
    gcol = fv(F_COLS + 7, 1)
    scol = fv(F_COLS + 8, 1)
    pfxc = fv(F_COLS + 9, 1)
    rcol = fv(F_COLS + 10, 1)
    vcol2 = fv(F_COLS + 11, 1)
    dcol = fv(F_COLS + 12, 1)
    idf = fv(F_COLS + 13, 1)
    idm = fv(F_COLS + 14, 1)
    a_sxc = fv(F_COLS + 40, 1)
    a_sxxc = fv(F_COLS + 41, 1)
    a_m = fv(F_COLS + 42, 1)
    a_m2 = fv(F_COLS + 43, 1)
    a_nm2 = fv(F_COLS + 44, 1)
    a_v = fv(F_COLS + 45, 1)
    a_r = fv(F_COLS + 46, 1)
    a_eps = fv(F_COLS + 47, 1)
    vlnc = [[fv(F_COLS + 50 + 8 * b + i, 1) for i in range(6)] for b in (0, 1)]

    for dst, src in ((baf[:], ba_in), (bef[:], be_in), (chpf[:], chp_in),
                     (tkbase[:], tkb_in), (oh8[:], oh8_in), (l8[:], l8_in),
                     (bc16[:], bc16_in), (sel16[:], sel16_in), (idn[:], idn_in),
                     (g128[:], g128_in),
                     (rep8[:], rep8_in), (sub8[:], sub8_in)):
        E("sync", lambda e, dst=dst, src=src: e.dma_start(out=dst, in_=src),
          inc="dsm", inck=16)
    E("gpsimd", lambda e: e.dma_start(
        out=wa_sb[:].rearrange("p (c n) -> p c n", c=NDC), in_=wa_in.rearrange("(c p) n -> p c n", p=128)),
      inc="dwA", inck=16)
    E("gpsimd", lambda e: e.dma_start(
        out=we_sb[:].rearrange("p (c n) -> p c n", c=NDC), in_=we_in.rearrange("(c p) n -> p c n", p=128)),
      inc="dwA", inck=16)
    E("sync", lambda e: e.dma_start(out=myexp_sb[:], in_=myexp_in),
      inc="dsm", inck=16)
    v_dsm = V("dsm")

    E("gpsimd", lambda e: e.iota(iotI[:], pattern=[[1, CAP]], base=0,
                                 channel_multiplier=0), inc="gpc")
    E("gpsimd", lambda e: e.tensor_copy(iotas[:], iotI[:]), inc="gpc",
      waits=[("gpc", V("gpc"))])
    E("gpsimd", lambda e: e.iota(iotI2[:], pattern=[[128, NTC]], base=0,
                                 channel_multiplier=1), inc="gpc",
      waits=[("gpc", V("gpc"))])
    E("gpsimd", lambda e: e.tensor_copy(tokc[:], iotI2[:]), inc="gpc",
      waits=[("gpc", V("gpc"))])
    E("gpsimd", lambda e: e.iota(iotI2[:, 0:1], pattern=[[1, 1]], base=0,
                                 channel_multiplier=1), inc="gpc",
      waits=[("gpc", V("gpc"))])
    E("gpsimd", lambda e: e.tensor_copy(iotac[:], iotI2[:, 0:1]), inc="gpc",
      waits=[("gpc", V("gpc"))])
    E("gpsimd", lambda e: e.memset(ones[:], 1.0), inc="gpc",
      waits=[("gpc", V("gpc"))])
    E("gpsimd", lambda e: e.memset(pihalf[:], float(PI / 2)), inc="gpc",
      waits=[("gpc", V("gpc"))])
    E("gpsimd", lambda e: e.memset(a_eps, float(1e-5)), inc="gpc",
      waits=[("gpc", V("gpc"))])
    v_gpc0 = V("gpc")

    sop(lambda e: e.activation(sgm[:, 0:3 * NH], chpf[:, 0:3 * NH], AF.Sigmoid),
        waits=[("dsm", v_dsm)])
    sop(lambda e: e.activation(sgm[:, 3 * NH:4 * NH], chpf[:, 3 * NH:4 * NH], AF.Tanh))
    sop(lambda e: e.activation(tctmp[:, NH:2 * NH], sgm[:, 0:NH], AF.Sin,
                               scale=float(PI)))
    sop(lambda e: e.activation(tctmp[:, 0:NH], sgm[:, 0:NH], AF.Sin,
                               scale=float(-PI), bias=pihalf[:]),
        waits=[("gpc", V("gpc"))])
    v_sc_ch = V("scc")
    vop(lambda e: e.tensor_mul(sgm[:, NH:2 * NH], sgm[:, NH:2 * NH],
                               sgm[:, 2 * NH:3 * NH]), waits=[("scc", v_sc_ch)])
    vop(lambda e: e.tensor_mul(sgm[:, NH:2 * NH], sgm[:, NH:2 * NH],
                               sgm[:, 3 * NH:4 * NH]))
    vop(lambda e: e.tensor_mul(casa[:, 0:NH], tctmp[:, 0:NH], sgm[:, NH:2 * NH]))
    v_casa = vop(lambda e: e.tensor_mul(casa[:, NH:2 * NH], tctmp[:, NH:2 * NH],
                                        sgm[:, NH:2 * NH]))
    E("tensor", lambda e: e.matmul(psB[0:64, 0:2 * NH], ones[0:1, 0:64], casa[:],
                                   start=True, stop=True),
      inc="tdc", waits=[("vec", v_casa), ("gpc", v_gpc0)])
    v_cb64 = vop(lambda e: e.tensor_copy(cb64[:], psB[0:64, 0:2 * NH]),
                 waits=[("tdc", V("tdc"))])

    wi_sb = bv(O_W, NDC * 512)
    wo_sb = bv(O_W + NDC * 512, NDC * 512)
    wv_sb = bv(O_W + 2 * NDC * 512, NDC * 512)
    wout_sb = bv(O_WOUT, NMC * 1024)
    v_dw = {}
    for nm, dst, src in (("wi", wi_sb, wi_in), ("wo", wo_sb, wo_in),
                         ("wv", wv_sb, wv_in)):
        E("gpsimd", lambda e, dst=dst, src=src: e.dma_start(
            out=dst.rearrange("p (c n) -> p c n", c=NDC), in_=src.rearrange("(c p) n -> p c n", p=128)),
          inc="d" + nm, inck=16)
        v_dw[nm] = ("d" + nm, V("d" + nm))
    v_dwA = V("dwA")

    xt = [fv(F_XT, 1024), fv(F_XT + 1024, 1024)]
    sA = fv(F_SA, 1024)
    sB = fv(F_SB, 1024)
    xstg = [bv(O_WOUT, 1024), bv(O_WOUT + 1024, 1024)]

    ln_war = [0]

    def ln_emit(src, dst, extra_w=(), dst_w=()):
        s0 = sop(lambda e: e.activation(sB, src, AF.Identity, accum_out=a_sxc),
                 waits=list(extra_w) + [("vec", ln_war[0]), ("gpc", v_gpc0)])
        sop(lambda e: e.activation(sB, src, AF.Square, accum_out=a_sxxc))
        sop(lambda e: e.activation(a_m, a_sxc, AF.Identity, scale=1.0 / 1024.0))
        sop(lambda e: e.activation(a_m2, a_m, AF.Square))
        sop(lambda e: e.activation(a_nm2, a_m2, AF.Identity, scale=-1.0))
        sop(lambda e: e.activation(a_v, a_sxxc, AF.Identity,
                                   scale=1.0 / 1024.0, bias=a_nm2))
        sop(lambda e: e.activation(a_r, a_v, AF.Ln, bias=a_eps))
        s3 = sop(lambda e: e.activation(a_r, a_r, AF.Exp, scale=-0.5))
        vn = vop(lambda e: e.tensor_scalar(dst, src, a_m, a_r,
                                           ALU.subtract, ALU.mult),
                 waits=list(dst_w) + [("scc", s3)])
        ln_war[0] = vn
        return vn

    def emit_a1_tile(t):
        b = t % 2
        E("sync", lambda e, t=t, b=b: e.dma_start(
            out=xt[b], in_=x_in[t * 128:(t + 1) * 128, :]),
          inc=f"dx{b}", inck=16,
          waits=[(f"xf{b}", P.counts.get(f"xf{b}", 0))])
        vl = V(f"dx{b}")
        vn = ln_emit(xt[b], xstg[b],
                     extra_w=[(f"dx{b}", vl)],
                     dst_w=[(f"xs{b}", P.counts.get(f"xs{b}", 0))])
        E("vector", lambda e: e.nop(), inc=f"xf{b}", waits=[("vec", vn)])
        E("gpsimd", lambda e, t=t, b=b: e.dma_start(
            out=xln_d[t * 128:(t + 1) * 128, :], in_=xstg[b]),
          inc=f"xs{b}", inck=16, waits=[("vec", vn)])

    xlt = [bv(O_XLT, NDC * 512), bv(O_WOUT + 2048, NDC * 512)]
    vx = bv(O_VX, NTT * VX * NH)
    innerT = bv(O_INNERT, NMC * T)
    outerT = bv(O_OUTERT, NMC * T)
    E("gpsimd", lambda e: e.memset(vx, 1.0), inc="gpc",
      waits=[("gpc", V("gpc"))])
    v_vxones = V("gpc")

    for t in range(8):
        emit_a1_tile(t)
    xlt_war = [0, 0]
    a2war = [0, 0, 0, 0]
    a2warv = [0, 0]
    sg_war = [0]
    ps_rot = [psC[:, 0:512], psC[:, 512:1024], psD[:, 0:512], psD[:, 512:1024]]

    def emit_xlt_trans(tc):
        tp = tc % 2
        for dc in range(NDC):
            E("sync", lambda e, tc=tc, dc=dc, tp=tp: e.dma_start_transpose(
                xlt[tp][:, dc * 512:(dc + 1) * 512],
                xln_d[tc * 512:(tc + 1) * 512, dc * 128:(dc + 1) * 128]),
              inc=f"tr{tp}", inck=16,
              waits=([("xs0", 16 * (2 * tc + 2)), ("xs1", 16 * (2 * tc + 2)),
                      ("tdc", xlt_war[tp])] if dc == 0 else []))

    emit_xlt_trans(0)
    es_war = [0, 0]
    for tc in range(NT2):
        tp = tc % 2
        vtr = V(f"tr{tp}")
        if tc + 1 < NT2:
            emit_xlt_trans(tc + 1)
        base_w = [(f"tr{tp}", vtr), ("dsm", v_dsm)]

        gi = 0
        for wnm, wsb, dstT in (("wi", wi_sb, innerT), ("wo", wo_sb, outerT)):
            for j in range(NMC):
                ps_o = ps_rot[gi % 4]
                for dc in range(NDC):
                    E("tensor", lambda e, wsb=wsb, j=j, dc=dc, tp=tp, ps_o=ps_o: e.matmul(
                        ps_o,
                        wsb[:, dc * 512 + j * 128:dc * 512 + (j + 1) * 128],
                        xlt[tp][:, dc * 512:(dc + 1) * 512],
                        start=(dc == 0), stop=(dc == NDC - 1)),
                      inc="tdc" if dc == NDC - 1 else None,
                      waits=(((base_w + [v_dw[wnm]]) if gi % 4 == 0 else [])
                             + [("vec", a2war[gi % 4])])
                      if dc == 0 else [])
                vmm = V("tdc")
                vop(lambda e, dstT=dstT, j=j, tc=tc, ps_o=ps_o: e.tensor_copy(
                    dstT[:, j * T + tc * 512:j * T + (tc + 1) * 512], ps_o),
                    waits=[("tdc", vmm)])
                a2war[gi % 4] = V("vec")
                if tc + 2 < NT2 and gi % 2 == 0:
                    emit_a1_tile(4 * (tc + 2) + gi // 2)
                gi += 1
        for ti in range(4):
            tt = tc * 4 + ti
            for dc in range(NDC):
                E("tensor", lambda e, ti=ti, dc=dc, tp=tp: e.matmul(
                    psB[:, 0:NH],
                    xlt[tp][:, dc * 512 + ti * 128:dc * 512 + (ti + 1) * 128],
                    we_sb[:, dc * NH:(dc + 1) * NH],
                    start=(dc == 0), stop=False),
                  waits=[("scc", sg_war[0]), ("dwA", v_dwA)] if dc == 0 else [])
            E("tensor", lambda e: e.matmul(psB[:, 0:NH], ones[0:1, 0:128],
                                           bef[:], start=False, stop=True),
              inc="tdc")
            vmm = V("tdc")
            ei = (ti % 2) * NH
            vsg = sop(lambda e, ei=ei: e.activation(
                estg[:, ei:ei + NH], psB[:, 0:NH], AF.Sigmoid),
                      waits=[("tdc", vmm), ("vec", es_war[ti % 2])])
            sg_war[0] = vsg
            vp = ti % 2
            for dc in range(NDC):
                E("tensor", lambda e, ti=ti, dc=dc, tp=tp, vp=vp: e.matmul(
                    psA[:, vp * 512:(vp + 1) * 512],
                    xlt[tp][:, dc * 512 + ti * 128:dc * 512 + (ti + 1) * 128],
                    wv_sb[:, dc * 512:(dc + 1) * 512],
                    start=(dc == 0), stop=(dc == NDC - 1)),
                  inc="tdc" if dc == NDC - 1 else None,
                  waits=[("vec", a2warv[vp]), ("gpc", v_vxones),
                         v_dw["wv"]] if dc == 0 else [])
            vmm2 = V("tdc")
            for h in range(NH):
                vop(lambda e, tt=tt, h=h, vp=vp, ei=ei: e.tensor_scalar(
                    vx[:, tt * VX * NH + h * VX:tt * VX * NH + h * VX + DH],
                    psA[:, vp * 512 + h * DH:vp * 512 + (h + 1) * DH],
                    estg[:, ei + h:ei + h + 1], None, ALU.mult),
                    waits=[("tdc", vmm2), ("scc", vsg)])
            vop(lambda e, tt=tt, ei=ei: e.tensor_copy(
                vx[:, tt * VX * NH:(tt + 1) * VX * NH].rearrange(
                    "p (h v) -> p h v", h=NH)[:, :, DH:DH + 1],
                estg[:, ei:ei + NH].rearrange("p (h o) -> p h o", h=NH)),
                waits=[("scc", vsg)])
            a2warv[vp] = V("vec")
            es_war[ti % 2] = V("vec")
        for dc in range(NDC):
            E("tensor", lambda e, dc=dc, tp=tp: e.matmul(
                psB[0:NH, 512:1024],
                wa_sb[:, dc * NH:(dc + 1) * NH],
                xlt[tp][:, dc * 512:(dc + 1) * 512],
                start=(dc == 0), stop=False),
              waits=[("scc", sg_war[0]), ("dwA", v_dwA),
                     ("das", P.counts.get("das", 0))] if dc == 0 else [])
        E("tensor", lambda e: e.matmul(psB[0:NH, 512:1024], baf[:], ones[:],
                                       start=False, stop=True), inc="tdc")
        xlt_war[tp] = V("tdc")
        vmm3 = V("tdc")
        vsa = sop(lambda e: e.activation(a_stg[:], psB[0:NH, 512:1024], AF.Sigmoid),
                  waits=[("tdc", vmm3)])
        sg_war[0] = vsa
        E("sync", lambda e, tc=tc: e.dma_start(
            out=a_d[:, tc * 512:(tc + 1) * 512], in_=a_stg[:]),
          inc="das", inck=16, waits=[("scc", vsa)])

    a2_tdc = V("tdc")
    E("gpsimd", lambda e: e.dma_start(
        out=wout_sb.rearrange("p (c n) -> p c n", c=NMC), in_=wout_in.rearrange("(c p) n -> p c n", p=128)),
      inc="dwA", inck=16, waits=[("tdc", a2_tdc)])
    v_dwout = V("dwA")
    a2_scc = V("scc")
    a2_vec = V("vec")
    a2_das = V("das")

    NPH = NH // 2
    stp = [psA, psB]
    Ut = [bv(O_UT, 2 * QB), bv(O_UT + 2 * QB, 2 * QB)]
    conv = [bv(O_CONV, NPH * QB), bv(O_CONV + NPH * QB, NPH * QB)]
    combT = [bv(O_COMBT, NPH * QB), bv(O_COMBT + NPH * QB, NPH * QB)]
    exstA = fv(F_T1, QB, 0, 2)
    exstB = fv(F_T1 + QB, QB, 0, 2)
    nzn = fv(F_T2, QB, 0, NH)
    nzz = fv(F_T3, QB, 0, NH)
    rby = fv(F_T4, QB, 0, NH)
    aq = fv(F_T5, QB, 0, NH)
    pstgb = [bv(O_STG, 512), bv(O_STG + 512, 512)]
    AGC = 1024
    NAG = T // AGC
    O_B1 = O_COMBT + 2 * NPH * QB + 2 * (2 * QB)
    pa_s = [bv(O_B1, 1024), bv(O_B1 + 1024, 1024)]
    pb_s = [bv(O_B1 + 2048, 1024), bv(O_B1 + 3072, 1024)]
    hstg = bv(O_B1 + 4096, 1024)
    assert O_B1 + 5120 <= O_XLT
    O_HT = O_STG + 1024
    assert O_HT + NDC * 512 <= NB
    hT = bv(O_HT, NDC * 512)
    r1T = nc.alloc_sbuf_tensor("r1Ts", [128, 256], BF16)[:, :]

    rw1_sb = bv(O_B1 + 5120, NDC * RH)
    rw2_sb = nc.alloc_sbuf_tensor("rw2s", [128, NE], BF16)[:, :]

    def ln_v_stats(src_ap, cols, extra_w=()):
        sx, sxx, m, vv, m2, rs = cols
        vop(lambda e: e.tensor_scalar(sB, src_ap, 1.0, 0.0, ALU.mult, ALU.add,
                                      accum_out=sx),
            waits=list(extra_w) + [("dlg", P.counts.get("dlg", 0))])
        vop(lambda e: e.scalar_tensor_tensor(sB, src_ap, 1.0, src_ap,
                                             ALU.mult, ALU.mult,
                                             accum_out=sxx))
        vop(lambda e: e.tensor_scalar(m, sx, 1.0 / 1024.0, None, ALU.mult))
        vop(lambda e: e.tensor_scalar(vv, sxx, 1.0 / 1024.0, None, ALU.mult))
        vop(lambda e: e.tensor_mul(m2, m, m))
        vop(lambda e: e.tensor_tensor(vv, vv, m2, ALU.subtract))
        return vop(lambda e: e.tensor_scalar(vv, vv, float(EPS), None, ALU.add))

    def ln_v_finish(src_ap, dst, cols, v0, extra_w=()):
        sx, sxx, m, vv, m2, rs = cols
        s2 = sop(lambda e: e.activation(rs, vv, AF.Ln), waits=[("vec", v0)])
        s3 = sop(lambda e: e.activation(rs, rs, AF.Exp, scale=-0.5))
        return vop(lambda e: e.tensor_scalar(dst, src_ap, m, rs,
                                             ALU.subtract, ALU.mult),
                   waits=list(extra_w) + [("scc", s3)])

    sAb = [sA, fv(F_SLOT, 1024)]
    _b1v = {}

    def emit_b1_part1(t):
        b = t % 2
        rowa, rowb, need = b1_src(t)
        E("sync", lambda e, rowa=rowa, b=b: e.dma_start(
            out=pa_s[b], in_=agout1.ap()[rowa:rowa + 128, :]),
          inc=f"pa{b}", inck=16,
          waits=[("dcc", need), (f"pf{b}", P.counts.get(f"pf{b}", 0))])
        E("sync", lambda e, rowb=rowb, b=b: e.dma_start(
            out=pb_s[b], in_=agout1.ap()[rowb:rowb + 128, :]),
          inc=f"pa{b}", inck=16)
        E("sync", lambda e, t=t, b=b: e.dma_start(
            out=xt[b], in_=x_in[t * 128:(t + 1) * 128, :]),
          inc=f"dx{b}", inck=16, waits=[(f"xf{b}", P.counts.get(f"xf{b}", 0))])
        vpa = V(f"pa{b}")
        vs = vop(lambda e, b=b: e.tensor_tensor(sAb[b], pa_s[b], pb_s[b],
                                                ALU.add),
                 waits=[(f"pa{b}", vpa), ("dx1", P.counts.get("dx1", 0))])
        E("vector", lambda e: e.nop(), inc=f"pf{b}", waits=[("vec", vs)])
        _b1v[t] = (ln_v_stats(sAb[b], vlnc[b]), V(f"dx{b}"))

    def emit_b1_part2(t):
        b = t % 2
        v0, vdx = _b1v[t]
        ln_v_finish(sAb[b], sB, vlnc[b], v0)
        vx1 = vop(lambda e, b=b: e.tensor_tensor(sAb[b], xt[b], sB, ALU.add),
                  waits=[(f"dx{b}", vdx)])
        E("vector", lambda e: e.nop(), inc=f"xf{b}", waits=[("vec", vx1)])
        E("sync", lambda e, t=t, b=b: e.dma_start(
            out=x1_d[t * 128:(t + 1) * 128, :], in_=sAb[b]),
          inc="dx1", inck=16,
          waits=[("vec", vx1), ("dx1", P.counts.get("dx1", 0))])
        _b1v[t] = ln_v_stats(sAb[b], vlnc[b])

    def emit_b1_part3(t):
        b = t % 2
        v0 = _b1v.pop(t)
        vh = ln_v_finish(sAb[b], hstg, vlnc[b], v0,
                         extra_w=[("dht", P.counts.get("dht", 0))])
        E("sync", lambda e, t=t: e.dma_start(
            out=h_d[t * 128:(t + 1) * 128, :], in_=hstg),
          inc="dht", inck=16,
          waits=[("vec", vh), ("dht", P.counts.get("dht", 0))])

    def emit_b1_tile(t):
        emit_b1_part1(t)
        emit_b1_part2(t)
        emit_b1_part3(t)

    def emit_router_trans(tc, half):
        need_dht = 4 * (tc + 1)
        for dc in range(4 * half, 4 * half + 4):
            E("sync", lambda e, tc=tc, dc=dc: e.dma_start_transpose(
                hT[:, dc * 512:(dc + 1) * 512],
                h_d[tc * 512:(tc + 1) * 512, dc * 128:(dc + 1) * 128]),
              inc="thx", inck=16,
              waits=([("dht", 16 * need_dht), ("tdc", V("tdc"))]
                     if dc == 4 * half else []))

    def emit_router_mm(tc):
        vth = V("thx")
        for dc in range(NDC):
            E("tensor", lambda e, dc=dc: e.matmul(
                psD[:, 512:1024], rw1_sb[:, dc * RH:(dc + 1) * RH],
                hT[:, dc * 512:(dc + 1) * 512],
                start=(dc == 0), stop=(dc == NDC - 1)),
              inc="tdc" if dc == NDC - 1 else None,
              waits=[("thx", vth), ("dwB", P.counts.get("dwB", 0)),
                     ("scc", V("scc")), ("rbu", P.counts.get("rbu", 0)),
                     ("woe", P.counts.get("woe", 0))] if dc == 0 else [])
        for hh in range(2):
            vsi = sop(lambda e, hh=hh: e.activation(
                r1T, psD[:, 512 + 256 * hh:768 + 256 * hh], AF.Sigmoid),
                      waits=[("tdc", V("tdc"))])
            vmu = vop(lambda e, hh=hh: e.tensor_tensor(
                r1T, r1T, psD[:, 512 + 256 * hh:768 + 256 * hh], ALU.mult),
                waits=[("scc", vsi)] + ([("vec", V("vec"))] if hh == 0 else []))
            E("tensor", lambda e, hh=hh: e.matmul(
                psD[0:NE, 512 + 256 * hh:768 + 256 * hh], rw2_sb, r1T,
                start=True, stop=True),
              inc="tdc", waits=[("vec", vmu)])
        vlg = V("tdc")
        vle = vop(lambda e: e.tensor_copy(fv(F_SB, 512, 0, NE), psD[0:NE, 512:1024]),
                  waits=[("tdc", vlg), ("dlg", P.counts.get("dlg", 0))])
        E("vector", lambda e: e.nop(), inc="rtl", waits=[("vec", vle)])
        E("sync", lambda e, tc=tc: e.dma_start(
            out=lgt_d[:, tc * 512:(tc + 1) * 512], in_=fv(F_SB, 512, 0, NE)),
          inc="dlg", inck=16, waits=[("vec", vle)])

    def emit_router_tc(tc):
        emit_router_trans(tc, 0)
        emit_router_trans(tc, 1)
        emit_router_mm(tc)

    E("gpsimd", lambda e: e.dma_start(
        out=rw1_sb.rearrange("p (c n) -> p c n", c=NDC),
        in_=rw1_in.rearrange("(c p) n -> p c n", p=128)),
      inc="dwB", inck=16, waits=[("tdc", a2_tdc)])
    E("gpsimd", lambda e: e.dma_start(out=rw2_sb, in_=rw2_in), inc="dwB", inck=16)

    inject = {}

    def add_inject(qb, p, kc, fn):
        inject.setdefault((qb, p, kc), []).append(fn)

    last_bcast_tdc = [0]

    def emit_chamber_step(qbp, h):
        j = h // 2
        pb_ = 64 * (h % 2)
        cbq = combT[qbp % 2]
        cvq = conv[qbp % 2]
        vb = E("tensor", lambda e, h=h: e.matmul(
            psD[0:64, 512:1024],
            oh8[:, h * 64:(h + 1) * 64],
            rby[:, 0:QB],
            start=True, stop=True),
          inc="tdc",
          waits=[("rch", P.counts.get("rch", 0)),
                 ("rbu", P.counts.get("rbu", 0)),
                 ("woe", P.counts.get("woe", 0)),
                 ("rtl", P.counts.get("rtl", 0))])
        last_bcast_tdc[0] = V("tdc")
        cm = fv(F_SB, 512, 0, 64)
        t1 = fv(F_SB + 512, 512, 0, 32)
        vcm = vop(lambda e, j=j, pb_=pb_, cvq=cvq: e.tensor_tensor(
            cm, cvq[pb_:pb_ + 64, j * QB:(j + 1) * QB],
            psD[0:64, 512:1024][:, 0:QB], ALU.mult),
            waits=[("tdc", vb), ("cve", NPH * (qbp + 1)),
                   ("dlg", P.counts.get("dlg", 0))])
        E("vector", lambda e: e.nop(), inc="rbu", waits=[("vec", vcm)])
        vop(lambda e, h=h: e.tensor_scalar(
            t1, cm[32:64, :], cb64[0:32, NH + h:NH + h + 1], None, ALU.mult),
            waits=[("vec", v_cb64)])
        vop(lambda e, h=h, j=j, pb_=pb_, cbq=cbq: e.scalar_tensor_tensor(
            cbq[pb_:pb_ + 32, j * QB:(j + 1) * QB],
            cm[0:32, :], cb64[0:32, h:h + 1], t1, ALU.mult, ALU.subtract))
        vop(lambda e, h=h: e.tensor_scalar(
            t1, cm[32:64, :], cb64[0:32, h:h + 1], None, ALU.mult))
        v4 = vop(lambda e, h=h, j=j, pb_=pb_, cbq=cbq: e.scalar_tensor_tensor(
            cbq[pb_ + 32:pb_ + 64, j * QB:(j + 1) * QB],
            cm[0:32, :], cb64[0:32, NH + h:NH + h + 1], t1, ALU.mult, ALU.add))
        E("vector", lambda e: e.nop(), inc="chd", waits=[("vec", v4)])

    def emit_wout_group(qbp, g):
        tt, ncol = g // 2, g % 2
        cbq = combT[qbp % 2]
        for dc in range(NMC):
            E("tensor", lambda e, tt=tt, ncol=ncol, dc=dc, cbq=cbq: e.matmul(
                psD[:, 512:1024],
                cbq[:, dc * QB + tt * 128:dc * QB + (tt + 1) * 128],
                wout_sb[:, dc * 1024 + ncol * 512:dc * 1024 + (ncol + 1) * 512],
                start=(dc == 0), stop=(dc == NMC - 1)),
              inc="tdc" if dc == NMC - 1 else None,
              waits=[("chd", P.counts.get("chd", 0)),
                     ("woe", P.counts.get("woe", 0)),
                     ("dwA", v_dwout),
                     ("rtl", P.counts.get("rtl", 0))] if dc == 0 else [])
        vwo = V("tdc")
        sbi = g % 2
        vev = E("vector", lambda e, sbi=sbi: e.tensor_copy(pstgb[sbi], psD[:, 512:1024]),
                inc="woe",
                waits=[("tdc", vwo), ("vec", V("vec")),
                       (f"ps{sbi}", P.counts.get(f"ps{sbi}", 0))])
        E("vector", lambda e: e.nop(), inc="vec", waits=[("woe", vev)])
        E("sync", lambda e, qbp=qbp, tt=tt, ncol=ncol, sbi=sbi: e.dma_start(
            out=agin1.ap()[qbp * QB + tt * 128:qbp * QB + (tt + 1) * 128,
                           ncol * 512:(ncol + 1) * 512],
            in_=pstgb[sbi]),
          inc=f"ps{sbi}", inck=16, waits=[("woe", vev)])

    def emit_ag1_rows(r0, nr, o0):
        vps = (V("ps0"), V("ps1"))
        E("gpsimd", lambda e, r0=r0, nr=nr, o0=o0: e.collective_compute(
            "AllGather", ALU.bypass,
            replica_groups=[[0, 1], [2, 3], [4, 5], [6, 7]],
            ins=[agin1.ap()[r0:r0 + nr, :].opt()],
            outs=[agout1.ap()[o0:o0 + 2 * nr, :].opt()]),
          inc="dcc", waits=[("ps0", vps[0]), ("ps1", vps[1])])

    def emit_ag1(c):
        emit_ag1_rows(c * AGC, AGC, c * 2 * AGC)

    def b1_src(t):
        TPC = AGC // 128
        if t < (NAG - 1) * TPC:
            cch = (t * 128) // AGC
            row = cch * 2 * AGC + (t * 128) % AGC
            return row, row + AGC, cch + 1
        SPL = AGC // 2
        off = t * 128 - (NAG - 1) * AGC
        half = off // SPL
        base = (NAG - 1) * 2 * AGC + half * AGC
        return base + off % SPL, base + SPL + off % SPL, NAG + half

    for qb in range(1, NQB):
        for h in range(NH):
            add_inject(qb, 1, 1 + (h * (NKC - 3)) // NH,
                       lambda qbp=qb - 1, h=h: emit_chamber_step(qbp, h))
        for g in range(2 * (QB // 128)):
            add_inject(qb, 2, 1 + (g * (NKC - 3)) // (2 * (QB // 128)),
                       lambda qbp=qb - 1, g=g: emit_wout_group(qbp, g))
    NSL = NKC - 4
    NS = 4 * NSL

    def slot_of(s):
        return (s // NSL, 2 + s % NSL)

    for c in range(NAG):
        qb_at = 2 * c + 2
        if qb_at < NQB:
            add_inject(qb_at, 3, 0, lambda c=c: emit_ag1(c))
        qb_b1 = 2 * c + 3
        if qb_b1 < NQB and not DISABLE_B1_INJECT:
            for i in range(8):
                t = c * 8 + i
                for k, fn in ((0, emit_b1_part1), (2, emit_b1_part2),
                              (4, emit_b1_part3)):
                    p_, kc_ = slot_of(6 * i + k)
                    add_inject(qb_b1, p_, kc_, lambda t=t, fn=fn: fn(t))
            for s, fn in ((min(26, NS - 1), lambda tc=2 * c: emit_router_trans(tc, 0)),
                          (min(28, NS - 1), lambda tc=2 * c: emit_router_trans(tc, 1)),
                          (min(34, NS - 1), lambda tc=2 * c: emit_router_mm(tc)),
                          (min(50, NS - 1), lambda tc=2 * c + 1: emit_router_trans(tc, 0)),
                          (min(54, NS - 1), lambda tc=2 * c + 1: emit_router_trans(tc, 1)),
                          (min(62, NS - 1), lambda tc=2 * c + 1: emit_router_mm(tc))):
                p_, kc_ = slot_of(s)
                add_inject(qb_b1, p_, kc_, fn)


    for qb in range(NQB):
        for p in range(NPH):
            def emit_S(kc):
                buf = kc % 2
                w0 = [(f"exp{buf}", P.counts.get(f"exp{buf}", 0))]
                if qb == 0 and p == 0 and kc < 2:
                    w0 += [("tdc", a2_tdc), ("vec", a2_vec), ("scc", a2_scc)]
                E("tensor", lambda e, p=p, kc=kc, qb=qb, buf=buf: e.matmul(
                    stp[buf][:, 0:512],
                    outerT[0:64, p * T + kc * 128:p * T + (kc + 1) * 128],
                    innerT[0:64, p * T + qb * QB:p * T + (qb + 1) * QB],
                    start=True, stop=True),
                  waits=w0)
                E("tensor", lambda e, p=p, kc=kc, qb=qb, buf=buf: e.matmul(
                    stp[buf][:, 512:1024],
                    outerT[64:128, p * T + kc * 128:p * T + (kc + 1) * 128],
                    innerT[64:128, p * T + qb * QB:p * T + (qb + 1) * QB],
                    start=True, stop=True),
                  inc="tdc")
                vst = V("tdc")
                E("scalar", lambda e, buf=buf: e.activation(
                    Ut[buf], stp[buf][:], AF.Exp, scale=0.125),
                  inc=f"exp{buf}",
                  waits=[("tdc", vst),
                         (f"pvd{buf}", P.counts.get(f"pvd{buf}", 0))])

            def emit_PVX(kc):
                buf = kc % 2
                vexp = V(f"exp{buf}")
                w1 = [(f"exp{buf}", vexp)]
                if kc == 0:
                    w1.append(("cve", P.counts.get("cve", 0)))
                E("tensor", lambda e, p=p, kc=kc, buf=buf: e.matmul(
                    psC[0:64, 0:512],
                    vx[:, kc * VX * NH + (2 * p) * VX:kc * VX * NH + (2 * p) * VX + DH],
                    Ut[buf][:, 0:512],
                    start=(kc == 0), stop=(kc == NKC - 1), tile_position=(0, 0)),
                  waits=w1)
                E("tensor", lambda e, p=p, kc=kc, buf=buf: e.matmul(
                    psC[64:128, 0:512],
                    vx[:, kc * VX * NH + (2 * p + 1) * VX:kc * VX * NH + (2 * p + 1) * VX + DH],
                    Ut[buf][:, 512:1024],
                    start=(kc == 0), stop=(kc == NKC - 1), tile_position=(0, 64),
                    skip_group_check=True))
                exm = 2 if kc == NKC - 1 else 64
                E("tensor", lambda e, p=p, kc=kc, buf=buf, exm=exm: e.matmul(
                    psC[0:exm, 512:1024],
                    vx[:, kc * VX * NH + (2 * p) * VX + DH:kc * VX * NH + (2 * p) * VX + DH + exm],
                    Ut[buf][:, 0:512],
                    start=(kc == 0), stop=(kc == NKC - 1), tile_position=(0, 0),
                    skip_group_check=True),
                  waits=[("exe", P.counts.get("exe", 0))] if kc == 0 else [])
                E("tensor", lambda e, p=p, kc=kc, buf=buf, exm=exm: e.matmul(
                    psD[64:64 + exm, 0:512],
                    vx[:, kc * VX * NH + (2 * p + 1) * VX + DH:kc * VX * NH + (2 * p + 1) * VX + DH + exm],
                    Ut[buf][:, 512:1024],
                    start=(kc == 0), stop=(kc == NKC - 1), tile_position=(0, 64),
                    skip_group_check=True),
                  inc=f"pvd{buf}")

            def do_inject(kc):
                for fn in inject.pop((qb, p, kc), []):
                    fn()

            do_inject(-1)
            if qb == 0 and p == 0:
                emit_S(0)
                emit_S(1)
            for kc in range(NKC):
                emit_PVX(kc)
                if kc + 2 < NKC:
                    emit_S(kc + 2)
                do_inject(kc)
            def prologue_kk(kk, nqb, np_):
                buf = kk % 2
                E("tensor", lambda e, p2=np_, kc=kk, qb2=nqb, buf=buf: e.matmul(
                    stp[buf][:, 0:512],
                    outerT[0:64, p2 * T + kc * 128:p2 * T + (kc + 1) * 128],
                    innerT[0:64, p2 * T + qb2 * QB:p2 * T + (qb2 + 1) * QB],
                    start=True, stop=True),
                  waits=[(f"exp{buf}", P.counts.get(f"exp{buf}", 0))])
                E("tensor", lambda e, p2=np_, kc=kk, qb2=nqb, buf=buf: e.matmul(
                    stp[buf][:, 512:1024],
                    outerT[64:128, p2 * T + kc * 128:p2 * T + (kc + 1) * 128],
                    innerT[64:128, p2 * T + qb2 * QB:p2 * T + (qb2 + 1) * QB],
                    start=True, stop=True),
                  inc="tdc")
                vst = V("tdc")
                E("scalar", lambda e, buf=buf: e.activation(
                    Ut[buf], stp[buf][:], AF.Exp, scale=0.125),
                  inc=f"exp{buf}",
                  waits=[("tdc", vst),
                         (f"pvd{buf}", P.counts.get(f"pvd{buf}", 0))])

            nqb, np_ = (qb, p + 1) if p + 1 < NPH else (qb + 1, 0)
            if nqb < NQB:
                prologue_kk(0, nqb, np_)
            vpv = V(f"pvd{(NKC - 1) % 2}")
            cw = [(f"pvd{(NKC - 1) % 2}", vpv)]
            if qb >= 2:
                cw.append(("rbu", NH * (qb - 1)))
            E("scalar", lambda e, p=p, qb=qb: e.activation(
                conv[qb % 2][:, p * QB:(p + 1) * QB], psC[0:128, 0:QB],
                AF.Identity),
              inc="cve", waits=cw)
            if nqb < NQB:
                prologue_kk(1, nqb, np_)
            E("scalar", lambda e: e.activation(exstA, psC[0:2, 512:512 + QB],
                                               AF.Identity),
              inc="exe", waits=[("dnzA", P.counts.get("dnzA", 0))])
            vexA = V("exe")
            E("scalar", lambda e: e.activation(exstB, psD[64:66, 0:QB],
                                               AF.Identity),
              inc="exe", waits=[("dnzB", P.counts.get("dnzB", 0))])
            vexB = V("exe")
            E("sync", lambda e, p=p: e.dma_start(out=_nzrow(AFR, F_T2, 2 * p, QB), in_=exstA[0:1, :]),
              inc="dnzA", inck=16,
              waits=[("exe", vexA), ("rch", P.counts.get("rch", 0))])
            E("sync", lambda e, p=p: e.dma_start(out=_nzrow(AFR, F_T3, 2 * p, QB), in_=exstA[1:2, :]),
              inc="dnzA", inck=16)
            E("sync", lambda e, p=p: e.dma_start(out=_nzrow(AFR, F_T2, 2 * p + 1, QB), in_=exstB[0:1, :]),
              inc="dnzB", inck=16,
              waits=[("exe", vexB), ("rch", P.counts.get("rch", 0))])
            E("sync", lambda e, p=p: e.dma_start(out=_nzrow(AFR, F_T3, 2 * p + 1, QB), in_=exstB[1:2, :]),
              inc="dnzB", inck=16)
        vnz_allA = V("dnzA")
        vnz_allB = V("dnzB")
        E("sync", lambda e, qb=qb: e.dma_start(
            out=aq, in_=a_d[:, qb * QB:(qb + 1) * QB]),
          inc="daq", inck=16, waits=[("das", a2_das), ("vec", V("vec"))])
        vdaq = V("daq")
        vop(lambda e: e.tensor_mul(rby, nzn, aq),
            waits=[("dnzA", vnz_allA), ("dnzB", vnz_allB), ("daq", vdaq),
                   ("tdc", last_bcast_tdc[0])])
        vop(lambda e: e.scalar_tensor_tensor(rby, nzz, 1e-8, rby, ALU.mult, ALU.add))
        vop(lambda e: e.reciprocal(rby, rby))
        vop(lambda e: e.tensor_mul(rby, rby, aq))
        E("vector", lambda e: e.nop(), inc="rch", waits=[("vec", V("vec"))])

    assert not inject, f"unfired injections: {list(inject.keys())}"
    SPL = AGC // 2
    emit_ag1_rows((NAG - 1) * AGC, SPL, (NAG - 1) * 2 * AGC)
    for h in range(NH):
        emit_chamber_step(NQB - 1, h)
    for g in range(2 * (QB // 128)):
        emit_wout_group(NQB - 1, g)
    for i in range(4):
        emit_b1_tile((NAG - 1) * 8 + i)
    emit_ag1_rows((NAG - 1) * AGC + SPL, SPL, (NAG - 1) * 2 * AGC + AGC)
    emit_router_tc(2 * (NAG - 1))
    for i in range(4, 8):
        emit_b1_tile((NAG - 1) * 8 + i)
    emit_router_tc(2 * (NAG - 1) + 1)
    b1_dht = V("dht")
    b1_dx1 = V("dx1")
    b_dlg = V("dlg")

    pbl = fv(F_T1, CPS)
    pbx = fv(F_T2, CPS)
    c1 = fv(F_T3, CPS)
    c2 = fv(F_T4, CPS)
    c3 = fv(F_T5, CPS)
    c4 = fv(F_T1 + 512, CPS)
    c5 = fv(F_T2 + 512, CPS)
    c6 = fv(F_T3 + 512, CPS)
    NCS = int(math.log2(CPS))
    locol = fv(F_COLS + 20, 1)
    hicol = fv(F_COLS + 21, 1)
    micol = fv(F_COLS + 22, 1)
    lo16 = g16[:, 0:1]
    hi16 = g16[:, 1:2]
    mi16 = g16[:, 2:3]
    ct16 = g16[:, 3:4]
    gg16 = g16[:, 4:5]
    t16a = g16[:, 5:6]
    t16b = g16[:, 6:7]
    v16 = g16[:, 7:8]
    s16 = g16[:, 8:9]
    r16 = g16[:, 9:10]
    d16 = g16[:, 10:11]

    E("sync", lambda e: e.dma_start(
        out=pbl, in_=lgt_d.rearrange("e (s c) -> (e s) c", s=SUB)),
      inc="dpb", inck=16, waits=[("dlg", b_dlg)])
    v_dpb = V("dpb")

    def gsum(src_col, dst16, extra=()):
        vt = E("tensor", lambda e: e.matmul(psB[0:NE, 512:513], sel16[:], src_col,
                                            start=True, stop=True),
               inc="tdc", waits=list(extra) + [("vec", V("vec")), ("g16u", P.counts.get("g16u", 0))])
        return vop(lambda e: e.tensor_copy(dst16, psB[0:NE, 512:513]),
                   waits=[("tdc", vt)])

    def bcast16(src16, dstcol, extra=()):
        vt = E("tensor", lambda e: e.matmul(psB[:, 513:514], bc16[:], src16,
                                            start=True, stop=True),
               inc="tdc", waits=list(extra) + [("vec", V("vec"))])
        return vop(lambda e: e.tensor_copy(dstcol, psB[:, 513:514]),
                   waits=[("tdc", vt)])

    vop(lambda e: e.tensor_reduce(mxc, pbl, mybir.AxisListType.X, ALU.max),
        waits=[("dpb", v_dpb)])
    vq = V("vec")
    E("sync", lambda e: e.dma_start(out=gmx_d, in_=mxc), inc="dgm", inck=16,
      waits=[("vec", vq)])
    E("sync", lambda e: e.dma_start(
        out=g16[:, 11:11 + SUB], in_=gmx_d.rearrange("(e s) o -> e (s o)", s=SUB)),
      inc="dgm", inck=16, waits=[("dgm", V("dgm"))])
    v_dgm = V("dgm")
    vop(lambda e: e.tensor_reduce(t16a, g16[:, 11:11 + SUB], mybir.AxisListType.X,
                                  ALU.max), waits=[("dgm", v_dgm)])
    bcast16(t16a, mxc)
    vop(lambda e: e.tensor_scalar(mxc, mxc, -1.0, None, ALU.mult))
    vexp0 = V("vec")
    sxp = sop(lambda e: e.activation(pbx, pbl, AF.Exp, bias=mxc, accum_out=scol),
              waits=[("vec", vexp0)])
    E("vector", lambda e: e.nop(), inc="vec", waits=[("scc", sxp)])
    gsum(scol, t16b)
    bcast16(t16b, dcol)
    vop(lambda e: e.reciprocal(dcol, dcol))
    vpr = vop(lambda e: e.tensor_scalar(pbx, pbx, dcol, None, ALU.mult))

    loc = fv(F_COLS + 30, 1)
    hic = fv(F_COLS + 31, 1)
    mic = fv(F_COLS + 32, 1)
    ggc = fv(F_COLS + 33, 1)
    tac = fv(F_COLS + 34, 1)
    E("gpsimd", lambda e: e.memset(loc, 0.0), inc="gpc", waits=[("vec", vpr)])
    E("gpsimd", lambda e: e.memset(hic, 1.0), inc="gpc", waits=[("gpc", V("gpc"))])
    vg0 = V("gpc")
    vop(lambda e: e.nop(), waits=[("gpc", vg0)])
    for it in range(BIS):
        vop(lambda e: e.tensor_tensor(mic, loc, hic, ALU.add))
        vop(lambda e: e.tensor_scalar(mic, mic, 0.5, None, ALU.mult))
        vcm_ = vop(lambda e: e.tensor_scalar(c1, pbx, mic, 0.0, ALU.is_gt, ALU.add,
                                             accum_out=scol))
        vmm_ = E("tensor", lambda e: e.matmul(psB[:, 512:513], g128[:], scol,
                                              start=True, stop=True),
                 inc="tdc", waits=[("vec", vcm_), ("g16u", P.counts.get("g16u", 0))])
        vop(lambda e: e.tensor_scalar(ggc, psB[:, 512:513], float(CAP) - 0.5, None,
                                      ALU.is_gt),
            waits=[("tdc", vmm_)])
        E("vector", lambda e: e.nop(), inc="g16u", waits=[("vec", V("vec"))])
        vop(lambda e: e.tensor_tensor(tac, mic, loc, ALU.subtract))
        vop(lambda e: e.tensor_mul(tac, tac, ggc))
        vop(lambda e: e.tensor_tensor(loc, loc, tac, ALU.add))
        vop(lambda e: e.tensor_tensor(tac, hic, mic, ALU.subtract))
        vop(lambda e: e.tensor_mul(tac, tac, ggc))
        vlast = vop(lambda e: e.tensor_tensor(hic, mic, tac, ALU.add))
    hicol = hic
    vop(lambda e: e.tensor_scalar(c1, pbx, hicol, None, ALU.is_gt))
    vop(lambda e: e.scalar_tensor_tensor(c1, c1, -9.0, pbx, ALU.mult, ALU.add))
    vop(lambda e: e.tensor_reduce(mxc, c1, mybir.AxisListType.X, ALU.max))
    vq2 = V("vec")
    E("sync", lambda e: e.dma_start(out=gmx_d, in_=mxc), inc="dgm", inck=16,
      waits=[("vec", vq2)])
    E("sync", lambda e: e.dma_start(
        out=g16[:, 11:11 + SUB], in_=gmx_d.rearrange("(e s) o -> e (s o)", s=SUB)),
      inc="dgm", inck=16, waits=[("dgm", V("dgm"))])
    v_dgm2 = V("dgm")
    vop(lambda e: e.tensor_reduce(v16, g16[:, 11:11 + SUB], mybir.AxisListType.X,
                                  ALU.max), waits=[("dgm", v_dgm2)])
    bcast16(v16, vcol2)
    vop(lambda e: e.tensor_scalar(c1, pbx, vcol2, 0.0, ALU.is_gt, ALU.add,
                                  accum_out=scol))
    gsum(scol, s16)
    vop(lambda e: e.tensor_scalar(c2, pbx, vcol2, None, ALU.is_equal))
    vop(lambda e: e.tensor_scalar(r16, s16, -1.0, float(CAP), ALU.mult, ALU.add))
    bcast16(r16, rcol)

    def cumsum_emit(src, tmp, prefix_col):
        cur, oth = src, tmp
        for k in range(NCS):
            s = 1 << k
            vop(lambda e, cur=cur, oth=oth, s=s: e.tensor_copy(oth[:, 0:s], cur[:, 0:s]))
            vop(lambda e, cur=cur, oth=oth, s=s: e.tensor_tensor(
                oth[:, s:CPS], cur[:, s:CPS], cur[:, 0:CPS - s], ALU.add))
            cur, oth = oth, cur
        vt = E("tensor", lambda e, cur=cur: e.matmul(
            psB[:, 514:515], l8[:], cur[:, CPS - 1:CPS], start=True, stop=True),
          inc="tdc", waits=[("vec", V("vec"))])
        vop(lambda e: e.tensor_copy(prefix_col, psB[:, 514:515]),
            waits=[("tdc", vt)])
        return cur

    vop(lambda e: e.tensor_copy(c3, c2))
    cq = cumsum_emit(c3, c4, pfxc)
    vop(lambda e, cq=cq: e.tensor_tensor(c5, cq, c2, ALU.subtract))
    vop(lambda e: e.tensor_scalar(c5, c5, pfxc, None, ALU.add))
    vop(lambda e: e.tensor_scalar(c5, c5, rcol, None, ALU.is_lt))
    vop(lambda e: e.tensor_mul(c5, c5, c2))
    vop(lambda e: e.tensor_tensor(c1, c1, c5, ALU.add))
    vop(lambda e: e.scalar_tensor_tensor(c2, c1, 1.0, pbx, ALU.mult, ALU.mult,
                                         accum_out=scol))
    gsum(scol, d16)
    vop(lambda e: e.tensor_scalar(d16, d16, 1e-9, None, ALU.add))
    bcast16(d16, dcol)
    vop(lambda e: e.reciprocal(dcol, dcol))
    vop(lambda e: e.tensor_scalar(c2, c2, dcol, None, ALU.mult))
    vop(lambda e: e.tensor_copy(c4, c1))
    cq2 = cumsum_emit(c4, c6, pfxc)
    vop(lambda e, cq2=cq2: e.tensor_tensor(c3, cq2, c1, ALU.subtract))
    vsel = vop(lambda e: e.tensor_scalar(c3, c3, pfxc, None, ALU.add))
    for k, src in ((0, c1), (1, c3), (2, c2)):
        E("sync", lambda e, k=k, src=src: e.dma_start(
            out=selcw_d[k].rearrange("e (s c) -> (e s) c", s=SUB), in_=src),
          inc="dsw", inck=16, waits=[("vec", vsel)])
    v_dsw = V("dsw")
    vmf = vop(lambda e: e.tensor_copy(myexpf[:], myexp_sb[:]),
              waits=[("dsm", v_dsm)])
    vmx = E("tensor", lambda e: e.matmul(psC[0:64, 64:65], rep8[:], myexpf[:],
                                         start=True, stop=True),
            inc="tdc", waits=[("vec", vmf)])
    vop(lambda e: e.tensor_copy(idx64[:], psC[0:64, 64:65]), waits=[("tdc", vmx)])
    vop(lambda e: e.tensor_scalar(idx64[:], idx64[:], float(SUB), None, ALU.mult))
    vop(lambda e: e.tensor_tensor(idx64[:], idx64[:], sub8[:], ALU.add))
    vi64 = None
    for ki in range(3):
        vop(lambda e, ki=ki: e.tensor_copy(idx64i[:, ki:ki + 1], idx64[:]))
        vi64 = vop(lambda e, ki=ki: e.tensor_scalar(
            idx64i[:, ki:ki + 1], idx64i[:, ki:ki + 1], ki * NE * SUB, None, ALU.add))
    c6z = fv(F_T3 + 512, 512)
    E("gpsimd", lambda e: e.memset(c6z, 0.0), inc="gpc",
      waits=[("dsw", v_dsw), ("gpc", V("gpc"))])
    vz = V("gpc")
    for ft in range(NFT):
        E("sync", lambda e, ft=ft: e.dma_start(
            out=acc_d[ft * 128:(ft + 1) * 128, 0:512], in_=c6z),
          inc="dza", inck=16, waits=[("gpc", vz)])
        E("sync", lambda e, ft=ft: e.dma_start(
            out=acc_d[ft * 128:(ft + 1) * 128, 512:1024], in_=c6z),
          inc="dza", inck=16)
    v_dza = V("dza")
    minepb = fv(F_T1, 3 * CPS, 0, 64)
    NCB = CPS // 128
    for ki in range(3):
        E("gpsimd", lambda e, ki=ki: e.indirect_dma_start(
            out=minepb[:, ki * CPS:(ki + 1) * CPS], out_offset=None,
            in_=selcw_d.rearrange("k e (s c) -> (k e s) c", s=SUB),
            in_offset=IOX(idx64i[:, ki:ki + 1], 0)),
          inc="dtm", inck=16,
          waits=[("vec", vi64), ("dsw", v_dsw), ("dza", v_dza)])
    v_dtm = V("dtm")
    for ki, dst in ((0, selTm), (1, ceTm), (2, wTm)):
        for cb in range(NCB):
            vt = E("tensor", lambda e, ki=ki, cb=cb: e.transpose(
                psC[:, 0:64],
                minepb[:, ki * CPS + cb * 128:ki * CPS + (cb + 1) * 128],
                idn[0:64, 0:64]),
              inc="tdc",
              waits=[("dtm", v_dtm), ("vec", V("vec")), ("dsm", v_dsm)])
            vop(lambda e, dst=dst, cb=cb: e.tensor_copy(
                dst[:, cb * 64:(cb + 1) * 64], psC[:, 0:64]),
                waits=[("tdc", vt)])
    v_tok = V("vec")

    if debug:
        E("sync", lambda e: e.dma_start(out=dbg["dbg_probs"], in_=pbx),
          inc="dbgs", inck=16, waits=[("vec", vsel)])
        E("sync", lambda e: e.dma_start(out=dbg["dbg_sel"], in_=c1),
          inc="dbgs", inck=16)

    xe = bv(B_XE, CAPH * 1024)
    xeT = [bv(B_XET, NDC * CAP), bv(B_XET + NDC * CAP, NDC * CAP)]
    w1q = [bv(B_W1, NDC * 512), bv(B_W1 + NDC * 512, NDC * 512)]
    w2q = [bv(B_W2, NDC * 512), bv(B_W2 + NDC * 512, NDC * 512)]
    w3q = [bv(B_W3, 4 * 1024), bv(B_W3 + 4 * 1024, 4 * 1024)]
    gv = bv(B_GV, CAP)
    slotb = fv(F_SLOT, 1032)
    gs_f = fv(F_SB, CAP)

    E("gpsimd", lambda e: e.memset(slotb[:, 1024:1032], 0.0), inc="gpc",
      waits=[("gpc", V("gpc"))])
    v_padz = V("gpc")

    _vxt_of = {}
    _pselb2 = [pselb[:], fv(F_SA, CAP)]
    _ps_war = [0, 0]
    exsb = fv(F_SB + 512, CAP, 0, 2)

    def emit_prep(j):
        exw = [("vec", v_tok)]
        vop(lambda e: e.tensor_copy(
            rhsa[:].rearrange("p (t two) -> p t two", two=2)[:, :, 0:1],
            tokc[:].rearrange("p (t one) -> p t one", one=1)),
            waits=exw + [("exr", P.counts.get("exr", 0))])
        for cb in range(NCB):
            vop(lambda e, cb=cb, j=j: e.tensor_copy(
                rhsa[:].rearrange("p (s ncb two) -> p s ncb two",
                                  ncb=NCB, two=2)[:, :, cb, 1:2],
                wTm[:, cb * 64 + SUB * j:cb * 64 + SUB * j + SUB].rearrange(
                    "p (s one) -> p s one", one=1)))
        vrh = V("vec")
        for tc in range(NTC):
            col = (tc % NCB) * 64 + SUB * j + tc // NCB
            pb = _pselb2[tc % 2]
            vps = vop(lambda e, col=col, pb=pb: e.tensor_scalar(
                pb, iotas[:], ceTm[:, col:col + 1],
                selTm[:, col:col + 1], ALU.is_equal, ALU.mult),
                waits=[("tdc", _ps_war[tc % 2]),
                       ("dx1", P.counts.get("dx1", 0))])
            E("tensor", lambda e, tc=tc, pb=pb: e.matmul(
                psB[0:2, 768:768 + CAP],
                rhsa[:, 2 * tc:2 * tc + 2],
                pb,
                start=(tc == 0), stop=(tc == NTC - 1)),
              inc="tdc",
              waits=[("vec", vps)] + ([("vec", vrh),
                     ("exu", P.counts.get("exu", 0))] if tc == 0 else []))
            _ps_war[tc % 2] = V("tdc")
        vexd = V("tdc")
        vio0 = vop(lambda e: e.tensor_copy(exsb, psB[0:2, 768:768 + CAP]),
                   waits=[("tdc", vexd)])
        for hc in range(CAPH):
            E("tensor", lambda e, hc=hc: e.matmul(
                psB[:, 770 + 2 * hc:772 + 2 * hc],
                exsb[:, hc * 128:(hc + 1) * 128],
                idn[0:2, 0:2],
                start=True, stop=True, skip_group_check=True),
              inc="tdc", waits=[("vec", vio0)] if hc == 0 else [])
        vtx = V("tdc")
        vio = vop(lambda e, j=j: e.tensor_copy(
            idxw[:, j * 2 * CAPH:j * 2 * CAPH + 2], psB[:, 770:772]),
            waits=[("tdc", vtx)])
        if CAPH > 1:
            vio = vop(lambda e, j=j: e.tensor_copy(
                idxw[:, j * 2 * CAPH + 2:j * 2 * CAPH + 4], psB[:, 772:774]))
        E("vector", lambda e: e.nop(), inc="exu", waits=[("vec", vio)])
        E("vector", lambda e: e.nop(), inc="exr", waits=[("vec", vio)])
        vii = vop(lambda e, j=j: e.tensor_copy(
            idxi[:, j * CAPH:(j + 1) * CAPH],
            idxw[:, j * 2 * CAPH:(j + 1) * 2 * CAPH].rearrange(
                "p (h w) -> p h w", h=CAPH)[:, :, 0:1]))
        E("vector", lambda e: e.nop(), inc=f"exj{j}", waits=[("vec", vii)])
        for hc in range(CAPH):
            E("gpsimd", lambda e, j=j, hc=hc: e.indirect_dma_start(
                out=xe[:, hc * 1024:(hc + 1) * 1024], out_offset=None,
                in_=h_d, in_offset=IOX(idxi[:, j * CAPH + hc:j * CAPH + hc + 1], 0)),
              inc="dga", inck=16,
              waits=[(f"exj{j}", 1), ("dht", b1_dht),
                     ("dxt0", P.counts.get("dxt0", 0)),
                     ("dxt1", P.counts.get("dxt1", 0)),
                     ("dbgs", P.counts.get("dbgs", 0))] if hc == 0 else [])
        vga = V("dga")
        xb = j % 2
        for hc in range(CAPH):
            for dc in range(NDC):
                E("sync", lambda e, hc=hc, dc=dc, xb=xb: e.dma_start_transpose(
                    xeT[xb][:, dc * CAP + hc * 128:dc * CAP + (hc + 1) * 128],
                    xe[:, hc * 1024 + dc * 128:hc * 1024 + (dc + 1) * 128]),
                  inc=f"dxt{xb}", inck=16,
                  waits=[("dga", vga), (f"xtu{xb}", P.counts.get(f"xtu{xb}", 0))]
                  if (hc == 0 and dc == 0) else [])
        _vxt_of[j] = V(f"dxt{xb}")
        if debug and j == 0:
            E("sync", lambda e: e.dma_start(out=dbg["dbg_xe"][:, 0:CAPH * 1024],
                                            in_=xe[:, 0:CAPH * 1024]),
              inc="dbgs", inck=16, waits=[("dga", vga)])

    def emit_ffn(j):
        xb = j % 2
        vxt = _vxt_of[j]
        for q4 in range(NQ4):
            wb = q4 % 2
            E("gpsimd", lambda e, j=j, q4=q4, wb=wb: e.dma_start(
            out=w1q[wb].rearrange("p (c n) -> p c n", c=NDC), in_=w1_in[j][:, q4 * 512:(q4 + 1) * 512]
                .rearrange("(c p) n -> p c n", p=128)),
              inc=f"dw1{wb}", inck=16,
              waits=[(f"wu{wb}", P.counts.get(f"wu{wb}", 0))])
            E("gpsimd", lambda e, j=j, q4=q4, wb=wb: e.dma_start(
                out=w2q[wb].rearrange("p (c n) -> p c n", c=NDC), in_=w2_in[j][:, q4 * 512:(q4 + 1) * 512]
                .rearrange("(c p) n -> p c n", p=128)),
              inc=f"dw1{wb}", inck=16)
            E("gpsimd", lambda e, j=j, q4=q4, wb=wb: e.dma_start(
                out=w3q[wb].rearrange("p (c n) -> p c n", c=4), in_=w3_in[j][q4 * 512:(q4 + 1) * 512, :]
                .rearrange("(c p) n -> p c n", p=128)),
              inc=f"dw1{wb}", inck=16)
            for _ in range(2):
                if SCAT_PEND:
                    SCAT_PEND.pop(0)()
            vw = V(f"dw1{wb}")
            for f4 in range(4):
                fc = q4 * 4 + f4
                for dc in range(NDC):
                    E("tensor", lambda e, wb=wb, f4=f4, dc=dc, xb=xb: e.matmul(
                        psC[:, 0:CAP],
                        w1q[wb][:, dc * 512 + f4 * 128:dc * 512 + (f4 + 1) * 128],
                        xeT[xb][:, dc * CAP:(dc + 1) * CAP],
                        start=(dc == 0), stop=(dc == NDC - 1)),
                      inc="tdc" if dc == NDC - 1 else None,
                      waits=[(f"dw1{wb}", vw), (f"dxt{xb}", vxt),
                             ("scc", V("scc"))] if dc == 0 else [])
                for dc in range(NDC):
                    E("tensor", lambda e, wb=wb, f4=f4, dc=dc, xb=xb: e.matmul(
                        psC[:, 512:512 + CAP],
                        w2q[wb][:, dc * 512 + f4 * 128:dc * 512 + (f4 + 1) * 128],
                        xeT[xb][:, dc * CAP:(dc + 1) * CAP],
                        start=(dc == 0), stop=(dc == NDC - 1)),
                      inc="tdc" if dc == NDC - 1 else None,
                      waits=[("vec", V("vec"))] if dc == 0 else [])
                vgv = V("tdc")
                ssl = sop(lambda e: e.activation(gs_f, psC[:, 0:CAP], AF.Silu),
                          waits=[("tdc", vgv), ("vec", V("vec"))])
                vgm = vop(lambda e: e.tensor_tensor(gv, gs_f, psC[:, 512:512 + CAP],
                                                    ALU.mult), waits=[("scc", ssl)])
                if debug and j == 0 and fc == 0:
                    E("sync", lambda e: e.dma_start(out=dbg["dbg_gv"], in_=gv),
                      inc="dbgs", inck=16, waits=[("vec", vgm)])
                    vgm = vop(lambda e: e.nop(),
                              waits=[("dbgs", P.counts.get("dbgs", 0))])
                for sc2 in range(CAPH):
                    ps_o = psD if sc2 == 0 else psA
                    for nc2 in range(2):
                        E("tensor", lambda e, sc2=sc2, nc2=nc2, f4=f4, wb=wb, ps_o=ps_o, fc=fc: e.matmul(
                            ps_o[:, nc2 * 512:(nc2 + 1) * 512],
                            gv[:, sc2 * 128:(sc2 + 1) * 128],
                            w3q[wb][:, f4 * 1024 + nc2 * 512:f4 * 1024 + (nc2 + 1) * 512],
                            start=(fc == 0), stop=(fc == NFC - 1)),
                          inc="tdc" if (sc2 == CAPH - 1 and nc2 == 1) else None,
                          waits=[("vec", vgm), ("oeu", P.counts.get("oeu", 0))]
                          if (sc2 == 0 and nc2 == 0 and fc == 0) else
                          ([("vec", vgm)] if (sc2 == 0 and nc2 == 0) else []))
            E("tensor", lambda e: e.nop(), inc=f"wu{wb}",
              waits=[("tdc", V("tdc"))])
        E("tensor", lambda e: e.nop(), inc=f"xtu{xb}",
          waits=[("tdc", V("tdc"))])
        vffn = V("tdc")
        for hc in range(CAPH):
            ps_o = psD if hc == 0 else psA
            vsl = vop(lambda e, j=j, hc=hc, ps_o=ps_o: e.tensor_scalar(
                slotb[:, 0:1024], ps_o[:, 0:1024],
                idxw[:, j * 2 * CAPH + hc * 2 + 1:j * 2 * CAPH + hc * 2 + 2],
                None, ALU.mult),
                waits=[("tdc", vffn), ("gpc", v_padz),
                       ("dx1", P.counts.get("dx1", 0)),
                       ("dsl", P.counts.get("dsl", 0))])
            vsl2 = vop(lambda e, j=j, hc=hc: e.tensor_copy(
                slotb[:, 1024:1025], idxw[:, j * 2 * CAPH + hc * 2:j * 2 * CAPH + hc * 2 + 1]))
            E("vector", lambda e: e.nop(), inc="oeu", waits=[("vec", vsl2)])
            E("sync", lambda e, j=j, hc=hc: e.dma_start(
                out=agin2.ap()[j * CAP + hc * 128:j * CAP + (hc + 1) * 128, :],
                in_=slotb),
              inc="dsl", inck=16, waits=[("vec", vsl2)])
    NC2 = EM // 2

    CPC = 2 * CAP

    def emit_ag2(c):
        vsl = V("dsl")
        E("gpsimd", lambda e, c=c: e.collective_compute(
            "AllGather", ALU.bypass,
            replica_groups=[[0, 1], [2, 3], [4, 5], [6, 7]],
            ins=[agin2.ap()[c * CPC:(c + 1) * CPC, :].opt()],
            outs=[agout2.ap()[c * 2 * CPC:(c + 1) * 2 * CPC, :].opt()]),
          inc="dcc", waits=[("dsl", vsl)])

    scb = [fv(F_T1, 1032), fv(F_T1 + 1032, 1032)]
    idsc = idxi[:, EM * CAPH:EM * CAPH + 1]
    _scat_war = [0, 0]

    def emit_scatter_group(c, gg):
        ngg = 2 * CPC // 128
        g = c * ngg + gg
        b = g % 2
        E("sync", lambda e, g=g, b=b: e.dma_start(
            out=scb[b], in_=agout2.ap()[g * 128:(g + 1) * 128, :]),
          inc=f"dsc{b}", inck=16,
          waits=[("dcc", NAG + 2 + c), ("dscat", _scat_war[b]),
                 ("dbgs", P.counts.get("dbgs", 0))])
        vld = V(f"dsc{b}")
        vop(lambda e, b=b: e.tensor_tensor(idf, scb[b][:, 1024:1025], tkbase[:],
                                           ALU.subtract),
            waits=[(f"dsc{b}", vld), ("dscat", P.counts.get("dscat", 0))])
        vop(lambda e: e.tensor_scalar(idm, idf, 0.0, None, ALU.is_lt))
        vop(lambda e: e.scalar_tensor_tensor(idf, idm, 1e6, idf, ALU.mult, ALU.add))
        vic = vop(lambda e: e.tensor_copy(idsc, idf))
        E("gpsimd", lambda e, b=b: e.indirect_dma_start(
            out=acc_d, out_offset=IOX(idsc, 0),
            in_=scb[b][:, 0:1024], in_offset=None,
            compute_op=ALU.add, bounds_check=TH - 1, oob_is_err=False),
          inc="dscat", inck=16,
          waits=[("vec", vic), ("dza", v_dza),
                 ("dscat", P.counts.get("dscat", 0))])
        _scat_war[b] = V("dscat")

    SCAT_PEND = []

    def emit_scatter_chunk(c):
        for gg in range(2 * CPC // 128):
            emit_scatter_group(c, gg)

    SCAT_INTERLEAVE = False
    for j in range(EM):
        emit_prep(j)
        if SCAT_INTERLEAVE and j >= 4 and j % 2 == 0:
            c = j // 2 - 2
            SCAT_PEND.extend(
                [lambda c=c, gg=gg: emit_scatter_group(c, gg)
                 for gg in range(2 * CPC // 128)])
        if j >= 1:
            emit_ffn(j - 1)
        if j >= 2 and j % 2 == 0:
            emit_ag2(j // 2 - 1)
        if not SCAT_INTERLEAVE and j >= 4 and j % 2 == 0:
            emit_scatter_chunk(j // 2 - 2)
    emit_ffn(EM - 1)
    emit_ag2(NC2 - 1)
    while SCAT_PEND:
        SCAT_PEND.pop(0)()
    if debug:
        E("sync", lambda e: e.dma_start(out=dbg["dbg_agout2"], in_=agout2.ap()),
          inc="dbgs", inck=16, waits=[("dcc", V("dcc"))])
    emit_scatter_chunk(NC2 - 2)
    emit_scatter_chunk(NC2 - 1)
    v_dsl = V("dsl")
    if debug:
        E("sync", lambda e: e.dma_start(out=dbg["dbg_idxw"], in_=idxw[:]),
          inc="dbgs", inck=16, waits=[("dsl", v_dsl)])

    v_scat = V("dscat")

    if debug:
        E("sync", lambda e: e.dma_start(out=dbg["dbg_acc"], in_=acc_d),
          inc="dbgs", inck=16, waits=[("dscat", v_scat)])
        E("sync", lambda e: e.dma_start(out=dbg["dbg_x1"], in_=x1_d),
          inc="dbgs", inck=16, waits=[("dx1", b1_dx1)])
        E("sync", lambda e: e.dma_start(out=dbg["dbg_xln"], in_=xln_d),
          inc="dbgs", inck=16, waits=[("xs0", 16 * NTT // 2), ("xs1", 16 * NTT // 2)])
        E("sync", lambda e: e.dma_start(out=dbg["dbg_p1"], in_=agout1.ap()),
          inc="dbgs", inck=16, waits=[("dcc", T // 1024)])

    idxf = idxi[:, EM * CAPH + 1:EM * CAPH + 2]
    saf = [fv(F_SA, 1024), fv(F_SLOT, 1024)]
    sbf = [sB, fv(F_T1, 1024)]
    for ft in range(NFT):
        b = ft % 2
        vop(lambda e, ft=ft: e.tensor_scalar(idf, tkbase[:], float(ft * 128),
                                             None, ALU.add),
            waits=[("dxg", P.counts.get("dxg", 0))])
        vop(lambda e: e.tensor_tensor(idf, idf, iotac[:], ALU.add))
        vif = vop(lambda e: e.tensor_copy(idxf, idf))
        E("gpsimd", lambda e, b=b: e.indirect_dma_start(
            out=xt[b], out_offset=None,
            in_=x1_d, in_offset=IOX(idxf, 0)),
          inc="dxg", inck=16,
          waits=[("vec", vif), ("dx1", b1_dx1),
                 (f"xf{b}", P.counts.get(f"xf{b}", 0))])
        vxg = V("dxg")
        E("sync", lambda e, ft=ft, b=b: e.dma_start(
            out=saf[b], in_=acc_d[ft * 128:(ft + 1) * 128, :]),
          inc="dac", inck=16,
          waits=[("dscat", v_scat),
                 ("dout", 16 * (ft - 1) if ft >= 2 else 0)]
          + ([("tdc", V("tdc"))] if ft == 0 else []))
        vac = V("dac")
        vno = ln_emit(saf[b], sbf[b], extra_w=[("dac", vac)]
                      + ([("tdc", V("tdc"))] if ft == 0 else []),
                      dst_w=[("acu", P.counts.get("acu", 0)),
                             ("dscat", v_scat)])
        vfin = vop(lambda e, b=b: e.tensor_tensor(saf[b], xt[b], sbf[b],
                                                  ALU.add),
                   waits=[("dxg", vxg)])
        E("vector", lambda e: e.nop(), inc="acu", waits=[("vec", vfin)])
        E("vector", lambda e: e.nop(), inc=f"xf{b}", waits=[("vec", vfin)])
        E("sync", lambda e, ft=ft, b=b: e.dma_start(
            out=out_d[ft * 128:(ft + 1) * 128, :], in_=saf[b]),
          inc="dout", inck=16,
          waits=[("vec", vfin), ("dout", P.counts.get("dout", 0))])
    E("sync", lambda e: e.nop(), waits=[("dout", V("dout"))])

    P.realize()
    return nc


def _nzrow(AFR, base, h, QB):
    return AFR[h:h + 1, base:base + QB]


_NC_CACHE = {}


def _host_inputs(inputs, core):
    T, D, NH, NE = 4096, 1024, 8, 16
    b = core // 2
    half = core % 2
    hs = slice(half * NH * 64, (half + 1) * NH * 64)
    hh = slice(half * NH, (half + 1) * NH)
    es = slice(half * NE // 2, (half + 1) * NE // 2)
    SUB = 128 // NE
    l8 = np.zeros((128, 128), np.float32)
    for p in range(128):
        for m in range(128):
            if p // SUB == m // SUB and p % SUB < m % SUB:
                l8[p, m] = 1.0
    bc16 = np.zeros((NE, 128), np.float32)
    for e in range(NE):
        bc16[e, e * SUB:(e + 1) * SUB] = 1.0
    sel16 = bc16.T.copy()
    oh8 = np.zeros((SUB, NH * 64), np.float32)
    for h in range(NH):
        oh8[h, h * 64:(h + 1) * 64] = 1.0
    chp = np.concatenate([inputs["L0_attn_beta_p"][hh], inputs["L0_attn_iv_p"][hh],
                          inputs["L0_attn_ov_p"][hh], inputs["L0_attn_chi"][hh]])
    return {
        "x": np.ascontiguousarray(inputs["L0_attn_x"][b]),
        "wi": np.ascontiguousarray(inputs["L0_attn_Wi"][:, hs]),
        "wo": np.ascontiguousarray(inputs["L0_attn_Wo"][:, hs]),
        "wv": np.ascontiguousarray(inputs["L0_attn_Wv"][:, hs]),
        "wout": np.ascontiguousarray(inputs["L0_attn_Wout"][hs, :]),
        "wa": np.ascontiguousarray(inputs["L0_attn_Wa"][:, hh]),
        "we": np.ascontiguousarray(inputs["L0_attn_We"][:, hh]),
        "ba": np.ascontiguousarray(inputs["L0_attn_ba"][hh]).reshape(1, NH),
        "be": np.ascontiguousarray(inputs["L0_attn_be"][hh]).reshape(1, NH),
        "chp": np.ascontiguousarray(chp).reshape(1, 4 * NH),
        "rw1": np.ascontiguousarray(inputs["L0_ffn_router_w1"]),
        "rw2": np.ascontiguousarray(inputs["L0_ffn_router_w2"]),
        "w1": np.ascontiguousarray(inputs["L0_ffn_w1"][es]),
        "w2": np.ascontiguousarray(inputs["L0_ffn_w2"][es]),
        "w3": np.ascontiguousarray(inputs["L0_ffn_w3"][es]),
        "tkb": np.full((128, 1), half * (T // 2), np.float32),
        "myexp": np.arange(half * NE // 2, (half + 1) * NE // 2,
                           dtype=np.int32).reshape(-1, 1),
        "oh8": oh8, "l8": l8, "bc16": bc16, "sel16": sel16,
        "g128": np.kron(np.eye(NE, dtype=np.float32),
                        np.ones((128 // NE, 128 // NE), np.float32)),
        "idn": np.eye(128, dtype=np.float32),
        "rep8": np.repeat(np.eye(8, dtype=np.float32), 8, axis=1),
        "sub8": np.tile(np.arange(8, dtype=np.float32), 8).reshape(64, 1),
    }


def kernel(**inputs):
    key = "full"
    if key not in _NC_CACHE:
        _NC_CACHE[key] = build()
    nc = _NC_CACHE[key]
    in_maps = [_host_inputs(inputs, c) for c in range(8)]
    res = run_bass_kernel_spmd(nc, in_maps, core_ids=list(range(8)))
    out = np.empty((4, 4096, 1024), np.float32)
    for c in range(8):
        b, half = c // 2, c % 2
        out[b, half * 2048:(half + 1) * 2048, :] = res.results[c]["out"]
    return out
